# version 01
# baseline (speedup 1.0000x reference)
import numpy as np
import ml_dtypes
import concourse.bass as bass
import concourse.mybir as mybir
from concourse.bass_utils import run_bass_kernel_spmd

F32 = mybir.dt.float32
BF16 = mybir.dt.bfloat16
I32 = mybir.dt.int32
AF = mybir.ActivationFunctionType
ALU = mybir.AluOpType
AX = mybir.AxisListType

NCORES = 8
C = 2048
H = 3
TC = C + H
BF = ml_dtypes.bfloat16
LN_EPS = 1e-5


def chunks_of(total, step=512):
    out, o = [], 0
    while o < total:
        out.append((o, min(step, total - o)))
        o += step
    return out


TCC = chunks_of(TC)
CC = chunks_of(C)


class Prog:
    def __init__(self, name):
        self.name = name
        self.nc = bass.Bass("TRN2", target_bir_lowering=False, debug=False)
        self.phases = []

    def inp(self, name, shape, dtype=F32):
        return self.nc.dram_tensor(name, list(shape), dtype, kind="ExternalInput").ap()

    def out(self, name, shape, dtype=F32):
        return self.nc.dram_tensor(name, list(shape), dtype, kind="ExternalOutput").ap()

    def dram(self, name, shape, dtype=F32):
        return self.nc.dram_tensor(name, list(shape), dtype).ap()

    def sbuf(self, name, shape, dtype=F32, side=None):
        kw = dict(side=side) if side else {}
        ctx = self.nc.sbuf_tensor(name, list(shape), dtype, **kw)
        t = ctx.__enter__()
        self._sbuf_ctx = getattr(self, "_sbuf_ctx", {})
        self._sbuf_ctx[id(t)] = ctx
        return t

    def free(self, *tiles):
        for t in reversed(tiles):
            self._sbuf_ctx.pop(id(t)).__exit__(None, None, None)

    def psum(self, name, shape, dtype=F32):
        return self.nc.psum_tensor(name, list(shape), dtype).__enter__()

    def phase(self, engine, fn, n_dma=0):
        self.phases.append((engine, fn, n_dma))

    def dma(self, engine, transfers, allow_slow=False):
        def fn(eng):
            kw = dict(allow_slow_non_contiguous=True) if allow_slow else {}
            return [eng.dma_start(out=o, in_=i, **kw) for (o, i) in transfers]
        self.phase(engine, fn, n_dma=len(transfers))

    def finalize(self):
        nc = self.nc
        gs = nc.semaphore("gs").__enter__()
        cs_sems = {e: nc.semaphore("cs_" + e).__enter__() for e in ("vector", "scalar", "gpsimd")}
        targets, tot = [], 0
        for (_, _, n_dma) in self.phases:
            targets.append(tot)
            tot += 16 * n_dma if n_dma else 1
        final_total = tot
        block = nc.Block().__enter__()

        GUARded = ("vector", "scalar", "gpsimd")
        WRAP = {"tensor_tensor", "tensor_scalar", "tensor_single_scalar", "tensor_scalar_add",
                "tensor_scalar_sub", "tensor_scalar_mul", "tensor_scalar_max", "tensor_scalar_min",
                "tensor_add", "tensor_sub", "tensor_mul", "tensor_max", "tensor_copy", "memset",
                "memzero", "scalar_tensor_tensor", "tensor_tensor_scan", "tensor_reduce",
                "reduce_max", "reduce_sum", "tensor_relu", "reciprocal", "select", "activation",
                "copy", "mul", "add", "square", "sign", "sqrt", "pool", "pool_max", "pool_avg",
                "iota", "transpose", "bn_stats", "bn_aggr", "stream_shuffle", "copy_predicated"}

        class GEng:
            def __init__(self, eng, sem):
                self._e = eng
                self._sem = sem
                self._n = 0

            def __getattr__(self, name):
                f = getattr(self._e, name)
                if name in WRAP:
                    def wrapped(*a, **kw):
                        inst = f(*a, **kw)
                        self._n += 1
                        inst.then_inc(self._sem, 1)
                        self._e.wait_ge(self._sem, self._n)
                        return inst
                    return wrapped
                return f

        def runner(engine_name):
            def run(eng):
                geng = GEng(eng, cs_sems[engine_name]) if engine_name in GUARded else eng
                for idx, (en, fn, n_dma) in enumerate(self.phases):
                    if en != engine_name:
                        continue
                    if targets[idx] > 0:
                        eng.wait_ge(gs, targets[idx])
                    last = fn(geng)
                    if n_dma:
                        for inst in last:
                            inst.then_inc(gs, 16)
                    elif engine_name == "tensor":
                        if last is None:
                            eng.sem_inc(gs, 1)
                        else:
                            last.then_inc(gs, 1)
                    else:
                        eng.sem_inc(gs, 1)
                if engine_name == "gpsimd":
                    eng.wait_ge(gs, final_total)
            return run

        @block.sync
        def _(eng):
            runner("sync")(eng)

        @block.scalar
        def _(eng):
            runner("scalar")(eng)

        @block.vector
        def _(eng):
            runner("vector")(eng)

        @block.tensor
        def _(eng):
            runner("tensor")(eng)

        @block.gpsimd
        def _(eng):
            runner("gpsimd")(eng)

        block.__exit__(None, None, None)
        return nc



class Common:
    def __init__(self, prog, width=TC):
        self.prog = prog
        self.width = width
        self.ones = prog.sbuf("cm_ones", [128, 1], F32)
        self.ones_row = prog.sbuf("cm_onesr", [1, 128], F32)
        self.sq = prog.sbuf("cm_sq", [128, 512], F32)
        self.stat_ps = prog.psum("cm_statps", [1, 512])
        self.bc_ps = prog.psum("cm_bcps", [128, 512])
        self.bc2_ps = prog.psum("cm_bc2ps", [128, 512])
        self.mm_ps = [prog.psum("cm_mmps0", [128, 512]), prog.psum("cm_mmps1", [128, 512])]
        self.bcs1 = prog.sbuf("cm_bcs1", [128, 512], F32)
        self.bcs2 = prog.sbuf("cm_bcs2", [128, 512], F32)
        self.mu = prog.sbuf("cm_mu", [1, width], F32)
        self.rs = prog.sbuf("cm_rs", [1, width], F32)
        self.tp = prog.sbuf("cm_tp", [1, width], F32)

        def init_ones(eng):
            eng.memset(self.ones[:, :], 0.0)
            eng.tensor_scalar_add(self.ones[:, :], self.ones[:, :], 1.0)
            eng.memset(self.ones_row[:, :], 0.0)
            return eng.tensor_scalar_add(self.ones_row[:, :], self.ones_row[:, :], 1.0)
        prog.phase("vector", init_ones)


def emit_ln(prog, cm, name, src_tiles, dst_tiles, width, gamma, beta):
    ntiles = len(src_tiles)
    D = 128 * ntiles
    mu, rs, tp = cm.mu, cm.rs, cm.tp
    cks = chunks_of(width)

    for (o, w) in cks:
        def mm(eng, o=o, w=w):
            last = None
            for k in range(ntiles):
                last = eng.matmul(cm.stat_ps[0:1, 0:w], cm.ones[:, 0:1],
                                  src_tiles[k][:, o:o + w], start=(k == 0), stop=(k == ntiles - 1))
            return last
        prog.phase("tensor", mm)

        def ev(eng, o=o, w=w):
            return eng.mul(mu[0:1, o:o + w], cm.stat_ps[0:1, 0:w], 1.0 / D)
        prog.phase("scalar", ev)

    for (o, w) in cks:
        for k in range(ntiles):
            def sq(eng, o=o, w=w, k=k):
                return eng.activation(cm.sq[:, 0:w], src_tiles[k][:, o:o + w], AF.Square)
            prog.phase("scalar", sq)

            def mm2(eng, o=o, w=w, k=k):
                return eng.matmul(cm.stat_ps[0:1, 0:w], cm.ones[:, 0:1], cm.sq[:, 0:w],
                                  start=(k == 0), stop=(k == ntiles - 1))
            prog.phase("tensor", mm2)

        def ev2(eng, o=o, w=w):
            return eng.mul(rs[0:1, o:o + w], cm.stat_ps[0:1, 0:w], 1.0 / D)
        prog.phase("scalar", ev2)

    def ph_v(eng):
        eng.tensor_tensor(tp[0:1, 0:width], mu[0:1, 0:width], mu[0:1, 0:width], ALU.mult)
        eng.tensor_tensor(rs[0:1, 0:width], rs[0:1, 0:width], tp[0:1, 0:width], ALU.subtract)
        return eng.tensor_scalar_add(rs[0:1, 0:width], rs[0:1, 0:width], LN_EPS)
    prog.phase("vector", ph_v)

    def ph_l(eng):
        eng.activation(rs[0:1, 0:width], rs[0:1, 0:width], AF.Ln)
        return eng.activation(rs[0:1, 0:width], rs[0:1, 0:width], AF.Exp, scale=-0.5)
    prog.phase("scalar", ph_l)

    for (o, w) in cks:
        def bc(eng, o=o, w=w):
            eng.matmul(cm.bc_ps[:, 0:w], cm.ones_row[0:1, :], mu[0:1, o:o + w], start=True, stop=True)
            return eng.matmul(cm.bc2_ps[:, 0:w], cm.ones_row[0:1, :], rs[0:1, o:o + w], start=True, stop=True)
        prog.phase("tensor", bc)

        def bce(eng, o=o, w=w):
            eng.copy(cm.bcs1[:, 0:w], cm.bc_ps[:, 0:w])
            return eng.copy(cm.bcs2[:, 0:w], cm.bc2_ps[:, 0:w])
        prog.phase("scalar", bce)

        def ap(eng, o=o, w=w):
            last = None
            for k in range(ntiles):
                g = 1.0 if gamma is None else gamma[k]
                b = 0.0 if beta is None else beta[k]
                eng.tensor_tensor(cm.sq[:, 0:w], src_tiles[k][:, o:o + w], cm.bcs1[:, 0:w], ALU.subtract)
                eng.scalar_tensor_tensor(cm.sq[:, 0:w], cm.sq[:, 0:w], g, cm.bcs2[:, 0:w], ALU.mult, ALU.mult)
                last = eng.tensor_scalar_add(dst_tiles[k][:, o:o + w], cm.sq[:, 0:w], b)
            return last
        prog.phase("vector", ap)
    return mu, rs


def emit_matmul(prog, cm, name, lhsT_tiles, rhs_tiles, out_cb, M_tiles, K_tiles, width,
                n_step=512, accum_sbuf=None):
    cks = chunks_of(width, n_step)
    pi = 0
    for m in range(M_tiles):
        for (o, w) in cks:
            ps = cm.mm_ps[pi % 2]
            pi += 1

            def mm(eng, m=m, o=o, w=w, ps=ps):
                last = None
                for k in range(K_tiles):
                    last = eng.matmul(ps[:, 0:w], lhsT_tiles(k, m), rhs_tiles(k)[:, o:o + w],
                                      start=(k == 0), stop=(k == K_tiles - 1))
                return last
            prog.phase("tensor", mm)
            out_cb(prog, m, o, w, ps)



def shard_tokens(x_t):
    outs = []
    d = x_t.shape[0]
    for r in range(NCORES):
        lo = r * C
        halo = x_t[:, lo - H:lo] if r > 0 else x_t[:, 0:H]
        outs.append(np.ascontiguousarray(np.concatenate([halo, x_t[:, lo:lo + C]], axis=1)))
    return outs


_prog_cache = {}


def run_prog(key, builder, in_maps, trace=False):
    if key not in _prog_cache:
        p = builder()
        nc = p.finalize()
        _prog_cache[key] = (p, nc)
    p, nc = _prog_cache[key]
    res = run_bass_kernel_spmd(nc, in_maps, core_ids=list(range(NCORES)))
    return res.results



def emit_mod_vectors(prog, cm, cT_in, wmod_in, tag=""):
    nc = prog.nc
    c8 = prog.sbuf("mv_c8" + tag, [128, 8], F32)
    sc8 = prog.sbuf("mv_sc8" + tag, [128, 8], F32)
    wbuf = prog.sbuf("mv_wbuf" + tag, [128, 4 * 128], F32)
    modx = prog.sbuf("mv_modx" + tag, [128, 64], F32)
    mps = cm.mm_ps[0]

    prog.dma("sync", [(c8[:, :], cT_in)])

    def silu_c(eng):
        return eng.activation(sc8[:, :], c8[:, :], AF.Silu)
    prog.phase("scalar", silu_c)


    for m in range(48):
        for hf in range(2):
            prog.dma("sync", [(wbuf[:, k * 128:(k + 1) * 128],
                               wmod_in[(4 * hf + k) * 128:(4 * hf + k + 1) * 128,
                                       m * 128:(m + 1) * 128]) for k in range(4)])

            def mm(eng, m=m, hf=hf):
                last = None
                for k in range(4):
                    last = eng.matmul(mps[:, 0:1], wbuf[:, k * 128:(k + 1) * 128],
                                      sc8[:, 4 * hf + k:4 * hf + k + 1],
                                      start=(hf == 0 and k == 0), stop=(hf == 1 and k == 3))
                return last
            prog.phase("tensor", mm)

        def ev(eng, m=m):
            return eng.copy(modx[:, m:m + 1], mps[:, 0:1])
        prog.phase("scalar", ev)

    def fix(eng):
        for base in (16, 40):
            sl = modx[:, base:base + 8]
            eng.tensor_scalar(cm.sq[:, 0:8], sl, 0.0, None, ALU.is_ge)
            eng.tensor_scalar_mul(cm.sq[:, 0:8], cm.sq[:, 0:8], 2.0)
            eng.tensor_scalar_add(cm.sq[:, 0:8], cm.sq[:, 0:8], -1.0)
            eng.tensor_scalar_mul(cm.sq[:, 8:16], sl, -1.0)
            eng.tensor_tensor(cm.sq[:, 8:16], cm.sq[:, 8:16], sl, ALU.max)
            eng.tensor_scalar_max(cm.sq[:, 8:16], cm.sq[:, 8:16], 0.25)
            eng.tensor_tensor(sl, cm.sq[:, 0:8], cm.sq[:, 8:16], ALU.mult)
        eng.tensor_scalar_add(modx[:, 48:56], modx[:, 8:16], 1.0)
        return eng.tensor_scalar_add(modx[:, 56:64], modx[:, 32:40], 1.0)
    prog.phase("vector", fix)
    return modx


def build_l0a():
    p = Prog("l0a")
    xT = p.inp("xT", [1024, TC])
    hmask = p.inp("hmask", [128, H])
    cT8 = p.inp("cT8", [128, 8])
    wmod = p.inp("wmod", [1024, 6144])
    lng8 = p.inp("lng8", [128, 8])
    lnb8 = p.inp("lnb8", [128, 8])
    w_inT = p.inp("w_inT", [1024, 4096])
    convw = p.inp("convw", [128, 64])
    convb = p.inp("convb", [128, 16])
    BmT = p.inp("BmT", [2048, 16])
    AT = p.inp("AT", [16, 16])

    S0_o = p.out("S0", [16, C])
    hdbg_o = p.out("hdbg", [1024, TC])
    modx_o = p.out("modx", [128, 64])
    xndbg_o = p.out("xndbg", [1024, TC])
    xc_o = p.out("xc", [2048, C])
    z_o = p.out("z", [2048, C])

    cm = Common(p)
    x_t = [p.sbuf(f"x{k}", [128, TC]) for k in range(8)]
    xn_t = [p.sbuf(f"xn{k}", [128, TC], BF16) for k in range(8)]
    cwb = p.sbuf("cwb", [128, 64], F32)
    cbb = p.sbuf("cbb", [128, 16], F32)
    hmk = p.sbuf("hmk", [128, H], F32)
    bmb = p.sbuf("bmb", [128, 16 * 16], F32)
    wkb = [p.sbuf(f"wkb{i}", [128, 8 * 128], BF16) for i in range(2)]
    xp = p.sbuf("xp", [128, TC], F32)
    cxs = p.sbuf("cxs", [128, C], F32)
    V = p.sbuf("V", [16, TC], F32)
    v_ps = p.psum("v_ps", [16, 512])
    Sb = [p.sbuf("Sb0", [16, C]), p.sbuf("Sb1", [16, C])]
    Pk = p.sbuf("Pk", [16, 16 * 11], F32)
    pq_ps = p.psum("pq_ps", [16, 16])
    ptmp = p.dram("ptmp", [16, 16])

    lngb = p.sbuf("lngb", [128, 16], F32)
    prog_loads = [(x_t[k][:, :], xT[k * 128:(k + 1) * 128, :]) for k in range(8)]
    prog_loads += [(cwb[:, :], convw), (cbb[:, :], convb), (hmk[:, :], hmask),
                   (lngb[:, 0:8], lng8), (lngb[:, 8:16], lnb8)]
    p.dma("sync", prog_loads)
    p.dma("sync", [(bmb[:, k * 16:(k + 1) * 16], BmT[k * 128:(k + 1) * 128, :]) for k in range(16)])
    p.dma("sync", [(Pk[:, 0:16], AT)])

    modx = emit_mod_vectors(p, cm, cT8, wmod)

    mu1, rs1 = emit_ln(p, cm, "ln1", x_t, x_t, TC,
            gamma=[modx[:, 48 + k:49 + k] for k in range(8)],
            beta=[modx[:, 0 + k:1 + k] for k in range(8)])
    st_o = p.out("st_dbg", [2, TC])
    p.dma("gpsimd", [(st_o[0:1, :], mu1[0:1, :]), (st_o[1:2, :], rs1[0:1, :])])
    p.dma("gpsimd", [(hdbg_o[k * 128:(k + 1) * 128, :], x_t[k][:, :]) for k in range(8)])
    emit_ln(p, cm, "ln2", x_t, xn_t, TC,
            gamma=[lngb[:, k:k + 1] for k in range(8)],
            beta=[lngb[:, 8 + k:9 + k] for k in range(8)])

    p.dma("gpsimd", [(modx_o, modx[:, :])])
    p.dma("gpsimd", [(xndbg_o[k * 128:(k + 1) * 128, :], xn_t[k][:, :]) for k in range(8)])

    for m in range(32):
        wb = wkb[m % 2]
        p.dma("gpsimd", [(wb[:, k * 128:(k + 1) * 128],
                          w_inT[k * 128:(k + 1) * 128, m * 128:(m + 1) * 128]) for k in range(8)])
        for (o, w) in TCC:
            ps = cm.mm_ps[0]

            def mm(eng, m=m, o=o, w=w, ps=ps, wb=wb):
                last = None
                for k in range(8):
                    last = eng.matmul(ps[:, 0:w], wb[:, k * 128:(k + 1) * 128],
                                      xn_t[k][:, o:o + w], start=(k == 0), stop=(k == 7))
                return last
            p.phase("tensor", mm)

            def ev(eng, m=m, o=o, w=w, ps=ps):
                return eng.copy(xp[:, o:o + w], ps[:, 0:w])
            p.phase("scalar", ev)

        if m < 16:
            def mask(eng, m=m):
                return eng.tensor_tensor(xp[:, 0:H], xp[:, 0:H], hmk[:, :], ALU.mult)
            p.phase("vector", mask)

            for ci, (o, w) in enumerate(TCC):
                def vmm2(eng, m=m, o=o, w=w):
                    return eng.matmul(v_ps[:, 0:w], bmb[:, m * 16:(m + 1) * 16], xp[:, o:o + w],
                                      start=True, stop=True)
                p.phase("tensor", vmm2)

                def vev(eng, o=o, w=w):
                    return eng.copy(cm.bcs1[0:16, 0:w], v_ps[:, 0:w])
                p.phase("scalar", vev)
                if m == 0:
                    def vadd(eng, o=o, w=w):
                        return eng.tensor_copy(V[:, o:o + w], cm.bcs1[0:16, 0:w])
                else:
                    def vadd(eng, o=o, w=w):
                        return eng.tensor_tensor(V[:, o:o + w], V[:, o:o + w], cm.bcs1[0:16, 0:w], ALU.add)
                p.phase("vector", vadd)

            def conv(eng, m=m):
                eng.tensor_scalar_mul(cxs[:, 0:C], xp[:, 0:C], cwb[:, 4 * m:4 * m + 1])
                for j in range(1, 4):
                    eng.scalar_tensor_tensor(cxs[:, 0:C], xp[:, j:j + C], cwb[:, 4 * m + j:4 * m + j + 1],
                                             cxs[:, 0:C], ALU.mult, ALU.add)
                return None
            p.phase("vector", conv)

            def sil(eng, m=m):
                return eng.activation(cxs[:, 0:C], cxs[:, 0:C], AF.Silu, bias=cbb[:, m:m + 1])
            p.phase("scalar", sil)
            p.dma("gpsimd", [(xc_o[m * 128:(m + 1) * 128, :], cxs[:, 0:C])])
        else:
            p.dma("gpsimd", [(z_o[(m - 16) * 128:(m - 15) * 128, :], xp[:, H:TC])])

    for k in range(10):
        src = Pk[:, 16 * k:16 * k + 16]
        dst = Pk[:, 16 * (k + 1):16 * (k + 1) + 16]
        p.dma("sync", [(ptmp[:, :], src)])
        p.dma("sync", [(cm.sq[0:16, 0:16], ptmp.rearrange("a b -> b a"))], allow_slow=True)

        def sq(eng, k=k, src=src):
            return eng.matmul(pq_ps[:, :], cm.sq[0:16, 0:16], src, start=True, stop=True)
        p.phase("tensor", sq)

        def ev(eng, dst=dst):
            return eng.copy(dst, pq_ps[:, :])
        p.phase("scalar", ev)

    def initS(eng):
        return eng.tensor_copy(Sb[0][:, :], V[:, H:TC])
    p.phase("vector", initS)
    cur = 0
    for k in range(11):
        d = 1 << k
        src_t, dst_t = Sb[cur], Sb[1 - cur]
        def cpre(eng, d=d, src_t=src_t, dst_t=dst_t):
            return eng.tensor_copy(dst_t[:, 0:d], src_t[:, 0:d])
        p.phase("vector", cpre)
        for (o, w) in chunks_of(C):
            lo = max(o, d)
            hi = o + w
            if lo >= hi:
                continue
            ww = hi - lo

            def mmk(eng, k=k, lo=lo, ww=ww, src_t=src_t):
                return eng.matmul(cm.mm_ps[0][0:16, 0:ww], Pk[:, 16 * k:16 * k + 16],
                                  src_t[:, lo - (1 << k):lo - (1 << k) + ww],
                                  start=True, stop=True)
            p.phase("tensor", mmk)

            def cpk(eng, ww=ww):
                return eng.copy(cm.bcs2[0:16, 0:ww], cm.mm_ps[0][0:16, 0:ww])
            p.phase("scalar", cpk)

            def addk(eng, lo=lo, ww=ww, src_t=src_t, dst_t=dst_t):
                return eng.tensor_tensor(dst_t[:, lo:lo + ww], src_t[:, lo:lo + ww],
                                         cm.bcs2[0:16, 0:ww], ALU.add)
            p.phase("vector", addk)
        cur = 1 - cur
    p.dma("gpsimd", [(S0_o, Sb[cur][:, :])])
    return p



def _r8(v):
    return np.ascontiguousarray(v.reshape(8, 128).T)


def _r16(v):
    return np.ascontiguousarray(v.reshape(16, 128).T)


def l0a_inputs(I):
    x2d = np.ascontiguousarray(I["L0_attn_x"][0].T)
    shards = shard_tokens(x2d)
    wmod = np.ascontiguousarray(I["L0_wmod"])
    cT8 = _r8(I["L0_c"][0])
    lng8 = _r8(I["L0_attn_ln_gamma"])
    lnb8 = _r8(I["L0_attn_ln_beta"])
    w_inT = np.ascontiguousarray(I["L0_attn_W_in"].T)
    cw = I["L0_attn_conv_w"].reshape(2048, 4)
    convw = np.ascontiguousarray(cw.reshape(16, 128, 4).transpose(1, 0, 2).reshape(128, 64))
    convb = _r16(I["L0_attn_conv_b"])
    BmT = np.ascontiguousarray(I["L0_attn_Bm"].T)
    AT = np.ascontiguousarray(I["L0_attn_A"].T)
    maps = []
    for r in range(NCORES):
        hm = np.zeros((128, H), np.float32) if r == 0 else np.ones((128, H), np.float32)
        maps.append(dict(xT=shards[r], hmask=hm, cT8=cT8, wmod=wmod, lng8=lng8,
                         lnb8=lnb8, w_inT=w_inT, convw=convw, convb=convb, BmT=BmT, AT=AT))
    return maps


def emit_l0scan(p, cm, name, Vap, Pk, out_sb):
    Sb = out_sb

    def initS(eng):
        return eng.tensor_copy(Sb[0][:, :], Vap)
    p.phase("vector", initS)
    cur = 0
    for k in range(11):
        d = 1 << k
        src_t, dst_t = Sb[cur], Sb[1 - cur]

        def cpre(eng, d=d, src_t=src_t, dst_t=dst_t):
            return eng.tensor_copy(dst_t[:, 0:d], src_t[:, 0:d])
        p.phase("vector", cpre)
        for (o, w) in chunks_of(C):
            lo = max(o, d)
            hi = o + w
            if lo >= hi:
                continue
            ww = hi - lo

            def mmk(eng, k=k, lo=lo, ww=ww, src_t=src_t, d=d):
                return eng.matmul(cm.mm_ps[0][0:16, 0:ww], Pk[:, 16 * k:16 * k + 16],
                                  src_t[:, lo - d:lo - d + ww], start=True, stop=True)
            p.phase("tensor", mmk)

            def cpk(eng, ww=ww):
                return eng.copy(cm.bcs2[0:16, 0:ww], cm.mm_ps[0][0:16, 0:ww])
            p.phase("scalar", cpk)

            def addk(eng, lo=lo, ww=ww, src_t=src_t, dst_t=dst_t):
                return eng.tensor_tensor(dst_t[:, lo:lo + ww], src_t[:, lo:lo + ww],
                                         cm.bcs2[0:16, 0:ww], ALU.add)
            p.phase("vector", addk)
        cur = 1 - cur
    return Sb[cur]


def emit_powers(p, cm, Pk, nsq, ps):
    ptmp = p.dram("ptmp_" + str(nsq), [16, 16])
    for k in range(nsq):
        src = Pk[:, 16 * k:16 * k + 16]
        dst = Pk[:, 16 * (k + 1):16 * (k + 1) + 16]
        p.dma("sync", [(ptmp[:, :], src)])
        p.dma("sync", [(cm.sq[0:16, 0:16], ptmp.rearrange("a b -> b a"))], allow_slow=True)

        def sq(eng, src=src):
            return eng.matmul(ps[:, 0:16], cm.sq[0:16, 0:16], src, start=True, stop=True)
        p.phase("tensor", sq)

        def ev(eng, dst=dst):
            return eng.copy(dst, ps[:, 0:16])
        p.phase("scalar", ev)


def build_l0b():
    p = Prog("l0b")
    xT = p.inp("xT", [1024, C])
    S0_i = p.inp("S0", [16, C])
    IordT = p.inp("IordT", [16, 8])
    AT = p.inp("AT", [16, 16])
    Cm_i = p.inp("Cm", [16, 2048])
    woT = p.inp("woT", [2048, 1024])
    cT8 = p.inp("cT8", [128, 8])
    wmod = p.inp("wmod", [1024, 6144])
    xc_i = p.inp("xc", [2048, C])
    z_i = p.inp("z", [2048, C])
    h_i = p.inp("hsp", [1024, TC])
    x1_o = p.out("x1out", [1024, C])

    cm = Common(p)
    x_t = [p.sbuf(f"x{k}", [128, C]) for k in range(8)]
    p.dma("sync", [(x_t[k][:, :], xT[k * 128:(k + 1) * 128, :]) for k in range(8)])
    modx = emit_mod_vectors(p, cm, cT8, wmod)

    Pk = p.sbuf("Pk", [16, 16 * 12], F32)
    pq_ps = p.psum("pq_ps", [16, 16])
    p.dma("sync", [(Pk[:, 0:16], AT)])
    emit_powers(p, cm, Pk, 11, pq_ps)
    P2048 = Pk[:, 16 * 11:16 * 12]
    Wp = p.sbuf("Wp", [16, 16 * 8], F32)
    iord = p.sbuf("iord", [16, 8], F32)
    p.dma("sync", [(Wp[:, 0:16], AT), (iord[:, :], IordT)])
    wtmp = p.dram("wtmp", [16, 16])
    for j in range(7):
        src = Wp[:, 16 * j:16 * j + 16]
        dst = Wp[:, 16 * (j + 1):16 * (j + 1) + 16]
        p.dma("sync", [(wtmp[:, :], src)])
        p.dma("sync", [(cm.sq[0:16, 0:16], wtmp.rearrange("a b -> b a"))], allow_slow=True)

        def mmw(eng, src=src):
            return eng.matmul(pq_ps[:, 0:16], cm.sq[0:16, 0:16], P2048, start=True, stop=True)
        p.phase("tensor", mmw)

        def evw(eng, dst=dst):
            return eng.copy(dst, pq_ps[:, 0:16])
        p.phase("scalar", evw)

    ha_ps = p.psum("ha_ps", [1, 16])
    harow = p.sbuf("harow", [1, 16], F32)

    def mmha(eng):
        last = None
        for j in range(8):
            last = eng.matmul(ha_ps[0:1, 0:16], iord[:, j:j + 1], Wp[:, 16 * j:16 * j + 16],
                              start=(j == 0), stop=(j == 7))
        return last
    p.phase("tensor", mmha)

    def evha(eng):
        return eng.copy(harow[0:1, 0:16], ha_ps[0:1, 0:16])
    p.phase("scalar", evha)

    hacol = p.dram("hacol", [16, 1])
    V2 = p.sbuf("V2", [16, C], F32)

    def clr(eng):
        return eng.memset(V2[:, :], 0.0)
    p.phase("vector", clr)
    hatmp = p.dram("hatmp", [1, 16])
    p.dma("sync", [(hatmp[:, :], harow[0:1, :])])
    p.dma("sync", [(V2[:, 0:1], hatmp.rearrange("a b -> b a"))], allow_slow=True)

    Sb = [p.sbuf("Sb0", [16, C]), p.sbuf("Sb1", [16, C])]
    cfin = emit_l0scan(p, cm, "corr", V2[:, :], Pk, Sb)

    for (o, w) in CC:
        p.dma("sync", [(cm.bcs1[0:16, 0:w], S0_i[:, o:o + w])])

        def sadd(eng, o=o, w=w):
            return eng.tensor_tensor(cfin[:, o:o + w], cfin[:, o:o + w], cm.bcs1[0:16, 0:w], ALU.add)
        p.phase("vector", sadd)

    Cms = p.sbuf("Cms", [16, 2048], F32)
    p.dma("sync", [(Cms[:, :], Cm_i)])
    xcs = p.sbuf("xcs", [128, 512], F32)
    zs = p.sbuf("zs", [128, 512], F32)
    yb = [p.sbuf(f"yb{m}", [128, C], BF16) for m in range(16)]
    for m in range(16):
        for (o, w) in CC:
            p.dma("sync", [(xcs[:, 0:w], xc_i[m * 128:(m + 1) * 128, o:o + w]),
                           (zs[:, 0:w], z_i[m * 128:(m + 1) * 128, o:o + w])])

            def sig(eng, o=o, w=w):
                return eng.activation(zs[:, 0:w], zs[:, 0:w], AF.Sigmoid)
            p.phase("scalar", sig)

            def gmm(eng, m=m, o=o, w=w):
                return eng.matmul(cm.mm_ps[0][:, 0:w], Cms[:, m * 128:(m + 1) * 128],
                                  cfin[:, o:o + w], start=True, stop=True)
            p.phase("tensor", gmm)

            def gev(eng, o=o, w=w):
                return eng.copy(cm.bcs1[:, 0:w], cm.mm_ps[0][:, 0:w])
            p.phase("scalar", gev)

            def gfin(eng, m=m, o=o, w=w):
                eng.tensor_tensor(cm.bcs1[:, 0:w], cm.bcs1[:, 0:w], xcs[:, 0:w], ALU.add)
                return eng.tensor_tensor(yb[m][:, o:o + w], cm.bcs1[:, 0:w], zs[:, 0:w], ALU.mult)
            p.phase("vector", gfin)

    wob = p.sbuf("wob", [128, 16 * 128], BF16)
    for m in range(8):
        p.dma("gpsimd", [(wob[:, k * 128:(k + 1) * 128],
                          woT[k * 128:(k + 1) * 128, m * 128:(m + 1) * 128]) for k in range(16)])
        for (o, w) in CC:
            p.dma("sync", [(xcs[:, 0:w], h_i[m * 128:(m + 1) * 128, H + o:H + o + w])])
            def rmm(eng, m=m, o=o, w=w):
                last = None
                for k in range(16):
                    last = eng.matmul(cm.mm_ps[1][:, 0:w], wob[:, k * 128:(k + 1) * 128],
                                      yb[k][:, o:o + w], start=(k == 0), stop=(k == 15))
                return last
            p.phase("tensor", rmm)

            def rev(eng, o=o, w=w):
                return eng.copy(cm.bcs2[:, 0:w], cm.mm_ps[1][:, 0:w])
            p.phase("scalar", rev)

            def rfin(eng, m=m, o=o, w=w):
                eng.tensor_tensor(cm.bcs2[:, 0:w], cm.bcs2[:, 0:w], xcs[:, 0:w], ALU.add)
                return eng.scalar_tensor_tensor(x_t[m][:, o:o + w], cm.bcs2[:, 0:w],
                                                modx[:, 16 + m:17 + m], x_t[m][:, o:o + w],
                                                ALU.mult, ALU.add)
            p.phase("vector", rfin)
        p.dma("gpsimd", [(x1_o[m * 128:(m + 1) * 128, :], x_t[m][:, :])])
    return p


def l0b_inputs(I, l0a_res):
    x2d = I["L0_attn_x"][0].T
    AT = np.ascontiguousarray(I["L0_attn_A"].T)
    cT8 = _r8(I["L0_c"][0])
    woT = np.ascontiguousarray(I["L0_attn_W_out"].T)
    Ivals = [l0a_res[q]["S0"][:, -1] for q in range(NCORES)]
    maps = []
    for r in range(NCORES):
        IordT = np.zeros((16, 8), np.float32)
        for j in range(8):
            q = r - 1 - j
            if q >= 0:
                IordT[:, j] = Ivals[q]
        maps.append(dict(xT=np.ascontiguousarray(x2d[:, r * C:(r + 1) * C]),
                         S0=l0a_res[r]["S0"], IordT=IordT, AT=AT,
                         Cm=I["L0_attn_Cm"], woT=woT, cT8=cT8, wmod=I["L0_wmod"],
                         xc=l0a_res[r]["xc"], z=l0a_res[r]["z"], hsp=l0a_res[r]["hdbg"]))
    return maps


GOLDEN_CENTER = np.float32(1.0 / np.e)
GOLDEN_LOWER = np.float32(0.5 - np.log(4.0 / 3.0))
GOLDEN_UPPER = np.float32(0.5)


def build_l0c():
    p = Prog("l0c")
    x1T = p.inp("x1T", [1024, C])
    cT8 = p.inp("cT8", [128, 8])
    wmod = p.inp("wmod", [1024, 6144])
    Wr_i = p.inp("Wr", [1024, 8])
    temp_i = p.inp("temp", [1, 1])
    Wg_i = p.inp("Wg", [8, 1024, 1024])
    Wu_i = p.inp("Wu", [8, 1024, 1024])
    Wd_i = p.inp("Wd", [8, 1024, 1024])
    eye8_i = p.inp("eye8", [8, 8])
    eye128_i = p.inp("eye128", [128, 128])
    cT8b = p.inp("cT8b", [128, 8])
    wmodb = p.inp("wmodb", [1024, 6144])
    x2_o = p.out("x2out", [1024, C])
    h3_o = p.out("h3out", [1024, C])

    cm = Common(p)
    wrs = p.sbuf("wrs", [128, 8 * 8], F32)
    e8 = p.sbuf("e8", [8, 8], F32)
    e128 = p.sbuf("e128", [128, 128], F32)
    tmp11 = p.sbuf("tmp11", [1, 1], F32)
    invt8 = p.sbuf("invt8", [8, 1], F32)
    inh = p.sbuf("inh", [8, C], F32)
    wfm = p.sbuf("wfm", [8, C], F32)
    ones8 = p.sbuf("ones8", [128, 8], F32)
    rw = p.sbuf("rw", [128, 8 * 6], F32)
    rc = p.sbuf("rc", [128, 8], F32)
    h2b = [p.sbuf(f"h2b{k}", [128, C], BF16) for k in range(8)]
    web = p.sbuf("web", [128, 8 * 128], BF16)
    gst = p.sbuf("gst", [128, C], F32)
    wbce = p.sbuf("wbce", [128, C], BF16)
    rt_ps = p.psum("rt_ps", [8, 512])
    tt_ps = p.psum("tt_ps", [128, 8])
    bt_ps = p.psum("bt_ps", [8, 128])
    modx = emit_mod_vectors(p, cm, cT8, wmod)
    x_t = [p.sbuf(f"x{k}", [128, C]) for k in range(8)]
    p.dma("sync", [(x_t[k][:, :], x1T[k * 128:(k + 1) * 128, :]) for k in range(8)])
    emit_ln(p, cm, "ln4", x_t, x_t, C,
            gamma=[modx[:, 56 + k:57 + k] for k in range(8)],
            beta=[modx[:, 24 + k:25 + k] for k in range(8)])
    h2f = x_t
    p.dma("sync", [(wrs[:, k * 8:(k + 1) * 8], Wr_i[k * 128:(k + 1) * 128, :]) for k in range(8)]
          + [(e8[:, :], eye8_i), (e128[:, :], eye128_i), (tmp11[:, :], temp_i)])

    def invt(eng):
        eng.reciprocal(tmp11[0:1, 0:1], tmp11[0:1, 0:1])
        eng.memset(ones8[:, :], 0.0)
        return eng.tensor_scalar_add(ones8[:, :], ones8[:, :], 1.0)
    p.phase("vector", invt)
    p.dma("sync", [(invt8[j:j + 1, 0:1], tmp11[0:1, 0:1]) for j in range(8)])

    def negt(eng):
        return eng.tensor_scalar_mul(invt8[:, 0:1], invt8[:, 0:1], -1.0)
    p.phase("vector", negt)

    for (o, w) in CC:
        def rmm(eng, o=o, w=w):
            last = None
            for k in range(8):
                last = eng.matmul(rt_ps[:, 0:w], wrs[:, k * 8:(k + 1) * 8],
                                  h2f[k][:, o:o + w], start=(k == 0), stop=(k == 7))
            return last
        p.phase("tensor", rmm)

        def rev(eng, o=o, w=w):
            return eng.copy(inh[:, o:o + w], rt_ps[:, 0:w])
        p.phase("scalar", rev)

    def sig1(eng):
        return eng.activation(inh[:, :], inh[:, :], AF.Exp, scale=invt8[:, 0:1])
    p.phase("scalar", sig1)

    def sig2(eng):
        eng.tensor_scalar_add(inh[:, :], inh[:, :], 1.0)
        return eng.reciprocal(inh[:, :], inh[:, :])
    p.phase("vector", sig2)

    for tt in range(C // 128):
        def ttr(eng, tt=tt):
            return eng.matmul(tt_ps[:, 0:8], inh[:, tt * 128:(tt + 1) * 128], e8[:, :],
                              start=True, stop=True)
        p.phase("tensor", ttr)

        def tte(eng):
            return eng.copy(rw[:, 0:8], tt_ps[:, 0:8])
        p.phase("scalar", tte)

        def rmath(eng):
            w0 = rw[:, 0:8]; dist = rw[:, 8:16]; wz = rw[:, 16:24]
            fb = rw[:, 24:32]; t1 = rw[:, 32:40]; t2 = rw[:, 40:48]
            eng.tensor_scalar_sub(dist, w0, float(GOLDEN_CENTER))
            eng.tensor_scalar_mul(t1, dist, -1.0)
            eng.tensor_tensor(dist, dist, t1, ALU.max)
            eng.tensor_scalar(t1, w0, float(GOLDEN_LOWER), None, ALU.is_ge)
            eng.tensor_scalar(t2, w0, float(GOLDEN_UPPER), None, ALU.is_le)
            return eng.tensor_tensor(t1, t1, t2, ALU.mult)
        p.phase("vector", rmath)

        def rexp(eng):
            dist = rw[:, 8:16]; wz = rw[:, 16:24]; fb = rw[:, 24:32]
            eng.activation(wz, dist, AF.Exp, scale=-10.0)
            return eng.activation(fb, dist, AF.Exp, scale=np.float32(-1.0 / 0.3))
        p.phase("scalar", rexp)

        def rmath2(eng):
            w0 = rw[:, 0:8]; dist = rw[:, 8:16]; wz = rw[:, 16:24]
            fb = rw[:, 24:32]; t1 = rw[:, 32:40]; t2 = rw[:, 40:48]
            m1 = rc[:, 0:1]; ne = rc[:, 1:2]; m2 = rc[:, 2:3]; sm = rc[:, 3:4]
            eng.tensor_tensor(wz, wz, t1, ALU.mult)
            eng.tensor_reduce(ne.rearrange("p (a b) -> p a b", b=1), wz.rearrange("p (a b) -> p a b", b=8), AX.X, ALU.add)
            eng.tensor_scalar(ne, ne, 1e-8, None, ALU.is_lt)
            eng.tensor_reduce(m1.rearrange("p (a b) -> p a b", b=1), fb.rearrange("p (a b) -> p a b", b=8), AX.X, ALU.max)
            eng.tensor_scalar(t1, fb, m1[:, 0:1], None, ALU.is_ge)
            eng.tensor_tensor_scan(t2, ones8[:, 0:8], t1, 0.0, ALU.mult, ALU.add)
            eng.tensor_tensor(t2, t2, t1, ALU.subtract)
            eng.tensor_scalar(t2, t2, 0.0, None, ALU.is_equal)
            eng.tensor_tensor(t1, t1, t2, ALU.mult)
            eng.scalar_tensor_tensor(t2, t1, -1e30, fb, ALU.mult, ALU.add)
            eng.tensor_reduce(m2.rearrange("p (a b) -> p a b", b=1), t2.rearrange("p (a b) -> p a b", b=8), AX.X, ALU.max)
            eng.tensor_scalar(t2, t2, m2[:, 0:1], None, ALU.is_ge)
            eng.tensor_tensor_scan(rw[:, 0:8], ones8[:, 0:8], t2, 0.0, ALU.mult, ALU.add)
            eng.tensor_tensor(rw[:, 0:8], rw[:, 0:8], t2, ALU.subtract)
            eng.tensor_scalar(rw[:, 0:8], rw[:, 0:8], 0.0, None, ALU.is_equal)
            eng.tensor_tensor(t2, t2, rw[:, 0:8], ALU.mult)
            eng.tensor_tensor(t1, t1, t2, ALU.add)
            eng.tensor_tensor(fb, fb, t1, ALU.mult)
            eng.tensor_reduce(sm.rearrange("p (a b) -> p a b", b=1), fb.rearrange("p (a b) -> p a b", b=8), AX.X, ALU.add)
            eng.tensor_scalar_max(sm, sm, 1e-8)
            eng.reciprocal(sm, sm)
            eng.tensor_scalar(fb, fb, sm[:, 0:1], None, ALU.mult)
            eng.tensor_tensor(t2, fb, wz, ALU.subtract)
            eng.tensor_scalar(t2, t2, ne[:, 0:1], None, ALU.mult)
            eng.tensor_tensor(wz, wz, t2, ALU.add)
            eng.tensor_reduce(sm.rearrange("p (a b) -> p a b", b=1), wz.rearrange("p (a b) -> p a b", b=8), AX.X, ALU.add)
            eng.tensor_scalar_max(sm, sm, 1e-8)
            eng.reciprocal(sm, sm)
            return eng.tensor_scalar(wz, wz, sm[:, 0:1], None, ALU.mult)
        p.phase("vector", rmath2)

        def btr(eng, tt=tt):
            return eng.matmul(bt_ps[:, 0:128], rw[:, 16:24], e128[:, :], start=True, stop=True)
        p.phase("tensor", btr)

        def bte(eng, tt=tt):
            return eng.copy(wfm[:, tt * 128:(tt + 1) * 128], bt_ps[:, 0:128])
        p.phase("scalar", bte)

    for k in range(8):
        def cst(eng, k=k):
            return eng.copy(h2b[k][:, :], h2f[k][:, :])
        p.phase("scalar", cst)

    p.free(*x_t)
    moe = [p.sbuf(f"moe{m}", [128, C], F32) for m in range(8)]
    hpb = [p.sbuf(f"hpb{m}", [128, C], BF16) for m in range(8)]
    wrow = cm.tp
    for e in range(8):
        p.dma("sync", [(wrow[0:1, 0:C], wfm[e:e + 1, :])])
        for (o, w) in CC:
            def wbm(eng, o=o, w=w):
                return eng.matmul(cm.bc_ps[:, 0:w], cm.ones_row[0:1, :], wrow[0:1, o:o + w],
                                  start=True, stop=True)
            p.phase("tensor", wbm)

            def wbe(eng, o=o, w=w):
                return eng.copy(wbce[:, o:o + w], cm.bc_ps[:, 0:w])
            p.phase("scalar", wbe)

        for m in range(8):
            p.dma("gpsimd", [(web[:, k * 128:(k + 1) * 128],
                              Wg_i[e, k * 128:(k + 1) * 128, m * 128:(m + 1) * 128]) for k in range(8)])
            for (o, w) in CC:
                def gmm(eng, o=o, w=w):
                    last = None
                    for k in range(8):
                        last = eng.matmul(cm.mm_ps[0][:, 0:w], web[:, k * 128:(k + 1) * 128],
                                          h2b[k][:, o:o + w], start=(k == 0), stop=(k == 7))
                    return last
                p.phase("tensor", gmm)

                def gev(eng, o=o, w=w):
                    return eng.copy(gst[:, o:o + w], cm.mm_ps[0][:, 0:w])
                p.phase("scalar", gev)

            def gsil(eng):
                return eng.activation(gst[:, :], gst[:, :], AF.Silu)
            p.phase("scalar", gsil)

            p.dma("gpsimd", [(web[:, k * 128:(k + 1) * 128],
                              Wu_i[e, k * 128:(k + 1) * 128, m * 128:(m + 1) * 128]) for k in range(8)])
            for (o, w) in CC:
                def umm(eng, o=o, w=w):
                    last = None
                    for k in range(8):
                        last = eng.matmul(cm.mm_ps[1][:, 0:w], web[:, k * 128:(k + 1) * 128],
                                          h2b[k][:, o:o + w], start=(k == 0), stop=(k == 7))
                    return last
                p.phase("tensor", umm)

                def uev(eng, o=o, w=w):
                    return eng.copy(cm.bcs1[:, 0:w], cm.mm_ps[1][:, 0:w])
                p.phase("scalar", uev)

                def ufin(eng, m=m, o=o, w=w):
                    eng.tensor_tensor(cm.bcs1[:, 0:w], cm.bcs1[:, 0:w], gst[:, o:o + w], ALU.mult)
                    return eng.tensor_tensor(hpb[m][:, o:o + w], cm.bcs1[:, 0:w], wbce[:, o:o + w], ALU.mult)
                p.phase("vector", ufin)

        for md in range(8):
            p.dma("gpsimd", [(web[:, k * 128:(k + 1) * 128],
                              Wd_i[e, k * 128:(k + 1) * 128, md * 128:(md + 1) * 128]) for k in range(8)])
            for (o, w) in CC:
                def dmm(eng, o=o, w=w):
                    last = None
                    for k in range(8):
                        last = eng.matmul(cm.mm_ps[0][:, 0:w], web[:, k * 128:(k + 1) * 128],
                                          hpb[k][:, o:o + w], start=(k == 0), stop=(k == 7))
                    return last
                p.phase("tensor", dmm)

                def dev(eng, o=o, w=w):
                    return eng.copy(cm.bcs1[:, 0:w], cm.mm_ps[0][:, 0:w])
                p.phase("scalar", dev)

                if e == 0:
                    def dacc(eng, md=md, o=o, w=w):
                        return eng.tensor_copy(moe[md][:, o:o + w], cm.bcs1[:, 0:w])
                else:
                    def dacc(eng, md=md, o=o, w=w):
                        return eng.tensor_tensor(moe[md][:, o:o + w], moe[md][:, o:o + w],
                                                 cm.bcs1[:, 0:w], ALU.add)
                p.phase("vector", dacc)

    for m in range(8):
        p.dma("sync", [(gst[:, :], x1T[m * 128:(m + 1) * 128, :])])

        def fin(eng, m=m):
            return eng.scalar_tensor_tensor(moe[m][:, :], moe[m][:, :], modx[:, 40 + m:41 + m],
                                            gst[:, :], ALU.mult, ALU.add)
        p.phase("vector", fin)
        p.dma("gpsimd", [(x2_o[m * 128:(m + 1) * 128, :], moe[m][:, :])])
    p.free(*hpb)
    modx2 = emit_mod_vectors(p, cm, cT8b, wmodb, tag="b")
    emit_ln(p, cm, "ln5", moe, moe, C,
            gamma=[modx2[:, 48 + k:49 + k] for k in range(8)],
            beta=[modx2[:, 0 + k:1 + k] for k in range(8)])
    p.dma("gpsimd", [(h3_o[m * 128:(m + 1) * 128, :], moe[m][:, :]) for m in range(8)])
    return p


def l0c_inputs(I, x1_shards):
    cT8 = _r8(I["L0_c"][0])
    maps = []
    eye8 = np.eye(8, dtype=np.float32)
    eye128 = np.eye(128, dtype=np.float32)
    for r in range(NCORES):
        maps.append(dict(x1T=x1_shards[r], cT8=cT8, wmod=I["L0_wmod"], Wr=I["L0_ffn_Wr"],
                         temp=I["L0_ffn_temperature"].reshape(1, 1), Wg=I["L0_ffn_Wg"],
                         Wu=I["L0_ffn_Wu"], Wd=I["L0_ffn_Wd"], eye8=eye8, eye128=eye128,
                         cT8b=_r8(I["L1_c"][0]), wmodb=I["L1_wmod"]))
    return maps



def emit_bcast_row(p, srow, dst, width=None):
    W = width or dst.shape[1]
    p.dma("sync", [(dst[0:1, 0:W], srow[0:1, 0:W])])
    k = 1
    while k < 128:
        kk = min(k, 128 - k)
        p.dma("sync", [(dst[k:k + kk, 0:W], dst[0:kk, 0:W])])
        k += kk



def build_l1s(kind):
    p = Prog("l1s_" + kind)
    cm = Common(p)
    ios = {}
    for D in ("f", "b"):
        if kind != "first":
            ios["ypc_" + D] = p.inp("ypc_" + D, [2048, TC], BF16)
            ios["szc_" + D] = p.inp("szc_" + D, [2048, TC], BF16)
            ios["ctc_" + D] = p.inp("ctc_" + D, [2048, TC])
            ios["crc_" + D] = p.inp("crc_" + D, [16, TC])
            ios["IoO_" + D] = p.inp("IoO_" + D, [1024, 256])
            ios["EoO_" + D] = p.inp("EoO_" + D, [1024, 256])
            ios["IoP_" + D] = p.inp("IoP_" + D, [1024, 256])
            ios["EoP_" + D] = p.inp("EoP_" + D, [1024, 256])
            ios["nAP_" + D] = p.inp("nAP_" + D, [128, 256])
            ios["owTP_" + D] = p.inp("owTP_" + D, [2048, 1024])
        ios["rT_" + D] = p.inp("rT_" + D, [1024, TC])
        if D == "f":
            ios["hmask"] = p.inp("hmask", [128, H])
        if kind != "last":
            ios["nwb_" + D] = p.inp("nwb_" + D, [128, 16])
            ios["inWT_" + D] = p.inp("inWT_" + D, [1024, 4096])
            ios["cwb_" + D] = p.inp("cwb_" + D, [128, 64])
            ios["cbb_" + D] = p.inp("cbb_" + D, [128, 16])
            ios["xpWT_" + D] = p.inp("xpWT_" + D, [2048, 96])
            ios["dtWT_" + D] = p.inp("dtWT_" + D, [64, 2048])
            ios["dtbb_" + D] = p.inp("dtbb_" + D, [128, 16])
            ios["nA_" + D] = p.inp("nA_" + D, [128, 256])
            ios["Dpb_" + D] = p.inp("Dpb_" + D, [128, 16])
            ios["ypost_" + D] = p.out("ypost_" + D, [2048, C], BF16)
            ios["szo_" + D] = p.out("szo_" + D, [2048, C], BF16)
            ios["cto_" + D] = p.out("cto_" + D, [2048, C])
            ios["cro_" + D] = p.out("cro_" + D, [16, C])
            ios["Io_" + D] = p.out("Io_" + D, [128, 256])
            ios["Eo_" + D] = p.out("Eo_" + D, [128, 256])
            ios["ro_" + D] = p.out("ro_" + D, [1024, TC])
            ios["usp_" + D] = p.dram("usp_" + D, [2048, C], BF16)
        else:
            ios["nfb_" + D] = p.inp("nfb_" + D, [128, 16])
            ios["fh2_" + D] = p.out("fh2_" + D, [1024, C])
        if kind != "first":
            ios["yfd_" + D] = p.dram("yfd_" + D, [2048, TC], BF16)

    web = p.sbuf("web", [128, 16 * 128], BF16)
    upre = p.sbuf("upre", [128, TC], F32)
    cstA = p.sbuf("cstA", [128, C], F32)
    cstB = p.sbuf("cstB", [128, C], F32)
    cstO = p.sbuf("cstO", [128, TC], F32)
    bf1 = p.sbuf("bf1", [128, TC], BF16)
    bf2 = p.sbuf("bf2", [128, TC], BF16)
    browb = p.sbuf("browb", [1, TC], BF16)
    nAt, nAPt, prm, cwt, Ist, Est, cte, Hown, Hprv = {}, {}, {}, {}, {}, {}, {}, {}, {}
    for D in ("f", "b"):
        nAt[D] = p.sbuf("nAt_" + D, [128, 256], F32)
        prm[D] = p.sbuf("prm_" + D, [128, 80], F32)
        cwt[D] = p.sbuf("cwt_" + D, [128, 64], F32)
        Ist[D] = p.sbuf("Ist_" + D, [128, 256], F32)
        Est[D] = p.sbuf("Est_" + D, [128, 256], F32)
        cte[D] = p.sbuf("cte_" + D, [128, 16], F32)
        if kind != "first":
            nAPt[D] = p.sbuf("nAPt_" + D, [128, 256], F32)
            Hown[D] = p.sbuf("Hown_" + D, [128, 256], F32)
            Hprv[D] = p.sbuf("Hprv_" + D, [128, 256], F32)
    IEld = p.sbuf("IEld", [128, 256], F32)
    hmk = p.sbuf("hmk", [128, H], F32)
    brow = cm.tp
    p.dma("sync", [(hmk[:, :], ios["hmask"])])

    for D in ("f", "b"):
        if kind != "last":
            p.dma("sync", [(prm[D][:, 0:16], ios["nwb_" + D]), (prm[D][:, 16:32], ios["cbb_" + D]),
                           (prm[D][:, 32:48], ios["dtbb_" + D]), (prm[D][:, 48:64], ios["Dpb_" + D]),
                           (cwt[D][:, :], ios["cwb_" + D]), (nAt[D][:, :], ios["nA_" + D])])
        else:
            p.dma("sync", [(prm[D][:, 0:16], ios["nfb_" + D])])
        if kind != "first":
            p.dma("sync", [(nAPt[D][:, :], ios["nAP_" + D])])
    for D in ("f", "b"):
        tensors = [nAt[D]] if kind != "last" else []
        if kind != "first":
            tensors.append(nAPt[D])
        for tt_ in tensors:
            def nexp(eng, tt_=tt_):
                return eng.activation(tt_[:, :], tt_[:, :], AF.Exp)
            p.phase("scalar", nexp)

            def nneg(eng, tt_=tt_):
                return eng.tensor_scalar_mul(tt_[:, :], tt_[:, :], -1.0)
            p.phase("vector", nneg)

    for D in ("f", "b"):
        if kind != "first":
            for (Ht, Isrc, Esrc) in ((Hown[D], "IoO_", "EoO_"), (Hprv[D], "IoP_", "EoP_")):
                def hz(eng, Ht=Ht):
                    return eng.memset(Ht[:, :], 0.0)
                p.phase("vector", hz)
                for j in range(8):
                    p.dma("sync", [(IEld[:, :], ios[Esrc + D][j * 128:(j + 1) * 128, :])])

                    def hc1(eng, Ht=Ht):
                        return eng.tensor_tensor(Ht[:, :], Ht[:, :], IEld[:, :], ALU.mult)
                    p.phase("vector", hc1)
                    p.dma("sync", [(IEld[:, :], ios[Isrc + D][j * 128:(j + 1) * 128, :])])

                    def hc2(eng, Ht=Ht):
                        return eng.tensor_tensor(Ht[:, :], Ht[:, :], IEld[:, :], ALU.add)
                    p.phase("vector", hc2)

            Cbc = p.sbuf("CbcP" + D, [128, TC], BF16, side="right")
            for quarter in range(4):
                ctq = [p.sbuf(f"ctq{D}{quarter}{i}", [128, TC], F32, side="right") for i in range(4)]
                qq = [p.sbuf(f"qq{D}{quarter}{i}", [128, TC], F32, side="right") for i in range(4)]
                dgs = [quarter * 4 + i for i in range(4)]
                p.dma("sync", [(ctq[i][:, :], ios["ctc_" + D][dg * 128:(dg + 1) * 128, :])
                               for i, dg in enumerate(dgs)])

                def qz(eng, qq=qq):
                    for i in range(4):
                        eng.memset(qq[i][:, :], 0.0)
                    return None
                p.phase("vector", qz)
                for s in range(16):
                    p.dma("sync", [(brow[0:1, 0:TC], ios["crc_" + D][s:s + 1, :])])

                    def cvb(eng):
                        return eng.copy(browb[0:1, :], brow[0:1, 0:TC])
                    p.phase("scalar", cvb)
                    emit_bcast_row(p, browb, Cbc)
                    for i, dg in enumerate(dgs):
                        col = dg * 16 + s

                        def corrA(eng, ct_=ctq[i], nap_=nAPt[D], col=col):
                            return eng.activation(cstO[:, :], ct_[:, :], AF.Exp,
                                                  scale=nap_[:, col:col + 1])
                        p.phase("scalar", corrA)

                        def corrB(eng, q_=qq[i], hp_=Hprv[D], ho_=Hown[D], cb_=Cbc, col=col):
                            eng.scalar_tensor_tensor(cstO[:, 0:H], cstO[:, 0:H], hp_[:, col:col + 1],
                                                     cb_[:, 0:H], ALU.mult, ALU.mult)
                            eng.scalar_tensor_tensor(cstO[:, H:TC], cstO[:, H:TC], ho_[:, col:col + 1],
                                                     cb_[:, H:TC], ALU.mult, ALU.mult)
                            return eng.tensor_tensor(q_[:, :], q_[:, :], cstO[:, :], ALU.add)
                        p.phase("vector", corrB)
                for i, dg in enumerate(dgs):
                    p.dma("sync", [(bf1[:, :], ios["ypc_" + D][dg * 128:(dg + 1) * 128, :]),
                                   (bf2[:, :], ios["szc_" + D][dg * 128:(dg + 1) * 128, :])])

                    def yf(eng, q_=qq[i]):
                        eng.tensor_tensor(q_[:, :], q_[:, :], bf2[:, :], ALU.mult)
                        eng.tensor_copy(cstO[:, :], bf1[:, :])
                        eng.tensor_tensor(cstO[:, :], cstO[:, :], q_[:, :], ALU.add)
                        return eng.tensor_copy(bf1[:, :], cstO[:, :])
                    p.phase("vector", yf)
                    p.dma("gpsimd", [(ios["yfd_" + D][dg * 128:(dg + 1) * 128, :], bf1[:, :])])
                p.free(*ctq, *qq)
            p.free(Cbc)

        r_t = [p.sbuf(f"r{D}{m}", [128, TC], F32, side="right") for m in range(8)]
        if kind == "first":
            p.dma("sync", [(r_t[m][:, :], ios["rT_" + D][m * 128:(m + 1) * 128, :]) for m in range(8)])
        else:
            for m in range(8):
                p.dma("gpsimd", [(web[:, k * 128:(k + 1) * 128],
                                  ios["owTP_" + D][k * 128:(k + 1) * 128, m * 128:(m + 1) * 128])
                                 for k in range(16)])
                p.dma("sync", [(upre[:, :], ios["rT_" + D][m * 128:(m + 1) * 128, :])])
                for ci, (o, w) in enumerate(TCC):
                    for k in range(16):
                        p.dma("sync", [(bf1[:, 0:w], ios["yfd_" + D][k * 128:(k + 1) * 128, o:o + w])])

                        def omm(eng, k=k, o=o, w=w):
                            return eng.matmul(cm.mm_ps[1][:, 0:w], web[:, k * 128:(k + 1) * 128],
                                              bf1[:, 0:w], start=(k == 0), stop=(k == 15))
                        p.phase("tensor", omm)

                    def oev(eng, o=o, w=w):
                        return eng.copy(cstO[:, o:o + w], cm.mm_ps[1][:, 0:w])
                    p.phase("scalar", oev)

                def radd(eng, rt_=r_t[m]):
                    return eng.tensor_tensor(rt_[:, :], cstO[:, :], upre[:, :], ALU.add)
                p.phase("vector", radd)

        if kind == "last":
            emit_ln(p, cm, "fin" + D, r_t, r_t, TC,
                    gamma=[prm[D][:, k:k + 1] for k in range(8)],
                    beta=[prm[D][:, 8 + k:9 + k] for k in range(8)])
            p.dma("gpsimd", [(ios["fh2_" + D][m * 128:(m + 1) * 128, :], r_t[m][:, H:TC])
                             for m in range(8)])
            p.free(*r_t)
            continue

        p.dma("gpsimd", [(ios["ro_" + D][m * 128:(m + 1) * 128, :], r_t[m][:, :]) for m in range(8)])

        hnb = [p.sbuf(f"hnb{D}{m}", [128, TC], BF16, side="right") for m in range(8)]
        emit_ln(p, cm, "hn" + D, r_t, hnb, TC,
                gamma=[prm[D][:, k:k + 1] for k in range(8)],
                beta=[prm[D][:, 8 + k:9 + k] for k in range(8)])

        for m in range(32):
            p.dma("gpsimd", [(web[:, k * 128:(k + 1) * 128],
                              ios["inWT_" + D][k * 128:(k + 1) * 128, m * 128:(m + 1) * 128])
                             for k in range(8)])
            for (o, w) in TCC:
                def mmx(eng, hnb_=hnb, o=o, w=w):
                    last = None
                    for k in range(8):
                        last = eng.matmul(cm.mm_ps[0][:, 0:w], web[:, k * 128:(k + 1) * 128],
                                          hnb_[k][:, o:o + w], start=(k == 0), stop=(k == 7))
                    return last
                p.phase("tensor", mmx)

                def evx(eng, o=o, w=w):
                    return eng.copy(upre[:, o:o + w], cm.mm_ps[0][:, 0:w])
                p.phase("scalar", evx)
            if m < 16:
                def hm0(eng):
                    return eng.tensor_tensor(upre[:, 0:H], upre[:, 0:H], hmk[:, :], ALU.mult)
                p.phase("vector", hm0)

                def conv2(eng, m=m, cw_=cwt[D]):
                    eng.tensor_scalar_mul(cstA[:, 0:C], upre[:, 0:C], cw_[:, 4 * m:4 * m + 1])
                    for j in range(1, 4):
                        eng.scalar_tensor_tensor(cstA[:, 0:C], upre[:, j:j + C],
                                                 cw_[:, 4 * m + j:4 * m + j + 1],
                                                 cstA[:, 0:C], ALU.mult, ALU.add)
                    return None
                p.phase("vector", conv2)

                def sil(eng, m=m, pr_=prm[D]):
                    return eng.activation(bf1[:, 0:C], cstA[:, 0:C], AF.Silu, bias=pr_[:, 16 + m:17 + m])
                p.phase("scalar", sil)
                p.dma("gpsimd", [(ios["usp_" + D][m * 128:(m + 1) * 128, :], bf1[:, 0:C])])
                if kind == "first" and m == 0:
                    cdb = p.out("cdbg_" + D, [128, TC + C])
                    p.dma("gpsimd", [(cdb[:, 0:TC], upre[:, :]), (cdb[:, TC:TC + C], cstA[:, 0:C])])
                    wdb = p.out("wdbg_" + D, [128, 64 + 80])
                    p.dma("gpsimd", [(wdb[:, 0:64], cwt[D][:, :]), (wdb[:, 64:144], prm[D][:, :])])
            else:
                def silz(eng, m=m):
                    return eng.activation(bf1[:, 0:C], upre[:, H:TC], AF.Silu)
                p.phase("scalar", silz)
                p.dma("gpsimd", [(ios["szo_" + D][(m - 16) * 128:(m - 15) * 128, :], bf1[:, 0:C])])
        p.free(*hnb)
        p.free(*r_t)

        ones_c = p.sbuf("ones_c" + D, [128, C], F32, side="right")
        BCst = p.sbuf("BCst" + D, [32, TC], F32, side="right")
        dtrB = p.sbuf("dtrB" + D, [64, TC], BF16, side="right")

        def initones(eng, ones_c=ones_c):
            eng.memset(ones_c[:, :], 0.0)
            return eng.tensor_scalar_add(ones_c[:, :], ones_c[:, :], 1.0)
        p.phase("vector", initones)

        for ci, (o, w) in enumerate(CC):
            for k in range(16):
                p.dma("sync", [(bf1[:, 0:w], ios["usp_" + D][k * 128:(k + 1) * 128, o:o + w])])
                p.dma("gpsimd", [(web[:, 0:96], ios["xpWT_" + D][k * 128:(k + 1) * 128, :])])

                def xmm2(eng, k=k, o=o, w=w):
                    return eng.matmul(cm.mm_ps[0][0:96, 0:w], web[:, 0:96], bf1[:, 0:w],
                                      start=(k == 0), stop=(k == 15))
                p.phase("tensor", xmm2)

            def xev(eng, o=o, w=w, dtrB=dtrB, BCst=BCst):
                eng.copy(dtrB[0:64, H + o:H + o + w], cm.mm_ps[0][0:64, 0:w])
                return eng.copy(BCst[0:32, H + o:H + o + w], cm.mm_ps[0][64:96, 0:w])
            p.phase("scalar", xev)
        p.dma("gpsimd", [(ios["cro_" + D], BCst[16:32, H:TC])])

        for half in range(4):
            dgs = [half * 4 + i for i in range(4)]
            dtb16 = [p.sbuf(f"dt{D}{half}{i}", [128, C], BF16, side="right") for i in range(4)]
            dtu = [p.sbuf(f"du{D}{half}{i}", [128, C], BF16, side="right") for i in range(4)]
            yacc = [p.sbuf(f"ya{D}{half}{i}", [128, C], F32, side="right") for i in range(4)]
            Bbc = p.sbuf("Bbc" + D + str(half), [128, TC], BF16, side="right")
            Cbc = p.sbuf("Cbc" + D + str(half), [128, TC], BF16, side="right")
            for i, dg in enumerate(dgs):
                p.dma("gpsimd", [(web[0:64, 0:128], ios["dtWT_" + D][0:64, dg * 128:(dg + 1) * 128])])
                for (o, w) in CC:
                    def dmm(eng, o=o, w=w, dtrB=dtrB):
                        return eng.matmul(cm.mm_ps[0][:, 0:w], web[0:64, 0:128],
                                          dtrB[0:64, H + o:H + o + w], start=True, stop=True)
                    p.phase("tensor", dmm)

                    def dev(eng, dg=dg, o=o, w=w):
                        return eng.copy(cstA[:, o:o + w], cm.mm_ps[0][:, 0:w])
                    p.phase("scalar", dev)

                def dbias(eng, dg=dg, pr_=prm[D]):
                    return eng.tensor_scalar_add(cstA[:, 0:C], cstA[:, 0:C], pr_[:, 32 + dg:33 + dg])
                p.phase("vector", dbias)

                def sp1(eng, _x=None):
                    eng.tensor_scalar_mul(cstB[:, 0:C], cstA[:, 0:C], -1.0)
                    eng.tensor_tensor(cstB[:, 0:C], cstB[:, 0:C], cstA[:, 0:C], ALU.min)
                    return eng.tensor_relu(cstO[:, 0:C], cstA[:, 0:C])
                p.phase("vector", sp1)

                def sp2(eng):
                    eng.activation(cstB[:, 0:C], cstB[:, 0:C], AF.Exp)
                    return eng.activation(cstB[:, 0:C], cstB[:, 0:C], AF.Ln, bias=1.0)
                p.phase("scalar", sp2)

                def sp3(eng, dt_=dtb16[i], ones_c=ones_c):
                    eng.tensor_tensor(cstA[:, 0:C], cstB[:, 0:C], cstO[:, 0:C], ALU.add)
                    eng.tensor_copy(dt_[:, :], cstA[:, 0:C])
                    return eng.tensor_tensor_scan(cstB[:, 0:C], ones_c[:, 0:C], cstA[:, 0:C],
                                                  0.0, ALU.mult, ALU.add)
                p.phase("vector", sp3)
                p.dma("gpsimd", [(ios["cto_" + D][dg * 128:(dg + 1) * 128, :], cstB[:, 0:C])])

                def cte1(eng, dg=dg, ct_=cte[D]):
                    return eng.tensor_copy(ct_[:, dg:dg + 1], cstB[:, C - 1:C])
                p.phase("vector", cte1)
                p.dma("sync", [(bf1[:, 0:C], ios["usp_" + D][dg * 128:(dg + 1) * 128, :])])

                def dtu1(eng, du_=dtu[i], dt_=dtb16[i], ya_=yacc[i], dg=dg, pr_=prm[D]):
                    eng.tensor_tensor(du_[:, :], dt_[:, :], bf1[:, 0:C], ALU.mult)
                    return eng.tensor_scalar(ya_[:, :], bf1[:, 0:C],
                                             pr_[:, 48 + dg:49 + dg],
                                             None, ALU.mult)
                p.phase("vector", dtu1)

            for s in range(16):
                p.dma("sync", [(brow[0:1, 0:C], BCst[s:s + 1, H:TC])])

                def cvb2(eng):
                    return eng.copy(browb[0:1, 0:C], brow[0:1, 0:C])
                p.phase("scalar", cvb2)
                emit_bcast_row(p, browb, Bbc, width=C)
                p.dma("sync", [(brow[0:1, 0:C], BCst[16 + s:17 + s, H:TC])])

                def cvb3(eng):
                    return eng.copy(browb[0:1, 0:C], brow[0:1, 0:C])
                p.phase("scalar", cvb3)
                emit_bcast_row(p, browb, Cbc, width=C)
                for i, dg in enumerate(dgs):
                    col = dg * 16 + s

                    def sca(eng, dt_=dtb16[i], na_=nAt[D], col=col):
                        return eng.activation(cstA[:, 0:C], dt_[:, :], AF.Exp,
                                              scale=na_[:, col:col + 1])
                    p.phase("scalar", sca)

                    def scb(eng, du_=dtu[i], ya_=yacc[i], is_=Ist[D], Bbc=Bbc, Cbc=Cbc, col=col):
                        eng.tensor_tensor(cstB[:, 0:C], du_[:, :], Bbc[:, 0:C], ALU.mult)
                        eng.tensor_tensor_scan(cstO[:, 0:C], cstA[:, 0:C], cstB[:, 0:C],
                                               0.0, ALU.mult, ALU.add)
                        eng.tensor_copy(is_[:, col:col + 1], cstO[:, C - 1:C])
                        eng.tensor_tensor(cstO[:, 0:C], cstO[:, 0:C], Cbc[:, 0:C], ALU.mult)
                        return eng.tensor_tensor(ya_[:, :], ya_[:, :], cstO[:, 0:C], ALU.add)
                    p.phase("vector", scb)
            for i, dg in enumerate(dgs):
                p.dma("sync", [(bf2[:, 0:C], ios["szo_" + D][dg * 128:(dg + 1) * 128, :])])

                def yp(eng, ya_=yacc[i]):
                    return eng.tensor_tensor(bf1[:, 0:C], ya_[:, :], bf2[:, 0:C], ALU.mult)
                p.phase("vector", yp)
                p.dma("gpsimd", [(ios["ypost_" + D][dg * 128:(dg + 1) * 128, :], bf1[:, 0:C])])
            p.free(*dtb16, *dtu, *yacc, Bbc, Cbc)

        def ecalc(eng, D=D):
            return eng.tensor_tensor(Est[D][:, :].rearrange("p (a b) -> p a b", b=16),
                                     nAt[D][:, :].rearrange("p (a b) -> p a b", b=16),
                                     cte[D][:, :].rearrange("p (a b) -> p a b", b=1).broadcast_to([128, 16, 16]),
                                     ALU.mult)
        p.phase("vector", ecalc)

        def ecalc2(eng, D=D):
            return eng.activation(Est[D][:, :], Est[D][:, :], AF.Exp)
        p.phase("scalar", ecalc2)
        p.dma("gpsimd", [(ios["Io_" + D], Ist[D][:, :]), (ios["Eo_" + D], Est[D][:, :])])
        if kind == "first":
            udb = p.out("udbg_" + D, [2048, C], BF16)
            p.dma("gpsimd", [(udb, ios["usp_" + D][:, :])])
        p.free(ones_c, BCst, dtrB)
    return p


def _l1_wt(I, tag, l):
    pre = f"L1_attn_{tag}_"
    nw = _r8(I[pre + "norm_w"][l]); nb = _r8(I[pre + "norm_b"][l])
    out = {}
    out["nwb"] = np.ascontiguousarray(np.concatenate([nw, nb], 1))
    out["inWT"] = np.ascontiguousarray(I[pre + "in_W"][l].T)
    cw = I[pre + "conv_w"][l]
    out["cwb"] = np.ascontiguousarray(cw.reshape(16, 128, 4).transpose(1, 0, 2).reshape(128, 64))
    out["cbb"] = _r16(I[pre + "conv_b"][l])
    out["xpWT"] = np.ascontiguousarray(I[pre + "xproj_W"][l].T)
    out["dtWT"] = np.ascontiguousarray(I[pre + "dt_W"][l].T)
    out["dtbb"] = _r16(I[pre + "dt_b"][l])
    al = I[pre + "Alog"][l]
    out["nA"] = np.ascontiguousarray(al.reshape(16, 128, 16).transpose(1, 0, 2).reshape(128, 256))
    out["Dpb"] = _r16(I[pre + "D"][l])
    out["owT"] = np.ascontiguousarray(I[pre + "out_W"][l].T)
    return out


def _concat_tail(prev, own, tail_w=H):
    if prev is None:
        t = np.zeros((own.shape[0], tail_w), own.dtype)
    else:
        t = prev[:, -tail_w:]
    return np.ascontiguousarray(np.concatenate([t, own], axis=1))


def _iord(res_list, key, r):
    out = np.zeros((1024, 256), np.float32)
    for j in range(min(r, 8)):
        out[j * 128:(j + 1) * 128, :] = res_list[j][key]
    return out


def _eord(res_list, key, r):
    out = np.ones((1024, 256), np.float32)
    for j in range(min(r, 8)):
        out[j * 128:(j + 1) * 128, :] = res_list[j][key]
    return out


def run_l1_chain(I, h3_full):
    h3b = np.ascontiguousarray(h3_full[:, ::-1])
    sh_f = shard_tokens(h3_full)
    sh_b = shard_tokens(h3b)
    wts = {("f", l): _l1_wt(I, "fw", l) for l in range(4)}
    wts.update({("b", l): _l1_wt(I, "bw", l) for l in range(4)})

    def weight_map(l):
        m = {}
        for D in ("f", "b"):
            w = wts[(D, l)]
            for k in ("nwb", "inWT", "cwb", "cbb", "xpWT", "dtWT", "dtbb", "nA", "Dpb"):
                m[k + "_" + D] = w[k]
        return m

    def prev_map(l, res, r):
        m = {}
        for D in ("f", "b"):
            w = wts[(D, l - 1)]
            pr = res[r - 1] if r > 0 else None
            m["ypc_" + D] = _concat_tail(None if pr is None else pr["ypost_" + D], res[r]["ypost_" + D])
            m["szc_" + D] = _concat_tail(None if pr is None else pr["szo_" + D], res[r]["szo_" + D])
            m["ctc_" + D] = _concat_tail(None if pr is None else pr["cto_" + D], res[r]["cto_" + D])
            m["crc_" + D] = _concat_tail(None if pr is None else pr["cro_" + D], res[r]["cro_" + D])
            m["IoO_" + D] = _iord([res[q] for q in range(NCORES)], "Io_" + D, r)
            m["EoO_" + D] = _eord([res[q] for q in range(NCORES)], "Eo_" + D, r)
            m["IoP_" + D] = _iord([res[q] for q in range(NCORES)], "Io_" + D, r - 1 if r > 0 else 0)
            m["EoP_" + D] = _eord([res[q] for q in range(NCORES)], "Eo_" + D, r - 1 if r > 0 else 0)
            m["nAP_" + D] = w["nA"]
            m["owTP_" + D] = w["owT"]
        return m

    maps = []
    for r in range(NCORES):
        m = dict(rT_f=sh_f[r], rT_b=sh_b[r],
                 hmask=(np.zeros((128, H), np.float32) if r == 0 else np.ones((128, H), np.float32)))
        m.update(weight_map(0))
        maps.append(m)
    res = run_prog("l1s_first", lambda: build_l1s("first"), maps)

    for l in (1, 2, 3):
        maps = []
        for r in range(NCORES):
            m = dict(rT_f=res[r]["ro_f"], rT_b=res[r]["ro_b"],
                     hmask=(np.zeros((128, H), np.float32) if r == 0 else np.ones((128, H), np.float32)))
            m.update(weight_map(l))
            m.update(prev_map(l, res, r))
            maps.append(m)
        res = run_prog("l1s_mid", lambda: build_l1s("mid"), maps)

    nf = np.ascontiguousarray(np.concatenate([_r8(I["L1_attn_normf_w"]), _r8(I["L1_attn_normf_b"])], 1))
    maps = []
    for r in range(NCORES):
        m = dict(rT_f=res[r]["ro_f"], rT_b=res[r]["ro_b"], nfb_f=nf, nfb_b=nf,
                 hmask=(np.zeros((128, H), np.float32) if r == 0 else np.ones((128, H), np.float32)))
        m.update(prev_map(4, res, r))
        maps.append(m)
    resf = run_prog("l1s_last", lambda: build_l1s("last"), maps)
    fh2 = np.concatenate([resf[r]["fh2_f"] for r in range(NCORES)], axis=1)
    bh2f = np.concatenate([resf[r]["fh2_b"] for r in range(NCORES)], axis=1)
    bh2u = np.ascontiguousarray(bh2f[:, ::-1])
    return fh2, bh2u, res, resf


def emit_part_max(p, cm, tiles, width, out_ap, pmx, pmd):
    def pm1(eng, tiles=tiles):
        for i, t in enumerate(tiles):
            eng.tensor_reduce(cm.sq[:, i:i + 1].rearrange("p (a b) -> p a b", b=1),
                              t[:, 0:width].rearrange("p (a b) -> p a b", b=width),
                              AX.X, ALU.max, apply_absolute_value=True)
        return eng.tensor_reduce(pmx[:, 0:1].rearrange("p (a b) -> p a b", b=1),
                                 cm.sq[:, 0:len(tiles)].rearrange("p (a b) -> p a b", b=len(tiles)),
                                 AX.X, ALU.max)
    p.phase("vector", pm1)
    p.dma("sync", [(pmd[:, :], pmx[:, 0:1])])
    p.dma("sync", [(pmx[0:1, 0:128], pmd.rearrange("a b -> b a"))], allow_slow=True)

    def pm2(eng):
        return eng.tensor_reduce(pmx[0:1, 128:129].rearrange("p (a b) -> p a b", b=1),
                                 pmx[0:1, 0:128].rearrange("p (a b) -> p a b", b=128),
                                 AX.X, ALU.max)
    p.phase("vector", pm2)
    p.dma("gpsimd", [(out_ap, pmx[0:1, 128:129])])


def build_l1m():
    p = Prog("l1m")
    fh2T = p.inp("fh2T", [1024, C])
    bh2T = p.inp("bh2T", [1024, C])
    x2T = p.inp("x2T", [1024, C])
    cT8b = p.inp("cT8b", [128, 8])
    wmodb = p.inp("wmodb", [1024, 6144])
    WfT = p.inp("WfT", [1024, 1024])
    WbT = p.inp("WbT", [1024, 1024])
    mb8 = p.inp("mb8", [128, 8])
    x3_o = p.out("x3out", [1024, C])
    h4_o = p.out("h4out", [1024, C])
    pm_o = p.out("pmax", [1, 1])

    cm = Common(p, width=C)
    web = p.sbuf("web", [128, 16 * 128], BF16)
    mbs = p.sbuf("mbs", [128, 8], F32)
    pmx = p.sbuf("pmx", [128, 129], F32)
    pmd = p.dram("pmd", [128, 1])
    fhb = [p.sbuf(f"fhb{k}", [128, C], BF16) for k in range(8)]
    bhb = [p.sbuf(f"bhb{k}", [128, C], BF16) for k in range(8)]
    x3 = [p.sbuf(f"x3_{k}", [128, C], F32) for k in range(8)]
    xst = p.sbuf("xst", [128, C], F32)
    modx = emit_mod_vectors(p, cm, cT8b, wmodb, tag="m")
    p.dma("gpsimd", [(fhb[k][:, :], fh2T[k * 128:(k + 1) * 128, :]) for k in range(8)])
    p.dma("gpsimd", [(bhb[k][:, :], bh2T[k * 128:(k + 1) * 128, :]) for k in range(8)])
    p.dma("sync", [(mbs[:, :], mb8)])

    for m in range(8):
        p.dma("gpsimd", [(web[:, k * 128:(k + 1) * 128],
                          WfT[k * 128:(k + 1) * 128, m * 128:(m + 1) * 128]) for k in range(8)]
              + [(web[:, (8 + k) * 128:(9 + k) * 128],
                  WbT[k * 128:(k + 1) * 128, m * 128:(m + 1) * 128]) for k in range(8)])
        p.dma("sync", [(xst[:, :], x2T[m * 128:(m + 1) * 128, :])])
        for (o, w) in CC:
            def mm(eng, m=m, o=o, w=w):
                last = None
                for k in range(8):
                    eng.matmul(cm.mm_ps[0][:, 0:w], web[:, k * 128:(k + 1) * 128],
                               fhb[k][:, o:o + w], start=(k == 0), stop=False)
                    last = eng.matmul(cm.mm_ps[0][:, 0:w], web[:, (8 + k) * 128:(9 + k) * 128],
                                      bhb[k][:, o:o + w], start=False, stop=(k == 7))
                return last
            p.phase("tensor", mm)

            def ev(eng, o=o, w=w):
                return eng.copy(cm.bcs1[:, 0:w], cm.mm_ps[0][:, 0:w])
            p.phase("scalar", ev)

            def fin(eng, m=m, o=o, w=w):
                eng.tensor_scalar_add(cm.bcs1[:, 0:w], cm.bcs1[:, 0:w], mbs[:, m:m + 1])
                return eng.scalar_tensor_tensor(x3[m][:, o:o + w], cm.bcs1[:, 0:w],
                                                modx[:, 16 + m:17 + m], xst[:, o:o + w],
                                                ALU.mult, ALU.add)
            p.phase("vector", fin)
        p.dma("gpsimd", [(x3_o[m * 128:(m + 1) * 128, :], x3[m][:, :])])

    emit_ln(p, cm, "lnm", x3, x3, C,
            gamma=[modx[:, 56 + k:57 + k] for k in range(8)],
            beta=[modx[:, 24 + k:25 + k] for k in range(8)])
    p.dma("gpsimd", [(h4_o[m * 128:(m + 1) * 128, :], x3[m][:, :]) for m in range(8)])
    emit_part_max(p, cm, x3, C, pm_o, pmx, pmd)
    return p


_GMN = [0]


def emit_gmax_scale(p, cm, maxes_in, s128, rs128, qmax, smin):
    _GMN[0] += 1
    m8 = p.sbuf("gm8_" + str(_GMN[0]), [1, 8], F32)
    p.dma("sync", [(m8[:, :], maxes_in)])

    def g1(eng):
        eng.tensor_reduce(s128[0:1, 0:1].rearrange("p (a b) -> p a b", b=1),
                          m8[0:1, 0:8].rearrange("p (a b) -> p a b", b=8), AX.X, ALU.max)
        eng.tensor_scalar_mul(s128[0:1, 0:1], s128[0:1, 0:1], float(np.float32(1.0 / qmax)))
        eng.tensor_scalar_max(s128[0:1, 0:1], s128[0:1, 0:1], smin)
        eng.tensor_copy(rs128[0:1, 0:1], s128[0:1, 0:1])
        return eng.reciprocal(rs128[0:1, 0:1], rs128[0:1, 0:1])
    p.phase("vector", g1)
    emit_bcast_row(p, s128, s128, width=1)
    emit_bcast_row(p, rs128, rs128, width=1)


def emit_qdq(eng, dst, src_ap, w, rs_ap, s_ap, iqt, f32t, lo, hi):
    eng.tensor_scalar(f32t[:, 0:w], src_ap, rs_ap, None, ALU.mult)
    eng.tensor_copy(iqt[:, 0:w], f32t[:, 0:w])
    eng.tensor_copy(f32t[:, 0:w], iqt[:, 0:w])
    eng.tensor_scalar_min(f32t[:, 0:w], f32t[:, 0:w], float(hi))
    eng.tensor_scalar_max(f32t[:, 0:w], f32t[:, 0:w], float(lo))
    return eng.tensor_scalar(dst, f32t[:, 0:w], s_ap, None, ALU.mult)


def build_f1():
    p = Prog("f1")
    h4T = p.inp("h4T", [1024, C])
    maxes = p.inp("maxes", [1, 8])
    wup = p.inp("wup", [3072, 1024])
    wgate = p.inp("wgate", [3072, 1024])
    up_o = p.out("up", [3072, C])
    gate_o = p.out("gate", [3072, C])
    pmu_o = p.out("pmu", [1, 1])
    pmg_o = p.out("pmg", [1, 1])

    cm = Common(p, width=C)
    s128 = p.sbuf("s128", [128, 1], F32)
    rs128 = p.sbuf("rs128", [128, 1], F32)
    iqt = p.sbuf("iqt", [128, C], I32)
    f32t = p.sbuf("f32t", [128, C], F32)
    wst = p.sbuf("wst", [128, 1024], F32)
    sw = p.sbuf("sw", [128, 32], F32)
    rsw = p.sbuf("rsw", [128, 32], F32)
    wqb = p.sbuf("wqb", [128, 1024], BF16)
    wqT = p.sbuf("wqT", [128, 1024], BF16)
    xqb = [p.sbuf(f"xqb{k}", [128, C], BF16) for k in range(8)]
    ust = p.sbuf("ust", [128, C], F32)
    pmx = p.sbuf("pmx", [128, 129], F32)
    pmacc = [p.sbuf("pma_u", [128, 24], F32), p.sbuf("pma_g", [128, 24], F32)]
    pmd = p.dram("pmd", [128, 1])

    emit_gmax_scale(p, cm, maxes, s128, rs128, 32767.0, 1e-12)

    for k in range(8):
        p.dma("sync", [(f32t[:, 0:C] if False else ust[:, :], h4T[k * 128:(k + 1) * 128, :])])

        def xq(eng, k=k):
            return emit_qdq(eng, xqb[k][:, :], ust[:, :], C, rs128[:, 0:1], s128[:, 0:1],
                            iqt, f32t, -32768.0, 32767.0)
        p.phase("vector", xq)

    for wi, (w_in, out_d, pma) in enumerate(((wup, up_o, pmacc[0]), (wgate, gate_o, pmacc[1]))):
        for m in range(24):
            p.dma("sync", [(wst[:, :], w_in[m * 128:(m + 1) * 128, :])])

            def dq(eng, m=m):
                eng.tensor_reduce(sw[:, :].rearrange("p (a b) -> p a b", b=1),
                                  wst[:, :].rearrange("p (a b) -> p a b", b=32),
                                  AX.X, ALU.max, apply_absolute_value=True)
                eng.tensor_scalar_mul(sw[:, :], sw[:, :], float(np.float32(1.0 / 7.0)))
                eng.tensor_scalar_max(sw[:, :], sw[:, :], 1e-12)
                eng.tensor_copy(rsw[:, :], sw[:, :])
                eng.reciprocal(rsw[:, :], rsw[:, :])
                eng.tensor_tensor(f32t[:, 0:1024].rearrange("p (a b) -> p a b", b=32),
                                  wst[:, :].rearrange("p (a b) -> p a b", b=32),
                                  rsw[:, :].rearrange("p (a b) -> p a b", b=1).broadcast_to([128, 32, 32]),
                                  ALU.mult)
                eng.tensor_copy(iqt[:, 0:1024], f32t[:, 0:1024])
                eng.tensor_copy(f32t[:, 0:1024], iqt[:, 0:1024])
                eng.tensor_scalar_min(f32t[:, 0:1024], f32t[:, 0:1024], 7.0)
                eng.tensor_scalar_max(f32t[:, 0:1024], f32t[:, 0:1024], -8.0)
                return eng.tensor_tensor(wqb[:, :].rearrange("p (a b) -> p a b", b=32),
                                         f32t[:, 0:1024].rearrange("p (a b) -> p a b", b=32),
                                         sw[:, :].rearrange("p (a b) -> p a b", b=1).broadcast_to([128, 32, 32]),
                                         ALU.mult)
            p.phase("vector", dq)
            for k in range(8):
                def trf(eng, k=k):
                    return [eng.dma_start(out=wqT[:, k * 128:(k + 1) * 128],
                                          in_=wqb[:, k * 128:(k + 1) * 128], transpose=True)]
                p.phase("sync", trf, n_dma=1)
            for (o, w) in CC:
                def umm(eng, o=o, w=w):
                    last = None
                    for k in range(8):
                        last = eng.matmul(cm.mm_ps[0][:, 0:w], wqT[:, k * 128:(k + 1) * 128],
                                          xqb[k][:, o:o + w], start=(k == 0), stop=(k == 7))
                    return last
                p.phase("tensor", umm)

                def uev(eng, o=o, w=w):
                    return eng.copy(ust[:, o:o + w], cm.mm_ps[0][:, 0:w])
                p.phase("scalar", uev)

            def upm(eng, m=m, pma=pma):
                return eng.tensor_reduce(pma[:, m:m + 1].rearrange("p (a b) -> p a b", b=1),
                                         ust[:, :].rearrange("p (a b) -> p a b", b=C),
                                         AX.X, ALU.max, apply_absolute_value=True)
            p.phase("vector", upm)
            p.dma("gpsimd", [(out_d[m * 128:(m + 1) * 128, :], ust[:, :])])

    for pma, po in ((pmacc[0], pmu_o), (pmacc[1], pmg_o)):
        def red(eng, pma=pma):
            return eng.tensor_reduce(pmx[:, 0:1].rearrange("p (a b) -> p a b", b=1),
                                     pma[:, :].rearrange("p (a b) -> p a b", b=24),
                                     AX.X, ALU.max)
        p.phase("vector", red)
        p.dma("sync", [(pmd[:, :], pmx[:, 0:1])])
        p.dma("sync", [(pmx[0:1, 0:128], pmd.rearrange("a b -> b a"))], allow_slow=True)

        def red2(eng):
            return eng.tensor_reduce(pmx[0:1, 128:129].rearrange("p (a b) -> p a b", b=1),
                                     pmx[0:1, 0:128].rearrange("p (a b) -> p a b", b=128),
                                     AX.X, ALU.max)
        p.phase("vector", red2)
        p.dma("gpsimd", [(po, pmx[0:1, 128:129])])
    return p


def build_f2():
    p = Prog("f2")
    gate_i = p.inp("gate", [3072, C])
    maxg = p.inp("maxg", [1, 8])
    t_o = p.out("tout", [3072, C])
    pmt_o = p.out("pmt", [1, 1])
    cm = Common(p, width=C)
    s128 = p.sbuf("s128", [128, 1], F32)
    rs128 = p.sbuf("rs128", [128, 1], F32)
    iqt = p.sbuf("iqt", [128, C], I32)
    f32t = p.sbuf("f32t", [128, C], F32)
    gst = p.sbuf("gst", [128, C], F32)
    gq = p.sbuf("gq", [128, C], F32)
    sg = p.sbuf("sg", [128, C], F32)
    pma = p.sbuf("pma", [128, 24], F32)
    pmx = p.sbuf("pmx", [128, 129], F32)
    pmd = p.dram("pmd", [128, 1])
    emit_gmax_scale(p, cm, maxg, s128, rs128, 32767.0, 1e-12)
    for m in range(24):
        p.dma("sync", [(gst[:, :], gate_i[m * 128:(m + 1) * 128, :])])

        def q1(eng):
            return emit_qdq(eng, gq[:, 0:C], gst[:, :], C, rs128[:, 0:1], s128[:, 0:1],
                            iqt, f32t, -32768.0, 32767.0)
        p.phase("vector", q1)

        def sg1(eng):
            return eng.activation(sg[:, :], gq[:, :], AF.Exp, scale=-1.0)
        p.phase("scalar", sg1)

        def sg2(eng):
            eng.tensor_scalar_add(sg[:, :], sg[:, :], 1.0)
            eng.reciprocal(sg[:, :], sg[:, :])
            eng.tensor_scalar_mul(f32t[:, :], sg[:, :], 65536.0)
            eng.tensor_copy(iqt[:, :], f32t[:, :])
            eng.tensor_copy(f32t[:, :], iqt[:, :])
            eng.tensor_scalar_min(f32t[:, :], f32t[:, :], 65535.0)
            eng.tensor_scalar_max(f32t[:, :], f32t[:, :], 0.0)
            eng.tensor_scalar_mul(sg[:, :], f32t[:, :], float(np.float32(1.0 / 65536.0)))
            eng.tensor_tensor(gq[:, :], gq[:, :], sg[:, :], ALU.mult)
            return eng.tensor_reduce(pma[:, m:m + 1].rearrange("p (a b) -> p a b", b=1),
                                     gq[:, :].rearrange("p (a b) -> p a b", b=C),
                                     AX.X, ALU.max, apply_absolute_value=True)
        p.phase("vector", sg2)
        p.dma("gpsimd", [(t_o[m * 128:(m + 1) * 128, :], gq[:, :])])

    def red(eng):
        return eng.tensor_reduce(pmx[:, 0:1].rearrange("p (a b) -> p a b", b=1),
                                 pma[:, :].rearrange("p (a b) -> p a b", b=24),
                                 AX.X, ALU.max)
    p.phase("vector", red)
    p.dma("sync", [(pmd[:, :], pmx[:, 0:1])])
    p.dma("sync", [(pmx[0:1, 0:128], pmd.rearrange("a b -> b a"))], allow_slow=True)

    def red2(eng):
        return eng.tensor_reduce(pmx[0:1, 128:129].rearrange("p (a b) -> p a b", b=1),
                                 pmx[0:1, 0:128].rearrange("p (a b) -> p a b", b=128),
                                 AX.X, ALU.max)
    p.phase("vector", red2)
    p.dma("gpsimd", [(pmt_o, pmx[0:1, 128:129])])
    return p


def build_f3():
    p = Prog("f3")
    t_i = p.inp("tin", [3072, C])
    up_i = p.inp("upin", [3072, C])
    maxt = p.inp("maxt", [1, 8])
    maxu = p.inp("maxu", [1, 8])
    o_o = p.out("opre", [3072, C])
    pmo_o = p.out("pmo", [1, 1])
    cm = Common(p, width=C)
    sT = p.sbuf("sT", [128, 1], F32); rsT = p.sbuf("rsT", [128, 1], F32)
    sU = p.sbuf("sU", [128, 1], F32); rsU = p.sbuf("rsU", [128, 1], F32)
    iqt = p.sbuf("iqt", [128, C], I32)
    f32t = p.sbuf("f32t", [128, C], F32)
    ast = p.sbuf("ast", [128, C], F32)
    bst = p.sbuf("bst", [128, C], F32)
    g2 = p.sbuf("g2", [128, C], F32)
    pma = p.sbuf("pma", [128, 24], F32)
    pmx = p.sbuf("pmx", [128, 129], F32)
    pmd = p.dram("pmd", [128, 1])
    emit_gmax_scale(p, cm, maxt, sT, rsT, 32767.0, 1e-12)
    emit_gmax_scale(p, cm, maxu, sU, rsU, 32767.0, 1e-12)
    for m in range(24):
        p.dma("sync", [(ast[:, :], t_i[m * 128:(m + 1) * 128, :]),
                       (bst[:, :], up_i[m * 128:(m + 1) * 128, :])])

        def q1(eng, m=m):
            emit_qdq(eng, g2[:, 0:C], ast[:, :], C, rsT[:, 0:1], sT[:, 0:1],
                     iqt, f32t, -32768.0, 32767.0)
            emit_qdq(eng, bst[:, 0:C], bst[:, :], C, rsU[:, 0:1], sU[:, 0:1],
                     iqt, f32t, -32768.0, 32767.0)
            eng.tensor_tensor(g2[:, :], g2[:, :], bst[:, :], ALU.mult)
            return eng.tensor_reduce(pma[:, m:m + 1].rearrange("p (a b) -> p a b", b=1),
                                     g2[:, :].rearrange("p (a b) -> p a b", b=C),
                                     AX.X, ALU.max, apply_absolute_value=True)
        p.phase("vector", q1)
        p.dma("gpsimd", [(o_o[m * 128:(m + 1) * 128, :], g2[:, :])])

    def red(eng):
        return eng.tensor_reduce(pmx[:, 0:1].rearrange("p (a b) -> p a b", b=1),
                                 pma[:, :].rearrange("p (a b) -> p a b", b=24),
                                 AX.X, ALU.max)
    p.phase("vector", red)
    p.dma("sync", [(pmd[:, :], pmx[:, 0:1])])
    p.dma("sync", [(pmx[0:1, 0:128], pmd.rearrange("a b -> b a"))], allow_slow=True)

    def red2(eng):
        return eng.tensor_reduce(pmx[0:1, 128:129].rearrange("p (a b) -> p a b", b=1),
                                 pmx[0:1, 0:128].rearrange("p (a b) -> p a b", b=128),
                                 AX.X, ALU.max)
    p.phase("vector", red2)
    p.dma("gpsimd", [(pmo_o, pmx[0:1, 128:129])])
    return p


def build_f4():
    p = Prog("f4")
    o_i = p.inp("oin", [3072, C])
    maxo = p.inp("maxo", [1, 8])
    wd_i = p.inp("wd", [1024, 3072])
    x3T = p.inp("x3T", [1024, C])
    cT8b = p.inp("cT8b", [128, 8])
    wmodb = p.inp("wmodb", [1024, 6144])
    x4_o = p.out("x4out", [1024, C])
    cm = Common(p, width=C)
    s128 = p.sbuf("s128", [128, 1], F32)
    rs128 = p.sbuf("rs128", [128, 1], F32)
    iqt = p.sbuf("iqt", [128, 3072], I32)
    f32t = p.sbuf("f32t", [128, 3072], F32)
    wst = p.sbuf("wst", [128, 3072], F32)
    sw = p.sbuf("sw", [128, 96], F32)
    rsw = p.sbuf("rsw", [128, 96], F32)
    wqb = p.sbuf("wqb", [128, 3072], BF16)
    wqT = p.sbuf("wqT", [128, 3072], BF16)
    ost = p.sbuf("ost", [128, C], F32)
    oqb = p.sbuf("oqb", [128, C], BF16)
    xst = p.sbuf("xst", [128, C], F32)
    oq_d = p.dram("oq_d", [3072, C], BF16)
    modx = emit_mod_vectors(p, cm, cT8b, wmodb, tag="m")
    emit_gmax_scale(p, cm, maxo, s128, rs128, 32767.0, 1e-12)
    for k in range(24):
        p.dma("sync", [(ost[:, :], o_i[k * 128:(k + 1) * 128, :])])

        def oq(eng):
            return emit_qdq(eng, oqb[:, 0:C], ost[:, :], C, rs128[:, 0:1], s128[:, 0:1],
                            iqt, f32t, -32768.0, 32767.0)
        p.phase("vector", oq)
        p.dma("gpsimd", [(oq_d[k * 128:(k + 1) * 128, :], oqb[:, :])])

    for m in range(8):
        p.dma("sync", [(wst[:, :], wd_i[m * 128:(m + 1) * 128, :])])

        def dq(eng, m=m):
            eng.tensor_reduce(sw[:, :].rearrange("p (a b) -> p a b", b=1),
                              wst[:, :].rearrange("p (a b) -> p a b", b=32),
                              AX.X, ALU.max, apply_absolute_value=True)
            eng.tensor_scalar_mul(sw[:, :], sw[:, :], float(np.float32(1.0 / 7.0)))
            eng.tensor_scalar_max(sw[:, :], sw[:, :], 1e-12)
            eng.tensor_copy(rsw[:, :], sw[:, :])
            eng.reciprocal(rsw[:, :], rsw[:, :])
            eng.tensor_tensor(f32t[:, :].rearrange("p (a b) -> p a b", b=32),
                              wst[:, :].rearrange("p (a b) -> p a b", b=32),
                              rsw[:, :].rearrange("p (a b) -> p a b", b=1).broadcast_to([128, 96, 32]),
                              ALU.mult)
            eng.tensor_copy(iqt[:, :], f32t[:, :])
            eng.tensor_copy(f32t[:, :], iqt[:, :])
            eng.tensor_scalar_min(f32t[:, :], f32t[:, :], 7.0)
            eng.tensor_scalar_max(f32t[:, :], f32t[:, :], -8.0)
            return eng.tensor_tensor(wqb[:, :].rearrange("p (a b) -> p a b", b=32),
                                     f32t[:, :].rearrange("p (a b) -> p a b", b=32),
                                     sw[:, :].rearrange("p (a b) -> p a b", b=1).broadcast_to([128, 96, 32]),
                                     ALU.mult)
        p.phase("vector", dq)
        for k in range(24):
            def trf(eng, k=k):
                return [eng.dma_start(out=wqT[:, k * 128:(k + 1) * 128],
                                      in_=wqb[:, k * 128:(k + 1) * 128], transpose=True)]
            p.phase("sync", trf, n_dma=1)
        p.dma("sync", [(xst[:, :], x3T[m * 128:(m + 1) * 128, :])])
        for (o, w) in CC:
            for k in range(24):
                p.dma("sync", [(oqb[:, 0:w], oq_d[k * 128:(k + 1) * 128, o:o + w])])

                def dmm(eng, k=k, o=o, w=w):
                    return eng.matmul(cm.mm_ps[0][:, 0:w], wqT[:, k * 128:(k + 1) * 128],
                                      oqb[:, 0:w], start=(k == 0), stop=(k == 23))
                p.phase("tensor", dmm)

            def dev(eng, o=o, w=w):
                return eng.copy(ost[:, o:o + w], cm.mm_ps[0][:, 0:w])
            p.phase("scalar", dev)

        def fin(eng, m=m):
            return eng.scalar_tensor_tensor(ost[:, :], ost[:, :], modx[:, 40 + m:41 + m],
                                            xst[:, :], ALU.mult, ALU.add)
        p.phase("vector", fin)
        p.dma("gpsimd", [(x4_o[m * 128:(m + 1) * 128, :], ost[:, :])])
    return p



def kernel(**I):
    I = {k: np.asarray(v) for k, v in I.items()}

    res_a = run_prog("l0a", build_l0a, l0a_inputs(I))
    res_b = run_prog("l0b", build_l0b, l0b_inputs(I, res_a))
    x1_shards = [res_b[r]["x1out"] for r in range(NCORES)]
    res_c = run_prog("l0c", build_l0c, l0c_inputs(I, x1_shards))
    h3_full = np.ascontiguousarray(np.concatenate([res_c[r]["h3out"] for r in range(NCORES)], axis=1))

    fh2, bh2u, _, _ = run_l1_chain(I, h3_full)

    mW = I["L1_attn_merge_W"]
    WfT = np.ascontiguousarray(mW[:, 0:1024].T)
    WbT = np.ascontiguousarray(mW[:, 1024:2048].T)
    mb8 = _r8(I["L1_attn_merge_b"])
    cT8b = _r8(I["L1_c"][0])
    maps = []
    for r in range(NCORES):
        lo = r * C
        maps.append(dict(fh2T=np.ascontiguousarray(fh2[:, lo:lo + C]),
                         bh2T=np.ascontiguousarray(bh2u[:, lo:lo + C]),
                         x2T=res_c[r]["x2out"], cT8b=cT8b, wmodb=I["L1_wmod"],
                         WfT=WfT, WbT=WbT, mb8=mb8))
    res_m = run_prog("l1m", build_l1m, maps)

    maxes = np.ascontiguousarray(np.array([[res_m[r]["pmax"][0, 0] for r in range(NCORES)]],
                                          np.float32))
    maps = [dict(h4T=res_m[r]["h4out"], maxes=maxes, wup=I["L1_ffn_w_up"],
                 wgate=I["L1_ffn_w_gate"]) for r in range(NCORES)]
    res_1 = run_prog("f1", build_f1, maps)

    maxg = np.ascontiguousarray(np.array([[res_1[r]["pmg"][0, 0] for r in range(NCORES)]], np.float32))
    maps = [dict(gate=res_1[r]["gate"], maxg=maxg) for r in range(NCORES)]
    res_2 = run_prog("f2", build_f2, maps)

    maxu = np.ascontiguousarray(np.array([[res_1[r]["pmu"][0, 0] for r in range(NCORES)]], np.float32))
    maxt = np.ascontiguousarray(np.array([[res_2[r]["pmt"][0, 0] for r in range(NCORES)]], np.float32))
    maps = [dict(tin=res_2[r]["tout"], upin=res_1[r]["up"], maxt=maxt, maxu=maxu)
            for r in range(NCORES)]
    res_3 = run_prog("f3", build_f3, maps)

    maxo = np.ascontiguousarray(np.array([[res_3[r]["pmo"][0, 0] for r in range(NCORES)]], np.float32))
    maps = [dict(oin=res_3[r]["opre"], maxo=maxo, wd=I["L1_ffn_w_down"],
                 x3T=res_m[r]["x3out"], cT8b=cT8b, wmodb=I["L1_wmod"]) for r in range(NCORES)]
    res_4 = run_prog("f4", build_f4, maps)

    full = np.concatenate([res_4[r]["x4out"] for r in range(NCORES)], axis=1)
    return np.ascontiguousarray(full.T)[None, :, :]


# version 02
# speedup vs baseline: 1.2918x; 1.2918x over previous
import numpy as np
import ml_dtypes
import concourse.bass as bass
import concourse.mybir as mybir
from concourse.bass_utils import run_bass_kernel_spmd

F32 = mybir.dt.float32
BF16 = mybir.dt.bfloat16
I32 = mybir.dt.int32
AF = mybir.ActivationFunctionType
ALU = mybir.AluOpType
AX = mybir.AxisListType

NCORES = 8
C = 2048
H = 3
TC = C + H
BF = ml_dtypes.bfloat16
LN_EPS = 1e-5


def chunks_of(total, step=512):
    out, o = [], 0
    while o < total:
        out.append((o, min(step, total - o)))
        o += step
    return out


TCC = chunks_of(TC)
CC = chunks_of(C)


class Prog:
    def __init__(self, name):
        self.name = name
        self.nc = bass.Bass("TRN2", target_bir_lowering=False, debug=False)
        self.phases = []

    def inp(self, name, shape, dtype=F32):
        return self.nc.dram_tensor(name, list(shape), dtype, kind="ExternalInput").ap()

    def out(self, name, shape, dtype=F32):
        return self.nc.dram_tensor(name, list(shape), dtype, kind="ExternalOutput").ap()

    def dram(self, name, shape, dtype=F32):
        return self.nc.dram_tensor(name, list(shape), dtype).ap()

    def sbuf(self, name, shape, dtype=F32, side=None):
        kw = dict(side=side) if side else {}
        ctx = self.nc.sbuf_tensor(name, list(shape), dtype, **kw)
        t = ctx.__enter__()
        self._sbuf_ctx = getattr(self, "_sbuf_ctx", {})
        self._sbuf_ctx[id(t)] = ctx
        return t

    def free(self, *tiles):
        for t in reversed(tiles):
            self._sbuf_ctx.pop(id(t)).__exit__(None, None, None)

    def psum(self, name, shape, dtype=F32):
        return self.nc.psum_tensor(name, list(shape), dtype).__enter__()

    def phase(self, engine, fn, n_dma=0):
        self.phases.append((engine, fn, n_dma))

    def dma(self, engine, transfers, allow_slow=False):
        def fn(eng):
            kw = dict(allow_slow_non_contiguous=True) if allow_slow else {}
            return [eng.dma_start(out=o, in_=i, **kw) for (o, i) in transfers]
        self.phase(engine, fn, n_dma=len(transfers))

    def finalize(self):
        nc = self.nc
        gs = nc.semaphore("gs").__enter__()
        cs_sems = {e: nc.semaphore("cs_" + e).__enter__() for e in ("vector", "scalar", "gpsimd")}
        targets, tot = [], 0
        for (_, _, n_dma) in self.phases:
            targets.append(tot)
            tot += 16 * n_dma if n_dma else 1
        final_total = tot
        block = nc.Block().__enter__()

        GUARded = ("vector", "scalar", "gpsimd")
        WRAP = {"tensor_tensor", "tensor_scalar", "tensor_single_scalar", "tensor_scalar_add",
                "tensor_scalar_sub", "tensor_scalar_mul", "tensor_scalar_max", "tensor_scalar_min",
                "tensor_add", "tensor_sub", "tensor_mul", "tensor_max", "tensor_copy", "memset",
                "memzero", "scalar_tensor_tensor", "tensor_tensor_scan", "tensor_reduce",
                "reduce_max", "reduce_sum", "tensor_relu", "reciprocal", "select", "activation",
                "copy", "mul", "add", "square", "sign", "sqrt", "pool", "pool_max", "pool_avg",
                "iota", "transpose", "bn_stats", "bn_aggr", "stream_shuffle", "copy_predicated"}

        class GEng:
            def __init__(self, eng, sem):
                self._e = eng
                self._sem = sem
                self._n = 0

            def __getattr__(self, name):
                f = getattr(self._e, name)
                if name in WRAP:
                    def wrapped(*a, **kw):
                        inst = f(*a, **kw)
                        self._n += 1
                        inst.then_inc(self._sem, 1)
                        self._e.wait_ge(self._sem, self._n)
                        return inst
                    return wrapped
                return f

        def runner(engine_name):
            def run(eng):
                geng = GEng(eng, cs_sems[engine_name]) if engine_name in GUARded else eng
                for idx, (en, fn, n_dma) in enumerate(self.phases):
                    if en != engine_name:
                        continue
                    if targets[idx] > 0:
                        eng.wait_ge(gs, targets[idx])
                    last = fn(geng)
                    if n_dma:
                        for inst in last:
                            inst.then_inc(gs, 16)
                    elif engine_name == "tensor":
                        if last is None:
                            eng.sem_inc(gs, 1)
                        else:
                            last.then_inc(gs, 1)
                    else:
                        eng.sem_inc(gs, 1)
                if engine_name == "gpsimd":
                    eng.wait_ge(gs, final_total)
            return run

        @block.sync
        def _(eng):
            runner("sync")(eng)

        @block.scalar
        def _(eng):
            runner("scalar")(eng)

        @block.vector
        def _(eng):
            runner("vector")(eng)

        @block.tensor
        def _(eng):
            runner("tensor")(eng)

        @block.gpsimd
        def _(eng):
            runner("gpsimd")(eng)

        block.__exit__(None, None, None)
        return nc



class Common:
    def __init__(self, prog, width=TC):
        self.prog = prog
        self.width = width
        self.ones = prog.sbuf("cm_ones", [128, 1], F32)
        self.ones_row = prog.sbuf("cm_onesr", [1, 128], F32)
        self.sq = prog.sbuf("cm_sq", [128, 512], F32)
        self.stat_ps = prog.psum("cm_statps", [1, 512])
        self.bc_ps = prog.psum("cm_bcps", [128, 512])
        self.bc2_ps = prog.psum("cm_bc2ps", [128, 512])
        self.mm_ps = [prog.psum("cm_mmps0", [128, 512]), prog.psum("cm_mmps1", [128, 512])]
        self.bcs1 = prog.sbuf("cm_bcs1", [128, 512], F32)
        self.bcs2 = prog.sbuf("cm_bcs2", [128, 512], F32)
        self.mu = prog.sbuf("cm_mu", [1, width], F32)
        self.rs = prog.sbuf("cm_rs", [1, width], F32)
        self.tp = prog.sbuf("cm_tp", [1, width], F32)

        def init_ones(eng):
            eng.memset(self.ones[:, :], 0.0)
            eng.tensor_scalar_add(self.ones[:, :], self.ones[:, :], 1.0)
            eng.memset(self.ones_row[:, :], 0.0)
            return eng.tensor_scalar_add(self.ones_row[:, :], self.ones_row[:, :], 1.0)
        prog.phase("vector", init_ones)


def emit_ln(prog, cm, name, src_tiles, dst_tiles, width, gamma, beta):
    ntiles = len(src_tiles)
    D = 128 * ntiles
    mu, rs, tp = cm.mu, cm.rs, cm.tp
    cks = chunks_of(width)

    for (o, w) in cks:
        def mm(eng, o=o, w=w):
            last = None
            for k in range(ntiles):
                last = eng.matmul(cm.stat_ps[0:1, 0:w], cm.ones[:, 0:1],
                                  src_tiles[k][:, o:o + w], start=(k == 0), stop=(k == ntiles - 1))
            return last
        prog.phase("tensor", mm)

        def ev(eng, o=o, w=w):
            return eng.mul(mu[0:1, o:o + w], cm.stat_ps[0:1, 0:w], 1.0 / D)
        prog.phase("scalar", ev)

    for (o, w) in cks:
        for k in range(ntiles):
            def sq(eng, o=o, w=w, k=k):
                return eng.activation(cm.sq[:, 0:w], src_tiles[k][:, o:o + w], AF.Square)
            prog.phase("scalar", sq)

            def mm2(eng, o=o, w=w, k=k):
                return eng.matmul(cm.stat_ps[0:1, 0:w], cm.ones[:, 0:1], cm.sq[:, 0:w],
                                  start=(k == 0), stop=(k == ntiles - 1))
            prog.phase("tensor", mm2)

        def ev2(eng, o=o, w=w):
            return eng.mul(rs[0:1, o:o + w], cm.stat_ps[0:1, 0:w], 1.0 / D)
        prog.phase("scalar", ev2)

    def ph_v(eng):
        eng.tensor_tensor(tp[0:1, 0:width], mu[0:1, 0:width], mu[0:1, 0:width], ALU.mult)
        eng.tensor_tensor(rs[0:1, 0:width], rs[0:1, 0:width], tp[0:1, 0:width], ALU.subtract)
        return eng.tensor_scalar_add(rs[0:1, 0:width], rs[0:1, 0:width], LN_EPS)
    prog.phase("vector", ph_v)

    def ph_l(eng):
        eng.activation(rs[0:1, 0:width], rs[0:1, 0:width], AF.Ln)
        return eng.activation(rs[0:1, 0:width], rs[0:1, 0:width], AF.Exp, scale=-0.5)
    prog.phase("scalar", ph_l)

    for (o, w) in cks:
        def bc(eng, o=o, w=w):
            eng.matmul(cm.bc_ps[:, 0:w], cm.ones_row[0:1, :], mu[0:1, o:o + w], start=True, stop=True)
            return eng.matmul(cm.bc2_ps[:, 0:w], cm.ones_row[0:1, :], rs[0:1, o:o + w], start=True, stop=True)
        prog.phase("tensor", bc)

        def bce(eng, o=o, w=w):
            eng.copy(cm.bcs1[:, 0:w], cm.bc_ps[:, 0:w])
            return eng.copy(cm.bcs2[:, 0:w], cm.bc2_ps[:, 0:w])
        prog.phase("scalar", bce)

        def ap(eng, o=o, w=w):
            last = None
            for k in range(ntiles):
                g = 1.0 if gamma is None else gamma[k]
                b = 0.0 if beta is None else beta[k]
                eng.tensor_tensor(cm.sq[:, 0:w], src_tiles[k][:, o:o + w], cm.bcs1[:, 0:w], ALU.subtract)
                eng.scalar_tensor_tensor(cm.sq[:, 0:w], cm.sq[:, 0:w], g, cm.bcs2[:, 0:w], ALU.mult, ALU.mult)
                last = eng.tensor_scalar_add(dst_tiles[k][:, o:o + w], cm.sq[:, 0:w], b)
            return last
        prog.phase("vector", ap)
    return mu, rs


def emit_matmul(prog, cm, name, lhsT_tiles, rhs_tiles, out_cb, M_tiles, K_tiles, width,
                n_step=512, accum_sbuf=None):
    cks = chunks_of(width, n_step)
    pi = 0
    for m in range(M_tiles):
        for (o, w) in cks:
            ps = cm.mm_ps[pi % 2]
            pi += 1

            def mm(eng, m=m, o=o, w=w, ps=ps):
                last = None
                for k in range(K_tiles):
                    last = eng.matmul(ps[:, 0:w], lhsT_tiles(k, m), rhs_tiles(k)[:, o:o + w],
                                      start=(k == 0), stop=(k == K_tiles - 1))
                return last
            prog.phase("tensor", mm)
            out_cb(prog, m, o, w, ps)



def shard_tokens(x_t):
    outs = []
    d = x_t.shape[0]
    for r in range(NCORES):
        lo = r * C
        halo = x_t[:, lo - H:lo] if r > 0 else x_t[:, 0:H]
        outs.append(np.ascontiguousarray(np.concatenate([halo, x_t[:, lo:lo + C]], axis=1)))
    return outs


_prog_cache = {}


def run_prog(key, builder, in_maps, trace=False):
    if key not in _prog_cache:
        p = builder()
        nc = p.finalize()
        _prog_cache[key] = (p, nc)
    p, nc = _prog_cache[key]
    res = run_bass_kernel_spmd(nc, in_maps, core_ids=list(range(NCORES)))
    return res.results



def emit_mod_vectors(prog, cm, cT_in, wmod_in, tag=""):
    nc = prog.nc
    c8 = prog.sbuf("mv_c8" + tag, [128, 8], F32)
    sc8 = prog.sbuf("mv_sc8" + tag, [128, 8], F32)
    wbuf = prog.sbuf("mv_wbuf" + tag, [128, 4 * 128], F32)
    modx = prog.sbuf("mv_modx" + tag, [128, 64], F32)
    mps = cm.mm_ps[0]

    prog.dma("sync", [(c8[:, :], cT_in)])

    def silu_c(eng):
        return eng.activation(sc8[:, :], c8[:, :], AF.Silu)
    prog.phase("scalar", silu_c)


    for m in range(48):
        for hf in range(2):
            prog.dma("sync", [(wbuf[:, k * 128:(k + 1) * 128],
                               wmod_in[(4 * hf + k) * 128:(4 * hf + k + 1) * 128,
                                       m * 128:(m + 1) * 128]) for k in range(4)])

            def mm(eng, m=m, hf=hf):
                last = None
                for k in range(4):
                    last = eng.matmul(mps[:, 0:1], wbuf[:, k * 128:(k + 1) * 128],
                                      sc8[:, 4 * hf + k:4 * hf + k + 1],
                                      start=(hf == 0 and k == 0), stop=(hf == 1 and k == 3))
                return last
            prog.phase("tensor", mm)

        def ev(eng, m=m):
            return eng.copy(modx[:, m:m + 1], mps[:, 0:1])
        prog.phase("scalar", ev)

    def fix(eng):
        for base in (16, 40):
            sl = modx[:, base:base + 8]
            eng.tensor_scalar(cm.sq[:, 0:8], sl, 0.0, None, ALU.is_ge)
            eng.tensor_scalar_mul(cm.sq[:, 0:8], cm.sq[:, 0:8], 2.0)
            eng.tensor_scalar_add(cm.sq[:, 0:8], cm.sq[:, 0:8], -1.0)
            eng.tensor_scalar_mul(cm.sq[:, 8:16], sl, -1.0)
            eng.tensor_tensor(cm.sq[:, 8:16], cm.sq[:, 8:16], sl, ALU.max)
            eng.tensor_scalar_max(cm.sq[:, 8:16], cm.sq[:, 8:16], 0.25)
            eng.tensor_tensor(sl, cm.sq[:, 0:8], cm.sq[:, 8:16], ALU.mult)
        eng.tensor_scalar_add(modx[:, 48:56], modx[:, 8:16], 1.0)
        return eng.tensor_scalar_add(modx[:, 56:64], modx[:, 32:40], 1.0)
    prog.phase("vector", fix)
    return modx


def build_l0a():
    p = Prog("l0a")
    xT = p.inp("xT", [1024, TC])
    hmask = p.inp("hmask", [128, H])
    cT8 = p.inp("cT8", [128, 8])
    wmod = p.inp("wmod", [1024, 6144])
    lng8 = p.inp("lng8", [128, 8])
    lnb8 = p.inp("lnb8", [128, 8])
    w_inT = p.inp("w_inT", [1024, 4096])
    convw = p.inp("convw", [128, 64])
    convb = p.inp("convb", [128, 16])
    BmT = p.inp("BmT", [2048, 16])
    AT = p.inp("AT", [16, 16])

    S0_o = p.out("S0", [16, C])
    hdbg_o = p.out("hdbg", [1024, TC])
    xc_o = p.out("xc", [2048, C])
    z_o = p.out("z", [2048, C])

    cm = Common(p)
    x_t = [p.sbuf(f"x{k}", [128, TC]) for k in range(8)]
    xn_t = [p.sbuf(f"xn{k}", [128, TC], BF16) for k in range(8)]
    cwb = p.sbuf("cwb", [128, 64], F32)
    cbb = p.sbuf("cbb", [128, 16], F32)
    hmk = p.sbuf("hmk", [128, H], F32)
    bmb = p.sbuf("bmb", [128, 16 * 16], F32)
    wkb = [p.sbuf(f"wkb{i}", [128, 8 * 128], BF16) for i in range(2)]
    xp = p.sbuf("xp", [128, TC], F32)
    cxs = p.sbuf("cxs", [128, C], F32)
    V = p.sbuf("V", [16, TC], F32)
    v_ps = p.psum("v_ps", [16, 512])
    Sb = [p.sbuf("Sb0", [16, C]), p.sbuf("Sb1", [16, C])]
    Pk = p.sbuf("Pk", [16, 16 * 11], F32)
    pq_ps = p.psum("pq_ps", [16, 16])
    ptmp = p.dram("ptmp", [16, 16])

    lngb = p.sbuf("lngb", [128, 16], F32)
    prog_loads = [(x_t[k][:, :], xT[k * 128:(k + 1) * 128, :]) for k in range(8)]
    prog_loads += [(cwb[:, :], convw), (cbb[:, :], convb), (hmk[:, :], hmask),
                   (lngb[:, 0:8], lng8), (lngb[:, 8:16], lnb8)]
    p.dma("sync", prog_loads)
    p.dma("sync", [(bmb[:, k * 16:(k + 1) * 16], BmT[k * 128:(k + 1) * 128, :]) for k in range(16)])
    p.dma("sync", [(Pk[:, 0:16], AT)])

    modx = emit_mod_vectors(p, cm, cT8, wmod)

    emit_ln(p, cm, "ln1", x_t, x_t, TC,
            gamma=[modx[:, 48 + k:49 + k] for k in range(8)],
            beta=[modx[:, 0 + k:1 + k] for k in range(8)])
    p.dma("gpsimd", [(hdbg_o[k * 128:(k + 1) * 128, :], x_t[k][:, :]) for k in range(8)])
    emit_ln(p, cm, "ln2", x_t, xn_t, TC,
            gamma=[lngb[:, k:k + 1] for k in range(8)],
            beta=[lngb[:, 8 + k:9 + k] for k in range(8)])

    for m in range(32):
        wb = wkb[m % 2]
        p.dma("gpsimd", [(wb[:, k * 128:(k + 1) * 128],
                          w_inT[k * 128:(k + 1) * 128, m * 128:(m + 1) * 128]) for k in range(8)])
        for (o, w) in TCC:
            ps = cm.mm_ps[0]

            def mm(eng, m=m, o=o, w=w, ps=ps, wb=wb):
                last = None
                for k in range(8):
                    last = eng.matmul(ps[:, 0:w], wb[:, k * 128:(k + 1) * 128],
                                      xn_t[k][:, o:o + w], start=(k == 0), stop=(k == 7))
                return last
            p.phase("tensor", mm)

            def ev(eng, m=m, o=o, w=w, ps=ps):
                return eng.copy(xp[:, o:o + w], ps[:, 0:w])
            p.phase("scalar", ev)

        if m < 16:
            def mask(eng, m=m):
                return eng.tensor_tensor(xp[:, 0:H], xp[:, 0:H], hmk[:, :], ALU.mult)
            p.phase("vector", mask)

            for ci, (o, w) in enumerate(TCC):
                def vmm2(eng, m=m, o=o, w=w):
                    return eng.matmul(v_ps[:, 0:w], bmb[:, m * 16:(m + 1) * 16], xp[:, o:o + w],
                                      start=True, stop=True)
                p.phase("tensor", vmm2)

                def vev(eng, o=o, w=w):
                    return eng.copy(cm.bcs1[0:16, 0:w], v_ps[:, 0:w])
                p.phase("scalar", vev)
                if m == 0:
                    def vadd(eng, o=o, w=w):
                        return eng.tensor_copy(V[:, o:o + w], cm.bcs1[0:16, 0:w])
                else:
                    def vadd(eng, o=o, w=w):
                        return eng.tensor_tensor(V[:, o:o + w], V[:, o:o + w], cm.bcs1[0:16, 0:w], ALU.add)
                p.phase("vector", vadd)

            def conv(eng, m=m):
                eng.tensor_scalar_mul(cxs[:, 0:C], xp[:, 0:C], cwb[:, 4 * m:4 * m + 1])
                for j in range(1, 4):
                    eng.scalar_tensor_tensor(cxs[:, 0:C], xp[:, j:j + C], cwb[:, 4 * m + j:4 * m + j + 1],
                                             cxs[:, 0:C], ALU.mult, ALU.add)
                return None
            p.phase("vector", conv)

            def sil(eng, m=m):
                return eng.activation(cxs[:, 0:C], cxs[:, 0:C], AF.Silu, bias=cbb[:, m:m + 1])
            p.phase("scalar", sil)
            p.dma("gpsimd", [(xc_o[m * 128:(m + 1) * 128, :], cxs[:, 0:C])])
        else:
            p.dma("gpsimd", [(z_o[(m - 16) * 128:(m - 15) * 128, :], xp[:, H:TC])])

    for k in range(10):
        src = Pk[:, 16 * k:16 * k + 16]
        dst = Pk[:, 16 * (k + 1):16 * (k + 1) + 16]
        p.dma("sync", [(ptmp[:, :], src)])
        p.dma("sync", [(cm.sq[0:16, 0:16], ptmp.rearrange("a b -> b a"))], allow_slow=True)

        def sq(eng, k=k, src=src):
            return eng.matmul(pq_ps[:, :], cm.sq[0:16, 0:16], src, start=True, stop=True)
        p.phase("tensor", sq)

        def ev(eng, dst=dst):
            return eng.copy(dst, pq_ps[:, :])
        p.phase("scalar", ev)

    def initS(eng):
        return eng.tensor_copy(Sb[0][:, :], V[:, H:TC])
    p.phase("vector", initS)
    cur = 0
    for k in range(11):
        d = 1 << k
        src_t, dst_t = Sb[cur], Sb[1 - cur]
        def cpre(eng, d=d, src_t=src_t, dst_t=dst_t):
            return eng.tensor_copy(dst_t[:, 0:d], src_t[:, 0:d])
        p.phase("vector", cpre)
        for (o, w) in chunks_of(C):
            lo = max(o, d)
            hi = o + w
            if lo >= hi:
                continue
            ww = hi - lo

            def mmk(eng, k=k, lo=lo, ww=ww, src_t=src_t):
                return eng.matmul(cm.mm_ps[0][0:16, 0:ww], Pk[:, 16 * k:16 * k + 16],
                                  src_t[:, lo - (1 << k):lo - (1 << k) + ww],
                                  start=True, stop=True)
            p.phase("tensor", mmk)

            def cpk(eng, ww=ww):
                return eng.copy(cm.bcs2[0:16, 0:ww], cm.mm_ps[0][0:16, 0:ww])
            p.phase("scalar", cpk)

            def addk(eng, lo=lo, ww=ww, src_t=src_t, dst_t=dst_t):
                return eng.tensor_tensor(dst_t[:, lo:lo + ww], src_t[:, lo:lo + ww],
                                         cm.bcs2[0:16, 0:ww], ALU.add)
            p.phase("vector", addk)
        cur = 1 - cur
    p.dma("gpsimd", [(S0_o, Sb[cur][:, :])])
    return p



def _r8(v):
    return np.ascontiguousarray(v.reshape(8, 128).T)


def _r16(v):
    return np.ascontiguousarray(v.reshape(16, 128).T)


def l0a_inputs(I):
    x2d = np.ascontiguousarray(I["L0_attn_x"][0].T)
    shards = shard_tokens(x2d)
    wmod = np.ascontiguousarray(I["L0_wmod"])
    cT8 = _r8(I["L0_c"][0])
    lng8 = _r8(I["L0_attn_ln_gamma"])
    lnb8 = _r8(I["L0_attn_ln_beta"])
    w_inT = np.ascontiguousarray(I["L0_attn_W_in"].T)
    cw = I["L0_attn_conv_w"].reshape(2048, 4)
    convw = np.ascontiguousarray(cw.reshape(16, 128, 4).transpose(1, 0, 2).reshape(128, 64))
    convb = _r16(I["L0_attn_conv_b"])
    BmT = np.ascontiguousarray(I["L0_attn_Bm"].T)
    AT = np.ascontiguousarray(I["L0_attn_A"].T)
    maps = []
    for r in range(NCORES):
        hm = np.zeros((128, H), np.float32) if r == 0 else np.ones((128, H), np.float32)
        maps.append(dict(xT=shards[r], hmask=hm, cT8=cT8, wmod=wmod, lng8=lng8,
                         lnb8=lnb8, w_inT=w_inT, convw=convw, convb=convb, BmT=BmT, AT=AT))
    return maps


def emit_l0scan(p, cm, name, Vap, Pk, out_sb):
    Sb = out_sb

    def initS(eng):
        return eng.tensor_copy(Sb[0][:, :], Vap)
    p.phase("vector", initS)
    cur = 0
    for k in range(11):
        d = 1 << k
        src_t, dst_t = Sb[cur], Sb[1 - cur]

        def cpre(eng, d=d, src_t=src_t, dst_t=dst_t):
            return eng.tensor_copy(dst_t[:, 0:d], src_t[:, 0:d])
        p.phase("vector", cpre)
        for (o, w) in chunks_of(C):
            lo = max(o, d)
            hi = o + w
            if lo >= hi:
                continue
            ww = hi - lo

            def mmk(eng, k=k, lo=lo, ww=ww, src_t=src_t, d=d):
                return eng.matmul(cm.mm_ps[0][0:16, 0:ww], Pk[:, 16 * k:16 * k + 16],
                                  src_t[:, lo - d:lo - d + ww], start=True, stop=True)
            p.phase("tensor", mmk)

            def cpk(eng, ww=ww):
                return eng.copy(cm.bcs2[0:16, 0:ww], cm.mm_ps[0][0:16, 0:ww])
            p.phase("scalar", cpk)

            def addk(eng, lo=lo, ww=ww, src_t=src_t, dst_t=dst_t):
                return eng.tensor_tensor(dst_t[:, lo:lo + ww], src_t[:, lo:lo + ww],
                                         cm.bcs2[0:16, 0:ww], ALU.add)
            p.phase("vector", addk)
        cur = 1 - cur
    return Sb[cur]


def emit_powers(p, cm, Pk, nsq, ps):
    ptmp = p.dram("ptmp_" + str(nsq), [16, 16])
    for k in range(nsq):
        src = Pk[:, 16 * k:16 * k + 16]
        dst = Pk[:, 16 * (k + 1):16 * (k + 1) + 16]
        p.dma("sync", [(ptmp[:, :], src)])
        p.dma("sync", [(cm.sq[0:16, 0:16], ptmp.rearrange("a b -> b a"))], allow_slow=True)

        def sq(eng, src=src):
            return eng.matmul(ps[:, 0:16], cm.sq[0:16, 0:16], src, start=True, stop=True)
        p.phase("tensor", sq)

        def ev(eng, dst=dst):
            return eng.copy(dst, ps[:, 0:16])
        p.phase("scalar", ev)


def build_l0b():
    p = Prog("l0b")
    xT = p.inp("xT", [1024, C])
    S0_i = p.inp("S0", [16, C])
    IordT = p.inp("IordT", [16, 8])
    AT = p.inp("AT", [16, 16])
    Cm_i = p.inp("Cm", [16, 2048])
    woT = p.inp("woT", [2048, 1024])
    cT8 = p.inp("cT8", [128, 8])
    wmod = p.inp("wmod", [1024, 6144])
    xc_i = p.inp("xc", [2048, C])
    z_i = p.inp("z", [2048, C])
    h_i = p.inp("hsp", [1024, TC])
    x1_o = p.out("x1out", [1024, C])

    cm = Common(p)
    x_t = [p.sbuf(f"x{k}", [128, C]) for k in range(8)]
    p.dma("sync", [(x_t[k][:, :], xT[k * 128:(k + 1) * 128, :]) for k in range(8)])
    modx = emit_mod_vectors(p, cm, cT8, wmod)

    Pk = p.sbuf("Pk", [16, 16 * 12], F32)
    pq_ps = p.psum("pq_ps", [16, 16])
    p.dma("sync", [(Pk[:, 0:16], AT)])
    emit_powers(p, cm, Pk, 11, pq_ps)
    P2048 = Pk[:, 16 * 11:16 * 12]
    Wp = p.sbuf("Wp", [16, 16 * 8], F32)
    iord = p.sbuf("iord", [16, 8], F32)
    p.dma("sync", [(Wp[:, 0:16], AT), (iord[:, :], IordT)])
    wtmp = p.dram("wtmp", [16, 16])
    for j in range(7):
        src = Wp[:, 16 * j:16 * j + 16]
        dst = Wp[:, 16 * (j + 1):16 * (j + 1) + 16]
        p.dma("sync", [(wtmp[:, :], src)])
        p.dma("sync", [(cm.sq[0:16, 0:16], wtmp.rearrange("a b -> b a"))], allow_slow=True)

        def mmw(eng, src=src):
            return eng.matmul(pq_ps[:, 0:16], cm.sq[0:16, 0:16], P2048, start=True, stop=True)
        p.phase("tensor", mmw)

        def evw(eng, dst=dst):
            return eng.copy(dst, pq_ps[:, 0:16])
        p.phase("scalar", evw)

    ha_ps = p.psum("ha_ps", [1, 16])
    harow = p.sbuf("harow", [1, 16], F32)

    def mmha(eng):
        last = None
        for j in range(8):
            last = eng.matmul(ha_ps[0:1, 0:16], iord[:, j:j + 1], Wp[:, 16 * j:16 * j + 16],
                              start=(j == 0), stop=(j == 7))
        return last
    p.phase("tensor", mmha)

    def evha(eng):
        return eng.copy(harow[0:1, 0:16], ha_ps[0:1, 0:16])
    p.phase("scalar", evha)

    hacol = p.dram("hacol", [16, 1])
    V2 = p.sbuf("V2", [16, C], F32)

    def clr(eng):
        return eng.memset(V2[:, :], 0.0)
    p.phase("vector", clr)
    hatmp = p.dram("hatmp", [1, 16])
    p.dma("sync", [(hatmp[:, :], harow[0:1, :])])
    p.dma("sync", [(V2[:, 0:1], hatmp.rearrange("a b -> b a"))], allow_slow=True)

    Sb = [p.sbuf("Sb0", [16, C]), p.sbuf("Sb1", [16, C])]
    cfin = emit_l0scan(p, cm, "corr", V2[:, :], Pk, Sb)

    for (o, w) in CC:
        p.dma("sync", [(cm.bcs1[0:16, 0:w], S0_i[:, o:o + w])])

        def sadd(eng, o=o, w=w):
            return eng.tensor_tensor(cfin[:, o:o + w], cfin[:, o:o + w], cm.bcs1[0:16, 0:w], ALU.add)
        p.phase("vector", sadd)

    Cms = p.sbuf("Cms", [16, 2048], F32)
    p.dma("sync", [(Cms[:, :], Cm_i)])
    xcs = p.sbuf("xcs", [128, 512], F32)
    zs = p.sbuf("zs", [128, 512], F32)
    yb = [p.sbuf(f"yb{m}", [128, C], BF16) for m in range(16)]
    for m in range(16):
        for (o, w) in CC:
            p.dma("sync", [(xcs[:, 0:w], xc_i[m * 128:(m + 1) * 128, o:o + w]),
                           (zs[:, 0:w], z_i[m * 128:(m + 1) * 128, o:o + w])])

            def sig(eng, o=o, w=w):
                return eng.activation(zs[:, 0:w], zs[:, 0:w], AF.Sigmoid)
            p.phase("scalar", sig)

            def gmm(eng, m=m, o=o, w=w):
                return eng.matmul(cm.mm_ps[0][:, 0:w], Cms[:, m * 128:(m + 1) * 128],
                                  cfin[:, o:o + w], start=True, stop=True)
            p.phase("tensor", gmm)

            def gev(eng, o=o, w=w):
                return eng.copy(cm.bcs1[:, 0:w], cm.mm_ps[0][:, 0:w])
            p.phase("scalar", gev)

            def gfin(eng, m=m, o=o, w=w):
                eng.tensor_tensor(cm.bcs1[:, 0:w], cm.bcs1[:, 0:w], xcs[:, 0:w], ALU.add)
                return eng.tensor_tensor(yb[m][:, o:o + w], cm.bcs1[:, 0:w], zs[:, 0:w], ALU.mult)
            p.phase("vector", gfin)

    wob = p.sbuf("wob", [128, 16 * 128], BF16)
    for m in range(8):
        p.dma("gpsimd", [(wob[:, k * 128:(k + 1) * 128],
                          woT[k * 128:(k + 1) * 128, m * 128:(m + 1) * 128]) for k in range(16)])
        for (o, w) in CC:
            p.dma("sync", [(xcs[:, 0:w], h_i[m * 128:(m + 1) * 128, H + o:H + o + w])])
            def rmm(eng, m=m, o=o, w=w):
                last = None
                for k in range(16):
                    last = eng.matmul(cm.mm_ps[1][:, 0:w], wob[:, k * 128:(k + 1) * 128],
                                      yb[k][:, o:o + w], start=(k == 0), stop=(k == 15))
                return last
            p.phase("tensor", rmm)

            def rev(eng, o=o, w=w):
                return eng.copy(cm.bcs2[:, 0:w], cm.mm_ps[1][:, 0:w])
            p.phase("scalar", rev)

            def rfin(eng, m=m, o=o, w=w):
                eng.tensor_tensor(cm.bcs2[:, 0:w], cm.bcs2[:, 0:w], xcs[:, 0:w], ALU.add)
                return eng.scalar_tensor_tensor(x_t[m][:, o:o + w], cm.bcs2[:, 0:w],
                                                modx[:, 16 + m:17 + m], x_t[m][:, o:o + w],
                                                ALU.mult, ALU.add)
            p.phase("vector", rfin)
        p.dma("gpsimd", [(x1_o[m * 128:(m + 1) * 128, :], x_t[m][:, :])])
    return p


def l0b_inputs(I, l0a_res):
    x2d = I["L0_attn_x"][0].T
    AT = np.ascontiguousarray(I["L0_attn_A"].T)
    cT8 = _r8(I["L0_c"][0])
    woT = np.ascontiguousarray(I["L0_attn_W_out"].T)
    Ivals = [l0a_res[q]["S0"][:, -1] for q in range(NCORES)]
    maps = []
    for r in range(NCORES):
        IordT = np.zeros((16, 8), np.float32)
        for j in range(8):
            q = r - 1 - j
            if q >= 0:
                IordT[:, j] = Ivals[q]
        maps.append(dict(xT=np.ascontiguousarray(x2d[:, r * C:(r + 1) * C]),
                         S0=l0a_res[r]["S0"], IordT=IordT, AT=AT,
                         Cm=I["L0_attn_Cm"], woT=woT, cT8=cT8, wmod=I["L0_wmod"],
                         xc=l0a_res[r]["xc"], z=l0a_res[r]["z"], hsp=l0a_res[r]["hdbg"]))
    return maps


GOLDEN_CENTER = np.float32(1.0 / np.e)
GOLDEN_LOWER = np.float32(0.5 - np.log(4.0 / 3.0))
GOLDEN_UPPER = np.float32(0.5)


def build_l0c():
    p = Prog("l0c")
    x1T = p.inp("x1T", [1024, C])
    cT8 = p.inp("cT8", [128, 8])
    wmod = p.inp("wmod", [1024, 6144])
    Wr_i = p.inp("Wr", [1024, 8])
    temp_i = p.inp("temp", [1, 1])
    Wg_i = p.inp("Wg", [8, 1024, 1024])
    Wu_i = p.inp("Wu", [8, 1024, 1024])
    Wd_i = p.inp("Wd", [8, 1024, 1024])
    eye8_i = p.inp("eye8", [8, 8])
    eye128_i = p.inp("eye128", [128, 128])
    cT8b = p.inp("cT8b", [128, 8])
    wmodb = p.inp("wmodb", [1024, 6144])
    x2_o = p.out("x2out", [1024, C])
    h3_o = p.out("h3out", [1024, C])

    cm = Common(p)
    wrs = p.sbuf("wrs", [128, 8 * 8], F32)
    e8 = p.sbuf("e8", [8, 8], F32)
    e128 = p.sbuf("e128", [128, 128], F32)
    tmp11 = p.sbuf("tmp11", [1, 1], F32)
    invt8 = p.sbuf("invt8", [8, 1], F32)
    inh = p.sbuf("inh", [8, C], F32)
    wfm = p.sbuf("wfm", [8, C], F32)
    ones8 = p.sbuf("ones8", [128, 8], F32)
    rw = p.sbuf("rw", [128, 8 * 6], F32)
    rc = p.sbuf("rc", [128, 8], F32)
    h2b = [p.sbuf(f"h2b{k}", [128, C], BF16) for k in range(8)]
    web = p.sbuf("web", [128, 8 * 128], BF16)
    gst = p.sbuf("gst", [128, C], F32)
    wbce = p.sbuf("wbce", [128, C], BF16)
    rt_ps = p.psum("rt_ps", [8, 512])
    tt_ps = p.psum("tt_ps", [128, 8])
    bt_ps = p.psum("bt_ps", [8, 128])
    modx = emit_mod_vectors(p, cm, cT8, wmod)
    x_t = [p.sbuf(f"x{k}", [128, C]) for k in range(8)]
    p.dma("sync", [(x_t[k][:, :], x1T[k * 128:(k + 1) * 128, :]) for k in range(8)])
    emit_ln(p, cm, "ln4", x_t, x_t, C,
            gamma=[modx[:, 56 + k:57 + k] for k in range(8)],
            beta=[modx[:, 24 + k:25 + k] for k in range(8)])
    h2f = x_t
    p.dma("sync", [(wrs[:, k * 8:(k + 1) * 8], Wr_i[k * 128:(k + 1) * 128, :]) for k in range(8)]
          + [(e8[:, :], eye8_i), (e128[:, :], eye128_i), (tmp11[:, :], temp_i)])

    def invt(eng):
        eng.reciprocal(tmp11[0:1, 0:1], tmp11[0:1, 0:1])
        eng.memset(ones8[:, :], 0.0)
        return eng.tensor_scalar_add(ones8[:, :], ones8[:, :], 1.0)
    p.phase("vector", invt)
    p.dma("sync", [(invt8[j:j + 1, 0:1], tmp11[0:1, 0:1]) for j in range(8)])

    def negt(eng):
        return eng.tensor_scalar_mul(invt8[:, 0:1], invt8[:, 0:1], -1.0)
    p.phase("vector", negt)

    for (o, w) in CC:
        def rmm(eng, o=o, w=w):
            last = None
            for k in range(8):
                last = eng.matmul(rt_ps[:, 0:w], wrs[:, k * 8:(k + 1) * 8],
                                  h2f[k][:, o:o + w], start=(k == 0), stop=(k == 7))
            return last
        p.phase("tensor", rmm)

        def rev(eng, o=o, w=w):
            return eng.copy(inh[:, o:o + w], rt_ps[:, 0:w])
        p.phase("scalar", rev)

    def sig1(eng):
        return eng.activation(inh[:, :], inh[:, :], AF.Exp, scale=invt8[:, 0:1])
    p.phase("scalar", sig1)

    def sig2(eng):
        eng.tensor_scalar_add(inh[:, :], inh[:, :], 1.0)
        return eng.reciprocal(inh[:, :], inh[:, :])
    p.phase("vector", sig2)

    for tt in range(C // 128):
        def ttr(eng, tt=tt):
            return eng.matmul(tt_ps[:, 0:8], inh[:, tt * 128:(tt + 1) * 128], e8[:, :],
                              start=True, stop=True)
        p.phase("tensor", ttr)

        def tte(eng):
            return eng.copy(rw[:, 0:8], tt_ps[:, 0:8])
        p.phase("scalar", tte)

        def rmath(eng):
            w0 = rw[:, 0:8]; dist = rw[:, 8:16]; wz = rw[:, 16:24]
            fb = rw[:, 24:32]; t1 = rw[:, 32:40]; t2 = rw[:, 40:48]
            eng.tensor_scalar_sub(dist, w0, float(GOLDEN_CENTER))
            eng.tensor_scalar_mul(t1, dist, -1.0)
            eng.tensor_tensor(dist, dist, t1, ALU.max)
            eng.tensor_scalar(t1, w0, float(GOLDEN_LOWER), None, ALU.is_ge)
            eng.tensor_scalar(t2, w0, float(GOLDEN_UPPER), None, ALU.is_le)
            return eng.tensor_tensor(t1, t1, t2, ALU.mult)
        p.phase("vector", rmath)

        def rexp(eng):
            dist = rw[:, 8:16]; wz = rw[:, 16:24]; fb = rw[:, 24:32]
            eng.activation(wz, dist, AF.Exp, scale=-10.0)
            return eng.activation(fb, dist, AF.Exp, scale=np.float32(-1.0 / 0.3))
        p.phase("scalar", rexp)

        def rmath2(eng):
            w0 = rw[:, 0:8]; dist = rw[:, 8:16]; wz = rw[:, 16:24]
            fb = rw[:, 24:32]; t1 = rw[:, 32:40]; t2 = rw[:, 40:48]
            m1 = rc[:, 0:1]; ne = rc[:, 1:2]; m2 = rc[:, 2:3]; sm = rc[:, 3:4]
            eng.tensor_tensor(wz, wz, t1, ALU.mult)
            eng.tensor_reduce(ne.rearrange("p (a b) -> p a b", b=1), wz.rearrange("p (a b) -> p a b", b=8), AX.X, ALU.add)
            eng.tensor_scalar(ne, ne, 1e-8, None, ALU.is_lt)
            eng.tensor_reduce(m1.rearrange("p (a b) -> p a b", b=1), fb.rearrange("p (a b) -> p a b", b=8), AX.X, ALU.max)
            eng.tensor_scalar(t1, fb, m1[:, 0:1], None, ALU.is_ge)
            eng.tensor_tensor_scan(t2, ones8[:, 0:8], t1, 0.0, ALU.mult, ALU.add)
            eng.tensor_tensor(t2, t2, t1, ALU.subtract)
            eng.tensor_scalar(t2, t2, 0.0, None, ALU.is_equal)
            eng.tensor_tensor(t1, t1, t2, ALU.mult)
            eng.scalar_tensor_tensor(t2, t1, -1e30, fb, ALU.mult, ALU.add)
            eng.tensor_reduce(m2.rearrange("p (a b) -> p a b", b=1), t2.rearrange("p (a b) -> p a b", b=8), AX.X, ALU.max)
            eng.tensor_scalar(t2, t2, m2[:, 0:1], None, ALU.is_ge)
            eng.tensor_tensor_scan(rw[:, 0:8], ones8[:, 0:8], t2, 0.0, ALU.mult, ALU.add)
            eng.tensor_tensor(rw[:, 0:8], rw[:, 0:8], t2, ALU.subtract)
            eng.tensor_scalar(rw[:, 0:8], rw[:, 0:8], 0.0, None, ALU.is_equal)
            eng.tensor_tensor(t2, t2, rw[:, 0:8], ALU.mult)
            eng.tensor_tensor(t1, t1, t2, ALU.add)
            eng.tensor_tensor(fb, fb, t1, ALU.mult)
            eng.tensor_reduce(sm.rearrange("p (a b) -> p a b", b=1), fb.rearrange("p (a b) -> p a b", b=8), AX.X, ALU.add)
            eng.tensor_scalar_max(sm, sm, 1e-8)
            eng.reciprocal(sm, sm)
            eng.tensor_scalar(fb, fb, sm[:, 0:1], None, ALU.mult)
            eng.tensor_tensor(t2, fb, wz, ALU.subtract)
            eng.tensor_scalar(t2, t2, ne[:, 0:1], None, ALU.mult)
            eng.tensor_tensor(wz, wz, t2, ALU.add)
            eng.tensor_reduce(sm.rearrange("p (a b) -> p a b", b=1), wz.rearrange("p (a b) -> p a b", b=8), AX.X, ALU.add)
            eng.tensor_scalar_max(sm, sm, 1e-8)
            eng.reciprocal(sm, sm)
            return eng.tensor_scalar(wz, wz, sm[:, 0:1], None, ALU.mult)
        p.phase("vector", rmath2)

        def btr(eng, tt=tt):
            return eng.matmul(bt_ps[:, 0:128], rw[:, 16:24], e128[:, :], start=True, stop=True)
        p.phase("tensor", btr)

        def bte(eng, tt=tt):
            return eng.copy(wfm[:, tt * 128:(tt + 1) * 128], bt_ps[:, 0:128])
        p.phase("scalar", bte)

    for k in range(8):
        def cst(eng, k=k):
            return eng.copy(h2b[k][:, :], h2f[k][:, :])
        p.phase("scalar", cst)

    p.free(*x_t)
    moe = [p.sbuf(f"moe{m}", [128, C], F32) for m in range(8)]
    hpb = [p.sbuf(f"hpb{m}", [128, C], BF16) for m in range(8)]
    wrow = cm.tp
    for e in range(8):
        p.dma("sync", [(wrow[0:1, 0:C], wfm[e:e + 1, :])])
        for (o, w) in CC:
            def wbm(eng, o=o, w=w):
                return eng.matmul(cm.bc_ps[:, 0:w], cm.ones_row[0:1, :], wrow[0:1, o:o + w],
                                  start=True, stop=True)
            p.phase("tensor", wbm)

            def wbe(eng, o=o, w=w):
                return eng.copy(wbce[:, o:o + w], cm.bc_ps[:, 0:w])
            p.phase("scalar", wbe)

        for m in range(8):
            p.dma("gpsimd", [(web[:, k * 128:(k + 1) * 128],
                              Wg_i[e, k * 128:(k + 1) * 128, m * 128:(m + 1) * 128]) for k in range(8)])
            for (o, w) in CC:
                def gmm(eng, o=o, w=w):
                    last = None
                    for k in range(8):
                        last = eng.matmul(cm.mm_ps[0][:, 0:w], web[:, k * 128:(k + 1) * 128],
                                          h2b[k][:, o:o + w], start=(k == 0), stop=(k == 7))
                    return last
                p.phase("tensor", gmm)

                def gev(eng, o=o, w=w):
                    return eng.copy(gst[:, o:o + w], cm.mm_ps[0][:, 0:w])
                p.phase("scalar", gev)

            def gsil(eng):
                return eng.activation(gst[:, :], gst[:, :], AF.Silu)
            p.phase("scalar", gsil)

            p.dma("gpsimd", [(web[:, k * 128:(k + 1) * 128],
                              Wu_i[e, k * 128:(k + 1) * 128, m * 128:(m + 1) * 128]) for k in range(8)])
            for (o, w) in CC:
                def umm(eng, o=o, w=w):
                    last = None
                    for k in range(8):
                        last = eng.matmul(cm.mm_ps[1][:, 0:w], web[:, k * 128:(k + 1) * 128],
                                          h2b[k][:, o:o + w], start=(k == 0), stop=(k == 7))
                    return last
                p.phase("tensor", umm)

                def uev(eng, o=o, w=w):
                    return eng.copy(cm.bcs1[:, 0:w], cm.mm_ps[1][:, 0:w])
                p.phase("scalar", uev)

                def ufin(eng, m=m, o=o, w=w):
                    eng.tensor_tensor(cm.bcs1[:, 0:w], cm.bcs1[:, 0:w], gst[:, o:o + w], ALU.mult)
                    return eng.tensor_tensor(hpb[m][:, o:o + w], cm.bcs1[:, 0:w], wbce[:, o:o + w], ALU.mult)
                p.phase("vector", ufin)

        for md in range(8):
            p.dma("gpsimd", [(web[:, k * 128:(k + 1) * 128],
                              Wd_i[e, k * 128:(k + 1) * 128, md * 128:(md + 1) * 128]) for k in range(8)])
            for (o, w) in CC:
                def dmm(eng, o=o, w=w):
                    last = None
                    for k in range(8):
                        last = eng.matmul(cm.mm_ps[0][:, 0:w], web[:, k * 128:(k + 1) * 128],
                                          hpb[k][:, o:o + w], start=(k == 0), stop=(k == 7))
                    return last
                p.phase("tensor", dmm)

                def dev(eng, o=o, w=w):
                    return eng.copy(cm.bcs1[:, 0:w], cm.mm_ps[0][:, 0:w])
                p.phase("scalar", dev)

                if e == 0:
                    def dacc(eng, md=md, o=o, w=w):
                        return eng.tensor_copy(moe[md][:, o:o + w], cm.bcs1[:, 0:w])
                else:
                    def dacc(eng, md=md, o=o, w=w):
                        return eng.tensor_tensor(moe[md][:, o:o + w], moe[md][:, o:o + w],
                                                 cm.bcs1[:, 0:w], ALU.add)
                p.phase("vector", dacc)

    for m in range(8):
        p.dma("sync", [(gst[:, :], x1T[m * 128:(m + 1) * 128, :])])

        def fin(eng, m=m):
            return eng.scalar_tensor_tensor(moe[m][:, :], moe[m][:, :], modx[:, 40 + m:41 + m],
                                            gst[:, :], ALU.mult, ALU.add)
        p.phase("vector", fin)
        p.dma("gpsimd", [(x2_o[m * 128:(m + 1) * 128, :], moe[m][:, :])])
    p.free(*hpb)
    modx2 = emit_mod_vectors(p, cm, cT8b, wmodb, tag="b")
    emit_ln(p, cm, "ln5", moe, moe, C,
            gamma=[modx2[:, 48 + k:49 + k] for k in range(8)],
            beta=[modx2[:, 0 + k:1 + k] for k in range(8)])
    p.dma("gpsimd", [(h3_o[m * 128:(m + 1) * 128, :], moe[m][:, :]) for m in range(8)])
    return p


def l0c_inputs(I, x1_shards):
    cT8 = _r8(I["L0_c"][0])
    maps = []
    eye8 = np.eye(8, dtype=np.float32)
    eye128 = np.eye(128, dtype=np.float32)
    for r in range(NCORES):
        maps.append(dict(x1T=x1_shards[r], cT8=cT8, wmod=I["L0_wmod"], Wr=I["L0_ffn_Wr"],
                         temp=I["L0_ffn_temperature"].reshape(1, 1), Wg=I["L0_ffn_Wg"],
                         Wu=I["L0_ffn_Wu"], Wd=I["L0_ffn_Wd"], eye8=eye8, eye128=eye128,
                         cT8b=_r8(I["L1_c"][0]), wmodb=I["L1_wmod"]))
    return maps




def emit_bcast_pe(p, cm, srow, dst, W):
    banks = [cm.bc_ps, cm.bc2_ps, cm.mm_ps[0], cm.mm_ps[1]]
    cks = chunks_of(W)
    for g in range(0, len(cks), 4):
        grp = cks[g:g + 4]

        def mmb(eng, grp=grp):
            last = None
            for bi, (o, w) in enumerate(grp):
                last = eng.matmul(banks[bi][:, 0:w], cm.ones_row[0:1, :], srow[0:1, o:o + w],
                                  start=True, stop=True)
            return last
        p.phase("tensor", mmb)

        def evb(eng, grp=grp):
            last = None
            for bi, (o, w) in enumerate(grp):
                last = eng.copy(dst[:, o:o + w], banks[bi][:, 0:w])
            return last
        p.phase("scalar", evb)


def emit_bcast_row(p, srow, dst, width=None):
    W = width or dst.shape[1]
    p.dma("sync", [(dst[0:1, 0:W], srow[0:1, 0:W])])
    k = 1
    while k < 128:
        kk = min(k, 128 - k)
        p.dma("sync", [(dst[k:k + kk, 0:W], dst[0:kk, 0:W])])
        k += kk



def build_l1s(kind):
    p = Prog("l1s_" + kind)
    cm = Common(p)
    ios = {}
    for D in ("f", "b"):
        if kind != "first":
            ios["ypc_" + D] = p.inp("ypc_" + D, [2048, TC], BF16)
            ios["szc_" + D] = p.inp("szc_" + D, [2048, TC], BF16)
            ios["ctc_" + D] = p.inp("ctc_" + D, [2048, TC])
            ios["crc_" + D] = p.inp("crc_" + D, [16, TC])
            ios["IoO_" + D] = p.inp("IoO_" + D, [1024, 256])
            ios["EoO_" + D] = p.inp("EoO_" + D, [1024, 256])
            ios["IoP_" + D] = p.inp("IoP_" + D, [1024, 256])
            ios["EoP_" + D] = p.inp("EoP_" + D, [1024, 256])
            ios["nAP_" + D] = p.inp("nAP_" + D, [128, 256])
            ios["owTP_" + D] = p.inp("owTP_" + D, [2048, 1024])
        ios["rT_" + D] = p.inp("rT_" + D, [1024, TC])
        if D == "f":
            ios["hmask"] = p.inp("hmask", [128, H])
        if kind != "last":
            ios["nwb_" + D] = p.inp("nwb_" + D, [128, 16])
            ios["inWT_" + D] = p.inp("inWT_" + D, [1024, 4096])
            ios["cwb_" + D] = p.inp("cwb_" + D, [128, 64])
            ios["cbb_" + D] = p.inp("cbb_" + D, [128, 16])
            ios["xpWT_" + D] = p.inp("xpWT_" + D, [2048, 96])
            ios["dtWT_" + D] = p.inp("dtWT_" + D, [64, 2048])
            ios["dtbb_" + D] = p.inp("dtbb_" + D, [128, 16])
            ios["nA_" + D] = p.inp("nA_" + D, [128, 256])
            ios["Dpb_" + D] = p.inp("Dpb_" + D, [128, 16])
            ios["ypost_" + D] = p.out("ypost_" + D, [2048, C], BF16)
            ios["szo_" + D] = p.out("szo_" + D, [2048, C], BF16)
            ios["cto_" + D] = p.out("cto_" + D, [2048, C])
            ios["cro_" + D] = p.out("cro_" + D, [16, C])
            ios["Io_" + D] = p.out("Io_" + D, [128, 256])
            ios["Eo_" + D] = p.out("Eo_" + D, [128, 256])
            ios["ro_" + D] = p.out("ro_" + D, [1024, TC])
            ios["usp_" + D] = p.dram("usp_" + D, [2048, C], BF16)
        else:
            ios["nfb_" + D] = p.inp("nfb_" + D, [128, 16])
            ios["fh2_" + D] = p.out("fh2_" + D, [1024, C])
        if kind != "first":
            ios["yfd_" + D] = p.dram("yfd_" + D, [2048, TC], BF16)

    web = p.sbuf("web", [128, 16 * 128], BF16)
    upre = p.sbuf("upre", [128, TC], F32)
    cstA = p.sbuf("cstA", [128, C], F32)
    cstB = p.sbuf("cstB", [128, C], F32)
    cstO = p.sbuf("cstO", [128, TC], F32)
    bf1 = p.sbuf("bf1", [128, TC], BF16)
    bf2 = p.sbuf("bf2", [128, TC], BF16)
    browb = p.sbuf("browb", [1, TC], BF16)
    nAt, nAPt, prm, cwt, Ist, Est, cte, Hown, Hprv = {}, {}, {}, {}, {}, {}, {}, {}, {}
    for D in ("f", "b"):
        nAt[D] = p.sbuf("nAt_" + D, [128, 256], F32)
        prm[D] = p.sbuf("prm_" + D, [128, 80], F32)
        cwt[D] = p.sbuf("cwt_" + D, [128, 64], F32)
        Ist[D] = p.sbuf("Ist_" + D, [128, 256], F32)
        Est[D] = p.sbuf("Est_" + D, [128, 256], F32)
        cte[D] = p.sbuf("cte_" + D, [128, 16], F32)
        if kind != "first":
            nAPt[D] = p.sbuf("nAPt_" + D, [128, 256], F32)
            Hown[D] = p.sbuf("Hown_" + D, [128, 256], F32)
            Hprv[D] = p.sbuf("Hprv_" + D, [128, 256], F32)
    IEld = p.sbuf("IEld", [128, 256], F32)
    hmk = p.sbuf("hmk", [128, H], F32)
    brow = cm.tp
    p.dma("sync", [(hmk[:, :], ios["hmask"])])

    for D in ("f", "b"):
        if kind != "last":
            p.dma("sync", [(prm[D][:, 0:16], ios["nwb_" + D]), (prm[D][:, 16:32], ios["cbb_" + D]),
                           (prm[D][:, 32:48], ios["dtbb_" + D]), (prm[D][:, 48:64], ios["Dpb_" + D]),
                           (cwt[D][:, :], ios["cwb_" + D]), (nAt[D][:, :], ios["nA_" + D])])
        else:
            p.dma("sync", [(prm[D][:, 0:16], ios["nfb_" + D])])
        if kind != "first":
            p.dma("sync", [(nAPt[D][:, :], ios["nAP_" + D])])
    for D in ("f", "b"):
        tensors = [nAt[D]] if kind != "last" else []
        if kind != "first":
            tensors.append(nAPt[D])
        for tt_ in tensors:
            def nexp(eng, tt_=tt_):
                return eng.activation(tt_[:, :], tt_[:, :], AF.Exp)
            p.phase("scalar", nexp)

            def nneg(eng, tt_=tt_):
                return eng.tensor_scalar_mul(tt_[:, :], tt_[:, :], -1.0)
            p.phase("vector", nneg)

    for D in ("f", "b"):
        if kind != "first":
            for (Ht, Isrc, Esrc) in ((Hown[D], "IoO_", "EoO_"), (Hprv[D], "IoP_", "EoP_")):
                def hz(eng, Ht=Ht):
                    return eng.memset(Ht[:, :], 0.0)
                p.phase("vector", hz)
                for j in range(8):
                    p.dma("sync", [(IEld[:, :], ios[Esrc + D][j * 128:(j + 1) * 128, :])])

                    def hc1(eng, Ht=Ht):
                        return eng.tensor_tensor(Ht[:, :], Ht[:, :], IEld[:, :], ALU.mult)
                    p.phase("vector", hc1)
                    p.dma("sync", [(IEld[:, :], ios[Isrc + D][j * 128:(j + 1) * 128, :])])

                    def hc2(eng, Ht=Ht):
                        return eng.tensor_tensor(Ht[:, :], Ht[:, :], IEld[:, :], ALU.add)
                    p.phase("vector", hc2)

            Cbc = p.sbuf("CbcP" + D, [128, TC], BF16, side="right")
            for quarter in range(4):
                ctq = [p.sbuf(f"ctq{D}{quarter}{i}", [128, TC], F32, side="right") for i in range(4)]
                qq = [p.sbuf(f"qq{D}{quarter}{i}", [128, TC], F32, side="right") for i in range(4)]
                dgs = [quarter * 4 + i for i in range(4)]
                p.dma("sync", [(ctq[i][:, :], ios["ctc_" + D][dg * 128:(dg + 1) * 128, :])
                               for i, dg in enumerate(dgs)])

                def qz(eng, qq=qq):
                    for i in range(4):
                        eng.memset(qq[i][:, :], 0.0)
                    return None
                p.phase("vector", qz)
                for s in range(16):
                    p.dma("sync", [(brow[0:1, 0:TC], ios["crc_" + D][s:s + 1, :])])
                    emit_bcast_pe(p, cm, brow, Cbc, TC)
                    for i, dg in enumerate(dgs):
                        col = dg * 16 + s

                        def corrA(eng, ct_=ctq[i], nap_=nAPt[D], col=col):
                            return eng.activation(cstO[:, :], ct_[:, :], AF.Exp,
                                                  scale=nap_[:, col:col + 1])
                        p.phase("scalar", corrA)

                        def corrB(eng, q_=qq[i], hp_=Hprv[D], ho_=Hown[D], cb_=Cbc, col=col):
                            eng.scalar_tensor_tensor(cstO[:, 0:H], cstO[:, 0:H], hp_[:, col:col + 1],
                                                     cb_[:, 0:H], ALU.mult, ALU.mult)
                            eng.scalar_tensor_tensor(cstO[:, H:TC], cstO[:, H:TC], ho_[:, col:col + 1],
                                                     cb_[:, H:TC], ALU.mult, ALU.mult)
                            return eng.tensor_tensor(q_[:, :], q_[:, :], cstO[:, :], ALU.add)
                        p.phase("vector", corrB)
                for i, dg in enumerate(dgs):
                    p.dma("sync", [(bf1[:, :], ios["ypc_" + D][dg * 128:(dg + 1) * 128, :]),
                                   (bf2[:, :], ios["szc_" + D][dg * 128:(dg + 1) * 128, :])])

                    def yf(eng, q_=qq[i]):
                        eng.tensor_tensor(q_[:, :], q_[:, :], bf2[:, :], ALU.mult)
                        eng.tensor_copy(cstO[:, :], bf1[:, :])
                        eng.tensor_tensor(cstO[:, :], cstO[:, :], q_[:, :], ALU.add)
                        return eng.tensor_copy(bf1[:, :], cstO[:, :])
                    p.phase("vector", yf)
                    p.dma("gpsimd", [(ios["yfd_" + D][dg * 128:(dg + 1) * 128, :], bf1[:, :])])
                p.free(*ctq, *qq)
            p.free(Cbc)

        r_t = [p.sbuf(f"r{D}{m}", [128, TC], F32, side="right") for m in range(8)]
        if kind == "first":
            p.dma("sync", [(r_t[m][:, :], ios["rT_" + D][m * 128:(m + 1) * 128, :]) for m in range(8)])
        else:
            ow_wb = p.sbuf("owwb" + D, [128, 64 * 128], BF16, side="right")
            ystr = p.sbuf("ystr" + D, [128, 16 * 512], BF16, side="right")
            p.dma("sync", [(r_t[m][:, :], ios["rT_" + D][m * 128:(m + 1) * 128, :])
                           for m in range(8)])
            for mg in range(2):
                p.dma("gpsimd", [(ow_wb[:, (m * 16 + k) * 128:(m * 16 + k + 1) * 128],
                                  ios["owTP_" + D][k * 128:(k + 1) * 128,
                                                   (mg * 4 + m) * 128:(mg * 4 + m + 1) * 128])
                                 for m in range(4) for k in range(16)])
                for ci, (o, w) in enumerate(TCC):
                    p.dma("sync", [(ystr[:, k * 512:k * 512 + w],
                                    ios["yfd_" + D][k * 128:(k + 1) * 128, o:o + w])
                                   for k in range(16)])
                    for m in range(4):
                        def omm(eng, m=m, o=o, w=w):
                            last = None
                            for k in range(16):
                                last = eng.matmul(cm.mm_ps[m % 2][:, 0:w],
                                                  ow_wb[:, (m * 16 + k) * 128:(m * 16 + k + 1) * 128],
                                                  ystr[:, k * 512:k * 512 + w],
                                                  start=(k == 0), stop=(k == 15))
                            return last
                        p.phase("tensor", omm)

                        def oev(eng, m=m, mg=mg, o=o, w=w):
                            return eng.copy(cstO[:, o:o + w], cm.mm_ps[m % 2][:, 0:w])
                        p.phase("scalar", oev)

                        def radd(eng, rt_=r_t[mg * 4 + m], o=o, w=w):
                            return eng.tensor_tensor(rt_[:, o:o + w], rt_[:, o:o + w],
                                                     cstO[:, o:o + w], ALU.add)
                        p.phase("vector", radd)
            p.free(ow_wb, ystr)


        if kind == "last":
            emit_ln(p, cm, "fin" + D, r_t, r_t, TC,
                    gamma=[prm[D][:, k:k + 1] for k in range(8)],
                    beta=[prm[D][:, 8 + k:9 + k] for k in range(8)])
            p.dma("gpsimd", [(ios["fh2_" + D][m * 128:(m + 1) * 128, :], r_t[m][:, H:TC])
                             for m in range(8)])
            p.free(*r_t)
            continue

        p.dma("gpsimd", [(ios["ro_" + D][m * 128:(m + 1) * 128, :], r_t[m][:, :]) for m in range(8)])

        hnb = [p.sbuf(f"hnb{D}{m}", [128, TC], BF16, side="right") for m in range(8)]
        emit_ln(p, cm, "hn" + D, r_t, hnb, TC,
                gamma=[prm[D][:, k:k + 1] for k in range(8)],
                beta=[prm[D][:, 8 + k:9 + k] for k in range(8)])

        for m in range(32):
            p.dma("gpsimd", [(web[:, k * 128:(k + 1) * 128],
                              ios["inWT_" + D][k * 128:(k + 1) * 128, m * 128:(m + 1) * 128])
                             for k in range(8)])
            for (o, w) in TCC:
                def mmx(eng, hnb_=hnb, o=o, w=w):
                    last = None
                    for k in range(8):
                        last = eng.matmul(cm.mm_ps[0][:, 0:w], web[:, k * 128:(k + 1) * 128],
                                          hnb_[k][:, o:o + w], start=(k == 0), stop=(k == 7))
                    return last
                p.phase("tensor", mmx)

                def evx(eng, o=o, w=w):
                    return eng.copy(upre[:, o:o + w], cm.mm_ps[0][:, 0:w])
                p.phase("scalar", evx)
            if m < 16:
                def hm0(eng):
                    return eng.tensor_tensor(upre[:, 0:H], upre[:, 0:H], hmk[:, :], ALU.mult)
                p.phase("vector", hm0)

                def conv2(eng, m=m, cw_=cwt[D]):
                    eng.tensor_scalar_mul(cstA[:, 0:C], upre[:, 0:C], cw_[:, 4 * m:4 * m + 1])
                    for j in range(1, 4):
                        eng.scalar_tensor_tensor(cstA[:, 0:C], upre[:, j:j + C],
                                                 cw_[:, 4 * m + j:4 * m + j + 1],
                                                 cstA[:, 0:C], ALU.mult, ALU.add)
                    return None
                p.phase("vector", conv2)

                def sil(eng, m=m, pr_=prm[D]):
                    return eng.activation(bf1[:, 0:C], cstA[:, 0:C], AF.Silu, bias=pr_[:, 16 + m:17 + m])
                p.phase("scalar", sil)
                p.dma("gpsimd", [(ios["usp_" + D][m * 128:(m + 1) * 128, :], bf1[:, 0:C])])

            else:
                def silz(eng, m=m):
                    return eng.activation(bf1[:, 0:C], upre[:, H:TC], AF.Silu)
                p.phase("scalar", silz)
                p.dma("gpsimd", [(ios["szo_" + D][(m - 16) * 128:(m - 15) * 128, :], bf1[:, 0:C])])
        p.free(*hnb)
        p.free(*r_t)

        ones_c = p.sbuf("ones_c" + D, [128, C], F32, side="right")
        BCst = p.sbuf("BCst" + D, [32, TC], F32, side="right")
        dtrB = p.sbuf("dtrB" + D, [64, TC], BF16, side="right")

        def initones(eng, ones_c=ones_c):
            eng.memset(ones_c[:, :], 0.0)
            return eng.tensor_scalar_add(ones_c[:, :], ones_c[:, :], 1.0)
        p.phase("vector", initones)

        ustr = p.sbuf("ustr" + D, [128, 16 * 512], BF16, side="right")
        p.dma("gpsimd", [(web[:, k * 96:(k + 1) * 96], ios["xpWT_" + D][k * 128:(k + 1) * 128, :])
                         for k in range(16)])
        for ci, (o, w) in enumerate(CC):
            p.dma("sync", [(ustr[:, k * 512:k * 512 + w],
                            ios["usp_" + D][k * 128:(k + 1) * 128, o:o + w]) for k in range(16)])

            def xmm2(eng, o=o, w=w, ustr=ustr):
                last = None
                for k in range(16):
                    last = eng.matmul(cm.mm_ps[0][0:96, 0:w], web[:, k * 96:(k + 1) * 96],
                                      ustr[:, k * 512:k * 512 + w], start=(k == 0), stop=(k == 15))
                return last
            p.phase("tensor", xmm2)

            def xev(eng, o=o, w=w, dtrB=dtrB, BCst=BCst):
                eng.copy(dtrB[0:64, H + o:H + o + w], cm.mm_ps[0][0:64, 0:w])
                return eng.copy(BCst[0:32, H + o:H + o + w], cm.mm_ps[0][64:96, 0:w])
            p.phase("scalar", xev)
        p.free(ustr)
        p.dma("gpsimd", [(ios["cro_" + D], BCst[16:32, H:TC])])

        for half in range(4):
            dgs = [half * 4 + i for i in range(4)]
            dtb16 = [p.sbuf(f"dt{D}{half}{i}", [128, C], BF16, side="right") for i in range(4)]
            dtu = [p.sbuf(f"du{D}{half}{i}", [128, C], BF16, side="right") for i in range(4)]
            yacc = [p.sbuf(f"ya{D}{half}{i}", [128, C], F32, side="right") for i in range(4)]
            Bbc = p.sbuf("Bbc" + D + str(half), [128, TC], BF16, side="right")
            Cbc = p.sbuf("Cbc" + D + str(half), [128, TC], BF16, side="right")
            for i, dg in enumerate(dgs):
                p.dma("gpsimd", [(web[0:64, 0:128], ios["dtWT_" + D][0:64, dg * 128:(dg + 1) * 128])])
                for (o, w) in CC:
                    def dmm(eng, o=o, w=w, dtrB=dtrB):
                        return eng.matmul(cm.mm_ps[0][:, 0:w], web[0:64, 0:128],
                                          dtrB[0:64, H + o:H + o + w], start=True, stop=True)
                    p.phase("tensor", dmm)

                    def dev(eng, dg=dg, o=o, w=w):
                        return eng.copy(cstA[:, o:o + w], cm.mm_ps[0][:, 0:w])
                    p.phase("scalar", dev)

                def dbias(eng, dg=dg, pr_=prm[D]):
                    return eng.tensor_scalar_add(cstA[:, 0:C], cstA[:, 0:C], pr_[:, 32 + dg:33 + dg])
                p.phase("vector", dbias)

                def sp1(eng, _x=None):
                    eng.tensor_scalar_mul(cstB[:, 0:C], cstA[:, 0:C], -1.0)
                    eng.tensor_tensor(cstB[:, 0:C], cstB[:, 0:C], cstA[:, 0:C], ALU.min)
                    return eng.tensor_relu(cstO[:, 0:C], cstA[:, 0:C])
                p.phase("vector", sp1)

                def sp2(eng):
                    eng.activation(cstB[:, 0:C], cstB[:, 0:C], AF.Exp)
                    return eng.activation(cstB[:, 0:C], cstB[:, 0:C], AF.Ln, bias=1.0)
                p.phase("scalar", sp2)

                def sp3(eng, dt_=dtb16[i], ones_c=ones_c):
                    eng.tensor_tensor(cstA[:, 0:C], cstB[:, 0:C], cstO[:, 0:C], ALU.add)
                    eng.tensor_copy(dt_[:, :], cstA[:, 0:C])
                    return eng.tensor_tensor_scan(cstB[:, 0:C], ones_c[:, 0:C], cstA[:, 0:C],
                                                  0.0, ALU.mult, ALU.add)
                p.phase("vector", sp3)
                p.dma("gpsimd", [(ios["cto_" + D][dg * 128:(dg + 1) * 128, :], cstB[:, 0:C])])

                def cte1(eng, dg=dg, ct_=cte[D]):
                    return eng.tensor_copy(ct_[:, dg:dg + 1], cstB[:, C - 1:C])
                p.phase("vector", cte1)
                p.dma("sync", [(bf1[:, 0:C], ios["usp_" + D][dg * 128:(dg + 1) * 128, :])])

                def dtu1(eng, du_=dtu[i], dt_=dtb16[i], ya_=yacc[i], dg=dg, pr_=prm[D]):
                    eng.tensor_tensor(du_[:, :], dt_[:, :], bf1[:, 0:C], ALU.mult)
                    return eng.tensor_scalar(ya_[:, :], bf1[:, 0:C],
                                             pr_[:, 48 + dg:49 + dg],
                                             None, ALU.mult)
                p.phase("vector", dtu1)

            for s in range(16):
                p.dma("sync", [(brow[0:1, 0:C], BCst[s:s + 1, H:TC])])
                emit_bcast_pe(p, cm, brow, Bbc, C)
                p.dma("sync", [(brow[0:1, 0:C], BCst[16 + s:17 + s, H:TC])])
                emit_bcast_pe(p, cm, brow, Cbc, C)
                for i, dg in enumerate(dgs):
                    col = dg * 16 + s

                    def sca(eng, dt_=dtb16[i], na_=nAt[D], col=col):
                        return eng.activation(cstA[:, 0:C], dt_[:, :], AF.Exp,
                                              scale=na_[:, col:col + 1])
                    p.phase("scalar", sca)

                    def scb(eng, du_=dtu[i], ya_=yacc[i], is_=Ist[D], Bbc=Bbc, Cbc=Cbc, col=col):
                        eng.tensor_tensor(cstB[:, 0:C], du_[:, :], Bbc[:, 0:C], ALU.mult)
                        eng.tensor_tensor_scan(cstO[:, 0:C], cstA[:, 0:C], cstB[:, 0:C],
                                               0.0, ALU.mult, ALU.add)
                        eng.tensor_copy(is_[:, col:col + 1], cstO[:, C - 1:C])
                        eng.tensor_tensor(cstO[:, 0:C], cstO[:, 0:C], Cbc[:, 0:C], ALU.mult)
                        return eng.tensor_tensor(ya_[:, :], ya_[:, :], cstO[:, 0:C], ALU.add)
                    p.phase("vector", scb)
            for i, dg in enumerate(dgs):
                p.dma("sync", [(bf2[:, 0:C], ios["szo_" + D][dg * 128:(dg + 1) * 128, :])])

                def yp(eng, ya_=yacc[i]):
                    return eng.tensor_tensor(bf1[:, 0:C], ya_[:, :], bf2[:, 0:C], ALU.mult)
                p.phase("vector", yp)
                p.dma("gpsimd", [(ios["ypost_" + D][dg * 128:(dg + 1) * 128, :], bf1[:, 0:C])])
            p.free(*dtb16, *dtu, *yacc, Bbc, Cbc)

        def ecalc(eng, D=D):
            return eng.tensor_tensor(Est[D][:, :].rearrange("p (a b) -> p a b", b=16),
                                     nAt[D][:, :].rearrange("p (a b) -> p a b", b=16),
                                     cte[D][:, :].rearrange("p (a b) -> p a b", b=1).broadcast_to([128, 16, 16]),
                                     ALU.mult)
        p.phase("vector", ecalc)

        def ecalc2(eng, D=D):
            return eng.activation(Est[D][:, :], Est[D][:, :], AF.Exp)
        p.phase("scalar", ecalc2)
        p.dma("gpsimd", [(ios["Io_" + D], Ist[D][:, :]), (ios["Eo_" + D], Est[D][:, :])])
        p.free(ones_c, BCst, dtrB)
    return p


def _l1_wt(I, tag, l):
    pre = f"L1_attn_{tag}_"
    nw = _r8(I[pre + "norm_w"][l]); nb = _r8(I[pre + "norm_b"][l])
    out = {}
    out["nwb"] = np.ascontiguousarray(np.concatenate([nw, nb], 1))
    out["inWT"] = np.ascontiguousarray(I[pre + "in_W"][l].T)
    cw = I[pre + "conv_w"][l]
    out["cwb"] = np.ascontiguousarray(cw.reshape(16, 128, 4).transpose(1, 0, 2).reshape(128, 64))
    out["cbb"] = _r16(I[pre + "conv_b"][l])
    out["xpWT"] = np.ascontiguousarray(I[pre + "xproj_W"][l].T)
    out["dtWT"] = np.ascontiguousarray(I[pre + "dt_W"][l].T)
    out["dtbb"] = _r16(I[pre + "dt_b"][l])
    al = I[pre + "Alog"][l]
    out["nA"] = np.ascontiguousarray(al.reshape(16, 128, 16).transpose(1, 0, 2).reshape(128, 256))
    out["Dpb"] = _r16(I[pre + "D"][l])
    out["owT"] = np.ascontiguousarray(I[pre + "out_W"][l].T)
    return out


def _concat_tail(prev, own, tail_w=H):
    if prev is None:
        t = np.zeros((own.shape[0], tail_w), own.dtype)
    else:
        t = prev[:, -tail_w:]
    return np.ascontiguousarray(np.concatenate([t, own], axis=1))


def _iord(res_list, key, r):
    out = np.zeros((1024, 256), np.float32)
    for j in range(min(r, 8)):
        out[j * 128:(j + 1) * 128, :] = res_list[j][key]
    return out


def _eord(res_list, key, r):
    out = np.ones((1024, 256), np.float32)
    for j in range(min(r, 8)):
        out[j * 128:(j + 1) * 128, :] = res_list[j][key]
    return out


def run_l1_chain(I, h3_full):
    h3b = np.ascontiguousarray(h3_full[:, ::-1])
    sh_f = shard_tokens(h3_full)
    sh_b = shard_tokens(h3b)
    wts = {("f", l): _l1_wt(I, "fw", l) for l in range(4)}
    wts.update({("b", l): _l1_wt(I, "bw", l) for l in range(4)})

    def weight_map(l):
        m = {}
        for D in ("f", "b"):
            w = wts[(D, l)]
            for k in ("nwb", "inWT", "cwb", "cbb", "xpWT", "dtWT", "dtbb", "nA", "Dpb"):
                m[k + "_" + D] = w[k]
        return m

    def prev_map(l, res, r):
        m = {}
        for D in ("f", "b"):
            w = wts[(D, l - 1)]
            pr = res[r - 1] if r > 0 else None
            m["ypc_" + D] = _concat_tail(None if pr is None else pr["ypost_" + D], res[r]["ypost_" + D])
            m["szc_" + D] = _concat_tail(None if pr is None else pr["szo_" + D], res[r]["szo_" + D])
            m["ctc_" + D] = _concat_tail(None if pr is None else pr["cto_" + D], res[r]["cto_" + D])
            m["crc_" + D] = _concat_tail(None if pr is None else pr["cro_" + D], res[r]["cro_" + D])
            m["IoO_" + D] = _iord([res[q] for q in range(NCORES)], "Io_" + D, r)
            m["EoO_" + D] = _eord([res[q] for q in range(NCORES)], "Eo_" + D, r)
            m["IoP_" + D] = _iord([res[q] for q in range(NCORES)], "Io_" + D, r - 1 if r > 0 else 0)
            m["EoP_" + D] = _eord([res[q] for q in range(NCORES)], "Eo_" + D, r - 1 if r > 0 else 0)
            m["nAP_" + D] = w["nA"]
            m["owTP_" + D] = w["owT"]
        return m

    maps = []
    for r in range(NCORES):
        m = dict(rT_f=sh_f[r], rT_b=sh_b[r],
                 hmask=(np.zeros((128, H), np.float32) if r == 0 else np.ones((128, H), np.float32)))
        m.update(weight_map(0))
        maps.append(m)
    res = run_prog("l1s_first", lambda: build_l1s("first"), maps)

    for l in (1, 2, 3):
        maps = []
        for r in range(NCORES):
            m = dict(rT_f=res[r]["ro_f"], rT_b=res[r]["ro_b"],
                     hmask=(np.zeros((128, H), np.float32) if r == 0 else np.ones((128, H), np.float32)))
            m.update(weight_map(l))
            m.update(prev_map(l, res, r))
            maps.append(m)
        res = run_prog("l1s_mid", lambda: build_l1s("mid"), maps)

    nf = np.ascontiguousarray(np.concatenate([_r8(I["L1_attn_normf_w"]), _r8(I["L1_attn_normf_b"])], 1))
    maps = []
    for r in range(NCORES):
        m = dict(rT_f=res[r]["ro_f"], rT_b=res[r]["ro_b"], nfb_f=nf, nfb_b=nf,
                 hmask=(np.zeros((128, H), np.float32) if r == 0 else np.ones((128, H), np.float32)))
        m.update(prev_map(4, res, r))
        maps.append(m)
    resf = run_prog("l1s_last", lambda: build_l1s("last"), maps)
    fh2 = np.concatenate([resf[r]["fh2_f"] for r in range(NCORES)], axis=1)
    bh2f = np.concatenate([resf[r]["fh2_b"] for r in range(NCORES)], axis=1)
    bh2u = np.ascontiguousarray(bh2f[:, ::-1])
    return fh2, bh2u, res, resf


def emit_part_max(p, cm, tiles, width, out_ap, pmx, pmd):
    def pm1(eng, tiles=tiles):
        for i, t in enumerate(tiles):
            eng.tensor_reduce(cm.sq[:, i:i + 1].rearrange("p (a b) -> p a b", b=1),
                              t[:, 0:width].rearrange("p (a b) -> p a b", b=width),
                              AX.X, ALU.max, apply_absolute_value=True)
        return eng.tensor_reduce(pmx[:, 0:1].rearrange("p (a b) -> p a b", b=1),
                                 cm.sq[:, 0:len(tiles)].rearrange("p (a b) -> p a b", b=len(tiles)),
                                 AX.X, ALU.max)
    p.phase("vector", pm1)
    p.dma("sync", [(pmd[:, :], pmx[:, 0:1])])
    p.dma("sync", [(pmx[0:1, 0:128], pmd.rearrange("a b -> b a"))], allow_slow=True)

    def pm2(eng):
        return eng.tensor_reduce(pmx[0:1, 128:129].rearrange("p (a b) -> p a b", b=1),
                                 pmx[0:1, 0:128].rearrange("p (a b) -> p a b", b=128),
                                 AX.X, ALU.max)
    p.phase("vector", pm2)
    p.dma("gpsimd", [(out_ap, pmx[0:1, 128:129])])


def build_l1m():
    p = Prog("l1m")
    fh2T = p.inp("fh2T", [1024, C])
    bh2T = p.inp("bh2T", [1024, C])
    x2T = p.inp("x2T", [1024, C])
    cT8b = p.inp("cT8b", [128, 8])
    wmodb = p.inp("wmodb", [1024, 6144])
    WfT = p.inp("WfT", [1024, 1024])
    WbT = p.inp("WbT", [1024, 1024])
    mb8 = p.inp("mb8", [128, 8])
    x3_o = p.out("x3out", [1024, C])
    h4_o = p.out("h4out", [1024, C])
    pm_o = p.out("pmax", [1, 1])

    cm = Common(p, width=C)
    web = p.sbuf("web", [128, 16 * 128], BF16)
    mbs = p.sbuf("mbs", [128, 8], F32)
    pmx = p.sbuf("pmx", [128, 129], F32)
    pmd = p.dram("pmd", [128, 1])
    fhb = [p.sbuf(f"fhb{k}", [128, C], BF16) for k in range(8)]
    bhb = [p.sbuf(f"bhb{k}", [128, C], BF16) for k in range(8)]
    x3 = [p.sbuf(f"x3_{k}", [128, C], F32) for k in range(8)]
    xst = p.sbuf("xst", [128, C], F32)
    modx = emit_mod_vectors(p, cm, cT8b, wmodb, tag="m")
    p.dma("gpsimd", [(fhb[k][:, :], fh2T[k * 128:(k + 1) * 128, :]) for k in range(8)])
    p.dma("gpsimd", [(bhb[k][:, :], bh2T[k * 128:(k + 1) * 128, :]) for k in range(8)])
    p.dma("sync", [(mbs[:, :], mb8)])

    for m in range(8):
        p.dma("gpsimd", [(web[:, k * 128:(k + 1) * 128],
                          WfT[k * 128:(k + 1) * 128, m * 128:(m + 1) * 128]) for k in range(8)]
              + [(web[:, (8 + k) * 128:(9 + k) * 128],
                  WbT[k * 128:(k + 1) * 128, m * 128:(m + 1) * 128]) for k in range(8)])
        p.dma("sync", [(xst[:, :], x2T[m * 128:(m + 1) * 128, :])])
        for (o, w) in CC:
            def mm(eng, m=m, o=o, w=w):
                last = None
                for k in range(8):
                    eng.matmul(cm.mm_ps[0][:, 0:w], web[:, k * 128:(k + 1) * 128],
                               fhb[k][:, o:o + w], start=(k == 0), stop=False)
                    last = eng.matmul(cm.mm_ps[0][:, 0:w], web[:, (8 + k) * 128:(9 + k) * 128],
                                      bhb[k][:, o:o + w], start=False, stop=(k == 7))
                return last
            p.phase("tensor", mm)

            def ev(eng, o=o, w=w):
                return eng.copy(cm.bcs1[:, 0:w], cm.mm_ps[0][:, 0:w])
            p.phase("scalar", ev)

            def fin(eng, m=m, o=o, w=w):
                eng.tensor_scalar_add(cm.bcs1[:, 0:w], cm.bcs1[:, 0:w], mbs[:, m:m + 1])
                return eng.scalar_tensor_tensor(x3[m][:, o:o + w], cm.bcs1[:, 0:w],
                                                modx[:, 16 + m:17 + m], xst[:, o:o + w],
                                                ALU.mult, ALU.add)
            p.phase("vector", fin)
        p.dma("gpsimd", [(x3_o[m * 128:(m + 1) * 128, :], x3[m][:, :])])

    emit_ln(p, cm, "lnm", x3, x3, C,
            gamma=[modx[:, 56 + k:57 + k] for k in range(8)],
            beta=[modx[:, 24 + k:25 + k] for k in range(8)])
    p.dma("gpsimd", [(h4_o[m * 128:(m + 1) * 128, :], x3[m][:, :]) for m in range(8)])
    emit_part_max(p, cm, x3, C, pm_o, pmx, pmd)
    return p


_GMN = [0]


def emit_gmax_scale(p, cm, maxes_in, s128, rs128, qmax, smin):
    _GMN[0] += 1
    m8 = p.sbuf("gm8_" + str(_GMN[0]), [1, 8], F32)
    p.dma("sync", [(m8[:, :], maxes_in)])

    def g1(eng):
        eng.tensor_reduce(s128[0:1, 0:1].rearrange("p (a b) -> p a b", b=1),
                          m8[0:1, 0:8].rearrange("p (a b) -> p a b", b=8), AX.X, ALU.max)
        eng.tensor_scalar_mul(s128[0:1, 0:1], s128[0:1, 0:1], float(np.float32(1.0 / qmax)))
        eng.tensor_scalar_max(s128[0:1, 0:1], s128[0:1, 0:1], smin)
        eng.tensor_copy(rs128[0:1, 0:1], s128[0:1, 0:1])
        return eng.reciprocal(rs128[0:1, 0:1], rs128[0:1, 0:1])
    p.phase("vector", g1)
    emit_bcast_row(p, s128, s128, width=1)
    emit_bcast_row(p, rs128, rs128, width=1)


def emit_qdq(eng, dst, src_ap, w, rs_ap, s_ap, iqt, f32t, lo, hi):
    eng.tensor_scalar(f32t[:, 0:w], src_ap, rs_ap, None, ALU.mult)
    eng.tensor_copy(iqt[:, 0:w], f32t[:, 0:w])
    eng.tensor_copy(f32t[:, 0:w], iqt[:, 0:w])
    eng.tensor_scalar_min(f32t[:, 0:w], f32t[:, 0:w], float(hi))
    eng.tensor_scalar_max(f32t[:, 0:w], f32t[:, 0:w], float(lo))
    return eng.tensor_scalar(dst, f32t[:, 0:w], s_ap, None, ALU.mult)


def build_f1():
    p = Prog("f1")
    h4T = p.inp("h4T", [1024, C])
    maxes = p.inp("maxes", [1, 8])
    wup = p.inp("wup", [3072, 1024])
    wgate = p.inp("wgate", [3072, 1024])
    up_o = p.out("up", [3072, C])
    gate_o = p.out("gate", [3072, C])
    pmu_o = p.out("pmu", [1, 1])
    pmg_o = p.out("pmg", [1, 1])

    cm = Common(p, width=C)
    s128 = p.sbuf("s128", [128, 1], F32)
    rs128 = p.sbuf("rs128", [128, 1], F32)
    iqt = p.sbuf("iqt", [128, C], I32)
    f32t = p.sbuf("f32t", [128, C], F32)
    wst = p.sbuf("wst", [128, 1024], F32)
    sw = p.sbuf("sw", [128, 32], F32)
    rsw = p.sbuf("rsw", [128, 32], F32)
    wqb = p.sbuf("wqb", [128, 1024], BF16)
    wqT = p.sbuf("wqT", [128, 1024], BF16)
    xqb = [p.sbuf(f"xqb{k}", [128, C], BF16) for k in range(8)]
    ust = p.sbuf("ust", [128, C], F32)
    pmx = p.sbuf("pmx", [128, 129], F32)
    pmacc = [p.sbuf("pma_u", [128, 24], F32), p.sbuf("pma_g", [128, 24], F32)]
    pmd = p.dram("pmd", [128, 1])

    emit_gmax_scale(p, cm, maxes, s128, rs128, 32767.0, 1e-12)

    for k in range(8):
        p.dma("sync", [(f32t[:, 0:C] if False else ust[:, :], h4T[k * 128:(k + 1) * 128, :])])

        def xq(eng, k=k):
            return emit_qdq(eng, xqb[k][:, :], ust[:, :], C, rs128[:, 0:1], s128[:, 0:1],
                            iqt, f32t, -32768.0, 32767.0)
        p.phase("vector", xq)

    for wi, (w_in, out_d, pma) in enumerate(((wup, up_o, pmacc[0]), (wgate, gate_o, pmacc[1]))):
        for m in range(24):
            p.dma("sync", [(wst[:, :], w_in[m * 128:(m + 1) * 128, :])])

            def dq(eng, m=m):
                eng.tensor_reduce(sw[:, :].rearrange("p (a b) -> p a b", b=1),
                                  wst[:, :].rearrange("p (a b) -> p a b", b=32),
                                  AX.X, ALU.max, apply_absolute_value=True)
                eng.tensor_scalar_mul(sw[:, :], sw[:, :], float(np.float32(1.0 / 7.0)))
                eng.tensor_scalar_max(sw[:, :], sw[:, :], 1e-12)
                eng.tensor_copy(rsw[:, :], sw[:, :])
                eng.reciprocal(rsw[:, :], rsw[:, :])
                eng.tensor_tensor(f32t[:, 0:1024].rearrange("p (a b) -> p a b", b=32),
                                  wst[:, :].rearrange("p (a b) -> p a b", b=32),
                                  rsw[:, :].rearrange("p (a b) -> p a b", b=1).broadcast_to([128, 32, 32]),
                                  ALU.mult)
                eng.tensor_copy(iqt[:, 0:1024], f32t[:, 0:1024])
                eng.tensor_copy(f32t[:, 0:1024], iqt[:, 0:1024])
                eng.tensor_scalar_min(f32t[:, 0:1024], f32t[:, 0:1024], 7.0)
                eng.tensor_scalar_max(f32t[:, 0:1024], f32t[:, 0:1024], -8.0)
                return eng.tensor_tensor(wqb[:, :].rearrange("p (a b) -> p a b", b=32),
                                         f32t[:, 0:1024].rearrange("p (a b) -> p a b", b=32),
                                         sw[:, :].rearrange("p (a b) -> p a b", b=1).broadcast_to([128, 32, 32]),
                                         ALU.mult)
            p.phase("vector", dq)
            for k in range(8):
                def trf(eng, k=k):
                    return [eng.dma_start(out=wqT[:, k * 128:(k + 1) * 128],
                                          in_=wqb[:, k * 128:(k + 1) * 128], transpose=True)]
                p.phase("sync", trf, n_dma=1)
            for (o, w) in CC:
                def umm(eng, o=o, w=w):
                    last = None
                    for k in range(8):
                        last = eng.matmul(cm.mm_ps[0][:, 0:w], wqT[:, k * 128:(k + 1) * 128],
                                          xqb[k][:, o:o + w], start=(k == 0), stop=(k == 7))
                    return last
                p.phase("tensor", umm)

                def uev(eng, o=o, w=w):
                    return eng.copy(ust[:, o:o + w], cm.mm_ps[0][:, 0:w])
                p.phase("scalar", uev)

            def upm(eng, m=m, pma=pma):
                return eng.tensor_reduce(pma[:, m:m + 1].rearrange("p (a b) -> p a b", b=1),
                                         ust[:, :].rearrange("p (a b) -> p a b", b=C),
                                         AX.X, ALU.max, apply_absolute_value=True)
            p.phase("vector", upm)
            p.dma("gpsimd", [(out_d[m * 128:(m + 1) * 128, :], ust[:, :])])

    for pma, po in ((pmacc[0], pmu_o), (pmacc[1], pmg_o)):
        def red(eng, pma=pma):
            return eng.tensor_reduce(pmx[:, 0:1].rearrange("p (a b) -> p a b", b=1),
                                     pma[:, :].rearrange("p (a b) -> p a b", b=24),
                                     AX.X, ALU.max)
        p.phase("vector", red)
        p.dma("sync", [(pmd[:, :], pmx[:, 0:1])])
        p.dma("sync", [(pmx[0:1, 0:128], pmd.rearrange("a b -> b a"))], allow_slow=True)

        def red2(eng):
            return eng.tensor_reduce(pmx[0:1, 128:129].rearrange("p (a b) -> p a b", b=1),
                                     pmx[0:1, 0:128].rearrange("p (a b) -> p a b", b=128),
                                     AX.X, ALU.max)
        p.phase("vector", red2)
        p.dma("gpsimd", [(po, pmx[0:1, 128:129])])
    return p


def build_f2():
    p = Prog("f2")
    gate_i = p.inp("gate", [3072, C])
    maxg = p.inp("maxg", [1, 8])
    t_o = p.out("tout", [3072, C])
    pmt_o = p.out("pmt", [1, 1])
    cm = Common(p, width=C)
    s128 = p.sbuf("s128", [128, 1], F32)
    rs128 = p.sbuf("rs128", [128, 1], F32)
    iqt = p.sbuf("iqt", [128, C], I32)
    f32t = p.sbuf("f32t", [128, C], F32)
    gst = p.sbuf("gst", [128, C], F32)
    gq = p.sbuf("gq", [128, C], F32)
    sg = p.sbuf("sg", [128, C], F32)
    pma = p.sbuf("pma", [128, 24], F32)
    pmx = p.sbuf("pmx", [128, 129], F32)
    pmd = p.dram("pmd", [128, 1])
    emit_gmax_scale(p, cm, maxg, s128, rs128, 32767.0, 1e-12)
    for m in range(24):
        p.dma("sync", [(gst[:, :], gate_i[m * 128:(m + 1) * 128, :])])

        def q1(eng):
            return emit_qdq(eng, gq[:, 0:C], gst[:, :], C, rs128[:, 0:1], s128[:, 0:1],
                            iqt, f32t, -32768.0, 32767.0)
        p.phase("vector", q1)

        def sg1(eng):
            return eng.activation(sg[:, :], gq[:, :], AF.Exp, scale=-1.0)
        p.phase("scalar", sg1)

        def sg2(eng):
            eng.tensor_scalar_add(sg[:, :], sg[:, :], 1.0)
            eng.reciprocal(sg[:, :], sg[:, :])
            eng.tensor_scalar_mul(f32t[:, :], sg[:, :], 65536.0)
            eng.tensor_copy(iqt[:, :], f32t[:, :])
            eng.tensor_copy(f32t[:, :], iqt[:, :])
            eng.tensor_scalar_min(f32t[:, :], f32t[:, :], 65535.0)
            eng.tensor_scalar_max(f32t[:, :], f32t[:, :], 0.0)
            eng.tensor_scalar_mul(sg[:, :], f32t[:, :], float(np.float32(1.0 / 65536.0)))
            eng.tensor_tensor(gq[:, :], gq[:, :], sg[:, :], ALU.mult)
            return eng.tensor_reduce(pma[:, m:m + 1].rearrange("p (a b) -> p a b", b=1),
                                     gq[:, :].rearrange("p (a b) -> p a b", b=C),
                                     AX.X, ALU.max, apply_absolute_value=True)
        p.phase("vector", sg2)
        p.dma("gpsimd", [(t_o[m * 128:(m + 1) * 128, :], gq[:, :])])

    def red(eng):
        return eng.tensor_reduce(pmx[:, 0:1].rearrange("p (a b) -> p a b", b=1),
                                 pma[:, :].rearrange("p (a b) -> p a b", b=24),
                                 AX.X, ALU.max)
    p.phase("vector", red)
    p.dma("sync", [(pmd[:, :], pmx[:, 0:1])])
    p.dma("sync", [(pmx[0:1, 0:128], pmd.rearrange("a b -> b a"))], allow_slow=True)

    def red2(eng):
        return eng.tensor_reduce(pmx[0:1, 128:129].rearrange("p (a b) -> p a b", b=1),
                                 pmx[0:1, 0:128].rearrange("p (a b) -> p a b", b=128),
                                 AX.X, ALU.max)
    p.phase("vector", red2)
    p.dma("gpsimd", [(pmt_o, pmx[0:1, 128:129])])
    return p


def build_f3():
    p = Prog("f3")
    t_i = p.inp("tin", [3072, C])
    up_i = p.inp("upin", [3072, C])
    maxt = p.inp("maxt", [1, 8])
    maxu = p.inp("maxu", [1, 8])
    o_o = p.out("opre", [3072, C])
    pmo_o = p.out("pmo", [1, 1])
    cm = Common(p, width=C)
    sT = p.sbuf("sT", [128, 1], F32); rsT = p.sbuf("rsT", [128, 1], F32)
    sU = p.sbuf("sU", [128, 1], F32); rsU = p.sbuf("rsU", [128, 1], F32)
    iqt = p.sbuf("iqt", [128, C], I32)
    f32t = p.sbuf("f32t", [128, C], F32)
    ast = p.sbuf("ast", [128, C], F32)
    bst = p.sbuf("bst", [128, C], F32)
    g2 = p.sbuf("g2", [128, C], F32)
    pma = p.sbuf("pma", [128, 24], F32)
    pmx = p.sbuf("pmx", [128, 129], F32)
    pmd = p.dram("pmd", [128, 1])
    emit_gmax_scale(p, cm, maxt, sT, rsT, 32767.0, 1e-12)
    emit_gmax_scale(p, cm, maxu, sU, rsU, 32767.0, 1e-12)
    for m in range(24):
        p.dma("sync", [(ast[:, :], t_i[m * 128:(m + 1) * 128, :]),
                       (bst[:, :], up_i[m * 128:(m + 1) * 128, :])])

        def q1(eng, m=m):
            emit_qdq(eng, g2[:, 0:C], ast[:, :], C, rsT[:, 0:1], sT[:, 0:1],
                     iqt, f32t, -32768.0, 32767.0)
            emit_qdq(eng, bst[:, 0:C], bst[:, :], C, rsU[:, 0:1], sU[:, 0:1],
                     iqt, f32t, -32768.0, 32767.0)
            eng.tensor_tensor(g2[:, :], g2[:, :], bst[:, :], ALU.mult)
            return eng.tensor_reduce(pma[:, m:m + 1].rearrange("p (a b) -> p a b", b=1),
                                     g2[:, :].rearrange("p (a b) -> p a b", b=C),
                                     AX.X, ALU.max, apply_absolute_value=True)
        p.phase("vector", q1)
        p.dma("gpsimd", [(o_o[m * 128:(m + 1) * 128, :], g2[:, :])])

    def red(eng):
        return eng.tensor_reduce(pmx[:, 0:1].rearrange("p (a b) -> p a b", b=1),
                                 pma[:, :].rearrange("p (a b) -> p a b", b=24),
                                 AX.X, ALU.max)
    p.phase("vector", red)
    p.dma("sync", [(pmd[:, :], pmx[:, 0:1])])
    p.dma("sync", [(pmx[0:1, 0:128], pmd.rearrange("a b -> b a"))], allow_slow=True)

    def red2(eng):
        return eng.tensor_reduce(pmx[0:1, 128:129].rearrange("p (a b) -> p a b", b=1),
                                 pmx[0:1, 0:128].rearrange("p (a b) -> p a b", b=128),
                                 AX.X, ALU.max)
    p.phase("vector", red2)
    p.dma("gpsimd", [(pmo_o, pmx[0:1, 128:129])])
    return p


def build_f4():
    p = Prog("f4")
    o_i = p.inp("oin", [3072, C])
    maxo = p.inp("maxo", [1, 8])
    wd_i = p.inp("wd", [1024, 3072])
    x3T = p.inp("x3T", [1024, C])
    cT8b = p.inp("cT8b", [128, 8])
    wmodb = p.inp("wmodb", [1024, 6144])
    x4_o = p.out("x4out", [1024, C])
    cm = Common(p, width=C)
    s128 = p.sbuf("s128", [128, 1], F32)
    rs128 = p.sbuf("rs128", [128, 1], F32)
    iqt = p.sbuf("iqt", [128, 3072], I32)
    f32t = p.sbuf("f32t", [128, 3072], F32)
    wst = p.sbuf("wst", [128, 3072], F32)
    sw = p.sbuf("sw", [128, 96], F32)
    rsw = p.sbuf("rsw", [128, 96], F32)
    wqb = p.sbuf("wqb", [128, 3072], BF16)
    wqT = p.sbuf("wqT", [128, 3072], BF16)
    ost = p.sbuf("ost", [128, C], F32)
    oqb = p.sbuf("oqb", [128, C], BF16)
    ostr = p.sbuf("ostr", [128, 24 * 512], BF16)
    xst = p.sbuf("xst", [128, C], F32)
    oq_d = p.dram("oq_d", [3072, C], BF16)
    modx = emit_mod_vectors(p, cm, cT8b, wmodb, tag="m")
    emit_gmax_scale(p, cm, maxo, s128, rs128, 32767.0, 1e-12)
    for k in range(24):
        p.dma("sync", [(ost[:, :], o_i[k * 128:(k + 1) * 128, :])])

        def oq(eng):
            return emit_qdq(eng, oqb[:, 0:C], ost[:, :], C, rs128[:, 0:1], s128[:, 0:1],
                            iqt, f32t, -32768.0, 32767.0)
        p.phase("vector", oq)
        p.dma("gpsimd", [(oq_d[k * 128:(k + 1) * 128, :], oqb[:, :])])

    for m in range(8):
        p.dma("sync", [(wst[:, :], wd_i[m * 128:(m + 1) * 128, :])])

        def dq(eng, m=m):
            eng.tensor_reduce(sw[:, :].rearrange("p (a b) -> p a b", b=1),
                              wst[:, :].rearrange("p (a b) -> p a b", b=32),
                              AX.X, ALU.max, apply_absolute_value=True)
            eng.tensor_scalar_mul(sw[:, :], sw[:, :], float(np.float32(1.0 / 7.0)))
            eng.tensor_scalar_max(sw[:, :], sw[:, :], 1e-12)
            eng.tensor_copy(rsw[:, :], sw[:, :])
            eng.reciprocal(rsw[:, :], rsw[:, :])
            eng.tensor_tensor(f32t[:, :].rearrange("p (a b) -> p a b", b=32),
                              wst[:, :].rearrange("p (a b) -> p a b", b=32),
                              rsw[:, :].rearrange("p (a b) -> p a b", b=1).broadcast_to([128, 96, 32]),
                              ALU.mult)
            eng.tensor_copy(iqt[:, :], f32t[:, :])
            eng.tensor_copy(f32t[:, :], iqt[:, :])
            eng.tensor_scalar_min(f32t[:, :], f32t[:, :], 7.0)
            eng.tensor_scalar_max(f32t[:, :], f32t[:, :], -8.0)
            return eng.tensor_tensor(wqb[:, :].rearrange("p (a b) -> p a b", b=32),
                                     f32t[:, :].rearrange("p (a b) -> p a b", b=32),
                                     sw[:, :].rearrange("p (a b) -> p a b", b=1).broadcast_to([128, 96, 32]),
                                     ALU.mult)
        p.phase("vector", dq)
        for k in range(24):
            def trf(eng, k=k):
                return [eng.dma_start(out=wqT[:, k * 128:(k + 1) * 128],
                                      in_=wqb[:, k * 128:(k + 1) * 128], transpose=True)]
            p.phase("sync", trf, n_dma=1)
        p.dma("sync", [(xst[:, :], x3T[m * 128:(m + 1) * 128, :])])
        for (o, w) in CC:
            p.dma("sync", [(ostr[:, k * 512:k * 512 + w], oq_d[k * 128:(k + 1) * 128, o:o + w])
                           for k in range(24)])

            def dmm(eng, o=o, w=w):
                last = None
                for k in range(24):
                    last = eng.matmul(cm.mm_ps[0][:, 0:w], wqT[:, k * 128:(k + 1) * 128],
                                      ostr[:, k * 512:k * 512 + w], start=(k == 0), stop=(k == 23))
                return last
            p.phase("tensor", dmm)

            def dev(eng, o=o, w=w):
                return eng.copy(ost[:, o:o + w], cm.mm_ps[0][:, 0:w])
            p.phase("scalar", dev)

        def fin(eng, m=m):
            return eng.scalar_tensor_tensor(ost[:, :], ost[:, :], modx[:, 40 + m:41 + m],
                                            xst[:, :], ALU.mult, ALU.add)
        p.phase("vector", fin)
        p.dma("gpsimd", [(x4_o[m * 128:(m + 1) * 128, :], ost[:, :])])
    return p



def kernel(**I):
    I = {k: np.asarray(v) for k, v in I.items()}

    res_a = run_prog("l0a", build_l0a, l0a_inputs(I))
    res_b = run_prog("l0b", build_l0b, l0b_inputs(I, res_a))
    x1_shards = [res_b[r]["x1out"] for r in range(NCORES)]
    res_c = run_prog("l0c", build_l0c, l0c_inputs(I, x1_shards))
    h3_full = np.ascontiguousarray(np.concatenate([res_c[r]["h3out"] for r in range(NCORES)], axis=1))

    fh2, bh2u, _, _ = run_l1_chain(I, h3_full)

    mW = I["L1_attn_merge_W"]
    WfT = np.ascontiguousarray(mW[:, 0:1024].T)
    WbT = np.ascontiguousarray(mW[:, 1024:2048].T)
    mb8 = _r8(I["L1_attn_merge_b"])
    cT8b = _r8(I["L1_c"][0])
    maps = []
    for r in range(NCORES):
        lo = r * C
        maps.append(dict(fh2T=np.ascontiguousarray(fh2[:, lo:lo + C]),
                         bh2T=np.ascontiguousarray(bh2u[:, lo:lo + C]),
                         x2T=res_c[r]["x2out"], cT8b=cT8b, wmodb=I["L1_wmod"],
                         WfT=WfT, WbT=WbT, mb8=mb8))
    res_m = run_prog("l1m", build_l1m, maps)

    maxes = np.ascontiguousarray(np.array([[res_m[r]["pmax"][0, 0] for r in range(NCORES)]],
                                          np.float32))
    maps = [dict(h4T=res_m[r]["h4out"], maxes=maxes, wup=I["L1_ffn_w_up"],
                 wgate=I["L1_ffn_w_gate"]) for r in range(NCORES)]
    res_1 = run_prog("f1", build_f1, maps)

    maxg = np.ascontiguousarray(np.array([[res_1[r]["pmg"][0, 0] for r in range(NCORES)]], np.float32))
    maps = [dict(gate=res_1[r]["gate"], maxg=maxg) for r in range(NCORES)]
    res_2 = run_prog("f2", build_f2, maps)

    maxu = np.ascontiguousarray(np.array([[res_1[r]["pmu"][0, 0] for r in range(NCORES)]], np.float32))
    maxt = np.ascontiguousarray(np.array([[res_2[r]["pmt"][0, 0] for r in range(NCORES)]], np.float32))
    maps = [dict(tin=res_2[r]["tout"], upin=res_1[r]["up"], maxt=maxt, maxu=maxu)
            for r in range(NCORES)]
    res_3 = run_prog("f3", build_f3, maps)

    maxo = np.ascontiguousarray(np.array([[res_3[r]["pmo"][0, 0] for r in range(NCORES)]], np.float32))
    maps = [dict(oin=res_3[r]["opre"], maxo=maxo, wd=I["L1_ffn_w_down"],
                 x3T=res_m[r]["x3out"], cT8b=cT8b, wmodb=I["L1_wmod"]) for r in range(NCORES)]
    res_4 = run_prog("f4", build_f4, maps)

    full = np.concatenate([res_4[r]["x4out"] for r in range(NCORES)], axis=1)
    return np.ascontiguousarray(full.T)[None, :, :]


# version 03
# speedup vs baseline: 1.5211x; 1.1775x over previous
import numpy as np
import ml_dtypes
import concourse.bass as bass
import concourse.mybir as mybir
from concourse.bass_utils import run_bass_kernel_spmd

F32 = mybir.dt.float32
BF16 = mybir.dt.bfloat16
I32 = mybir.dt.int32
AF = mybir.ActivationFunctionType
ALU = mybir.AluOpType
AX = mybir.AxisListType

NCORES = 8
C = 2048
H = 3
TC = C + H
BF = ml_dtypes.bfloat16
LN_EPS = 1e-5


def chunks_of(total, step=512):
    out, o = [], 0
    while o < total:
        out.append((o, min(step, total - o)))
        o += step
    return out


LAGS_ON = True


def _L(x):
    return x if LAGS_ON else 0


TCC = chunks_of(TC)
CC = chunks_of(C)


class Prog:
    def __init__(self, name):
        self.name = name
        self.nc = bass.Bass("TRN2", target_bir_lowering=False, debug=False)
        self.phases = []

    def inp(self, name, shape, dtype=F32):
        return self.nc.dram_tensor(name, list(shape), dtype, kind="ExternalInput").ap()

    def out(self, name, shape, dtype=F32):
        return self.nc.dram_tensor(name, list(shape), dtype, kind="ExternalOutput").ap()

    def dram(self, name, shape, dtype=F32):
        return self.nc.dram_tensor(name, list(shape), dtype).ap()

    def sbuf(self, name, shape, dtype=F32, side=None):
        kw = dict(side=side) if side else {}
        ctx = self.nc.sbuf_tensor(name, list(shape), dtype, **kw)
        t = ctx.__enter__()
        self._sbuf_ctx = getattr(self, "_sbuf_ctx", {})
        self._sbuf_ctx[id(t)] = ctx
        return t

    def free(self, *tiles):
        for t in reversed(tiles):
            self._sbuf_ctx.pop(id(t)).__exit__(None, None, None)

    def psum(self, name, shape, dtype=F32):
        return self.nc.psum_tensor(name, list(shape), dtype).__enter__()

    def phase(self, engine, fn, n_dma=0, lag=0):
        self.phases.append((engine, fn, n_dma, lag))

    def dma(self, engine, transfers, allow_slow=False, lag=0):
        def fn(eng):
            kw = dict(allow_slow_non_contiguous=True) if allow_slow else {}
            return [eng.dma_start(out=o, in_=i, **kw) for (o, i) in transfers]
        self.phase(engine, fn, n_dma=len(transfers), lag=lag)

    def finalize(self):
        nc = self.nc
        gs = nc.semaphore("gs").__enter__()
        cs_sems = {e: nc.semaphore("cs_" + e).__enter__() for e in ("vector", "scalar", "gpsimd")}
        targets, tot = [], 0
        for ph in self.phases:
            targets.append(tot)
            tot += 16 * ph[2] if ph[2] else 1
        final_total = tot
        waits = []
        for idx, ph in enumerate(self.phases):
            j = max(0, idx - ph[3])
            waits.append(targets[j])
        block = nc.Block().__enter__()

        GUARded = ("vector", "scalar", "gpsimd")
        WRAP = {"tensor_tensor", "tensor_scalar", "tensor_single_scalar", "tensor_scalar_add",
                "tensor_scalar_sub", "tensor_scalar_mul", "tensor_scalar_max", "tensor_scalar_min",
                "tensor_add", "tensor_sub", "tensor_mul", "tensor_max", "tensor_copy", "memset",
                "memzero", "scalar_tensor_tensor", "tensor_tensor_scan", "tensor_reduce",
                "reduce_max", "reduce_sum", "tensor_relu", "reciprocal", "select", "activation",
                "copy", "mul", "add", "square", "sign", "sqrt", "pool", "pool_max", "pool_avg",
                "iota", "transpose", "bn_stats", "bn_aggr", "stream_shuffle", "copy_predicated"}

        class GEng:
            def __init__(self, eng, sem):
                self._e = eng
                self._sem = sem
                self._n = 0

            def __getattr__(self, name):
                f = getattr(self._e, name)
                if name in WRAP:
                    def wrapped(*a, **kw):
                        inst = f(*a, **kw)
                        self._n += 1
                        inst.then_inc(self._sem, 1)
                        self._e.wait_ge(self._sem, self._n)
                        return inst
                    return wrapped
                return f

        def runner(engine_name):
            def run(eng):
                geng = GEng(eng, cs_sems[engine_name]) if engine_name in GUARded else eng
                for idx, (en, fn, n_dma, _lag) in enumerate(self.phases):
                    if en != engine_name:
                        continue
                    if waits[idx] > 0:
                        eng.wait_ge(gs, waits[idx])
                    last = fn(geng)
                    if n_dma:
                        for inst in last:
                            inst.then_inc(gs, 16)
                    elif engine_name == "tensor":
                        if last is None:
                            eng.sem_inc(gs, 1)
                        else:
                            last.then_inc(gs, 1)
                    else:
                        eng.sem_inc(gs, 1)
                if engine_name == "gpsimd":
                    eng.wait_ge(gs, final_total)
            return run

        @block.sync
        def _(eng):
            runner("sync")(eng)

        @block.scalar
        def _(eng):
            runner("scalar")(eng)

        @block.vector
        def _(eng):
            runner("vector")(eng)

        @block.tensor
        def _(eng):
            runner("tensor")(eng)

        @block.gpsimd
        def _(eng):
            runner("gpsimd")(eng)

        block.__exit__(None, None, None)
        return nc



class Common:
    def __init__(self, prog, width=TC):
        self.prog = prog
        self.width = width
        self.ones = prog.sbuf("cm_ones", [128, 1], F32)
        self.ones_row = prog.sbuf("cm_onesr", [1, 128], F32)
        self.sq = prog.sbuf("cm_sq", [128, 512], F32)
        self.stat_ps = prog.psum("cm_statps", [1, 512])
        self.bc_ps = prog.psum("cm_bcps", [128, 512])
        self.bc2_ps = prog.psum("cm_bc2ps", [128, 512])
        self.mm_ps = [prog.psum("cm_mmps0", [128, 512]), prog.psum("cm_mmps1", [128, 512])]
        self.bcs1 = prog.sbuf("cm_bcs1", [128, 512], F32)
        self.bcs2 = prog.sbuf("cm_bcs2", [128, 512], F32)
        self.mu = prog.sbuf("cm_mu", [1, width], F32)
        self.rs = prog.sbuf("cm_rs", [1, width], F32)
        self.tp = prog.sbuf("cm_tp", [1, width], F32)

        def init_ones(eng):
            eng.memset(self.ones[:, :], 0.0)
            eng.tensor_scalar_add(self.ones[:, :], self.ones[:, :], 1.0)
            eng.memset(self.ones_row[:, :], 0.0)
            return eng.tensor_scalar_add(self.ones_row[:, :], self.ones_row[:, :], 1.0)
        prog.phase("vector", init_ones)


def emit_ln(prog, cm, name, src_tiles, dst_tiles, width, gamma, beta):
    ntiles = len(src_tiles)
    D = 128 * ntiles
    mu, rs, tp = cm.mu, cm.rs, cm.tp
    cks = chunks_of(width)

    for (o, w) in cks:
        def mm(eng, o=o, w=w):
            last = None
            for k in range(ntiles):
                last = eng.matmul(cm.stat_ps[0:1, 0:w], cm.ones[:, 0:1],
                                  src_tiles[k][:, o:o + w], start=(k == 0), stop=(k == ntiles - 1))
            return last
        prog.phase("tensor", mm)

        def ev(eng, o=o, w=w):
            return eng.mul(mu[0:1, o:o + w], cm.stat_ps[0:1, 0:w], 1.0 / D)
        prog.phase("scalar", ev)

    for (o, w) in cks:
        for k in range(ntiles):
            def sq(eng, o=o, w=w, k=k):
                return eng.activation(cm.sq[:, 0:w], src_tiles[k][:, o:o + w], AF.Square)
            prog.phase("scalar", sq)

            def mm2(eng, o=o, w=w, k=k):
                return eng.matmul(cm.stat_ps[0:1, 0:w], cm.ones[:, 0:1], cm.sq[:, 0:w],
                                  start=(k == 0), stop=(k == ntiles - 1))
            prog.phase("tensor", mm2)

        def ev2(eng, o=o, w=w):
            return eng.mul(rs[0:1, o:o + w], cm.stat_ps[0:1, 0:w], 1.0 / D)
        prog.phase("scalar", ev2)

    def ph_v(eng):
        eng.tensor_tensor(tp[0:1, 0:width], mu[0:1, 0:width], mu[0:1, 0:width], ALU.mult)
        eng.tensor_tensor(rs[0:1, 0:width], rs[0:1, 0:width], tp[0:1, 0:width], ALU.subtract)
        return eng.tensor_scalar_add(rs[0:1, 0:width], rs[0:1, 0:width], LN_EPS)
    prog.phase("vector", ph_v)

    def ph_l(eng):
        eng.activation(rs[0:1, 0:width], rs[0:1, 0:width], AF.Ln)
        return eng.activation(rs[0:1, 0:width], rs[0:1, 0:width], AF.Exp, scale=-0.5)
    prog.phase("scalar", ph_l)

    for (o, w) in cks:
        def bc(eng, o=o, w=w):
            eng.matmul(cm.bc_ps[:, 0:w], cm.ones_row[0:1, :], mu[0:1, o:o + w], start=True, stop=True)
            return eng.matmul(cm.bc2_ps[:, 0:w], cm.ones_row[0:1, :], rs[0:1, o:o + w], start=True, stop=True)
        prog.phase("tensor", bc)

        def bce(eng, o=o, w=w):
            eng.copy(cm.bcs1[:, 0:w], cm.bc_ps[:, 0:w])
            return eng.copy(cm.bcs2[:, 0:w], cm.bc2_ps[:, 0:w])
        prog.phase("scalar", bce)

        def ap(eng, o=o, w=w):
            last = None
            for k in range(ntiles):
                g = 1.0 if gamma is None else gamma[k]
                b = 0.0 if beta is None else beta[k]
                eng.tensor_tensor(cm.sq[:, 0:w], src_tiles[k][:, o:o + w], cm.bcs1[:, 0:w], ALU.subtract)
                eng.scalar_tensor_tensor(cm.sq[:, 0:w], cm.sq[:, 0:w], g, cm.bcs2[:, 0:w], ALU.mult, ALU.mult)
                last = eng.tensor_scalar_add(dst_tiles[k][:, o:o + w], cm.sq[:, 0:w], b)
            return last
        prog.phase("vector", ap)
    return mu, rs


def emit_matmul(prog, cm, name, lhsT_tiles, rhs_tiles, out_cb, M_tiles, K_tiles, width,
                n_step=512, accum_sbuf=None):
    cks = chunks_of(width, n_step)
    pi = 0
    for m in range(M_tiles):
        for (o, w) in cks:
            ps = cm.mm_ps[pi % 2]
            pi += 1

            def mm(eng, m=m, o=o, w=w, ps=ps):
                last = None
                for k in range(K_tiles):
                    last = eng.matmul(ps[:, 0:w], lhsT_tiles(k, m), rhs_tiles(k)[:, o:o + w],
                                      start=(k == 0), stop=(k == K_tiles - 1))
                return last
            prog.phase("tensor", mm)
            out_cb(prog, m, o, w, ps)



def shard_tokens(x_t):
    outs = []
    d = x_t.shape[0]
    for r in range(NCORES):
        lo = r * C
        halo = x_t[:, lo - H:lo] if r > 0 else x_t[:, 0:H]
        outs.append(np.ascontiguousarray(np.concatenate([halo, x_t[:, lo:lo + C]], axis=1)))
    return outs


_prog_cache = {}


def run_prog(key, builder, in_maps, trace=False):
    if key not in _prog_cache:
        p = builder()
        nc = p.finalize()
        _prog_cache[key] = (p, nc)
    p, nc = _prog_cache[key]
    res = run_bass_kernel_spmd(nc, in_maps, core_ids=list(range(NCORES)))
    return res.results



def emit_mod_vectors(prog, cm, cT_in, wmod_in, tag=""):
    nc = prog.nc
    c8 = prog.sbuf("mv_c8" + tag, [128, 8], F32)
    sc8 = prog.sbuf("mv_sc8" + tag, [128, 8], F32)
    wbuf = prog.sbuf("mv_wbuf" + tag, [128, 4 * 128], F32)
    modx = prog.sbuf("mv_modx" + tag, [128, 64], F32)
    mps = cm.mm_ps[0]

    prog.dma("sync", [(c8[:, :], cT_in)])

    def silu_c(eng):
        return eng.activation(sc8[:, :], c8[:, :], AF.Silu)
    prog.phase("scalar", silu_c)


    for m in range(48):
        for hf in range(2):
            prog.dma("sync", [(wbuf[:, k * 128:(k + 1) * 128],
                               wmod_in[(4 * hf + k) * 128:(4 * hf + k + 1) * 128,
                                       m * 128:(m + 1) * 128]) for k in range(4)])

            def mm(eng, m=m, hf=hf):
                last = None
                for k in range(4):
                    last = eng.matmul(mps[:, 0:1], wbuf[:, k * 128:(k + 1) * 128],
                                      sc8[:, 4 * hf + k:4 * hf + k + 1],
                                      start=(hf == 0 and k == 0), stop=(hf == 1 and k == 3))
                return last
            prog.phase("tensor", mm)

        def ev(eng, m=m):
            return eng.copy(modx[:, m:m + 1], mps[:, 0:1])
        prog.phase("scalar", ev)

    def fix(eng):
        for base in (16, 40):
            sl = modx[:, base:base + 8]
            eng.tensor_scalar(cm.sq[:, 0:8], sl, 0.0, None, ALU.is_ge)
            eng.tensor_scalar_mul(cm.sq[:, 0:8], cm.sq[:, 0:8], 2.0)
            eng.tensor_scalar_add(cm.sq[:, 0:8], cm.sq[:, 0:8], -1.0)
            eng.tensor_scalar_mul(cm.sq[:, 8:16], sl, -1.0)
            eng.tensor_tensor(cm.sq[:, 8:16], cm.sq[:, 8:16], sl, ALU.max)
            eng.tensor_scalar_max(cm.sq[:, 8:16], cm.sq[:, 8:16], 0.25)
            eng.tensor_tensor(sl, cm.sq[:, 0:8], cm.sq[:, 8:16], ALU.mult)
        eng.tensor_scalar_add(modx[:, 48:56], modx[:, 8:16], 1.0)
        return eng.tensor_scalar_add(modx[:, 56:64], modx[:, 32:40], 1.0)
    prog.phase("vector", fix)
    return modx


def build_l0a():
    p = Prog("l0a")
    xT = p.inp("xT", [1024, TC])
    hmask = p.inp("hmask", [128, H])
    cT8 = p.inp("cT8", [128, 8])
    wmod = p.inp("wmod", [1024, 6144])
    lng8 = p.inp("lng8", [128, 8])
    lnb8 = p.inp("lnb8", [128, 8])
    w_inT = p.inp("w_inT", [1024, 4096])
    convw = p.inp("convw", [128, 64])
    convb = p.inp("convb", [128, 16])
    BmT = p.inp("BmT", [2048, 16])
    AT = p.inp("AT", [16, 16])

    S0_o = p.out("S0", [16, C])
    hdbg_o = p.out("hdbg", [1024, TC])
    xc_o = p.out("xc", [2048, C])
    z_o = p.out("z", [2048, C])

    cm = Common(p)
    x_t = [p.sbuf(f"x{k}", [128, TC]) for k in range(8)]
    xn_t = [p.sbuf(f"xn{k}", [128, TC], BF16) for k in range(8)]
    cwb = p.sbuf("cwb", [128, 64], F32)
    cbb = p.sbuf("cbb", [128, 16], F32)
    hmk = p.sbuf("hmk", [128, H], F32)
    bmb = p.sbuf("bmb", [128, 16 * 16], F32)
    wkb = [p.sbuf(f"wkb{i}", [128, 8 * 128], BF16) for i in range(2)]
    xp = p.sbuf("xp", [128, TC], F32)
    cxs = p.sbuf("cxs", [128, C], F32)
    V = p.sbuf("V", [16, TC], F32)
    v_ps = p.psum("v_ps", [16, 512])
    Sb = [p.sbuf("Sb0", [16, C]), p.sbuf("Sb1", [16, C])]
    Pk = p.sbuf("Pk", [16, 16 * 11], F32)
    pq_ps = p.psum("pq_ps", [16, 16])
    ptmp = p.dram("ptmp", [16, 16])

    lngb = p.sbuf("lngb", [128, 16], F32)
    prog_loads = [(x_t[k][:, :], xT[k * 128:(k + 1) * 128, :]) for k in range(8)]
    prog_loads += [(cwb[:, :], convw), (cbb[:, :], convb), (hmk[:, :], hmask),
                   (lngb[:, 0:8], lng8), (lngb[:, 8:16], lnb8)]
    p.dma("sync", prog_loads)
    p.dma("sync", [(bmb[:, k * 16:(k + 1) * 16], BmT[k * 128:(k + 1) * 128, :]) for k in range(16)])
    p.dma("sync", [(Pk[:, 0:16], AT)])

    modx = emit_mod_vectors(p, cm, cT8, wmod)

    emit_ln(p, cm, "ln1", x_t, x_t, TC,
            gamma=[modx[:, 48 + k:49 + k] for k in range(8)],
            beta=[modx[:, 0 + k:1 + k] for k in range(8)])
    p.dma("gpsimd", [(hdbg_o[k * 128:(k + 1) * 128, :], x_t[k][:, :]) for k in range(8)])
    emit_ln(p, cm, "ln2", x_t, xn_t, TC,
            gamma=[lngb[:, k:k + 1] for k in range(8)],
            beta=[lngb[:, 8 + k:9 + k] for k in range(8)])

    for m in range(32):
        wb = wkb[m % 2]
        p.dma("gpsimd", [(wb[:, k * 128:(k + 1) * 128],
                          w_inT[k * 128:(k + 1) * 128, m * 128:(m + 1) * 128]) for k in range(8)])
        for (o, w) in TCC:
            ps = cm.mm_ps[0]

            def mm(eng, m=m, o=o, w=w, ps=ps, wb=wb):
                last = None
                for k in range(8):
                    last = eng.matmul(ps[:, 0:w], wb[:, k * 128:(k + 1) * 128],
                                      xn_t[k][:, o:o + w], start=(k == 0), stop=(k == 7))
                return last
            p.phase("tensor", mm)

            def ev(eng, m=m, o=o, w=w, ps=ps):
                return eng.copy(xp[:, o:o + w], ps[:, 0:w])
            p.phase("scalar", ev)

        if m < 16:
            def mask(eng, m=m):
                return eng.tensor_tensor(xp[:, 0:H], xp[:, 0:H], hmk[:, :], ALU.mult)
            p.phase("vector", mask)

            for ci, (o, w) in enumerate(TCC):
                def vmm2(eng, m=m, o=o, w=w):
                    return eng.matmul(v_ps[:, 0:w], bmb[:, m * 16:(m + 1) * 16], xp[:, o:o + w],
                                      start=True, stop=True)
                p.phase("tensor", vmm2)

                def vev(eng, o=o, w=w):
                    return eng.copy(cm.bcs1[0:16, 0:w], v_ps[:, 0:w])
                p.phase("scalar", vev)
                if m == 0:
                    def vadd(eng, o=o, w=w):
                        return eng.tensor_copy(V[:, o:o + w], cm.bcs1[0:16, 0:w])
                else:
                    def vadd(eng, o=o, w=w):
                        return eng.tensor_tensor(V[:, o:o + w], V[:, o:o + w], cm.bcs1[0:16, 0:w], ALU.add)
                p.phase("vector", vadd)

            def conv(eng, m=m):
                eng.tensor_scalar_mul(cxs[:, 0:C], xp[:, 0:C], cwb[:, 4 * m:4 * m + 1])
                for j in range(1, 4):
                    eng.scalar_tensor_tensor(cxs[:, 0:C], xp[:, j:j + C], cwb[:, 4 * m + j:4 * m + j + 1],
                                             cxs[:, 0:C], ALU.mult, ALU.add)
                return None
            p.phase("vector", conv)

            def sil(eng, m=m):
                return eng.activation(cxs[:, 0:C], cxs[:, 0:C], AF.Silu, bias=cbb[:, m:m + 1])
            p.phase("scalar", sil)
            p.dma("gpsimd", [(xc_o[m * 128:(m + 1) * 128, :], cxs[:, 0:C])])
        else:
            p.dma("gpsimd", [(z_o[(m - 16) * 128:(m - 15) * 128, :], xp[:, H:TC])])

    for k in range(10):
        src = Pk[:, 16 * k:16 * k + 16]
        dst = Pk[:, 16 * (k + 1):16 * (k + 1) + 16]
        p.dma("sync", [(ptmp[:, :], src)])
        p.dma("sync", [(cm.sq[0:16, 0:16], ptmp.rearrange("a b -> b a"))], allow_slow=True)

        def sq(eng, k=k, src=src):
            return eng.matmul(pq_ps[:, :], cm.sq[0:16, 0:16], src, start=True, stop=True)
        p.phase("tensor", sq)

        def ev(eng, dst=dst):
            return eng.copy(dst, pq_ps[:, :])
        p.phase("scalar", ev)

    def initS(eng):
        return eng.tensor_copy(Sb[0][:, :], V[:, H:TC])
    p.phase("vector", initS)
    cur = 0
    for k in range(11):
        d = 1 << k
        src_t, dst_t = Sb[cur], Sb[1 - cur]
        def cpre(eng, d=d, src_t=src_t, dst_t=dst_t):
            return eng.tensor_copy(dst_t[:, 0:d], src_t[:, 0:d])
        p.phase("vector", cpre)
        for (o, w) in chunks_of(C):
            lo = max(o, d)
            hi = o + w
            if lo >= hi:
                continue
            ww = hi - lo

            def mmk(eng, k=k, lo=lo, ww=ww, src_t=src_t):
                return eng.matmul(cm.mm_ps[0][0:16, 0:ww], Pk[:, 16 * k:16 * k + 16],
                                  src_t[:, lo - (1 << k):lo - (1 << k) + ww],
                                  start=True, stop=True)
            p.phase("tensor", mmk)

            def cpk(eng, ww=ww):
                return eng.copy(cm.bcs2[0:16, 0:ww], cm.mm_ps[0][0:16, 0:ww])
            p.phase("scalar", cpk)

            def addk(eng, lo=lo, ww=ww, src_t=src_t, dst_t=dst_t):
                return eng.tensor_tensor(dst_t[:, lo:lo + ww], src_t[:, lo:lo + ww],
                                         cm.bcs2[0:16, 0:ww], ALU.add)
            p.phase("vector", addk)
        cur = 1 - cur
    p.dma("gpsimd", [(S0_o, Sb[cur][:, :])])
    return p



def _r8(v):
    return np.ascontiguousarray(v.reshape(8, 128).T)


def _r16(v):
    return np.ascontiguousarray(v.reshape(16, 128).T)


def l0a_inputs(I):
    x2d = np.ascontiguousarray(I["L0_attn_x"][0].T)
    shards = shard_tokens(x2d)
    wmod = np.ascontiguousarray(I["L0_wmod"])
    cT8 = _r8(I["L0_c"][0])
    lng8 = _r8(I["L0_attn_ln_gamma"])
    lnb8 = _r8(I["L0_attn_ln_beta"])
    w_inT = np.ascontiguousarray(I["L0_attn_W_in"].T)
    cw = I["L0_attn_conv_w"].reshape(2048, 4)
    convw = np.ascontiguousarray(cw.reshape(16, 128, 4).transpose(1, 0, 2).reshape(128, 64))
    convb = _r16(I["L0_attn_conv_b"])
    BmT = np.ascontiguousarray(I["L0_attn_Bm"].T)
    AT = np.ascontiguousarray(I["L0_attn_A"].T)
    maps = []
    for r in range(NCORES):
        hm = np.zeros((128, H), np.float32) if r == 0 else np.ones((128, H), np.float32)
        maps.append(dict(xT=shards[r], hmask=hm, cT8=cT8, wmod=wmod, lng8=lng8,
                         lnb8=lnb8, w_inT=w_inT, convw=convw, convb=convb, BmT=BmT, AT=AT))
    return maps


def emit_l0scan(p, cm, name, Vap, Pk, out_sb):
    Sb = out_sb

    def initS(eng):
        return eng.tensor_copy(Sb[0][:, :], Vap)
    p.phase("vector", initS)
    cur = 0
    for k in range(11):
        d = 1 << k
        src_t, dst_t = Sb[cur], Sb[1 - cur]

        def cpre(eng, d=d, src_t=src_t, dst_t=dst_t):
            return eng.tensor_copy(dst_t[:, 0:d], src_t[:, 0:d])
        p.phase("vector", cpre)
        for (o, w) in chunks_of(C):
            lo = max(o, d)
            hi = o + w
            if lo >= hi:
                continue
            ww = hi - lo

            def mmk(eng, k=k, lo=lo, ww=ww, src_t=src_t, d=d):
                return eng.matmul(cm.mm_ps[0][0:16, 0:ww], Pk[:, 16 * k:16 * k + 16],
                                  src_t[:, lo - d:lo - d + ww], start=True, stop=True)
            p.phase("tensor", mmk)

            def cpk(eng, ww=ww):
                return eng.copy(cm.bcs2[0:16, 0:ww], cm.mm_ps[0][0:16, 0:ww])
            p.phase("scalar", cpk)

            def addk(eng, lo=lo, ww=ww, src_t=src_t, dst_t=dst_t):
                return eng.tensor_tensor(dst_t[:, lo:lo + ww], src_t[:, lo:lo + ww],
                                         cm.bcs2[0:16, 0:ww], ALU.add)
            p.phase("vector", addk)
        cur = 1 - cur
    return Sb[cur]


def emit_powers(p, cm, Pk, nsq, ps):
    ptmp = p.dram("ptmp_" + str(nsq), [16, 16])
    for k in range(nsq):
        src = Pk[:, 16 * k:16 * k + 16]
        dst = Pk[:, 16 * (k + 1):16 * (k + 1) + 16]
        p.dma("sync", [(ptmp[:, :], src)])
        p.dma("sync", [(cm.sq[0:16, 0:16], ptmp.rearrange("a b -> b a"))], allow_slow=True)

        def sq(eng, src=src):
            return eng.matmul(ps[:, 0:16], cm.sq[0:16, 0:16], src, start=True, stop=True)
        p.phase("tensor", sq)

        def ev(eng, dst=dst):
            return eng.copy(dst, ps[:, 0:16])
        p.phase("scalar", ev)


def build_l0b():
    p = Prog("l0b")
    xT = p.inp("xT", [1024, C])
    S0_i = p.inp("S0", [16, C])
    IordT = p.inp("IordT", [16, 8])
    AT = p.inp("AT", [16, 16])
    Cm_i = p.inp("Cm", [16, 2048])
    woT = p.inp("woT", [2048, 1024])
    cT8 = p.inp("cT8", [128, 8])
    wmod = p.inp("wmod", [1024, 6144])
    xc_i = p.inp("xc", [2048, C])
    z_i = p.inp("z", [2048, C])
    h_i = p.inp("hsp", [1024, TC])
    x1_o = p.out("x1out", [1024, C])

    cm = Common(p)
    x_t = [p.sbuf(f"x{k}", [128, C]) for k in range(8)]
    p.dma("sync", [(x_t[k][:, :], xT[k * 128:(k + 1) * 128, :]) for k in range(8)])
    modx = emit_mod_vectors(p, cm, cT8, wmod)

    Pk = p.sbuf("Pk", [16, 16 * 12], F32)
    pq_ps = p.psum("pq_ps", [16, 16])
    p.dma("sync", [(Pk[:, 0:16], AT)])
    emit_powers(p, cm, Pk, 11, pq_ps)
    P2048 = Pk[:, 16 * 11:16 * 12]
    Wp = p.sbuf("Wp", [16, 16 * 8], F32)
    iord = p.sbuf("iord", [16, 8], F32)
    p.dma("sync", [(Wp[:, 0:16], AT), (iord[:, :], IordT)])
    wtmp = p.dram("wtmp", [16, 16])
    for j in range(7):
        src = Wp[:, 16 * j:16 * j + 16]
        dst = Wp[:, 16 * (j + 1):16 * (j + 1) + 16]
        p.dma("sync", [(wtmp[:, :], src)])
        p.dma("sync", [(cm.sq[0:16, 0:16], wtmp.rearrange("a b -> b a"))], allow_slow=True)

        def mmw(eng, src=src):
            return eng.matmul(pq_ps[:, 0:16], cm.sq[0:16, 0:16], P2048, start=True, stop=True)
        p.phase("tensor", mmw)

        def evw(eng, dst=dst):
            return eng.copy(dst, pq_ps[:, 0:16])
        p.phase("scalar", evw)

    ha_ps = p.psum("ha_ps", [1, 16])
    harow = p.sbuf("harow", [1, 16], F32)

    def mmha(eng):
        last = None
        for j in range(8):
            last = eng.matmul(ha_ps[0:1, 0:16], iord[:, j:j + 1], Wp[:, 16 * j:16 * j + 16],
                              start=(j == 0), stop=(j == 7))
        return last
    p.phase("tensor", mmha)

    def evha(eng):
        return eng.copy(harow[0:1, 0:16], ha_ps[0:1, 0:16])
    p.phase("scalar", evha)

    hacol = p.dram("hacol", [16, 1])
    V2 = p.sbuf("V2", [16, C], F32)

    def clr(eng):
        return eng.memset(V2[:, :], 0.0)
    p.phase("vector", clr)
    hatmp = p.dram("hatmp", [1, 16])
    p.dma("sync", [(hatmp[:, :], harow[0:1, :])])
    p.dma("sync", [(V2[:, 0:1], hatmp.rearrange("a b -> b a"))], allow_slow=True)

    Sb = [p.sbuf("Sb0", [16, C]), p.sbuf("Sb1", [16, C])]
    cfin = emit_l0scan(p, cm, "corr", V2[:, :], Pk, Sb)

    for (o, w) in CC:
        p.dma("sync", [(cm.bcs1[0:16, 0:w], S0_i[:, o:o + w])])

        def sadd(eng, o=o, w=w):
            return eng.tensor_tensor(cfin[:, o:o + w], cfin[:, o:o + w], cm.bcs1[0:16, 0:w], ALU.add)
        p.phase("vector", sadd)

    Cms = p.sbuf("Cms", [16, 2048], F32)
    p.dma("sync", [(Cms[:, :], Cm_i)])
    xcs = p.sbuf("xcs", [128, 512], F32)
    zs = p.sbuf("zs", [128, 512], F32)
    yb = [p.sbuf(f"yb{m}", [128, C], BF16) for m in range(16)]
    for m in range(16):
        for (o, w) in CC:
            p.dma("sync", [(xcs[:, 0:w], xc_i[m * 128:(m + 1) * 128, o:o + w]),
                           (zs[:, 0:w], z_i[m * 128:(m + 1) * 128, o:o + w])])

            def sig(eng, o=o, w=w):
                return eng.activation(zs[:, 0:w], zs[:, 0:w], AF.Sigmoid)
            p.phase("scalar", sig)

            def gmm(eng, m=m, o=o, w=w):
                return eng.matmul(cm.mm_ps[0][:, 0:w], Cms[:, m * 128:(m + 1) * 128],
                                  cfin[:, o:o + w], start=True, stop=True)
            p.phase("tensor", gmm)

            def gev(eng, o=o, w=w):
                return eng.copy(cm.bcs1[:, 0:w], cm.mm_ps[0][:, 0:w])
            p.phase("scalar", gev)

            def gfin(eng, m=m, o=o, w=w):
                eng.tensor_tensor(cm.bcs1[:, 0:w], cm.bcs1[:, 0:w], xcs[:, 0:w], ALU.add)
                return eng.tensor_tensor(yb[m][:, o:o + w], cm.bcs1[:, 0:w], zs[:, 0:w], ALU.mult)
            p.phase("vector", gfin)

    wob = p.sbuf("wob", [128, 16 * 128], BF16)
    for m in range(8):
        p.dma("gpsimd", [(wob[:, k * 128:(k + 1) * 128],
                          woT[k * 128:(k + 1) * 128, m * 128:(m + 1) * 128]) for k in range(16)])
        for (o, w) in CC:
            p.dma("sync", [(xcs[:, 0:w], h_i[m * 128:(m + 1) * 128, H + o:H + o + w])])
            def rmm(eng, m=m, o=o, w=w):
                last = None
                for k in range(16):
                    last = eng.matmul(cm.mm_ps[1][:, 0:w], wob[:, k * 128:(k + 1) * 128],
                                      yb[k][:, o:o + w], start=(k == 0), stop=(k == 15))
                return last
            p.phase("tensor", rmm)

            def rev(eng, o=o, w=w):
                return eng.copy(cm.bcs2[:, 0:w], cm.mm_ps[1][:, 0:w])
            p.phase("scalar", rev)

            def rfin(eng, m=m, o=o, w=w):
                eng.tensor_tensor(cm.bcs2[:, 0:w], cm.bcs2[:, 0:w], xcs[:, 0:w], ALU.add)
                return eng.scalar_tensor_tensor(x_t[m][:, o:o + w], cm.bcs2[:, 0:w],
                                                modx[:, 16 + m:17 + m], x_t[m][:, o:o + w],
                                                ALU.mult, ALU.add)
            p.phase("vector", rfin)
        p.dma("gpsimd", [(x1_o[m * 128:(m + 1) * 128, :], x_t[m][:, :])])
    return p


def l0b_inputs(I, l0a_res):
    x2d = I["L0_attn_x"][0].T
    AT = np.ascontiguousarray(I["L0_attn_A"].T)
    cT8 = _r8(I["L0_c"][0])
    woT = np.ascontiguousarray(I["L0_attn_W_out"].T)
    Ivals = [l0a_res[q]["S0"][:, -1] for q in range(NCORES)]
    maps = []
    for r in range(NCORES):
        IordT = np.zeros((16, 8), np.float32)
        for j in range(8):
            q = r - 1 - j
            if q >= 0:
                IordT[:, j] = Ivals[q]
        maps.append(dict(xT=np.ascontiguousarray(x2d[:, r * C:(r + 1) * C]),
                         S0=l0a_res[r]["S0"], IordT=IordT, AT=AT,
                         Cm=I["L0_attn_Cm"], woT=woT, cT8=cT8, wmod=I["L0_wmod"],
                         xc=l0a_res[r]["xc"], z=l0a_res[r]["z"], hsp=l0a_res[r]["hdbg"]))
    return maps


GOLDEN_CENTER = np.float32(1.0 / np.e)
GOLDEN_LOWER = np.float32(0.5 - np.log(4.0 / 3.0))
GOLDEN_UPPER = np.float32(0.5)


def build_l0c():
    p = Prog("l0c")
    x1T = p.inp("x1T", [1024, C])
    cT8 = p.inp("cT8", [128, 8])
    wmod = p.inp("wmod", [1024, 6144])
    Wr_i = p.inp("Wr", [1024, 8])
    temp_i = p.inp("temp", [1, 1])
    Wg_i = p.inp("Wg", [8, 1024, 1024])
    Wu_i = p.inp("Wu", [8, 1024, 1024])
    Wd_i = p.inp("Wd", [8, 1024, 1024])
    eye8_i = p.inp("eye8", [8, 8])
    eye128_i = p.inp("eye128", [128, 128])
    cT8b = p.inp("cT8b", [128, 8])
    wmodb = p.inp("wmodb", [1024, 6144])
    x2_o = p.out("x2out", [1024, C])
    h3_o = p.out("h3out", [1024, C])

    cm = Common(p)
    wrs = p.sbuf("wrs", [128, 8 * 8], F32)
    e8 = p.sbuf("e8", [8, 8], F32)
    e128 = p.sbuf("e128", [128, 128], F32)
    tmp11 = p.sbuf("tmp11", [1, 1], F32)
    invt8 = p.sbuf("invt8", [8, 1], F32)
    inh = p.sbuf("inh", [8, C], F32)
    wfm = p.sbuf("wfm", [8, C], F32)
    ones8 = p.sbuf("ones8", [128, 8], F32)
    rw = p.sbuf("rw", [128, 8 * 6], F32)
    rc = p.sbuf("rc", [128, 8], F32)
    h2b = [p.sbuf(f"h2b{k}", [128, C], BF16) for k in range(8)]
    web = p.sbuf("web", [128, 8 * 128], BF16)
    gst = p.sbuf("gst", [128, C], F32)
    wbce = p.sbuf("wbce", [128, C], BF16)
    rt_ps = p.psum("rt_ps", [8, 512])
    tt_ps = p.psum("tt_ps", [128, 8])
    bt_ps = p.psum("bt_ps", [8, 128])
    modx = emit_mod_vectors(p, cm, cT8, wmod)
    x_t = [p.sbuf(f"x{k}", [128, C]) for k in range(8)]
    p.dma("sync", [(x_t[k][:, :], x1T[k * 128:(k + 1) * 128, :]) for k in range(8)])
    emit_ln(p, cm, "ln4", x_t, x_t, C,
            gamma=[modx[:, 56 + k:57 + k] for k in range(8)],
            beta=[modx[:, 24 + k:25 + k] for k in range(8)])
    h2f = x_t
    p.dma("sync", [(wrs[:, k * 8:(k + 1) * 8], Wr_i[k * 128:(k + 1) * 128, :]) for k in range(8)]
          + [(e8[:, :], eye8_i), (e128[:, :], eye128_i), (tmp11[:, :], temp_i)])

    def invt(eng):
        eng.reciprocal(tmp11[0:1, 0:1], tmp11[0:1, 0:1])
        eng.memset(ones8[:, :], 0.0)
        return eng.tensor_scalar_add(ones8[:, :], ones8[:, :], 1.0)
    p.phase("vector", invt)
    p.dma("sync", [(invt8[j:j + 1, 0:1], tmp11[0:1, 0:1]) for j in range(8)])

    def negt(eng):
        return eng.tensor_scalar_mul(invt8[:, 0:1], invt8[:, 0:1], -1.0)
    p.phase("vector", negt)

    for (o, w) in CC:
        def rmm(eng, o=o, w=w):
            last = None
            for k in range(8):
                last = eng.matmul(rt_ps[:, 0:w], wrs[:, k * 8:(k + 1) * 8],
                                  h2f[k][:, o:o + w], start=(k == 0), stop=(k == 7))
            return last
        p.phase("tensor", rmm)

        def rev(eng, o=o, w=w):
            return eng.copy(inh[:, o:o + w], rt_ps[:, 0:w])
        p.phase("scalar", rev)

    def sig1(eng):
        return eng.activation(inh[:, :], inh[:, :], AF.Exp, scale=invt8[:, 0:1])
    p.phase("scalar", sig1)

    def sig2(eng):
        eng.tensor_scalar_add(inh[:, :], inh[:, :], 1.0)
        return eng.reciprocal(inh[:, :], inh[:, :])
    p.phase("vector", sig2)

    for tt in range(C // 128):
        def ttr(eng, tt=tt):
            return eng.matmul(tt_ps[:, 0:8], inh[:, tt * 128:(tt + 1) * 128], e8[:, :],
                              start=True, stop=True)
        p.phase("tensor", ttr)

        def tte(eng):
            return eng.copy(rw[:, 0:8], tt_ps[:, 0:8])
        p.phase("scalar", tte)

        def rmath(eng):
            w0 = rw[:, 0:8]; dist = rw[:, 8:16]; wz = rw[:, 16:24]
            fb = rw[:, 24:32]; t1 = rw[:, 32:40]; t2 = rw[:, 40:48]
            eng.tensor_scalar_sub(dist, w0, float(GOLDEN_CENTER))
            eng.tensor_scalar_mul(t1, dist, -1.0)
            eng.tensor_tensor(dist, dist, t1, ALU.max)
            eng.tensor_scalar(t1, w0, float(GOLDEN_LOWER), None, ALU.is_ge)
            eng.tensor_scalar(t2, w0, float(GOLDEN_UPPER), None, ALU.is_le)
            return eng.tensor_tensor(t1, t1, t2, ALU.mult)
        p.phase("vector", rmath)

        def rexp(eng):
            dist = rw[:, 8:16]; wz = rw[:, 16:24]; fb = rw[:, 24:32]
            eng.activation(wz, dist, AF.Exp, scale=-10.0)
            return eng.activation(fb, dist, AF.Exp, scale=np.float32(-1.0 / 0.3))
        p.phase("scalar", rexp)

        def rmath2(eng):
            w0 = rw[:, 0:8]; dist = rw[:, 8:16]; wz = rw[:, 16:24]
            fb = rw[:, 24:32]; t1 = rw[:, 32:40]; t2 = rw[:, 40:48]
            m1 = rc[:, 0:1]; ne = rc[:, 1:2]; m2 = rc[:, 2:3]; sm = rc[:, 3:4]
            eng.tensor_tensor(wz, wz, t1, ALU.mult)
            eng.tensor_reduce(ne.rearrange("p (a b) -> p a b", b=1), wz.rearrange("p (a b) -> p a b", b=8), AX.X, ALU.add)
            eng.tensor_scalar(ne, ne, 1e-8, None, ALU.is_lt)
            eng.tensor_reduce(m1.rearrange("p (a b) -> p a b", b=1), fb.rearrange("p (a b) -> p a b", b=8), AX.X, ALU.max)
            eng.tensor_scalar(t1, fb, m1[:, 0:1], None, ALU.is_ge)
            eng.tensor_tensor_scan(t2, ones8[:, 0:8], t1, 0.0, ALU.mult, ALU.add)
            eng.tensor_tensor(t2, t2, t1, ALU.subtract)
            eng.tensor_scalar(t2, t2, 0.0, None, ALU.is_equal)
            eng.tensor_tensor(t1, t1, t2, ALU.mult)
            eng.scalar_tensor_tensor(t2, t1, -1e30, fb, ALU.mult, ALU.add)
            eng.tensor_reduce(m2.rearrange("p (a b) -> p a b", b=1), t2.rearrange("p (a b) -> p a b", b=8), AX.X, ALU.max)
            eng.tensor_scalar(t2, t2, m2[:, 0:1], None, ALU.is_ge)
            eng.tensor_tensor_scan(rw[:, 0:8], ones8[:, 0:8], t2, 0.0, ALU.mult, ALU.add)
            eng.tensor_tensor(rw[:, 0:8], rw[:, 0:8], t2, ALU.subtract)
            eng.tensor_scalar(rw[:, 0:8], rw[:, 0:8], 0.0, None, ALU.is_equal)
            eng.tensor_tensor(t2, t2, rw[:, 0:8], ALU.mult)
            eng.tensor_tensor(t1, t1, t2, ALU.add)
            eng.tensor_tensor(fb, fb, t1, ALU.mult)
            eng.tensor_reduce(sm.rearrange("p (a b) -> p a b", b=1), fb.rearrange("p (a b) -> p a b", b=8), AX.X, ALU.add)
            eng.tensor_scalar_max(sm, sm, 1e-8)
            eng.reciprocal(sm, sm)
            eng.tensor_scalar(fb, fb, sm[:, 0:1], None, ALU.mult)
            eng.tensor_tensor(t2, fb, wz, ALU.subtract)
            eng.tensor_scalar(t2, t2, ne[:, 0:1], None, ALU.mult)
            eng.tensor_tensor(wz, wz, t2, ALU.add)
            eng.tensor_reduce(sm.rearrange("p (a b) -> p a b", b=1), wz.rearrange("p (a b) -> p a b", b=8), AX.X, ALU.add)
            eng.tensor_scalar_max(sm, sm, 1e-8)
            eng.reciprocal(sm, sm)
            return eng.tensor_scalar(wz, wz, sm[:, 0:1], None, ALU.mult)
        p.phase("vector", rmath2)

        def btr(eng, tt=tt):
            return eng.matmul(bt_ps[:, 0:128], rw[:, 16:24], e128[:, :], start=True, stop=True)
        p.phase("tensor", btr)

        def bte(eng, tt=tt):
            return eng.copy(wfm[:, tt * 128:(tt + 1) * 128], bt_ps[:, 0:128])
        p.phase("scalar", bte)

    for k in range(8):
        def cst(eng, k=k):
            return eng.copy(h2b[k][:, :], h2f[k][:, :])
        p.phase("scalar", cst)

    p.free(*x_t)
    moe = [p.sbuf(f"moe{m}", [128, C], F32) for m in range(8)]
    hpb = [p.sbuf(f"hpb{m}", [128, C], BF16) for m in range(8)]
    wrow = cm.tp
    for e in range(8):
        p.dma("sync", [(wrow[0:1, 0:C], wfm[e:e + 1, :])])
        for (o, w) in CC:
            def wbm(eng, o=o, w=w):
                return eng.matmul(cm.bc_ps[:, 0:w], cm.ones_row[0:1, :], wrow[0:1, o:o + w],
                                  start=True, stop=True)
            p.phase("tensor", wbm)

            def wbe(eng, o=o, w=w):
                return eng.copy(wbce[:, o:o + w], cm.bc_ps[:, 0:w])
            p.phase("scalar", wbe)

        for m in range(8):
            p.dma("gpsimd", [(web[:, k * 128:(k + 1) * 128],
                              Wg_i[e, k * 128:(k + 1) * 128, m * 128:(m + 1) * 128]) for k in range(8)])
            for (o, w) in CC:
                def gmm(eng, o=o, w=w):
                    last = None
                    for k in range(8):
                        last = eng.matmul(cm.mm_ps[0][:, 0:w], web[:, k * 128:(k + 1) * 128],
                                          h2b[k][:, o:o + w], start=(k == 0), stop=(k == 7))
                    return last
                p.phase("tensor", gmm)

                def gev(eng, o=o, w=w):
                    return eng.copy(gst[:, o:o + w], cm.mm_ps[0][:, 0:w])
                p.phase("scalar", gev)

            def gsil(eng):
                return eng.activation(gst[:, :], gst[:, :], AF.Silu)
            p.phase("scalar", gsil)

            p.dma("gpsimd", [(web[:, k * 128:(k + 1) * 128],
                              Wu_i[e, k * 128:(k + 1) * 128, m * 128:(m + 1) * 128]) for k in range(8)])
            for (o, w) in CC:
                def umm(eng, o=o, w=w):
                    last = None
                    for k in range(8):
                        last = eng.matmul(cm.mm_ps[1][:, 0:w], web[:, k * 128:(k + 1) * 128],
                                          h2b[k][:, o:o + w], start=(k == 0), stop=(k == 7))
                    return last
                p.phase("tensor", umm)

                def uev(eng, o=o, w=w):
                    return eng.copy(cm.bcs1[:, 0:w], cm.mm_ps[1][:, 0:w])
                p.phase("scalar", uev)

                def ufin(eng, m=m, o=o, w=w):
                    eng.tensor_tensor(cm.bcs1[:, 0:w], cm.bcs1[:, 0:w], gst[:, o:o + w], ALU.mult)
                    return eng.tensor_tensor(hpb[m][:, o:o + w], cm.bcs1[:, 0:w], wbce[:, o:o + w], ALU.mult)
                p.phase("vector", ufin)

        for md in range(8):
            p.dma("gpsimd", [(web[:, k * 128:(k + 1) * 128],
                              Wd_i[e, k * 128:(k + 1) * 128, md * 128:(md + 1) * 128]) for k in range(8)])
            for (o, w) in CC:
                def dmm(eng, o=o, w=w):
                    last = None
                    for k in range(8):
                        last = eng.matmul(cm.mm_ps[0][:, 0:w], web[:, k * 128:(k + 1) * 128],
                                          hpb[k][:, o:o + w], start=(k == 0), stop=(k == 7))
                    return last
                p.phase("tensor", dmm)

                def dev(eng, o=o, w=w):
                    return eng.copy(cm.bcs1[:, 0:w], cm.mm_ps[0][:, 0:w])
                p.phase("scalar", dev)

                if e == 0:
                    def dacc(eng, md=md, o=o, w=w):
                        return eng.tensor_copy(moe[md][:, o:o + w], cm.bcs1[:, 0:w])
                else:
                    def dacc(eng, md=md, o=o, w=w):
                        return eng.tensor_tensor(moe[md][:, o:o + w], moe[md][:, o:o + w],
                                                 cm.bcs1[:, 0:w], ALU.add)
                p.phase("vector", dacc)

    for m in range(8):
        p.dma("sync", [(gst[:, :], x1T[m * 128:(m + 1) * 128, :])])

        def fin(eng, m=m):
            return eng.scalar_tensor_tensor(moe[m][:, :], moe[m][:, :], modx[:, 40 + m:41 + m],
                                            gst[:, :], ALU.mult, ALU.add)
        p.phase("vector", fin)
        p.dma("gpsimd", [(x2_o[m * 128:(m + 1) * 128, :], moe[m][:, :])])
    p.free(*hpb)
    modx2 = emit_mod_vectors(p, cm, cT8b, wmodb, tag="b")
    emit_ln(p, cm, "ln5", moe, moe, C,
            gamma=[modx2[:, 48 + k:49 + k] for k in range(8)],
            beta=[modx2[:, 0 + k:1 + k] for k in range(8)])
    p.dma("gpsimd", [(h3_o[m * 128:(m + 1) * 128, :], moe[m][:, :]) for m in range(8)])
    return p


def l0c_inputs(I, x1_shards):
    cT8 = _r8(I["L0_c"][0])
    maps = []
    eye8 = np.eye(8, dtype=np.float32)
    eye128 = np.eye(128, dtype=np.float32)
    for r in range(NCORES):
        maps.append(dict(x1T=x1_shards[r], cT8=cT8, wmod=I["L0_wmod"], Wr=I["L0_ffn_Wr"],
                         temp=I["L0_ffn_temperature"].reshape(1, 1), Wg=I["L0_ffn_Wg"],
                         Wu=I["L0_ffn_Wu"], Wd=I["L0_ffn_Wd"], eye8=eye8, eye128=eye128,
                         cT8b=_r8(I["L1_c"][0]), wmodb=I["L1_wmod"]))
    return maps




def emit_bcast_pe(p, cm, srow, dst, W):
    banks = [cm.bc_ps, cm.bc2_ps, cm.mm_ps[0], cm.mm_ps[1]]
    cks = chunks_of(W)
    for g in range(0, len(cks), 4):
        grp = cks[g:g + 4]

        def mmb(eng, grp=grp):
            last = None
            for bi, (o, w) in enumerate(grp):
                last = eng.matmul(banks[bi][:, 0:w], cm.ones_row[0:1, :], srow[0:1, o:o + w],
                                  start=True, stop=True)
            return last
        p.phase("tensor", mmb)

        def evb(eng, grp=grp):
            last = None
            for bi, (o, w) in enumerate(grp):
                last = eng.copy(dst[:, o:o + w], banks[bi][:, 0:w])
            return last
        p.phase("scalar", evb)


def emit_bcast_row(p, srow, dst, width=None):
    W = width or dst.shape[1]
    p.dma("sync", [(dst[0:1, 0:W], srow[0:1, 0:W])])
    k = 1
    while k < 128:
        kk = min(k, 128 - k)
        p.dma("sync", [(dst[k:k + kk, 0:W], dst[0:kk, 0:W])])
        k += kk



def build_l1s(kind):
    p = Prog("l1s_" + kind)
    cm = Common(p)
    ios = {}
    for D in ("f", "b"):
        if kind != "first":
            ios["ypc_" + D] = p.inp("ypc_" + D, [2048, TC], BF16)
            ios["szc_" + D] = p.inp("szc_" + D, [2048, TC], BF16)
            ios["ctc_" + D] = p.inp("ctc_" + D, [2048, TC])
            ios["crc_" + D] = p.inp("crc_" + D, [16, TC])
            ios["IoO_" + D] = p.inp("IoO_" + D, [1024, 256])
            ios["EoO_" + D] = p.inp("EoO_" + D, [1024, 256])
            ios["IoP_" + D] = p.inp("IoP_" + D, [1024, 256])
            ios["EoP_" + D] = p.inp("EoP_" + D, [1024, 256])
            ios["nAP_" + D] = p.inp("nAP_" + D, [128, 256])
            ios["owTP_" + D] = p.inp("owTP_" + D, [2048, 1024])
        ios["rT_" + D] = p.inp("rT_" + D, [1024, TC])
        if D == "f":
            ios["hmask"] = p.inp("hmask", [128, H])
        if kind != "last":
            ios["nwb_" + D] = p.inp("nwb_" + D, [128, 16])
            ios["inWT_" + D] = p.inp("inWT_" + D, [1024, 4096])
            ios["cwb_" + D] = p.inp("cwb_" + D, [128, 64])
            ios["cbb_" + D] = p.inp("cbb_" + D, [128, 16])
            ios["xpWT_" + D] = p.inp("xpWT_" + D, [2048, 96])
            ios["dtWT_" + D] = p.inp("dtWT_" + D, [64, 2048])
            ios["dtbb_" + D] = p.inp("dtbb_" + D, [128, 16])
            ios["nA_" + D] = p.inp("nA_" + D, [128, 256])
            ios["Dpb_" + D] = p.inp("Dpb_" + D, [128, 16])
            ios["ypost_" + D] = p.out("ypost_" + D, [2048, C], BF16)
            ios["szo_" + D] = p.out("szo_" + D, [2048, C], BF16)
            ios["cto_" + D] = p.out("cto_" + D, [2048, C])
            ios["cro_" + D] = p.out("cro_" + D, [16, C])
            ios["Io_" + D] = p.out("Io_" + D, [128, 256])
            ios["Eo_" + D] = p.out("Eo_" + D, [128, 256])
            ios["ro_" + D] = p.out("ro_" + D, [1024, TC])
            ios["usp_" + D] = p.dram("usp_" + D, [2048, C], BF16)
        else:
            ios["nfb_" + D] = p.inp("nfb_" + D, [128, 16])
            ios["fh2_" + D] = p.out("fh2_" + D, [1024, C])
        if kind != "first":
            ios["yfd_" + D] = p.dram("yfd_" + D, [2048, TC], BF16)

    web = p.sbuf("web", [128, 16 * 128], BF16)
    upre = p.sbuf("upre", [128, TC], F32)
    cstA = p.sbuf("cstA", [128, C], F32)
    cstB = p.sbuf("cstB", [128, C], F32)
    cstO = p.sbuf("cstO", [128, TC], F32)
    bf1 = p.sbuf("bf1", [128, TC], BF16)
    bf2 = p.sbuf("bf2", [128, TC], BF16)
    browb = p.sbuf("browb", [1, TC], F32)
    nAt, nAPt, prm, cwt, Ist, Est, cte, Hown, Hprv = {}, {}, {}, {}, {}, {}, {}, {}, {}
    for D in ("f", "b"):
        nAt[D] = p.sbuf("nAt_" + D, [128, 256], F32)
        prm[D] = p.sbuf("prm_" + D, [128, 80], F32)
        cwt[D] = p.sbuf("cwt_" + D, [128, 64], F32)
        Ist[D] = p.sbuf("Ist_" + D, [128, 256], F32)
        Est[D] = p.sbuf("Est_" + D, [128, 256], F32)
        cte[D] = p.sbuf("cte_" + D, [128, 16], F32)
        if kind != "first":
            nAPt[D] = p.sbuf("nAPt_" + D, [128, 256], F32)
            Hown[D] = p.sbuf("Hown_" + D, [128, 256], F32)
            Hprv[D] = p.sbuf("Hprv_" + D, [128, 256], F32)
    IEld = p.sbuf("IEld", [128, 256], F32)
    hmk = p.sbuf("hmk", [128, H], F32)
    brow = cm.tp
    p.dma("sync", [(hmk[:, :], ios["hmask"])])

    for D in ("f", "b"):
        if kind != "last":
            p.dma("sync", [(prm[D][:, 0:16], ios["nwb_" + D]), (prm[D][:, 16:32], ios["cbb_" + D]),
                           (prm[D][:, 32:48], ios["dtbb_" + D]), (prm[D][:, 48:64], ios["Dpb_" + D]),
                           (cwt[D][:, :], ios["cwb_" + D]), (nAt[D][:, :], ios["nA_" + D])])
        else:
            p.dma("sync", [(prm[D][:, 0:16], ios["nfb_" + D])])
        if kind != "first":
            p.dma("sync", [(nAPt[D][:, :], ios["nAP_" + D])])
    for D in ("f", "b"):
        tensors = [nAt[D]] if kind != "last" else []
        if kind != "first":
            tensors.append(nAPt[D])
        for tt_ in tensors:
            def nexp(eng, tt_=tt_):
                return eng.activation(tt_[:, :], tt_[:, :], AF.Exp)
            p.phase("scalar", nexp)

            def nneg(eng, tt_=tt_):
                return eng.tensor_scalar_mul(tt_[:, :], tt_[:, :], -1.0)
            p.phase("vector", nneg)

    for D in ("f", "b"):
        if kind != "first":
            for (Ht, Isrc, Esrc) in ((Hown[D], "IoO_", "EoO_"), (Hprv[D], "IoP_", "EoP_")):
                def hz(eng, Ht=Ht):
                    return eng.memset(Ht[:, :], 0.0)
                p.phase("vector", hz)
                for j in range(8):
                    p.dma("sync", [(IEld[:, :], ios[Esrc + D][j * 128:(j + 1) * 128, :])])

                    def hc1(eng, Ht=Ht):
                        return eng.tensor_tensor(Ht[:, :], Ht[:, :], IEld[:, :], ALU.mult)
                    p.phase("vector", hc1)
                    p.dma("sync", [(IEld[:, :], ios[Isrc + D][j * 128:(j + 1) * 128, :])])

                    def hc2(eng, Ht=Ht):
                        return eng.tensor_tensor(Ht[:, :], Ht[:, :], IEld[:, :], ALU.add)
                    p.phase("vector", hc2)

            Cbc = p.sbuf("CbcP" + D, [128, TC], BF16, side="right")
            for quarter in range(4):
                ctq = [p.sbuf(f"ctq{D}{quarter}{i}", [128, TC], F32, side="right") for i in range(4)]
                qq = [p.sbuf(f"qq{D}{quarter}{i}", [128, TC], F32, side="right") for i in range(4)]
                dgs = [quarter * 4 + i for i in range(4)]
                p.dma("sync", [(ctq[i][:, :], ios["ctc_" + D][dg * 128:(dg + 1) * 128, :])
                               for i, dg in enumerate(dgs)])

                def qz(eng, qq=qq):
                    for i in range(4):
                        eng.memset(qq[i][:, :], 0.0)
                    return None
                p.phase("vector", qz)
                for s in range(16):
                    p.dma("sync", [(brow[0:1, 0:TC], ios["crc_" + D][s:s + 1, :])])
                    emit_bcast_pe(p, cm, brow, Cbc, TC)
                    for i, dg in enumerate(dgs):
                        col = dg * 16 + s

                        def corrA(eng, ct_=ctq[i], nap_=nAPt[D], col=col):
                            return eng.activation(cstO[:, :], ct_[:, :], AF.Exp,
                                                  scale=nap_[:, col:col + 1])
                        p.phase("scalar", corrA)

                        def corrB(eng, q_=qq[i], hp_=Hprv[D], ho_=Hown[D], cb_=Cbc, col=col):
                            eng.scalar_tensor_tensor(cstO[:, 0:H], cstO[:, 0:H], hp_[:, col:col + 1],
                                                     cb_[:, 0:H], ALU.mult, ALU.mult)
                            eng.scalar_tensor_tensor(cstO[:, H:TC], cstO[:, H:TC], ho_[:, col:col + 1],
                                                     cb_[:, H:TC], ALU.mult, ALU.mult)
                            return eng.tensor_tensor(q_[:, :], q_[:, :], cstO[:, :], ALU.add)
                        p.phase("vector", corrB)
                for i, dg in enumerate(dgs):
                    p.dma("sync", [(bf1[:, :], ios["ypc_" + D][dg * 128:(dg + 1) * 128, :]),
                                   (bf2[:, :], ios["szc_" + D][dg * 128:(dg + 1) * 128, :])])

                    def yf(eng, q_=qq[i]):
                        eng.tensor_tensor(q_[:, :], q_[:, :], bf2[:, :], ALU.mult)
                        eng.tensor_copy(cstO[:, :], bf1[:, :])
                        eng.tensor_tensor(cstO[:, :], cstO[:, :], q_[:, :], ALU.add)
                        return eng.tensor_copy(bf1[:, :], cstO[:, :])
                    p.phase("vector", yf)
                    p.dma("gpsimd", [(ios["yfd_" + D][dg * 128:(dg + 1) * 128, :], bf1[:, :])])
                p.free(*ctq, *qq)
            p.free(Cbc)

        r_t = [p.sbuf(f"r{D}{m}", [128, TC], F32, side="right") for m in range(8)]
        if kind == "first":
            p.dma("sync", [(r_t[m][:, :], ios["rT_" + D][m * 128:(m + 1) * 128, :]) for m in range(8)])
        else:
            ow_wb = p.sbuf("owwb" + D, [128, 64 * 128], BF16, side="right")
            ystr = p.sbuf("ystr" + D, [128, 16 * 512], BF16, side="right")
            p.dma("sync", [(r_t[m][:, :], ios["rT_" + D][m * 128:(m + 1) * 128, :])
                           for m in range(8)])
            for mg in range(2):
                p.dma("gpsimd", [(ow_wb[:, (m * 16 + k) * 128:(m * 16 + k + 1) * 128],
                                  ios["owTP_" + D][k * 128:(k + 1) * 128,
                                                   (mg * 4 + m) * 128:(mg * 4 + m + 1) * 128])
                                 for m in range(4) for k in range(16)])
                for ci, (o, w) in enumerate(TCC):
                    p.dma("sync", [(ystr[:, k * 512:k * 512 + w],
                                    ios["yfd_" + D][k * 128:(k + 1) * 128, o:o + w])
                                   for k in range(16)])
                    for m in range(4):
                        def omm(eng, m=m, o=o, w=w):
                            last = None
                            for k in range(16):
                                last = eng.matmul(cm.mm_ps[m % 2][:, 0:w],
                                                  ow_wb[:, (m * 16 + k) * 128:(m * 16 + k + 1) * 128],
                                                  ystr[:, k * 512:k * 512 + w],
                                                  start=(k == 0), stop=(k == 15))
                            return last
                        p.phase("tensor", omm)

                        def oev(eng, m=m, mg=mg, o=o, w=w):
                            return eng.copy(cstO[:, o:o + w], cm.mm_ps[m % 2][:, 0:w])
                        p.phase("scalar", oev)

                        def radd(eng, rt_=r_t[mg * 4 + m], o=o, w=w):
                            return eng.tensor_tensor(rt_[:, o:o + w], rt_[:, o:o + w],
                                                     cstO[:, o:o + w], ALU.add)
                        p.phase("vector", radd)
            p.free(ow_wb, ystr)


        if kind == "last":
            emit_ln(p, cm, "fin" + D, r_t, r_t, TC,
                    gamma=[prm[D][:, k:k + 1] for k in range(8)],
                    beta=[prm[D][:, 8 + k:9 + k] for k in range(8)])
            p.dma("gpsimd", [(ios["fh2_" + D][m * 128:(m + 1) * 128, :], r_t[m][:, H:TC])
                             for m in range(8)])
            p.free(*r_t)
            continue

        p.dma("gpsimd", [(ios["ro_" + D][m * 128:(m + 1) * 128, :], r_t[m][:, :]) for m in range(8)])

        hnb = [p.sbuf(f"hnb{D}{m}", [128, TC], BF16, side="right") for m in range(8)]
        emit_ln(p, cm, "hn" + D, r_t, hnb, TC,
                gamma=[prm[D][:, k:k + 1] for k in range(8)],
                beta=[prm[D][:, 8 + k:9 + k] for k in range(8)])

        def wload(mm_, lag=0):
            hb = (mm_ % 2) * 1024
            p.dma("gpsimd", [(web[:, hb + k * 128:hb + (k + 1) * 128],
                              ios["inWT_" + D][k * 128:(k + 1) * 128, mm_ * 128:(mm_ + 1) * 128])
                             for k in range(8)], lag=lag)
        wload(0)
        for m in range(32):
            for ci, (o, w) in enumerate(TCC):
                def mmx(eng, hnb_=hnb, m=m, o=o, w=w, ci=ci):
                    hb = (m % 2) * 1024
                    last = None
                    for k in range(8):
                        last = eng.matmul(cm.mm_ps[ci % 2][:, 0:w], web[:, hb + k * 128:hb + (k + 1) * 128],
                                          hnb_[k][:, o:o + w], start=(k == 0), stop=(k == 7))
                    return last
                p.phase("tensor", mmx, lag=_L(1) if ci > 0 else 0)

                def evx(eng, o=o, w=w, ci=ci):
                    return eng.copy(upre[:, o:o + w], cm.mm_ps[ci % 2][:, 0:w])
                p.phase("scalar", evx)
            if m + 1 < 32:
                wload(m + 1)
            if m < 16:
                def hm0(eng):
                    return eng.tensor_tensor(upre[:, 0:H], upre[:, 0:H], hmk[:, :], ALU.mult)
                p.phase("vector", hm0)

                def conv2(eng, m=m, cw_=cwt[D]):
                    eng.tensor_scalar_mul(cstA[:, 0:C], upre[:, 0:C], cw_[:, 4 * m:4 * m + 1])
                    for j in range(1, 4):
                        eng.scalar_tensor_tensor(cstA[:, 0:C], upre[:, j:j + C],
                                                 cw_[:, 4 * m + j:4 * m + j + 1],
                                                 cstA[:, 0:C], ALU.mult, ALU.add)
                    return None
                p.phase("vector", conv2)

                def sil(eng, m=m, pr_=prm[D]):
                    return eng.activation(bf1[:, 0:C], cstA[:, 0:C], AF.Silu, bias=pr_[:, 16 + m:17 + m])
                p.phase("scalar", sil)
                p.dma("gpsimd", [(ios["usp_" + D][m * 128:(m + 1) * 128, :], bf1[:, 0:C])])

            else:
                def silz(eng, m=m):
                    return eng.activation(bf1[:, 0:C], upre[:, H:TC], AF.Silu)
                p.phase("scalar", silz)
                p.dma("gpsimd", [(ios["szo_" + D][(m - 16) * 128:(m - 15) * 128, :], bf1[:, 0:C])])
        p.free(*hnb)
        p.free(*r_t)

        ones_c = p.sbuf("ones_c" + D, [128, C], F32, side="right")
        BCst = p.sbuf("BCst" + D, [32, TC], F32, side="right")
        dtrB = p.sbuf("dtrB" + D, [64, TC], BF16, side="right")

        def initones(eng, ones_c=ones_c):
            eng.memset(ones_c[:, :], 0.0)
            return eng.tensor_scalar_add(ones_c[:, :], ones_c[:, :], 1.0)
        p.phase("vector", initones)

        ustr = p.sbuf("ustr" + D, [128, 16 * 512], BF16, side="right")
        p.dma("gpsimd", [(web[:, k * 96:(k + 1) * 96], ios["xpWT_" + D][k * 128:(k + 1) * 128, :])
                         for k in range(16)])
        for ci, (o, w) in enumerate(CC):
            p.dma("sync", [(ustr[:, k * 512:k * 512 + w],
                            ios["usp_" + D][k * 128:(k + 1) * 128, o:o + w]) for k in range(16)])

            def xmm2(eng, o=o, w=w, ustr=ustr):
                last = None
                for k in range(16):
                    last = eng.matmul(cm.mm_ps[0][0:96, 0:w], web[:, k * 96:(k + 1) * 96],
                                      ustr[:, k * 512:k * 512 + w], start=(k == 0), stop=(k == 15))
                return last
            p.phase("tensor", xmm2)

            def xev(eng, o=o, w=w, dtrB=dtrB, BCst=BCst):
                eng.copy(dtrB[0:64, H + o:H + o + w], cm.mm_ps[0][0:64, 0:w])
                return eng.copy(BCst[0:32, H + o:H + o + w], cm.mm_ps[0][64:96, 0:w])
            p.phase("scalar", xev)
        p.free(ustr)
        p.dma("gpsimd", [(ios["cro_" + D], BCst[16:32, H:TC])])

        for half in range(4):
            dgs = [half * 4 + i for i in range(4)]
            dtb16 = [p.sbuf(f"dt{D}{half}{i}", [128, C], BF16, side="right") for i in range(4)]
            dtu = [p.sbuf(f"du{D}{half}{i}", [128, C], BF16, side="right") for i in range(4)]
            yacc = [p.sbuf(f"ya{D}{half}{i}", [128, C], F32, side="right") for i in range(4)]
            Bbc2 = [p.sbuf("Bbc" + D + str(half) + str(j), [128, C], BF16, side="right") for j in range(2)]
            Cbc2 = [p.sbuf("Cbc" + D + str(half) + str(j), [128, C], BF16, side="right") for j in range(2)]
            cstA2 = [cstA, upre]
            browB2 = [cm.mu, cm.rs]
            browC2 = [cm.tp, browb]

            for i, dg in enumerate(dgs):
                p.dma("gpsimd", [(web[0:64, 0:128], ios["dtWT_" + D][0:64, dg * 128:(dg + 1) * 128])])
                for (o, w) in CC:
                    def dmm(eng, o=o, w=w, dtrB=dtrB):
                        return eng.matmul(cm.mm_ps[0][:, 0:w], web[0:64, 0:128],
                                          dtrB[0:64, H + o:H + o + w], start=True, stop=True)
                    p.phase("tensor", dmm)

                    def dev(eng, dg=dg, o=o, w=w):
                        return eng.copy(cstA[:, o:o + w], cm.mm_ps[0][:, 0:w])
                    p.phase("scalar", dev)

                def dbias(eng, dg=dg, pr_=prm[D]):
                    return eng.tensor_scalar_add(cstA[:, 0:C], cstA[:, 0:C], pr_[:, 32 + dg:33 + dg])
                p.phase("vector", dbias)

                def sp1(eng, _x=None):
                    eng.tensor_scalar_mul(cstB[:, 0:C], cstA[:, 0:C], -1.0)
                    eng.tensor_tensor(cstB[:, 0:C], cstB[:, 0:C], cstA[:, 0:C], ALU.min)
                    return eng.tensor_relu(cstO[:, 0:C], cstA[:, 0:C])
                p.phase("vector", sp1)

                def sp2(eng):
                    eng.activation(cstB[:, 0:C], cstB[:, 0:C], AF.Exp)
                    return eng.activation(cstB[:, 0:C], cstB[:, 0:C], AF.Ln, bias=1.0)
                p.phase("scalar", sp2)

                def sp3(eng, dt_=dtb16[i], ones_c=ones_c):
                    eng.tensor_tensor(cstA[:, 0:C], cstB[:, 0:C], cstO[:, 0:C], ALU.add)
                    eng.tensor_copy(dt_[:, :], cstA[:, 0:C])
                    return eng.tensor_tensor_scan(cstB[:, 0:C], ones_c[:, 0:C], cstA[:, 0:C],
                                                  0.0, ALU.mult, ALU.add)
                p.phase("vector", sp3)
                p.dma("gpsimd", [(ios["cto_" + D][dg * 128:(dg + 1) * 128, :], cstB[:, 0:C])])

                def cte1(eng, dg=dg, ct_=cte[D]):
                    return eng.tensor_copy(ct_[:, dg:dg + 1], cstB[:, C - 1:C])
                p.phase("vector", cte1)
                p.dma("sync", [(bf1[:, 0:C], ios["usp_" + D][dg * 128:(dg + 1) * 128, :])])

                def dtu1(eng, du_=dtu[i], dt_=dtb16[i], ya_=yacc[i], dg=dg, pr_=prm[D]):
                    eng.tensor_tensor(du_[:, :], dt_[:, :], bf1[:, 0:C], ALU.mult)
                    return eng.tensor_scalar(ya_[:, :], bf1[:, 0:C],
                                             pr_[:, 48 + dg:49 + dg],
                                             None, ALU.mult)
                p.phase("vector", dtu1)

            banks = [cm.bc_ps, cm.bc2_ps, cm.mm_ps[0], cm.mm_ps[1]]

            def mk_dma(si):
                pr2 = si % 2
                p.dma("sync", [(browB2[pr2][0:1, 0:C], BCst[si:si + 1, H:TC]),
                               (browC2[pr2][0:1, 0:C], BCst[16 + si:17 + si, H:TC])],
                      lag=_L(2) if si >= 2 else 0)

            def mk_bmm(si, lag):
                def bmm(eng, bB=browB2[si % 2]):
                    last = None
                    for bi, (o, w) in enumerate(chunks_of(C)):
                        last = eng.matmul(banks[bi][:, 0:w], cm.ones_row[0:1, :],
                                          bB[0:1, o:o + w], start=True, stop=True)
                    return last
                p.phase("tensor", bmm, lag=_L(lag))

            def mk_bev(si, lag):
                def bev(eng, B_=Bbc2[si % 2]):
                    last = None
                    for bi, (o, w) in enumerate(chunks_of(C)):
                        last = eng.copy(B_[:, o:o + w], banks[bi][:, 0:w])
                    return last
                p.phase("scalar", bev, lag=_L(lag))

            def mk_cmm(si, lag):
                def cmm(eng, bC=browC2[si % 2]):
                    last = None
                    for bi, (o, w) in enumerate(chunks_of(C)):
                        last = eng.matmul(banks[bi][:, 0:w], cm.ones_row[0:1, :],
                                          bC[0:1, o:o + w], start=True, stop=True)
                    return last
                p.phase("tensor", cmm, lag=_L(lag))

            def mk_cev(si, lag):
                def cev(eng, C_=Cbc2[si % 2]):
                    last = None
                    for bi, (o, w) in enumerate(chunks_of(C)):
                        last = eng.copy(C_[:, o:o + w], banks[bi][:, 0:w])
                    return last
                p.phase("scalar", cev, lag=_L(lag))

            def mk_sca(si, i, dg, lag):
                def sca(eng, dt_=dtb16[i], na_=nAt[D], col=dg * 16 + si, ca_=cstA2[i % 2]):
                    return eng.activation(ca_[:, 0:C], dt_[:, :], AF.Exp,
                                          scale=na_[:, col:col + 1])
                p.phase("scalar", sca, lag=_L(lag))

            def mk_scb(si, i, dg):
                def scb(eng, du_=dtu[i], ya_=yacc[i], is_=Ist[D], B_=Bbc2[si % 2],
                        C_=Cbc2[si % 2], col=dg * 16 + si, ca_=cstA2[i % 2]):
                    eng.tensor_tensor(cstB[:, 0:C], du_[:, :], B_[:, 0:C], ALU.mult)
                    eng.tensor_tensor_scan(cstO[:, 0:C], ca_[:, 0:C], cstB[:, 0:C],
                                           0.0, ALU.mult, ALU.add)
                    eng.tensor_copy(is_[:, col:col + 1], cstO[:, C - 1:C])
                    eng.tensor_tensor(cstO[:, 0:C], cstO[:, 0:C], C_[:, 0:C], ALU.mult)
                    return eng.tensor_tensor(ya_[:, :], ya_[:, :], cstO[:, 0:C], ALU.add)
                p.phase("vector", scb)

            mk_dma(0)
            mk_bmm(0, 0); mk_bev(0, 0); mk_cmm(0, 0); mk_cev(0, 0)
            mk_dma(1)
            for s in range(16):
                mk_sca(s, 0, dgs[0], lag=_L(5) if s > 0 else 0)
                mk_scb(s, 0, dgs[0])
                if s + 1 < 16:
                    mk_bmm(s + 1, 2)
                mk_sca(s, 1, dgs[1], lag=2)
                mk_scb(s, 1, dgs[1])
                if s + 1 < 16:
                    mk_bev(s + 1, 2)
                mk_sca(s, 2, dgs[2], lag=2)
                mk_scb(s, 2, dgs[2])
                if s + 1 < 16:
                    mk_cmm(s + 1, 2)
                if s + 2 < 16:
                    mk_dma(s + 2)
                mk_sca(s, 3, dgs[3], lag=2)
                mk_scb(s, 3, dgs[3])
                if s + 1 < 16:
                    mk_cev(s + 1, 2)
            for i, dg in enumerate(dgs):
                p.dma("sync", [(bf2[:, 0:C], ios["szo_" + D][dg * 128:(dg + 1) * 128, :])])

                def yp(eng, ya_=yacc[i]):
                    return eng.tensor_tensor(bf1[:, 0:C], ya_[:, :], bf2[:, 0:C], ALU.mult)
                p.phase("vector", yp)
                p.dma("gpsimd", [(ios["ypost_" + D][dg * 128:(dg + 1) * 128, :], bf1[:, 0:C])])
            p.free(*dtb16, *dtu, *yacc, *Bbc2, *Cbc2)

        def ecalc(eng, D=D):
            return eng.tensor_tensor(Est[D][:, :].rearrange("p (a b) -> p a b", b=16),
                                     nAt[D][:, :].rearrange("p (a b) -> p a b", b=16),
                                     cte[D][:, :].rearrange("p (a b) -> p a b", b=1).broadcast_to([128, 16, 16]),
                                     ALU.mult)
        p.phase("vector", ecalc)

        def ecalc2(eng, D=D):
            return eng.activation(Est[D][:, :], Est[D][:, :], AF.Exp)
        p.phase("scalar", ecalc2)
        p.dma("gpsimd", [(ios["Io_" + D], Ist[D][:, :]), (ios["Eo_" + D], Est[D][:, :])])
        p.free(ones_c, BCst, dtrB)
    return p


def _l1_wt(I, tag, l):
    pre = f"L1_attn_{tag}_"
    nw = _r8(I[pre + "norm_w"][l]); nb = _r8(I[pre + "norm_b"][l])
    out = {}
    out["nwb"] = np.ascontiguousarray(np.concatenate([nw, nb], 1))
    out["inWT"] = np.ascontiguousarray(I[pre + "in_W"][l].T)
    cw = I[pre + "conv_w"][l]
    out["cwb"] = np.ascontiguousarray(cw.reshape(16, 128, 4).transpose(1, 0, 2).reshape(128, 64))
    out["cbb"] = _r16(I[pre + "conv_b"][l])
    out["xpWT"] = np.ascontiguousarray(I[pre + "xproj_W"][l].T)
    out["dtWT"] = np.ascontiguousarray(I[pre + "dt_W"][l].T)
    out["dtbb"] = _r16(I[pre + "dt_b"][l])
    al = I[pre + "Alog"][l]
    out["nA"] = np.ascontiguousarray(al.reshape(16, 128, 16).transpose(1, 0, 2).reshape(128, 256))
    out["Dpb"] = _r16(I[pre + "D"][l])
    out["owT"] = np.ascontiguousarray(I[pre + "out_W"][l].T)
    return out


def _concat_tail(prev, own, tail_w=H):
    if prev is None:
        t = np.zeros((own.shape[0], tail_w), own.dtype)
    else:
        t = prev[:, -tail_w:]
    return np.ascontiguousarray(np.concatenate([t, own], axis=1))


def _iord(res_list, key, r):
    out = np.zeros((1024, 256), np.float32)
    for j in range(min(r, 8)):
        out[j * 128:(j + 1) * 128, :] = res_list[j][key]
    return out


def _eord(res_list, key, r):
    out = np.ones((1024, 256), np.float32)
    for j in range(min(r, 8)):
        out[j * 128:(j + 1) * 128, :] = res_list[j][key]
    return out


def run_l1_chain(I, h3_full):
    h3b = np.ascontiguousarray(h3_full[:, ::-1])
    sh_f = shard_tokens(h3_full)
    sh_b = shard_tokens(h3b)
    wts = {("f", l): _l1_wt(I, "fw", l) for l in range(4)}
    wts.update({("b", l): _l1_wt(I, "bw", l) for l in range(4)})

    def weight_map(l):
        m = {}
        for D in ("f", "b"):
            w = wts[(D, l)]
            for k in ("nwb", "inWT", "cwb", "cbb", "xpWT", "dtWT", "dtbb", "nA", "Dpb"):
                m[k + "_" + D] = w[k]
        return m

    def prev_map(l, res, r):
        m = {}
        for D in ("f", "b"):
            w = wts[(D, l - 1)]
            pr = res[r - 1] if r > 0 else None
            m["ypc_" + D] = _concat_tail(None if pr is None else pr["ypost_" + D], res[r]["ypost_" + D])
            m["szc_" + D] = _concat_tail(None if pr is None else pr["szo_" + D], res[r]["szo_" + D])
            m["ctc_" + D] = _concat_tail(None if pr is None else pr["cto_" + D], res[r]["cto_" + D])
            m["crc_" + D] = _concat_tail(None if pr is None else pr["cro_" + D], res[r]["cro_" + D])
            m["IoO_" + D] = _iord([res[q] for q in range(NCORES)], "Io_" + D, r)
            m["EoO_" + D] = _eord([res[q] for q in range(NCORES)], "Eo_" + D, r)
            m["IoP_" + D] = _iord([res[q] for q in range(NCORES)], "Io_" + D, r - 1 if r > 0 else 0)
            m["EoP_" + D] = _eord([res[q] for q in range(NCORES)], "Eo_" + D, r - 1 if r > 0 else 0)
            m["nAP_" + D] = w["nA"]
            m["owTP_" + D] = w["owT"]
        return m

    maps = []
    for r in range(NCORES):
        m = dict(rT_f=sh_f[r], rT_b=sh_b[r],
                 hmask=(np.zeros((128, H), np.float32) if r == 0 else np.ones((128, H), np.float32)))
        m.update(weight_map(0))
        maps.append(m)
    res = run_prog("l1s_first", lambda: build_l1s("first"), maps)

    for l in (1, 2, 3):
        maps = []
        for r in range(NCORES):
            m = dict(rT_f=res[r]["ro_f"], rT_b=res[r]["ro_b"],
                     hmask=(np.zeros((128, H), np.float32) if r == 0 else np.ones((128, H), np.float32)))
            m.update(weight_map(l))
            m.update(prev_map(l, res, r))
            maps.append(m)
        res = run_prog("l1s_mid", lambda: build_l1s("mid"), maps)

    nf = np.ascontiguousarray(np.concatenate([_r8(I["L1_attn_normf_w"]), _r8(I["L1_attn_normf_b"])], 1))
    maps = []
    for r in range(NCORES):
        m = dict(rT_f=res[r]["ro_f"], rT_b=res[r]["ro_b"], nfb_f=nf, nfb_b=nf,
                 hmask=(np.zeros((128, H), np.float32) if r == 0 else np.ones((128, H), np.float32)))
        m.update(prev_map(4, res, r))
        maps.append(m)
    resf = run_prog("l1s_last", lambda: build_l1s("last"), maps)
    fh2 = np.concatenate([resf[r]["fh2_f"] for r in range(NCORES)], axis=1)
    bh2f = np.concatenate([resf[r]["fh2_b"] for r in range(NCORES)], axis=1)
    bh2u = np.ascontiguousarray(bh2f[:, ::-1])
    return fh2, bh2u, res, resf


def emit_part_max(p, cm, tiles, width, out_ap, pmx, pmd):
    def pm1(eng, tiles=tiles):
        for i, t in enumerate(tiles):
            eng.tensor_reduce(cm.sq[:, i:i + 1].rearrange("p (a b) -> p a b", b=1),
                              t[:, 0:width].rearrange("p (a b) -> p a b", b=width),
                              AX.X, ALU.max, apply_absolute_value=True)
        return eng.tensor_reduce(pmx[:, 0:1].rearrange("p (a b) -> p a b", b=1),
                                 cm.sq[:, 0:len(tiles)].rearrange("p (a b) -> p a b", b=len(tiles)),
                                 AX.X, ALU.max)
    p.phase("vector", pm1)
    p.dma("sync", [(pmd[:, :], pmx[:, 0:1])])
    p.dma("sync", [(pmx[0:1, 0:128], pmd.rearrange("a b -> b a"))], allow_slow=True)

    def pm2(eng):
        return eng.tensor_reduce(pmx[0:1, 128:129].rearrange("p (a b) -> p a b", b=1),
                                 pmx[0:1, 0:128].rearrange("p (a b) -> p a b", b=128),
                                 AX.X, ALU.max)
    p.phase("vector", pm2)
    p.dma("gpsimd", [(out_ap, pmx[0:1, 128:129])])


def build_l1m():
    p = Prog("l1m")
    fh2T = p.inp("fh2T", [1024, C])
    bh2T = p.inp("bh2T", [1024, C])
    x2T = p.inp("x2T", [1024, C])
    cT8b = p.inp("cT8b", [128, 8])
    wmodb = p.inp("wmodb", [1024, 6144])
    WfT = p.inp("WfT", [1024, 1024])
    WbT = p.inp("WbT", [1024, 1024])
    mb8 = p.inp("mb8", [128, 8])
    x3_o = p.out("x3out", [1024, C])
    h4_o = p.out("h4out", [1024, C])
    pm_o = p.out("pmax", [1, 1])

    cm = Common(p, width=C)
    web = p.sbuf("web", [128, 16 * 128], BF16)
    mbs = p.sbuf("mbs", [128, 8], F32)
    pmx = p.sbuf("pmx", [128, 129], F32)
    pmd = p.dram("pmd", [128, 1])
    fhb = [p.sbuf(f"fhb{k}", [128, C], BF16) for k in range(8)]
    bhb = [p.sbuf(f"bhb{k}", [128, C], BF16) for k in range(8)]
    x3 = [p.sbuf(f"x3_{k}", [128, C], F32) for k in range(8)]
    xst = p.sbuf("xst", [128, C], F32)
    modx = emit_mod_vectors(p, cm, cT8b, wmodb, tag="m")
    p.dma("gpsimd", [(fhb[k][:, :], fh2T[k * 128:(k + 1) * 128, :]) for k in range(8)])
    p.dma("gpsimd", [(bhb[k][:, :], bh2T[k * 128:(k + 1) * 128, :]) for k in range(8)])
    p.dma("sync", [(mbs[:, :], mb8)])

    for m in range(8):
        p.dma("gpsimd", [(web[:, k * 128:(k + 1) * 128],
                          WfT[k * 128:(k + 1) * 128, m * 128:(m + 1) * 128]) for k in range(8)]
              + [(web[:, (8 + k) * 128:(9 + k) * 128],
                  WbT[k * 128:(k + 1) * 128, m * 128:(m + 1) * 128]) for k in range(8)])
        p.dma("sync", [(xst[:, :], x2T[m * 128:(m + 1) * 128, :])])
        for (o, w) in CC:
            def mm(eng, m=m, o=o, w=w):
                last = None
                for k in range(8):
                    eng.matmul(cm.mm_ps[0][:, 0:w], web[:, k * 128:(k + 1) * 128],
                               fhb[k][:, o:o + w], start=(k == 0), stop=False)
                    last = eng.matmul(cm.mm_ps[0][:, 0:w], web[:, (8 + k) * 128:(9 + k) * 128],
                                      bhb[k][:, o:o + w], start=False, stop=(k == 7))
                return last
            p.phase("tensor", mm)

            def ev(eng, o=o, w=w):
                return eng.copy(cm.bcs1[:, 0:w], cm.mm_ps[0][:, 0:w])
            p.phase("scalar", ev)

            def fin(eng, m=m, o=o, w=w):
                eng.tensor_scalar_add(cm.bcs1[:, 0:w], cm.bcs1[:, 0:w], mbs[:, m:m + 1])
                return eng.scalar_tensor_tensor(x3[m][:, o:o + w], cm.bcs1[:, 0:w],
                                                modx[:, 16 + m:17 + m], xst[:, o:o + w],
                                                ALU.mult, ALU.add)
            p.phase("vector", fin)
        p.dma("gpsimd", [(x3_o[m * 128:(m + 1) * 128, :], x3[m][:, :])])

    emit_ln(p, cm, "lnm", x3, x3, C,
            gamma=[modx[:, 56 + k:57 + k] for k in range(8)],
            beta=[modx[:, 24 + k:25 + k] for k in range(8)])
    p.dma("gpsimd", [(h4_o[m * 128:(m + 1) * 128, :], x3[m][:, :]) for m in range(8)])
    emit_part_max(p, cm, x3, C, pm_o, pmx, pmd)
    return p


_GMN = [0]


def emit_gmax_scale(p, cm, maxes_in, s128, rs128, qmax, smin):
    _GMN[0] += 1
    m8 = p.sbuf("gm8_" + str(_GMN[0]), [1, 8], F32)
    p.dma("sync", [(m8[:, :], maxes_in)])

    def g1(eng):
        eng.tensor_reduce(s128[0:1, 0:1].rearrange("p (a b) -> p a b", b=1),
                          m8[0:1, 0:8].rearrange("p (a b) -> p a b", b=8), AX.X, ALU.max)
        eng.tensor_scalar_mul(s128[0:1, 0:1], s128[0:1, 0:1], float(np.float32(1.0 / qmax)))
        eng.tensor_scalar_max(s128[0:1, 0:1], s128[0:1, 0:1], smin)
        eng.tensor_copy(rs128[0:1, 0:1], s128[0:1, 0:1])
        return eng.reciprocal(rs128[0:1, 0:1], rs128[0:1, 0:1])
    p.phase("vector", g1)
    emit_bcast_row(p, s128, s128, width=1)
    emit_bcast_row(p, rs128, rs128, width=1)


def emit_qdq(eng, dst, src_ap, w, rs_ap, s_ap, iqt, f32t, lo, hi):
    eng.tensor_scalar(f32t[:, 0:w], src_ap, rs_ap, None, ALU.mult)
    eng.tensor_copy(iqt[:, 0:w], f32t[:, 0:w])
    eng.tensor_copy(f32t[:, 0:w], iqt[:, 0:w])
    eng.tensor_scalar_min(f32t[:, 0:w], f32t[:, 0:w], float(hi))
    eng.tensor_scalar_max(f32t[:, 0:w], f32t[:, 0:w], float(lo))
    return eng.tensor_scalar(dst, f32t[:, 0:w], s_ap, None, ALU.mult)


def build_f1():
    p = Prog("f1")
    h4T = p.inp("h4T", [1024, C])
    maxes = p.inp("maxes", [1, 8])
    wup = p.inp("wup", [3072, 1024])
    wgate = p.inp("wgate", [3072, 1024])
    up_o = p.out("up", [3072, C])
    gate_o = p.out("gate", [3072, C])
    pmu_o = p.out("pmu", [1, 1])
    pmg_o = p.out("pmg", [1, 1])

    cm = Common(p, width=C)
    s128 = p.sbuf("s128", [128, 1], F32)
    rs128 = p.sbuf("rs128", [128, 1], F32)
    iqt = p.sbuf("iqt", [128, C], I32)
    f32t = p.sbuf("f32t", [128, C], F32)
    wst = p.sbuf("wst", [128, 1024], F32)
    sw = p.sbuf("sw", [128, 32], F32)
    rsw = p.sbuf("rsw", [128, 32], F32)
    wqb = p.sbuf("wqb", [128, 1024], BF16)
    wqT = p.sbuf("wqT", [128, 1024], BF16)
    xqb = [p.sbuf(f"xqb{k}", [128, C], BF16) for k in range(8)]
    ust = p.sbuf("ust", [128, C], F32)
    pmx = p.sbuf("pmx", [128, 129], F32)
    pmacc = [p.sbuf("pma_u", [128, 24], F32), p.sbuf("pma_g", [128, 24], F32)]
    pmd = p.dram("pmd", [128, 1])

    emit_gmax_scale(p, cm, maxes, s128, rs128, 32767.0, 1e-12)

    for k in range(8):
        p.dma("sync", [(f32t[:, 0:C] if False else ust[:, :], h4T[k * 128:(k + 1) * 128, :])])

        def xq(eng, k=k):
            return emit_qdq(eng, xqb[k][:, :], ust[:, :], C, rs128[:, 0:1], s128[:, 0:1],
                            iqt, f32t, -32768.0, 32767.0)
        p.phase("vector", xq)

    for wi, (w_in, out_d, pma) in enumerate(((wup, up_o, pmacc[0]), (wgate, gate_o, pmacc[1]))):
        for m in range(24):
            p.dma("sync", [(wst[:, :], w_in[m * 128:(m + 1) * 128, :])])

            def dq(eng, m=m):
                eng.tensor_reduce(sw[:, :].rearrange("p (a b) -> p a b", b=1),
                                  wst[:, :].rearrange("p (a b) -> p a b", b=32),
                                  AX.X, ALU.max, apply_absolute_value=True)
                eng.tensor_scalar_mul(sw[:, :], sw[:, :], float(np.float32(1.0 / 7.0)))
                eng.tensor_scalar_max(sw[:, :], sw[:, :], 1e-12)
                eng.tensor_copy(rsw[:, :], sw[:, :])
                eng.reciprocal(rsw[:, :], rsw[:, :])
                eng.tensor_tensor(f32t[:, 0:1024].rearrange("p (a b) -> p a b", b=32),
                                  wst[:, :].rearrange("p (a b) -> p a b", b=32),
                                  rsw[:, :].rearrange("p (a b) -> p a b", b=1).broadcast_to([128, 32, 32]),
                                  ALU.mult)
                eng.tensor_copy(iqt[:, 0:1024], f32t[:, 0:1024])
                eng.tensor_copy(f32t[:, 0:1024], iqt[:, 0:1024])
                eng.tensor_scalar_min(f32t[:, 0:1024], f32t[:, 0:1024], 7.0)
                eng.tensor_scalar_max(f32t[:, 0:1024], f32t[:, 0:1024], -8.0)
                return eng.tensor_tensor(wqb[:, :].rearrange("p (a b) -> p a b", b=32),
                                         f32t[:, 0:1024].rearrange("p (a b) -> p a b", b=32),
                                         sw[:, :].rearrange("p (a b) -> p a b", b=1).broadcast_to([128, 32, 32]),
                                         ALU.mult)
            p.phase("vector", dq)
            for k in range(8):
                def trf(eng, k=k):
                    return [eng.dma_start(out=wqT[:, k * 128:(k + 1) * 128],
                                          in_=wqb[:, k * 128:(k + 1) * 128], transpose=True)]
                p.phase("sync", trf, n_dma=1)
            for (o, w) in CC:
                def umm(eng, o=o, w=w):
                    last = None
                    for k in range(8):
                        last = eng.matmul(cm.mm_ps[0][:, 0:w], wqT[:, k * 128:(k + 1) * 128],
                                          xqb[k][:, o:o + w], start=(k == 0), stop=(k == 7))
                    return last
                p.phase("tensor", umm)

                def uev(eng, o=o, w=w):
                    return eng.copy(ust[:, o:o + w], cm.mm_ps[0][:, 0:w])
                p.phase("scalar", uev)

            def upm(eng, m=m, pma=pma):
                return eng.tensor_reduce(pma[:, m:m + 1].rearrange("p (a b) -> p a b", b=1),
                                         ust[:, :].rearrange("p (a b) -> p a b", b=C),
                                         AX.X, ALU.max, apply_absolute_value=True)
            p.phase("vector", upm)
            p.dma("gpsimd", [(out_d[m * 128:(m + 1) * 128, :], ust[:, :])])

    for pma, po in ((pmacc[0], pmu_o), (pmacc[1], pmg_o)):
        def red(eng, pma=pma):
            return eng.tensor_reduce(pmx[:, 0:1].rearrange("p (a b) -> p a b", b=1),
                                     pma[:, :].rearrange("p (a b) -> p a b", b=24),
                                     AX.X, ALU.max)
        p.phase("vector", red)
        p.dma("sync", [(pmd[:, :], pmx[:, 0:1])])
        p.dma("sync", [(pmx[0:1, 0:128], pmd.rearrange("a b -> b a"))], allow_slow=True)

        def red2(eng):
            return eng.tensor_reduce(pmx[0:1, 128:129].rearrange("p (a b) -> p a b", b=1),
                                     pmx[0:1, 0:128].rearrange("p (a b) -> p a b", b=128),
                                     AX.X, ALU.max)
        p.phase("vector", red2)
        p.dma("gpsimd", [(po, pmx[0:1, 128:129])])
    return p


def build_f2():
    p = Prog("f2")
    gate_i = p.inp("gate", [3072, C])
    maxg = p.inp("maxg", [1, 8])
    t_o = p.out("tout", [3072, C])
    pmt_o = p.out("pmt", [1, 1])
    cm = Common(p, width=C)
    s128 = p.sbuf("s128", [128, 1], F32)
    rs128 = p.sbuf("rs128", [128, 1], F32)
    iqt = p.sbuf("iqt", [128, C], I32)
    f32t = p.sbuf("f32t", [128, C], F32)
    gst = p.sbuf("gst", [128, C], F32)
    gq = p.sbuf("gq", [128, C], F32)
    sg = p.sbuf("sg", [128, C], F32)
    pma = p.sbuf("pma", [128, 24], F32)
    pmx = p.sbuf("pmx", [128, 129], F32)
    pmd = p.dram("pmd", [128, 1])
    emit_gmax_scale(p, cm, maxg, s128, rs128, 32767.0, 1e-12)
    for m in range(24):
        p.dma("sync", [(gst[:, :], gate_i[m * 128:(m + 1) * 128, :])])

        def q1(eng):
            return emit_qdq(eng, gq[:, 0:C], gst[:, :], C, rs128[:, 0:1], s128[:, 0:1],
                            iqt, f32t, -32768.0, 32767.0)
        p.phase("vector", q1)

        def sg1(eng):
            return eng.activation(sg[:, :], gq[:, :], AF.Exp, scale=-1.0)
        p.phase("scalar", sg1)

        def sg2(eng):
            eng.tensor_scalar_add(sg[:, :], sg[:, :], 1.0)
            eng.reciprocal(sg[:, :], sg[:, :])
            eng.tensor_scalar_mul(f32t[:, :], sg[:, :], 65536.0)
            eng.tensor_copy(iqt[:, :], f32t[:, :])
            eng.tensor_copy(f32t[:, :], iqt[:, :])
            eng.tensor_scalar_min(f32t[:, :], f32t[:, :], 65535.0)
            eng.tensor_scalar_max(f32t[:, :], f32t[:, :], 0.0)
            eng.tensor_scalar_mul(sg[:, :], f32t[:, :], float(np.float32(1.0 / 65536.0)))
            eng.tensor_tensor(gq[:, :], gq[:, :], sg[:, :], ALU.mult)
            return eng.tensor_reduce(pma[:, m:m + 1].rearrange("p (a b) -> p a b", b=1),
                                     gq[:, :].rearrange("p (a b) -> p a b", b=C),
                                     AX.X, ALU.max, apply_absolute_value=True)
        p.phase("vector", sg2)
        p.dma("gpsimd", [(t_o[m * 128:(m + 1) * 128, :], gq[:, :])])

    def red(eng):
        return eng.tensor_reduce(pmx[:, 0:1].rearrange("p (a b) -> p a b", b=1),
                                 pma[:, :].rearrange("p (a b) -> p a b", b=24),
                                 AX.X, ALU.max)
    p.phase("vector", red)
    p.dma("sync", [(pmd[:, :], pmx[:, 0:1])])
    p.dma("sync", [(pmx[0:1, 0:128], pmd.rearrange("a b -> b a"))], allow_slow=True)

    def red2(eng):
        return eng.tensor_reduce(pmx[0:1, 128:129].rearrange("p (a b) -> p a b", b=1),
                                 pmx[0:1, 0:128].rearrange("p (a b) -> p a b", b=128),
                                 AX.X, ALU.max)
    p.phase("vector", red2)
    p.dma("gpsimd", [(pmt_o, pmx[0:1, 128:129])])
    return p


def build_f3():
    p = Prog("f3")
    t_i = p.inp("tin", [3072, C])
    up_i = p.inp("upin", [3072, C])
    maxt = p.inp("maxt", [1, 8])
    maxu = p.inp("maxu", [1, 8])
    o_o = p.out("opre", [3072, C])
    pmo_o = p.out("pmo", [1, 1])
    cm = Common(p, width=C)
    sT = p.sbuf("sT", [128, 1], F32); rsT = p.sbuf("rsT", [128, 1], F32)
    sU = p.sbuf("sU", [128, 1], F32); rsU = p.sbuf("rsU", [128, 1], F32)
    iqt = p.sbuf("iqt", [128, C], I32)
    f32t = p.sbuf("f32t", [128, C], F32)
    ast = p.sbuf("ast", [128, C], F32)
    bst = p.sbuf("bst", [128, C], F32)
    g2 = p.sbuf("g2", [128, C], F32)
    pma = p.sbuf("pma", [128, 24], F32)
    pmx = p.sbuf("pmx", [128, 129], F32)
    pmd = p.dram("pmd", [128, 1])
    emit_gmax_scale(p, cm, maxt, sT, rsT, 32767.0, 1e-12)
    emit_gmax_scale(p, cm, maxu, sU, rsU, 32767.0, 1e-12)
    for m in range(24):
        p.dma("sync", [(ast[:, :], t_i[m * 128:(m + 1) * 128, :]),
                       (bst[:, :], up_i[m * 128:(m + 1) * 128, :])])

        def q1(eng, m=m):
            emit_qdq(eng, g2[:, 0:C], ast[:, :], C, rsT[:, 0:1], sT[:, 0:1],
                     iqt, f32t, -32768.0, 32767.0)
            emit_qdq(eng, bst[:, 0:C], bst[:, :], C, rsU[:, 0:1], sU[:, 0:1],
                     iqt, f32t, -32768.0, 32767.0)
            eng.tensor_tensor(g2[:, :], g2[:, :], bst[:, :], ALU.mult)
            return eng.tensor_reduce(pma[:, m:m + 1].rearrange("p (a b) -> p a b", b=1),
                                     g2[:, :].rearrange("p (a b) -> p a b", b=C),
                                     AX.X, ALU.max, apply_absolute_value=True)
        p.phase("vector", q1)
        p.dma("gpsimd", [(o_o[m * 128:(m + 1) * 128, :], g2[:, :])])

    def red(eng):
        return eng.tensor_reduce(pmx[:, 0:1].rearrange("p (a b) -> p a b", b=1),
                                 pma[:, :].rearrange("p (a b) -> p a b", b=24),
                                 AX.X, ALU.max)
    p.phase("vector", red)
    p.dma("sync", [(pmd[:, :], pmx[:, 0:1])])
    p.dma("sync", [(pmx[0:1, 0:128], pmd.rearrange("a b -> b a"))], allow_slow=True)

    def red2(eng):
        return eng.tensor_reduce(pmx[0:1, 128:129].rearrange("p (a b) -> p a b", b=1),
                                 pmx[0:1, 0:128].rearrange("p (a b) -> p a b", b=128),
                                 AX.X, ALU.max)
    p.phase("vector", red2)
    p.dma("gpsimd", [(pmo_o, pmx[0:1, 128:129])])
    return p


def build_f4():
    p = Prog("f4")
    o_i = p.inp("oin", [3072, C])
    maxo = p.inp("maxo", [1, 8])
    wd_i = p.inp("wd", [1024, 3072])
    x3T = p.inp("x3T", [1024, C])
    cT8b = p.inp("cT8b", [128, 8])
    wmodb = p.inp("wmodb", [1024, 6144])
    x4_o = p.out("x4out", [1024, C])
    cm = Common(p, width=C)
    s128 = p.sbuf("s128", [128, 1], F32)
    rs128 = p.sbuf("rs128", [128, 1], F32)
    iqt = p.sbuf("iqt", [128, 3072], I32)
    f32t = p.sbuf("f32t", [128, 3072], F32)
    wst = p.sbuf("wst", [128, 3072], F32)
    sw = p.sbuf("sw", [128, 96], F32)
    rsw = p.sbuf("rsw", [128, 96], F32)
    wqb = p.sbuf("wqb", [128, 3072], BF16)
    wqT = p.sbuf("wqT", [128, 3072], BF16)
    ost = p.sbuf("ost", [128, C], F32)
    oqb = p.sbuf("oqb", [128, C], BF16)
    ostr = p.sbuf("ostr", [128, 24 * 512], BF16)
    xst = p.sbuf("xst", [128, C], F32)
    oq_d = p.dram("oq_d", [3072, C], BF16)
    modx = emit_mod_vectors(p, cm, cT8b, wmodb, tag="m")
    emit_gmax_scale(p, cm, maxo, s128, rs128, 32767.0, 1e-12)
    for k in range(24):
        p.dma("sync", [(ost[:, :], o_i[k * 128:(k + 1) * 128, :])])

        def oq(eng):
            return emit_qdq(eng, oqb[:, 0:C], ost[:, :], C, rs128[:, 0:1], s128[:, 0:1],
                            iqt, f32t, -32768.0, 32767.0)
        p.phase("vector", oq)
        p.dma("gpsimd", [(oq_d[k * 128:(k + 1) * 128, :], oqb[:, :])])

    for m in range(8):
        p.dma("sync", [(wst[:, :], wd_i[m * 128:(m + 1) * 128, :])])

        def dq(eng, m=m):
            eng.tensor_reduce(sw[:, :].rearrange("p (a b) -> p a b", b=1),
                              wst[:, :].rearrange("p (a b) -> p a b", b=32),
                              AX.X, ALU.max, apply_absolute_value=True)
            eng.tensor_scalar_mul(sw[:, :], sw[:, :], float(np.float32(1.0 / 7.0)))
            eng.tensor_scalar_max(sw[:, :], sw[:, :], 1e-12)
            eng.tensor_copy(rsw[:, :], sw[:, :])
            eng.reciprocal(rsw[:, :], rsw[:, :])
            eng.tensor_tensor(f32t[:, :].rearrange("p (a b) -> p a b", b=32),
                              wst[:, :].rearrange("p (a b) -> p a b", b=32),
                              rsw[:, :].rearrange("p (a b) -> p a b", b=1).broadcast_to([128, 96, 32]),
                              ALU.mult)
            eng.tensor_copy(iqt[:, :], f32t[:, :])
            eng.tensor_copy(f32t[:, :], iqt[:, :])
            eng.tensor_scalar_min(f32t[:, :], f32t[:, :], 7.0)
            eng.tensor_scalar_max(f32t[:, :], f32t[:, :], -8.0)
            return eng.tensor_tensor(wqb[:, :].rearrange("p (a b) -> p a b", b=32),
                                     f32t[:, :].rearrange("p (a b) -> p a b", b=32),
                                     sw[:, :].rearrange("p (a b) -> p a b", b=1).broadcast_to([128, 96, 32]),
                                     ALU.mult)
        p.phase("vector", dq)
        for k in range(24):
            def trf(eng, k=k):
                return [eng.dma_start(out=wqT[:, k * 128:(k + 1) * 128],
                                      in_=wqb[:, k * 128:(k + 1) * 128], transpose=True)]
            p.phase("sync", trf, n_dma=1)
        p.dma("sync", [(xst[:, :], x3T[m * 128:(m + 1) * 128, :])])
        for (o, w) in CC:
            p.dma("sync", [(ostr[:, k * 512:k * 512 + w], oq_d[k * 128:(k + 1) * 128, o:o + w])
                           for k in range(24)])

            def dmm(eng, o=o, w=w):
                last = None
                for k in range(24):
                    last = eng.matmul(cm.mm_ps[0][:, 0:w], wqT[:, k * 128:(k + 1) * 128],
                                      ostr[:, k * 512:k * 512 + w], start=(k == 0), stop=(k == 23))
                return last
            p.phase("tensor", dmm)

            def dev(eng, o=o, w=w):
                return eng.copy(ost[:, o:o + w], cm.mm_ps[0][:, 0:w])
            p.phase("scalar", dev)

        def fin(eng, m=m):
            return eng.scalar_tensor_tensor(ost[:, :], ost[:, :], modx[:, 40 + m:41 + m],
                                            xst[:, :], ALU.mult, ALU.add)
        p.phase("vector", fin)
        p.dma("gpsimd", [(x4_o[m * 128:(m + 1) * 128, :], ost[:, :])])
    return p



def kernel(**I):
    I = {k: np.asarray(v) for k, v in I.items()}

    res_a = run_prog("l0a", build_l0a, l0a_inputs(I))
    res_b = run_prog("l0b", build_l0b, l0b_inputs(I, res_a))
    x1_shards = [res_b[r]["x1out"] for r in range(NCORES)]
    res_c = run_prog("l0c", build_l0c, l0c_inputs(I, x1_shards))
    h3_full = np.ascontiguousarray(np.concatenate([res_c[r]["h3out"] for r in range(NCORES)], axis=1))

    fh2, bh2u, _, _ = run_l1_chain(I, h3_full)

    mW = I["L1_attn_merge_W"]
    WfT = np.ascontiguousarray(mW[:, 0:1024].T)
    WbT = np.ascontiguousarray(mW[:, 1024:2048].T)
    mb8 = _r8(I["L1_attn_merge_b"])
    cT8b = _r8(I["L1_c"][0])
    maps = []
    for r in range(NCORES):
        lo = r * C
        maps.append(dict(fh2T=np.ascontiguousarray(fh2[:, lo:lo + C]),
                         bh2T=np.ascontiguousarray(bh2u[:, lo:lo + C]),
                         x2T=res_c[r]["x2out"], cT8b=cT8b, wmodb=I["L1_wmod"],
                         WfT=WfT, WbT=WbT, mb8=mb8))
    res_m = run_prog("l1m", build_l1m, maps)

    maxes = np.ascontiguousarray(np.array([[res_m[r]["pmax"][0, 0] for r in range(NCORES)]],
                                          np.float32))
    maps = [dict(h4T=res_m[r]["h4out"], maxes=maxes, wup=I["L1_ffn_w_up"],
                 wgate=I["L1_ffn_w_gate"]) for r in range(NCORES)]
    res_1 = run_prog("f1", build_f1, maps)

    maxg = np.ascontiguousarray(np.array([[res_1[r]["pmg"][0, 0] for r in range(NCORES)]], np.float32))
    maps = [dict(gate=res_1[r]["gate"], maxg=maxg) for r in range(NCORES)]
    res_2 = run_prog("f2", build_f2, maps)

    maxu = np.ascontiguousarray(np.array([[res_1[r]["pmu"][0, 0] for r in range(NCORES)]], np.float32))
    maxt = np.ascontiguousarray(np.array([[res_2[r]["pmt"][0, 0] for r in range(NCORES)]], np.float32))
    maps = [dict(tin=res_2[r]["tout"], upin=res_1[r]["up"], maxt=maxt, maxu=maxu)
            for r in range(NCORES)]
    res_3 = run_prog("f3", build_f3, maps)

    maxo = np.ascontiguousarray(np.array([[res_3[r]["pmo"][0, 0] for r in range(NCORES)]], np.float32))
    maps = [dict(oin=res_3[r]["opre"], maxo=maxo, wd=I["L1_ffn_w_down"],
                 x3T=res_m[r]["x3out"], cT8b=cT8b, wmodb=I["L1_wmod"]) for r in range(NCORES)]
    res_4 = run_prog("f4", build_f4, maps)

    full = np.concatenate([res_4[r]["x4out"] for r in range(NCORES)], axis=1)
    return np.ascontiguousarray(full.T)[None, :, :]


# version 04 (final)
# speedup vs baseline: 1.6600x; 1.0913x over previous
import numpy as np
import ml_dtypes
import concourse.bass as bass
import concourse.mybir as mybir
from concourse.bass_utils import run_bass_kernel_spmd

F32 = mybir.dt.float32
BF16 = mybir.dt.bfloat16
I32 = mybir.dt.int32
AF = mybir.ActivationFunctionType
ALU = mybir.AluOpType
AX = mybir.AxisListType

NCORES = 8
C = 2048
H = 3
TC = C + H
BF = ml_dtypes.bfloat16
LN_EPS = 1e-5


def chunks_of(total, step=512):
    out, o = [], 0
    while o < total:
        out.append((o, min(step, total - o)))
        o += step
    return out


LAGS_ON = True
LAG_SCAN = True
LAG_CORR = True
LAG_INW = True


def _L(x):
    return x if LAGS_ON else 0


LAG_SCAN_Y = True


def _Ls(x):
    return x if (LAGS_ON and LAG_SCAN) else 0


def _Lsy(x):
    return x if (LAGS_ON and LAG_SCAN and LAG_SCAN_Y) else 0


def _Lc(x):
    return x if (LAGS_ON and LAG_CORR) else 0


def _Li(x):
    return x if (LAGS_ON and LAG_INW) else 0


TCC = chunks_of(TC)
CC = chunks_of(C)


class Prog:
    def __init__(self, name):
        self.name = name
        self.nc = bass.Bass("TRN2", target_bir_lowering=False, debug=False)
        self.phases = []

    def inp(self, name, shape, dtype=F32):
        return self.nc.dram_tensor(name, list(shape), dtype, kind="ExternalInput").ap()

    def out(self, name, shape, dtype=F32):
        return self.nc.dram_tensor(name, list(shape), dtype, kind="ExternalOutput").ap()

    def dram(self, name, shape, dtype=F32):
        return self.nc.dram_tensor(name, list(shape), dtype).ap()

    def sbuf(self, name, shape, dtype=F32, side=None):
        kw = dict(side=side) if side else {}
        ctx = self.nc.sbuf_tensor(name, list(shape), dtype, **kw)
        t = ctx.__enter__()
        self._sbuf_ctx = getattr(self, "_sbuf_ctx", {})
        self._sbuf_ctx[id(t)] = ctx
        return t

    def free(self, *tiles):
        for t in reversed(tiles):
            self._sbuf_ctx.pop(id(t)).__exit__(None, None, None)

    def psum(self, name, shape, dtype=F32):
        return self.nc.psum_tensor(name, list(shape), dtype).__enter__()

    def phase(self, engine, fn, n_dma=0, lag=0):
        self.phases.append((engine, fn, n_dma, lag))

    def dma(self, engine, transfers, allow_slow=False, lag=0):
        def fn(eng):
            kw = dict(allow_slow_non_contiguous=True) if allow_slow else {}
            return [eng.dma_start(out=o, in_=i, **kw) for (o, i) in transfers]
        self.phase(engine, fn, n_dma=len(transfers), lag=lag)

    def finalize(self):
        nc = self.nc
        gs = nc.semaphore("gs").__enter__()
        cs_sems = {e: nc.semaphore("cs_" + e).__enter__() for e in ("vector", "scalar", "gpsimd")}
        targets, tot = [], 0
        for ph in self.phases:
            targets.append(tot)
            tot += 16 * ph[2] if ph[2] else 1
        final_total = tot
        waits = []
        for idx, ph in enumerate(self.phases):
            j = max(0, idx - ph[3])
            waits.append(targets[j])
        block = nc.Block().__enter__()

        GUARded = ("vector", "scalar", "gpsimd")
        WRAP = {"tensor_tensor", "tensor_scalar", "tensor_single_scalar", "tensor_scalar_add",
                "tensor_scalar_sub", "tensor_scalar_mul", "tensor_scalar_max", "tensor_scalar_min",
                "tensor_add", "tensor_sub", "tensor_mul", "tensor_max", "tensor_copy", "memset",
                "memzero", "scalar_tensor_tensor", "tensor_tensor_scan", "tensor_reduce",
                "reduce_max", "reduce_sum", "tensor_relu", "reciprocal", "select", "activation",
                "copy", "mul", "add", "square", "sign", "sqrt", "pool", "pool_max", "pool_avg",
                "iota", "transpose", "bn_stats", "bn_aggr", "stream_shuffle", "copy_predicated"}

        class GEng:
            def __init__(self, eng, sem):
                self._e = eng
                self._sem = sem
                self._n = 0

            def __getattr__(self, name):
                f = getattr(self._e, name)
                if name in WRAP:
                    def wrapped(*a, **kw):
                        inst = f(*a, **kw)
                        self._n += 1
                        inst.then_inc(self._sem, 1)
                        self._e.wait_ge(self._sem, self._n)
                        return inst
                    return wrapped
                return f

        def runner(engine_name):
            def run(eng):
                geng = GEng(eng, cs_sems[engine_name]) if engine_name in GUARded else eng
                for idx, (en, fn, n_dma, _lag) in enumerate(self.phases):
                    if en != engine_name:
                        continue
                    if waits[idx] > 0:
                        eng.wait_ge(gs, waits[idx])
                    last = fn(geng)
                    if n_dma:
                        for inst in last:
                            inst.then_inc(gs, 16)
                    elif engine_name == "tensor":
                        if last is None:
                            eng.sem_inc(gs, 1)
                        else:
                            last.then_inc(gs, 1)
                    else:
                        eng.sem_inc(gs, 1)
                if engine_name == "gpsimd":
                    eng.wait_ge(gs, final_total)
            return run

        @block.sync
        def _(eng):
            runner("sync")(eng)

        @block.scalar
        def _(eng):
            runner("scalar")(eng)

        @block.vector
        def _(eng):
            runner("vector")(eng)

        @block.tensor
        def _(eng):
            runner("tensor")(eng)

        @block.gpsimd
        def _(eng):
            runner("gpsimd")(eng)

        block.__exit__(None, None, None)
        return nc



class Common:
    def __init__(self, prog, width=TC):
        self.prog = prog
        self.width = width
        self.ones = prog.sbuf("cm_ones", [128, 1], F32)
        self.ones_row = prog.sbuf("cm_onesr", [1, 128], F32)
        self.sq = prog.sbuf("cm_sq", [128, 512], F32)
        self.stat_ps = prog.psum("cm_statps", [1, 512])
        self.bc_ps = prog.psum("cm_bcps", [128, 512])
        self.bc2_ps = prog.psum("cm_bc2ps", [128, 512])
        self.mm_ps = [prog.psum("cm_mmps0", [128, 512]), prog.psum("cm_mmps1", [128, 512])]
        self.bcs1 = prog.sbuf("cm_bcs1", [128, 512], F32)
        self.bcs2 = prog.sbuf("cm_bcs2", [128, 512], F32)
        self.mu = prog.sbuf("cm_mu", [1, width], F32)
        self.rs = prog.sbuf("cm_rs", [1, width], F32)
        self.tp = prog.sbuf("cm_tp", [1, width], F32)

        def init_ones(eng):
            eng.memset(self.ones[:, :], 0.0)
            eng.tensor_scalar_add(self.ones[:, :], self.ones[:, :], 1.0)
            eng.memset(self.ones_row[:, :], 0.0)
            return eng.tensor_scalar_add(self.ones_row[:, :], self.ones_row[:, :], 1.0)
        prog.phase("vector", init_ones)


def emit_ln(prog, cm, name, src_tiles, dst_tiles, width, gamma, beta):
    ntiles = len(src_tiles)
    D = 128 * ntiles
    mu, rs, tp = cm.mu, cm.rs, cm.tp
    cks = chunks_of(width)

    for (o, w) in cks:
        def mm(eng, o=o, w=w):
            last = None
            for k in range(ntiles):
                last = eng.matmul(cm.stat_ps[0:1, 0:w], cm.ones[:, 0:1],
                                  src_tiles[k][:, o:o + w], start=(k == 0), stop=(k == ntiles - 1))
            return last
        prog.phase("tensor", mm)

        def ev(eng, o=o, w=w):
            return eng.mul(mu[0:1, o:o + w], cm.stat_ps[0:1, 0:w], 1.0 / D)
        prog.phase("scalar", ev)

    for (o, w) in cks:
        for k in range(ntiles):
            def sq(eng, o=o, w=w, k=k):
                return eng.activation(cm.sq[:, 0:w], src_tiles[k][:, o:o + w], AF.Square)
            prog.phase("scalar", sq)

            def mm2(eng, o=o, w=w, k=k):
                return eng.matmul(cm.stat_ps[0:1, 0:w], cm.ones[:, 0:1], cm.sq[:, 0:w],
                                  start=(k == 0), stop=(k == ntiles - 1))
            prog.phase("tensor", mm2)

        def ev2(eng, o=o, w=w):
            return eng.mul(rs[0:1, o:o + w], cm.stat_ps[0:1, 0:w], 1.0 / D)
        prog.phase("scalar", ev2)

    def ph_v(eng):
        eng.tensor_tensor(tp[0:1, 0:width], mu[0:1, 0:width], mu[0:1, 0:width], ALU.mult)
        eng.tensor_tensor(rs[0:1, 0:width], rs[0:1, 0:width], tp[0:1, 0:width], ALU.subtract)
        return eng.tensor_scalar_add(rs[0:1, 0:width], rs[0:1, 0:width], LN_EPS)
    prog.phase("vector", ph_v)

    def ph_l(eng):
        eng.activation(rs[0:1, 0:width], rs[0:1, 0:width], AF.Ln)
        return eng.activation(rs[0:1, 0:width], rs[0:1, 0:width], AF.Exp, scale=-0.5)
    prog.phase("scalar", ph_l)

    for (o, w) in cks:
        def bc(eng, o=o, w=w):
            eng.matmul(cm.bc_ps[:, 0:w], cm.ones_row[0:1, :], mu[0:1, o:o + w], start=True, stop=True)
            return eng.matmul(cm.bc2_ps[:, 0:w], cm.ones_row[0:1, :], rs[0:1, o:o + w], start=True, stop=True)
        prog.phase("tensor", bc)

        def bce(eng, o=o, w=w):
            eng.copy(cm.bcs1[:, 0:w], cm.bc_ps[:, 0:w])
            return eng.copy(cm.bcs2[:, 0:w], cm.bc2_ps[:, 0:w])
        prog.phase("scalar", bce)

        def ap(eng, o=o, w=w):
            last = None
            for k in range(ntiles):
                g = 1.0 if gamma is None else gamma[k]
                b = 0.0 if beta is None else beta[k]
                eng.tensor_tensor(cm.sq[:, 0:w], src_tiles[k][:, o:o + w], cm.bcs1[:, 0:w], ALU.subtract)
                eng.scalar_tensor_tensor(cm.sq[:, 0:w], cm.sq[:, 0:w], g, cm.bcs2[:, 0:w], ALU.mult, ALU.mult)
                last = eng.tensor_scalar_add(dst_tiles[k][:, o:o + w], cm.sq[:, 0:w], b)
            return last
        prog.phase("vector", ap)
    return mu, rs


def emit_matmul(prog, cm, name, lhsT_tiles, rhs_tiles, out_cb, M_tiles, K_tiles, width,
                n_step=512, accum_sbuf=None):
    cks = chunks_of(width, n_step)
    pi = 0
    for m in range(M_tiles):
        for (o, w) in cks:
            ps = cm.mm_ps[pi % 2]
            pi += 1

            def mm(eng, m=m, o=o, w=w, ps=ps):
                last = None
                for k in range(K_tiles):
                    last = eng.matmul(ps[:, 0:w], lhsT_tiles(k, m), rhs_tiles(k)[:, o:o + w],
                                      start=(k == 0), stop=(k == K_tiles - 1))
                return last
            prog.phase("tensor", mm)
            out_cb(prog, m, o, w, ps)



def shard_tokens(x_t):
    outs = []
    d = x_t.shape[0]
    for r in range(NCORES):
        lo = r * C
        halo = x_t[:, lo - H:lo] if r > 0 else x_t[:, 0:H]
        outs.append(np.ascontiguousarray(np.concatenate([halo, x_t[:, lo:lo + C]], axis=1)))
    return outs


_prog_cache = {}


def run_prog(key, builder, in_maps, trace=False):
    if key not in _prog_cache:
        p = builder()
        nc = p.finalize()
        _prog_cache[key] = (p, nc)
    p, nc = _prog_cache[key]
    res = run_bass_kernel_spmd(nc, in_maps, core_ids=list(range(NCORES)))
    return res.results



def emit_mod_vectors(prog, cm, cT_in, wmod_in, tag=""):
    nc = prog.nc
    c8 = prog.sbuf("mv_c8" + tag, [128, 8], F32)
    sc8 = prog.sbuf("mv_sc8" + tag, [128, 8], F32)
    wbuf = prog.sbuf("mv_wbuf" + tag, [128, 4 * 128], F32)
    modx = prog.sbuf("mv_modx" + tag, [128, 64], F32)
    mps = cm.mm_ps[0]

    prog.dma("sync", [(c8[:, :], cT_in)])

    def silu_c(eng):
        return eng.activation(sc8[:, :], c8[:, :], AF.Silu)
    prog.phase("scalar", silu_c)


    for m in range(48):
        for hf in range(2):
            prog.dma("sync", [(wbuf[:, k * 128:(k + 1) * 128],
                               wmod_in[(4 * hf + k) * 128:(4 * hf + k + 1) * 128,
                                       m * 128:(m + 1) * 128]) for k in range(4)])

            def mm(eng, m=m, hf=hf):
                last = None
                for k in range(4):
                    last = eng.matmul(mps[:, 0:1], wbuf[:, k * 128:(k + 1) * 128],
                                      sc8[:, 4 * hf + k:4 * hf + k + 1],
                                      start=(hf == 0 and k == 0), stop=(hf == 1 and k == 3))
                return last
            prog.phase("tensor", mm)

        def ev(eng, m=m):
            return eng.copy(modx[:, m:m + 1], mps[:, 0:1])
        prog.phase("scalar", ev)

    def fix(eng):
        for base in (16, 40):
            sl = modx[:, base:base + 8]
            eng.tensor_scalar(cm.sq[:, 0:8], sl, 0.0, None, ALU.is_ge)
            eng.tensor_scalar_mul(cm.sq[:, 0:8], cm.sq[:, 0:8], 2.0)
            eng.tensor_scalar_add(cm.sq[:, 0:8], cm.sq[:, 0:8], -1.0)
            eng.tensor_scalar_mul(cm.sq[:, 8:16], sl, -1.0)
            eng.tensor_tensor(cm.sq[:, 8:16], cm.sq[:, 8:16], sl, ALU.max)
            eng.tensor_scalar_max(cm.sq[:, 8:16], cm.sq[:, 8:16], 0.25)
            eng.tensor_tensor(sl, cm.sq[:, 0:8], cm.sq[:, 8:16], ALU.mult)
        eng.tensor_scalar_add(modx[:, 48:56], modx[:, 8:16], 1.0)
        return eng.tensor_scalar_add(modx[:, 56:64], modx[:, 32:40], 1.0)
    prog.phase("vector", fix)
    return modx


def build_l0a():
    p = Prog("l0a")
    xT = p.inp("xT", [1024, TC])
    hmask = p.inp("hmask", [128, H])
    cT8 = p.inp("cT8", [128, 8])
    wmod = p.inp("wmod", [1024, 6144])
    lng8 = p.inp("lng8", [128, 8])
    lnb8 = p.inp("lnb8", [128, 8])
    w_inT = p.inp("w_inT", [1024, 4096])
    convw = p.inp("convw", [128, 64])
    convb = p.inp("convb", [128, 16])
    BmT = p.inp("BmT", [2048, 16])
    AT = p.inp("AT", [16, 16])

    S0_o = p.out("S0", [16, C])
    hdbg_o = p.out("hdbg", [1024, TC])
    xc_o = p.out("xc", [2048, C])
    z_o = p.out("z", [2048, C])

    cm = Common(p)
    x_t = [p.sbuf(f"x{k}", [128, TC]) for k in range(8)]
    xn_t = [p.sbuf(f"xn{k}", [128, TC], BF16) for k in range(8)]
    cwb = p.sbuf("cwb", [128, 64], F32)
    cbb = p.sbuf("cbb", [128, 16], F32)
    hmk = p.sbuf("hmk", [128, H], F32)
    bmb = p.sbuf("bmb", [128, 16 * 16], F32)
    wkb = [p.sbuf(f"wkb{i}", [128, 8 * 128], BF16) for i in range(2)]
    xp = p.sbuf("xp", [128, TC], F32)
    cxs = p.sbuf("cxs", [128, C], F32)
    V = p.sbuf("V", [16, TC], F32)
    v_ps = p.psum("v_ps", [16, 512])
    Sb = [p.sbuf("Sb0", [16, C]), p.sbuf("Sb1", [16, C])]
    Pk = p.sbuf("Pk", [16, 16 * 11], F32)
    pq_ps = p.psum("pq_ps", [16, 16])
    ptmp = p.dram("ptmp", [16, 16])

    lngb = p.sbuf("lngb", [128, 16], F32)
    prog_loads = [(x_t[k][:, :], xT[k * 128:(k + 1) * 128, :]) for k in range(8)]
    prog_loads += [(cwb[:, :], convw), (cbb[:, :], convb), (hmk[:, :], hmask),
                   (lngb[:, 0:8], lng8), (lngb[:, 8:16], lnb8)]
    p.dma("sync", prog_loads)
    p.dma("sync", [(bmb[:, k * 16:(k + 1) * 16], BmT[k * 128:(k + 1) * 128, :]) for k in range(16)])
    p.dma("sync", [(Pk[:, 0:16], AT)])

    modx = emit_mod_vectors(p, cm, cT8, wmod)

    emit_ln(p, cm, "ln1", x_t, x_t, TC,
            gamma=[modx[:, 48 + k:49 + k] for k in range(8)],
            beta=[modx[:, 0 + k:1 + k] for k in range(8)])
    p.dma("gpsimd", [(hdbg_o[k * 128:(k + 1) * 128, :], x_t[k][:, :]) for k in range(8)])
    emit_ln(p, cm, "ln2", x_t, xn_t, TC,
            gamma=[lngb[:, k:k + 1] for k in range(8)],
            beta=[lngb[:, 8 + k:9 + k] for k in range(8)])

    for m in range(32):
        wb = wkb[m % 2]
        p.dma("gpsimd", [(wb[:, k * 128:(k + 1) * 128],
                          w_inT[k * 128:(k + 1) * 128, m * 128:(m + 1) * 128]) for k in range(8)])
        for (o, w) in TCC:
            ps = cm.mm_ps[0]

            def mm(eng, m=m, o=o, w=w, ps=ps, wb=wb):
                last = None
                for k in range(8):
                    last = eng.matmul(ps[:, 0:w], wb[:, k * 128:(k + 1) * 128],
                                      xn_t[k][:, o:o + w], start=(k == 0), stop=(k == 7))
                return last
            p.phase("tensor", mm)

            def ev(eng, m=m, o=o, w=w, ps=ps):
                return eng.copy(xp[:, o:o + w], ps[:, 0:w])
            p.phase("scalar", ev)

        if m < 16:
            def mask(eng, m=m):
                return eng.tensor_tensor(xp[:, 0:H], xp[:, 0:H], hmk[:, :], ALU.mult)
            p.phase("vector", mask)

            for ci, (o, w) in enumerate(TCC):
                def vmm2(eng, m=m, o=o, w=w):
                    return eng.matmul(v_ps[:, 0:w], bmb[:, m * 16:(m + 1) * 16], xp[:, o:o + w],
                                      start=True, stop=True)
                p.phase("tensor", vmm2)

                def vev(eng, o=o, w=w):
                    return eng.copy(cm.bcs1[0:16, 0:w], v_ps[:, 0:w])
                p.phase("scalar", vev)
                if m == 0:
                    def vadd(eng, o=o, w=w):
                        return eng.tensor_copy(V[:, o:o + w], cm.bcs1[0:16, 0:w])
                else:
                    def vadd(eng, o=o, w=w):
                        return eng.tensor_tensor(V[:, o:o + w], V[:, o:o + w], cm.bcs1[0:16, 0:w], ALU.add)
                p.phase("vector", vadd)

            def conv(eng, m=m):
                eng.tensor_scalar_mul(cxs[:, 0:C], xp[:, 0:C], cwb[:, 4 * m:4 * m + 1])
                for j in range(1, 4):
                    eng.scalar_tensor_tensor(cxs[:, 0:C], xp[:, j:j + C], cwb[:, 4 * m + j:4 * m + j + 1],
                                             cxs[:, 0:C], ALU.mult, ALU.add)
                return None
            p.phase("vector", conv)

            def sil(eng, m=m):
                return eng.activation(cxs[:, 0:C], cxs[:, 0:C], AF.Silu, bias=cbb[:, m:m + 1])
            p.phase("scalar", sil)
            p.dma("gpsimd", [(xc_o[m * 128:(m + 1) * 128, :], cxs[:, 0:C])])
        else:
            p.dma("gpsimd", [(z_o[(m - 16) * 128:(m - 15) * 128, :], xp[:, H:TC])])

    for k in range(10):
        src = Pk[:, 16 * k:16 * k + 16]
        dst = Pk[:, 16 * (k + 1):16 * (k + 1) + 16]
        p.dma("sync", [(ptmp[:, :], src)])
        p.dma("sync", [(cm.sq[0:16, 0:16], ptmp.rearrange("a b -> b a"))], allow_slow=True)

        def sq(eng, k=k, src=src):
            return eng.matmul(pq_ps[:, :], cm.sq[0:16, 0:16], src, start=True, stop=True)
        p.phase("tensor", sq)

        def ev(eng, dst=dst):
            return eng.copy(dst, pq_ps[:, :])
        p.phase("scalar", ev)

    def initS(eng):
        return eng.tensor_copy(Sb[0][:, :], V[:, H:TC])
    p.phase("vector", initS)
    cur = 0
    for k in range(11):
        d = 1 << k
        src_t, dst_t = Sb[cur], Sb[1 - cur]
        def cpre(eng, d=d, src_t=src_t, dst_t=dst_t):
            return eng.tensor_copy(dst_t[:, 0:d], src_t[:, 0:d])
        p.phase("vector", cpre)
        for (o, w) in chunks_of(C):
            lo = max(o, d)
            hi = o + w
            if lo >= hi:
                continue
            ww = hi - lo

            def mmk(eng, k=k, lo=lo, ww=ww, src_t=src_t):
                return eng.matmul(cm.mm_ps[0][0:16, 0:ww], Pk[:, 16 * k:16 * k + 16],
                                  src_t[:, lo - (1 << k):lo - (1 << k) + ww],
                                  start=True, stop=True)
            p.phase("tensor", mmk)

            def cpk(eng, ww=ww):
                return eng.copy(cm.bcs2[0:16, 0:ww], cm.mm_ps[0][0:16, 0:ww])
            p.phase("scalar", cpk)

            def addk(eng, lo=lo, ww=ww, src_t=src_t, dst_t=dst_t):
                return eng.tensor_tensor(dst_t[:, lo:lo + ww], src_t[:, lo:lo + ww],
                                         cm.bcs2[0:16, 0:ww], ALU.add)
            p.phase("vector", addk)
        cur = 1 - cur
    p.dma("gpsimd", [(S0_o, Sb[cur][:, :])])
    return p



def _r8(v):
    return np.ascontiguousarray(v.reshape(8, 128).T)


def _r16(v):
    return np.ascontiguousarray(v.reshape(16, 128).T)


def l0a_inputs(I):
    x2d = np.ascontiguousarray(I["L0_attn_x"][0].T)
    shards = shard_tokens(x2d)
    wmod = np.ascontiguousarray(I["L0_wmod"])
    cT8 = _r8(I["L0_c"][0])
    lng8 = _r8(I["L0_attn_ln_gamma"])
    lnb8 = _r8(I["L0_attn_ln_beta"])
    w_inT = np.ascontiguousarray(I["L0_attn_W_in"].T)
    cw = I["L0_attn_conv_w"].reshape(2048, 4)
    convw = np.ascontiguousarray(cw.reshape(16, 128, 4).transpose(1, 0, 2).reshape(128, 64))
    convb = _r16(I["L0_attn_conv_b"])
    BmT = np.ascontiguousarray(I["L0_attn_Bm"].T)
    AT = np.ascontiguousarray(I["L0_attn_A"].T)
    maps = []
    for r in range(NCORES):
        hm = np.zeros((128, H), np.float32) if r == 0 else np.ones((128, H), np.float32)
        maps.append(dict(xT=shards[r], hmask=hm, cT8=cT8, wmod=wmod, lng8=lng8,
                         lnb8=lnb8, w_inT=w_inT, convw=convw, convb=convb, BmT=BmT, AT=AT))
    return maps


def emit_l0scan(p, cm, name, Vap, Pk, out_sb):
    Sb = out_sb

    def initS(eng):
        return eng.tensor_copy(Sb[0][:, :], Vap)
    p.phase("vector", initS)
    cur = 0
    for k in range(11):
        d = 1 << k
        src_t, dst_t = Sb[cur], Sb[1 - cur]

        def cpre(eng, d=d, src_t=src_t, dst_t=dst_t):
            return eng.tensor_copy(dst_t[:, 0:d], src_t[:, 0:d])
        p.phase("vector", cpre)
        for (o, w) in chunks_of(C):
            lo = max(o, d)
            hi = o + w
            if lo >= hi:
                continue
            ww = hi - lo

            def mmk(eng, k=k, lo=lo, ww=ww, src_t=src_t, d=d):
                return eng.matmul(cm.mm_ps[0][0:16, 0:ww], Pk[:, 16 * k:16 * k + 16],
                                  src_t[:, lo - d:lo - d + ww], start=True, stop=True)
            p.phase("tensor", mmk)

            def cpk(eng, ww=ww):
                return eng.copy(cm.bcs2[0:16, 0:ww], cm.mm_ps[0][0:16, 0:ww])
            p.phase("scalar", cpk)

            def addk(eng, lo=lo, ww=ww, src_t=src_t, dst_t=dst_t):
                return eng.tensor_tensor(dst_t[:, lo:lo + ww], src_t[:, lo:lo + ww],
                                         cm.bcs2[0:16, 0:ww], ALU.add)
            p.phase("vector", addk)
        cur = 1 - cur
    return Sb[cur]


def emit_powers(p, cm, Pk, nsq, ps):
    ptmp = p.dram("ptmp_" + str(nsq), [16, 16])
    for k in range(nsq):
        src = Pk[:, 16 * k:16 * k + 16]
        dst = Pk[:, 16 * (k + 1):16 * (k + 1) + 16]
        p.dma("sync", [(ptmp[:, :], src)])
        p.dma("sync", [(cm.sq[0:16, 0:16], ptmp.rearrange("a b -> b a"))], allow_slow=True)

        def sq(eng, src=src):
            return eng.matmul(ps[:, 0:16], cm.sq[0:16, 0:16], src, start=True, stop=True)
        p.phase("tensor", sq)

        def ev(eng, dst=dst):
            return eng.copy(dst, ps[:, 0:16])
        p.phase("scalar", ev)


def build_l0b():
    p = Prog("l0b")
    xT = p.inp("xT", [1024, C])
    S0_i = p.inp("S0", [16, C])
    IordT = p.inp("IordT", [16, 8])
    AT = p.inp("AT", [16, 16])
    Cm_i = p.inp("Cm", [16, 2048])
    woT = p.inp("woT", [2048, 1024])
    cT8 = p.inp("cT8", [128, 8])
    wmod = p.inp("wmod", [1024, 6144])
    xc_i = p.inp("xc", [2048, C])
    z_i = p.inp("z", [2048, C])
    h_i = p.inp("hsp", [1024, TC])
    x1_o = p.out("x1out", [1024, C])

    cm = Common(p)
    x_t = [p.sbuf(f"x{k}", [128, C]) for k in range(8)]
    p.dma("sync", [(x_t[k][:, :], xT[k * 128:(k + 1) * 128, :]) for k in range(8)])
    modx = emit_mod_vectors(p, cm, cT8, wmod)

    Pk = p.sbuf("Pk", [16, 16 * 12], F32)
    pq_ps = p.psum("pq_ps", [16, 16])
    p.dma("sync", [(Pk[:, 0:16], AT)])
    emit_powers(p, cm, Pk, 11, pq_ps)
    P2048 = Pk[:, 16 * 11:16 * 12]
    Wp = p.sbuf("Wp", [16, 16 * 8], F32)
    iord = p.sbuf("iord", [16, 8], F32)
    p.dma("sync", [(Wp[:, 0:16], AT), (iord[:, :], IordT)])
    wtmp = p.dram("wtmp", [16, 16])
    for j in range(7):
        src = Wp[:, 16 * j:16 * j + 16]
        dst = Wp[:, 16 * (j + 1):16 * (j + 1) + 16]
        p.dma("sync", [(wtmp[:, :], src)])
        p.dma("sync", [(cm.sq[0:16, 0:16], wtmp.rearrange("a b -> b a"))], allow_slow=True)

        def mmw(eng, src=src):
            return eng.matmul(pq_ps[:, 0:16], cm.sq[0:16, 0:16], P2048, start=True, stop=True)
        p.phase("tensor", mmw)

        def evw(eng, dst=dst):
            return eng.copy(dst, pq_ps[:, 0:16])
        p.phase("scalar", evw)

    ha_ps = p.psum("ha_ps", [1, 16])
    harow = p.sbuf("harow", [1, 16], F32)

    def mmha(eng):
        last = None
        for j in range(8):
            last = eng.matmul(ha_ps[0:1, 0:16], iord[:, j:j + 1], Wp[:, 16 * j:16 * j + 16],
                              start=(j == 0), stop=(j == 7))
        return last
    p.phase("tensor", mmha)

    def evha(eng):
        return eng.copy(harow[0:1, 0:16], ha_ps[0:1, 0:16])
    p.phase("scalar", evha)

    hacol = p.dram("hacol", [16, 1])
    V2 = p.sbuf("V2", [16, C], F32)

    def clr(eng):
        return eng.memset(V2[:, :], 0.0)
    p.phase("vector", clr)
    hatmp = p.dram("hatmp", [1, 16])
    p.dma("sync", [(hatmp[:, :], harow[0:1, :])])
    p.dma("sync", [(V2[:, 0:1], hatmp.rearrange("a b -> b a"))], allow_slow=True)

    Sb = [p.sbuf("Sb0", [16, C]), p.sbuf("Sb1", [16, C])]
    cfin = emit_l0scan(p, cm, "corr", V2[:, :], Pk, Sb)

    for (o, w) in CC:
        p.dma("sync", [(cm.bcs1[0:16, 0:w], S0_i[:, o:o + w])])

        def sadd(eng, o=o, w=w):
            return eng.tensor_tensor(cfin[:, o:o + w], cfin[:, o:o + w], cm.bcs1[0:16, 0:w], ALU.add)
        p.phase("vector", sadd)

    Cms = p.sbuf("Cms", [16, 2048], F32)
    p.dma("sync", [(Cms[:, :], Cm_i)])
    xcs = p.sbuf("xcs", [128, 512], F32)
    zs = p.sbuf("zs", [128, 512], F32)
    yb = [p.sbuf(f"yb{m}", [128, C], BF16) for m in range(16)]
    for m in range(16):
        for (o, w) in CC:
            p.dma("sync", [(xcs[:, 0:w], xc_i[m * 128:(m + 1) * 128, o:o + w]),
                           (zs[:, 0:w], z_i[m * 128:(m + 1) * 128, o:o + w])])

            def sig(eng, o=o, w=w):
                return eng.activation(zs[:, 0:w], zs[:, 0:w], AF.Sigmoid)
            p.phase("scalar", sig)

            def gmm(eng, m=m, o=o, w=w):
                return eng.matmul(cm.mm_ps[0][:, 0:w], Cms[:, m * 128:(m + 1) * 128],
                                  cfin[:, o:o + w], start=True, stop=True)
            p.phase("tensor", gmm)

            def gev(eng, o=o, w=w):
                return eng.copy(cm.bcs1[:, 0:w], cm.mm_ps[0][:, 0:w])
            p.phase("scalar", gev)

            def gfin(eng, m=m, o=o, w=w):
                eng.tensor_tensor(cm.bcs1[:, 0:w], cm.bcs1[:, 0:w], xcs[:, 0:w], ALU.add)
                return eng.tensor_tensor(yb[m][:, o:o + w], cm.bcs1[:, 0:w], zs[:, 0:w], ALU.mult)
            p.phase("vector", gfin)

    wob = p.sbuf("wob", [128, 16 * 128], BF16)
    for m in range(8):
        p.dma("gpsimd", [(wob[:, k * 128:(k + 1) * 128],
                          woT[k * 128:(k + 1) * 128, m * 128:(m + 1) * 128]) for k in range(16)])
        for (o, w) in CC:
            p.dma("sync", [(xcs[:, 0:w], h_i[m * 128:(m + 1) * 128, H + o:H + o + w])])
            def rmm(eng, m=m, o=o, w=w):
                last = None
                for k in range(16):
                    last = eng.matmul(cm.mm_ps[1][:, 0:w], wob[:, k * 128:(k + 1) * 128],
                                      yb[k][:, o:o + w], start=(k == 0), stop=(k == 15))
                return last
            p.phase("tensor", rmm)

            def rev(eng, o=o, w=w):
                return eng.copy(cm.bcs2[:, 0:w], cm.mm_ps[1][:, 0:w])
            p.phase("scalar", rev)

            def rfin(eng, m=m, o=o, w=w):
                eng.tensor_tensor(cm.bcs2[:, 0:w], cm.bcs2[:, 0:w], xcs[:, 0:w], ALU.add)
                return eng.scalar_tensor_tensor(x_t[m][:, o:o + w], cm.bcs2[:, 0:w],
                                                modx[:, 16 + m:17 + m], x_t[m][:, o:o + w],
                                                ALU.mult, ALU.add)
            p.phase("vector", rfin)
        p.dma("gpsimd", [(x1_o[m * 128:(m + 1) * 128, :], x_t[m][:, :])])
    return p


def l0b_inputs(I, l0a_res):
    x2d = I["L0_attn_x"][0].T
    AT = np.ascontiguousarray(I["L0_attn_A"].T)
    cT8 = _r8(I["L0_c"][0])
    woT = np.ascontiguousarray(I["L0_attn_W_out"].T)
    Ivals = [l0a_res[q]["S0"][:, -1] for q in range(NCORES)]
    maps = []
    for r in range(NCORES):
        IordT = np.zeros((16, 8), np.float32)
        for j in range(8):
            q = r - 1 - j
            if q >= 0:
                IordT[:, j] = Ivals[q]
        maps.append(dict(xT=np.ascontiguousarray(x2d[:, r * C:(r + 1) * C]),
                         S0=l0a_res[r]["S0"], IordT=IordT, AT=AT,
                         Cm=I["L0_attn_Cm"], woT=woT, cT8=cT8, wmod=I["L0_wmod"],
                         xc=l0a_res[r]["xc"], z=l0a_res[r]["z"], hsp=l0a_res[r]["hdbg"]))
    return maps


GOLDEN_CENTER = np.float32(1.0 / np.e)
GOLDEN_LOWER = np.float32(0.5 - np.log(4.0 / 3.0))
GOLDEN_UPPER = np.float32(0.5)


def build_l0c():
    p = Prog("l0c")
    x1T = p.inp("x1T", [1024, C])
    cT8 = p.inp("cT8", [128, 8])
    wmod = p.inp("wmod", [1024, 6144])
    Wr_i = p.inp("Wr", [1024, 8])
    temp_i = p.inp("temp", [1, 1])
    Wg_i = p.inp("Wg", [8, 1024, 1024])
    Wu_i = p.inp("Wu", [8, 1024, 1024])
    Wd_i = p.inp("Wd", [8, 1024, 1024])
    eye8_i = p.inp("eye8", [8, 8])
    eye128_i = p.inp("eye128", [128, 128])
    cT8b = p.inp("cT8b", [128, 8])
    wmodb = p.inp("wmodb", [1024, 6144])
    x2_o = p.out("x2out", [1024, C])
    h3_o = p.out("h3out", [1024, C])

    cm = Common(p)
    wrs = p.sbuf("wrs", [128, 8 * 8], F32)
    e8 = p.sbuf("e8", [8, 8], F32)
    e128 = p.sbuf("e128", [128, 128], F32)
    tmp11 = p.sbuf("tmp11", [1, 1], F32)
    invt8 = p.sbuf("invt8", [8, 1], F32)
    inh = p.sbuf("inh", [8, C], F32)
    wfm = p.sbuf("wfm", [8, C], F32)
    ones8 = p.sbuf("ones8", [128, 8], F32)
    rw = p.sbuf("rw", [128, 8 * 6], F32)
    rc = p.sbuf("rc", [128, 8], F32)
    h2b = [p.sbuf(f"h2b{k}", [128, C], BF16) for k in range(8)]
    web = p.sbuf("web", [128, 8 * 128], BF16)
    gst = p.sbuf("gst", [128, C], F32)
    wbce = p.sbuf("wbce", [128, C], BF16)
    rt_ps = p.psum("rt_ps", [8, 512])
    tt_ps = p.psum("tt_ps", [128, 8])
    bt_ps = p.psum("bt_ps", [8, 128])
    modx = emit_mod_vectors(p, cm, cT8, wmod)
    x_t = [p.sbuf(f"x{k}", [128, C]) for k in range(8)]
    p.dma("sync", [(x_t[k][:, :], x1T[k * 128:(k + 1) * 128, :]) for k in range(8)])
    emit_ln(p, cm, "ln4", x_t, x_t, C,
            gamma=[modx[:, 56 + k:57 + k] for k in range(8)],
            beta=[modx[:, 24 + k:25 + k] for k in range(8)])
    h2f = x_t
    p.dma("sync", [(wrs[:, k * 8:(k + 1) * 8], Wr_i[k * 128:(k + 1) * 128, :]) for k in range(8)]
          + [(e8[:, :], eye8_i), (e128[:, :], eye128_i), (tmp11[:, :], temp_i)])

    def invt(eng):
        eng.reciprocal(tmp11[0:1, 0:1], tmp11[0:1, 0:1])
        eng.memset(ones8[:, :], 0.0)
        return eng.tensor_scalar_add(ones8[:, :], ones8[:, :], 1.0)
    p.phase("vector", invt)
    p.dma("sync", [(invt8[j:j + 1, 0:1], tmp11[0:1, 0:1]) for j in range(8)])

    def negt(eng):
        return eng.tensor_scalar_mul(invt8[:, 0:1], invt8[:, 0:1], -1.0)
    p.phase("vector", negt)

    for (o, w) in CC:
        def rmm(eng, o=o, w=w):
            last = None
            for k in range(8):
                last = eng.matmul(rt_ps[:, 0:w], wrs[:, k * 8:(k + 1) * 8],
                                  h2f[k][:, o:o + w], start=(k == 0), stop=(k == 7))
            return last
        p.phase("tensor", rmm)

        def rev(eng, o=o, w=w):
            return eng.copy(inh[:, o:o + w], rt_ps[:, 0:w])
        p.phase("scalar", rev)

    def sig1(eng):
        return eng.activation(inh[:, :], inh[:, :], AF.Exp, scale=invt8[:, 0:1])
    p.phase("scalar", sig1)

    def sig2(eng):
        eng.tensor_scalar_add(inh[:, :], inh[:, :], 1.0)
        return eng.reciprocal(inh[:, :], inh[:, :])
    p.phase("vector", sig2)

    for tt in range(C // 128):
        def ttr(eng, tt=tt):
            return eng.matmul(tt_ps[:, 0:8], inh[:, tt * 128:(tt + 1) * 128], e8[:, :],
                              start=True, stop=True)
        p.phase("tensor", ttr)

        def tte(eng):
            return eng.copy(rw[:, 0:8], tt_ps[:, 0:8])
        p.phase("scalar", tte)

        def rmath(eng):
            w0 = rw[:, 0:8]; dist = rw[:, 8:16]; wz = rw[:, 16:24]
            fb = rw[:, 24:32]; t1 = rw[:, 32:40]; t2 = rw[:, 40:48]
            eng.tensor_scalar_sub(dist, w0, float(GOLDEN_CENTER))
            eng.tensor_scalar_mul(t1, dist, -1.0)
            eng.tensor_tensor(dist, dist, t1, ALU.max)
            eng.tensor_scalar(t1, w0, float(GOLDEN_LOWER), None, ALU.is_ge)
            eng.tensor_scalar(t2, w0, float(GOLDEN_UPPER), None, ALU.is_le)
            return eng.tensor_tensor(t1, t1, t2, ALU.mult)
        p.phase("vector", rmath)

        def rexp(eng):
            dist = rw[:, 8:16]; wz = rw[:, 16:24]; fb = rw[:, 24:32]
            eng.activation(wz, dist, AF.Exp, scale=-10.0)
            return eng.activation(fb, dist, AF.Exp, scale=np.float32(-1.0 / 0.3))
        p.phase("scalar", rexp)

        def rmath2(eng):
            w0 = rw[:, 0:8]; dist = rw[:, 8:16]; wz = rw[:, 16:24]
            fb = rw[:, 24:32]; t1 = rw[:, 32:40]; t2 = rw[:, 40:48]
            m1 = rc[:, 0:1]; ne = rc[:, 1:2]; m2 = rc[:, 2:3]; sm = rc[:, 3:4]
            eng.tensor_tensor(wz, wz, t1, ALU.mult)
            eng.tensor_reduce(ne.rearrange("p (a b) -> p a b", b=1), wz.rearrange("p (a b) -> p a b", b=8), AX.X, ALU.add)
            eng.tensor_scalar(ne, ne, 1e-8, None, ALU.is_lt)
            eng.tensor_reduce(m1.rearrange("p (a b) -> p a b", b=1), fb.rearrange("p (a b) -> p a b", b=8), AX.X, ALU.max)
            eng.tensor_scalar(t1, fb, m1[:, 0:1], None, ALU.is_ge)
            eng.tensor_tensor_scan(t2, ones8[:, 0:8], t1, 0.0, ALU.mult, ALU.add)
            eng.tensor_tensor(t2, t2, t1, ALU.subtract)
            eng.tensor_scalar(t2, t2, 0.0, None, ALU.is_equal)
            eng.tensor_tensor(t1, t1, t2, ALU.mult)
            eng.scalar_tensor_tensor(t2, t1, -1e30, fb, ALU.mult, ALU.add)
            eng.tensor_reduce(m2.rearrange("p (a b) -> p a b", b=1), t2.rearrange("p (a b) -> p a b", b=8), AX.X, ALU.max)
            eng.tensor_scalar(t2, t2, m2[:, 0:1], None, ALU.is_ge)
            eng.tensor_tensor_scan(rw[:, 0:8], ones8[:, 0:8], t2, 0.0, ALU.mult, ALU.add)
            eng.tensor_tensor(rw[:, 0:8], rw[:, 0:8], t2, ALU.subtract)
            eng.tensor_scalar(rw[:, 0:8], rw[:, 0:8], 0.0, None, ALU.is_equal)
            eng.tensor_tensor(t2, t2, rw[:, 0:8], ALU.mult)
            eng.tensor_tensor(t1, t1, t2, ALU.add)
            eng.tensor_tensor(fb, fb, t1, ALU.mult)
            eng.tensor_reduce(sm.rearrange("p (a b) -> p a b", b=1), fb.rearrange("p (a b) -> p a b", b=8), AX.X, ALU.add)
            eng.tensor_scalar_max(sm, sm, 1e-8)
            eng.reciprocal(sm, sm)
            eng.tensor_scalar(fb, fb, sm[:, 0:1], None, ALU.mult)
            eng.tensor_tensor(t2, fb, wz, ALU.subtract)
            eng.tensor_scalar(t2, t2, ne[:, 0:1], None, ALU.mult)
            eng.tensor_tensor(wz, wz, t2, ALU.add)
            eng.tensor_reduce(sm.rearrange("p (a b) -> p a b", b=1), wz.rearrange("p (a b) -> p a b", b=8), AX.X, ALU.add)
            eng.tensor_scalar_max(sm, sm, 1e-8)
            eng.reciprocal(sm, sm)
            return eng.tensor_scalar(wz, wz, sm[:, 0:1], None, ALU.mult)
        p.phase("vector", rmath2)

        def btr(eng, tt=tt):
            return eng.matmul(bt_ps[:, 0:128], rw[:, 16:24], e128[:, :], start=True, stop=True)
        p.phase("tensor", btr)

        def bte(eng, tt=tt):
            return eng.copy(wfm[:, tt * 128:(tt + 1) * 128], bt_ps[:, 0:128])
        p.phase("scalar", bte)

    for k in range(8):
        def cst(eng, k=k):
            return eng.copy(h2b[k][:, :], h2f[k][:, :])
        p.phase("scalar", cst)

    p.free(*x_t)
    moe = [p.sbuf(f"moe{m}", [128, C], F32) for m in range(8)]
    hpb = [p.sbuf(f"hpb{m}", [128, C], BF16) for m in range(8)]
    wrow = cm.tp
    for e in range(8):
        p.dma("sync", [(wrow[0:1, 0:C], wfm[e:e + 1, :])])
        for (o, w) in CC:
            def wbm(eng, o=o, w=w):
                return eng.matmul(cm.bc_ps[:, 0:w], cm.ones_row[0:1, :], wrow[0:1, o:o + w],
                                  start=True, stop=True)
            p.phase("tensor", wbm)

            def wbe(eng, o=o, w=w):
                return eng.copy(wbce[:, o:o + w], cm.bc_ps[:, 0:w])
            p.phase("scalar", wbe)

        for m in range(8):
            p.dma("gpsimd", [(web[:, k * 128:(k + 1) * 128],
                              Wg_i[e, k * 128:(k + 1) * 128, m * 128:(m + 1) * 128]) for k in range(8)])
            for (o, w) in CC:
                def gmm(eng, o=o, w=w):
                    last = None
                    for k in range(8):
                        last = eng.matmul(cm.mm_ps[0][:, 0:w], web[:, k * 128:(k + 1) * 128],
                                          h2b[k][:, o:o + w], start=(k == 0), stop=(k == 7))
                    return last
                p.phase("tensor", gmm)

                def gev(eng, o=o, w=w):
                    return eng.copy(gst[:, o:o + w], cm.mm_ps[0][:, 0:w])
                p.phase("scalar", gev)

            def gsil(eng):
                return eng.activation(gst[:, :], gst[:, :], AF.Silu)
            p.phase("scalar", gsil)

            p.dma("gpsimd", [(web[:, k * 128:(k + 1) * 128],
                              Wu_i[e, k * 128:(k + 1) * 128, m * 128:(m + 1) * 128]) for k in range(8)])
            for (o, w) in CC:
                def umm(eng, o=o, w=w):
                    last = None
                    for k in range(8):
                        last = eng.matmul(cm.mm_ps[1][:, 0:w], web[:, k * 128:(k + 1) * 128],
                                          h2b[k][:, o:o + w], start=(k == 0), stop=(k == 7))
                    return last
                p.phase("tensor", umm)

                def uev(eng, o=o, w=w):
                    return eng.copy(cm.bcs1[:, 0:w], cm.mm_ps[1][:, 0:w])
                p.phase("scalar", uev)

                def ufin(eng, m=m, o=o, w=w):
                    eng.tensor_tensor(cm.bcs1[:, 0:w], cm.bcs1[:, 0:w], gst[:, o:o + w], ALU.mult)
                    return eng.tensor_tensor(hpb[m][:, o:o + w], cm.bcs1[:, 0:w], wbce[:, o:o + w], ALU.mult)
                p.phase("vector", ufin)

        for md in range(8):
            p.dma("gpsimd", [(web[:, k * 128:(k + 1) * 128],
                              Wd_i[e, k * 128:(k + 1) * 128, md * 128:(md + 1) * 128]) for k in range(8)])
            for (o, w) in CC:
                def dmm(eng, o=o, w=w):
                    last = None
                    for k in range(8):
                        last = eng.matmul(cm.mm_ps[0][:, 0:w], web[:, k * 128:(k + 1) * 128],
                                          hpb[k][:, o:o + w], start=(k == 0), stop=(k == 7))
                    return last
                p.phase("tensor", dmm)

                def dev(eng, o=o, w=w):
                    return eng.copy(cm.bcs1[:, 0:w], cm.mm_ps[0][:, 0:w])
                p.phase("scalar", dev)

                if e == 0:
                    def dacc(eng, md=md, o=o, w=w):
                        return eng.tensor_copy(moe[md][:, o:o + w], cm.bcs1[:, 0:w])
                else:
                    def dacc(eng, md=md, o=o, w=w):
                        return eng.tensor_tensor(moe[md][:, o:o + w], moe[md][:, o:o + w],
                                                 cm.bcs1[:, 0:w], ALU.add)
                p.phase("vector", dacc)

    for m in range(8):
        p.dma("sync", [(gst[:, :], x1T[m * 128:(m + 1) * 128, :])])

        def fin(eng, m=m):
            return eng.scalar_tensor_tensor(moe[m][:, :], moe[m][:, :], modx[:, 40 + m:41 + m],
                                            gst[:, :], ALU.mult, ALU.add)
        p.phase("vector", fin)
        p.dma("gpsimd", [(x2_o[m * 128:(m + 1) * 128, :], moe[m][:, :])])
    p.free(*hpb)
    modx2 = emit_mod_vectors(p, cm, cT8b, wmodb, tag="b")
    emit_ln(p, cm, "ln5", moe, moe, C,
            gamma=[modx2[:, 48 + k:49 + k] for k in range(8)],
            beta=[modx2[:, 0 + k:1 + k] for k in range(8)])
    p.dma("gpsimd", [(h3_o[m * 128:(m + 1) * 128, :], moe[m][:, :]) for m in range(8)])
    return p


def l0c_inputs(I, x1_shards):
    cT8 = _r8(I["L0_c"][0])
    maps = []
    eye8 = np.eye(8, dtype=np.float32)
    eye128 = np.eye(128, dtype=np.float32)
    for r in range(NCORES):
        maps.append(dict(x1T=x1_shards[r], cT8=cT8, wmod=I["L0_wmod"], Wr=I["L0_ffn_Wr"],
                         temp=I["L0_ffn_temperature"].reshape(1, 1), Wg=I["L0_ffn_Wg"],
                         Wu=I["L0_ffn_Wu"], Wd=I["L0_ffn_Wd"], eye8=eye8, eye128=eye128,
                         cT8b=_r8(I["L1_c"][0]), wmodb=I["L1_wmod"]))
    return maps




def emit_bcast_pe(p, cm, srow, dst, W):
    banks = [cm.bc_ps, cm.bc2_ps, cm.mm_ps[0], cm.mm_ps[1]]
    cks = chunks_of(W)
    for g in range(0, len(cks), 4):
        grp = cks[g:g + 4]

        def mmb(eng, grp=grp):
            last = None
            for bi, (o, w) in enumerate(grp):
                last = eng.matmul(banks[bi][:, 0:w], cm.ones_row[0:1, :], srow[0:1, o:o + w],
                                  start=True, stop=True)
            return last
        p.phase("tensor", mmb)

        def evb(eng, grp=grp):
            last = None
            for bi, (o, w) in enumerate(grp):
                last = eng.copy(dst[:, o:o + w], banks[bi][:, 0:w])
            return last
        p.phase("scalar", evb)


def emit_bcast_row(p, srow, dst, width=None):
    W = width or dst.shape[1]
    p.dma("sync", [(dst[0:1, 0:W], srow[0:1, 0:W])])
    k = 1
    while k < 128:
        kk = min(k, 128 - k)
        p.dma("sync", [(dst[k:k + kk, 0:W], dst[0:kk, 0:W])])
        k += kk



def build_l1s(kind):
    p = Prog("l1s_" + kind)
    cm = Common(p)
    ios = {}
    for D in ("f", "b"):
        if kind != "first":
            ios["ypc_" + D] = p.inp("ypc_" + D, [2048, TC], BF16)
            ios["szc_" + D] = p.inp("szc_" + D, [2048, TC], BF16)
            ios["ctc_" + D] = p.inp("ctc_" + D, [2048, TC])
            ios["crc_" + D] = p.inp("crc_" + D, [16, TC])
            ios["IoO_" + D] = p.inp("IoO_" + D, [1024, 256])
            ios["EoO_" + D] = p.inp("EoO_" + D, [1024, 256])
            ios["IoP_" + D] = p.inp("IoP_" + D, [1024, 256])
            ios["EoP_" + D] = p.inp("EoP_" + D, [1024, 256])
            ios["nAP_" + D] = p.inp("nAP_" + D, [128, 256])
            ios["owTP_" + D] = p.inp("owTP_" + D, [2048, 1024])
        ios["rT_" + D] = p.inp("rT_" + D, [1024, TC])
        if D == "f":
            ios["hmask"] = p.inp("hmask", [128, H])
        if kind != "last":
            ios["nwb_" + D] = p.inp("nwb_" + D, [128, 16])
            ios["inWT_" + D] = p.inp("inWT_" + D, [1024, 4096])
            ios["cwb_" + D] = p.inp("cwb_" + D, [128, 64])
            ios["cbb_" + D] = p.inp("cbb_" + D, [128, 16])
            ios["xpWT_" + D] = p.inp("xpWT_" + D, [2048, 96])
            ios["dtWT_" + D] = p.inp("dtWT_" + D, [64, 2048])
            ios["dtbb_" + D] = p.inp("dtbb_" + D, [128, 16])
            ios["nA_" + D] = p.inp("nA_" + D, [128, 256])
            ios["Dpb_" + D] = p.inp("Dpb_" + D, [128, 16])
            ios["ypost_" + D] = p.out("ypost_" + D, [2048, C], BF16)
            ios["szo_" + D] = p.out("szo_" + D, [2048, C], BF16)
            ios["cto_" + D] = p.out("cto_" + D, [2048, C])
            ios["cro_" + D] = p.out("cro_" + D, [16, C])
            ios["Io_" + D] = p.out("Io_" + D, [128, 256])
            ios["Eo_" + D] = p.out("Eo_" + D, [128, 256])
            ios["ro_" + D] = p.out("ro_" + D, [1024, TC])
            ios["usp_" + D] = p.dram("usp_" + D, [2048, C], BF16)
        else:
            ios["nfb_" + D] = p.inp("nfb_" + D, [128, 16])
            ios["fh2_" + D] = p.out("fh2_" + D, [1024, C])
        if kind != "first":
            ios["yfd_" + D] = p.dram("yfd_" + D, [2048, TC], BF16)

    web = p.sbuf("web", [128, 16 * 128], BF16)
    upre = p.sbuf("upre", [128, TC], F32)
    cstA = p.sbuf("cstA", [128, C], F32)
    cstB = p.sbuf("cstB", [128, C], F32)
    cstO = p.sbuf("cstO", [128, TC], F32)
    bf1 = p.sbuf("bf1", [128, TC], BF16)
    bf2 = p.sbuf("bf2", [128, TC], BF16)
    browb = p.sbuf("browb", [1, TC], F32)
    nAt, nAPt, prm, cwt, Ist, Est, cte, Hown, Hprv = {}, {}, {}, {}, {}, {}, {}, {}, {}
    for D in ("f", "b"):
        nAt[D] = p.sbuf("nAt_" + D, [128, 256], F32)
        prm[D] = p.sbuf("prm_" + D, [128, 80], F32)
        cwt[D] = p.sbuf("cwt_" + D, [128, 64], F32)
        Ist[D] = p.sbuf("Ist_" + D, [128, 256], F32)
        Est[D] = p.sbuf("Est_" + D, [128, 256], F32)
        cte[D] = p.sbuf("cte_" + D, [128, 16], F32)
        if kind != "first":
            nAPt[D] = p.sbuf("nAPt_" + D, [128, 256], F32)
            Hown[D] = p.sbuf("Hown_" + D, [128, 256], F32)
            Hprv[D] = p.sbuf("Hprv_" + D, [128, 256], F32)
    IEld = p.sbuf("IEld", [128, 256], F32)
    hmk = p.sbuf("hmk", [128, H], F32)
    brow = cm.tp
    p.dma("sync", [(hmk[:, :], ios["hmask"])])

    for D in ("f", "b"):
        if kind != "last":
            p.dma("sync", [(prm[D][:, 0:16], ios["nwb_" + D]), (prm[D][:, 16:32], ios["cbb_" + D]),
                           (prm[D][:, 32:48], ios["dtbb_" + D]), (prm[D][:, 48:64], ios["Dpb_" + D]),
                           (cwt[D][:, :], ios["cwb_" + D]), (nAt[D][:, :], ios["nA_" + D])])
        else:
            p.dma("sync", [(prm[D][:, 0:16], ios["nfb_" + D])])
        if kind != "first":
            p.dma("sync", [(nAPt[D][:, :], ios["nAP_" + D])])
    for D in ("f", "b"):
        tensors = [nAt[D]] if kind != "last" else []
        if kind != "first":
            tensors.append(nAPt[D])
        for tt_ in tensors:
            def nexp(eng, tt_=tt_):
                return eng.activation(tt_[:, :], tt_[:, :], AF.Exp)
            p.phase("scalar", nexp)

            def nneg(eng, tt_=tt_):
                return eng.tensor_scalar_mul(tt_[:, :], tt_[:, :], -1.0)
            p.phase("vector", nneg)

    for D in ("f", "b"):
        if kind != "first":
            for (Ht, Isrc, Esrc) in ((Hown[D], "IoO_", "EoO_"), (Hprv[D], "IoP_", "EoP_")):
                def hz(eng, Ht=Ht):
                    return eng.memset(Ht[:, :], 0.0)
                p.phase("vector", hz)
                for j in range(8):
                    p.dma("sync", [(IEld[:, :], ios[Esrc + D][j * 128:(j + 1) * 128, :])])

                    def hc1(eng, Ht=Ht):
                        return eng.tensor_tensor(Ht[:, :], Ht[:, :], IEld[:, :], ALU.mult)
                    p.phase("vector", hc1)
                    p.dma("sync", [(IEld[:, :], ios[Isrc + D][j * 128:(j + 1) * 128, :])])

                    def hc2(eng, Ht=Ht):
                        return eng.tensor_tensor(Ht[:, :], Ht[:, :], IEld[:, :], ALU.add)
                    p.phase("vector", hc2)

            Cbc = p.sbuf("CbcP" + D, [128, TC], BF16, side="right")
            CbcB = p.sbuf("CbcPB" + D, [128, TC], BF16, side="right")
            for quarter in range(4):
                ctq = [p.sbuf(f"ctq{D}{quarter}{i}", [128, TC], F32, side="right") for i in range(4)]
                qq = [p.sbuf(f"qq{D}{quarter}{i}", [128, TC], F32, side="right") for i in range(4)]
                dgs = [quarter * 4 + i for i in range(4)]
                p.dma("sync", [(ctq[i][:, :], ios["ctc_" + D][dg * 128:(dg + 1) * 128, :])
                               for i, dg in enumerate(dgs)])

                def qz(eng, qq=qq):
                    for i in range(4):
                        eng.memset(qq[i][:, :], 0.0)
                    return None
                p.phase("vector", qz)
                banks = [cm.bc_ps, cm.bc2_ps, cm.mm_ps[0], cm.mm_ps[1]]
                crow2 = [cm.mu, cm.rs]
                Cbc2 = [Cbc, CbcB]
                cstO2 = [cstO, upre]

                def c_dma(si, lag):
                    p.dma("sync", [(crow2[si % 2][0:1, 0:TC], ios["crc_" + D][si:si + 1, :])], lag=lag)

                def c_mm(si, part, lag):
                    cks = chunks_of(TC)[0:4] if part == 0 else chunks_of(TC)[4:]

                    def mm(eng, cks=cks, cr=crow2[si % 2]):
                        last = None
                        for bi, (o, w) in enumerate(cks):
                            last = eng.matmul(banks[bi][:, 0:w], cm.ones_row[0:1, :],
                                              cr[0:1, o:o + w], start=True, stop=True)
                        return last
                    p.phase("tensor", mm, lag=_Lc(lag))

                def c_ev(si, part, lag):
                    cks = chunks_of(TC)[0:4] if part == 0 else chunks_of(TC)[4:]

                    def ev(eng, cks=cks, C_=Cbc2[si % 2]):
                        for bi, (o, w) in enumerate(cks):
                            eng.copy(C_[:, o:o + w], banks[bi][:, 0:w])
                        o, w = cks[-1]
                        return eng.copy(C_[:, o + w - 3:o + w], banks[len(cks) - 1][:, w - 3:w])
                    p.phase("scalar", ev, lag=_Lc(lag))

                def c_A(si, i, lag):
                    col_base = dgs[i] * 16 + si

                    def corrA(eng, ct_=ctq[i], nap_=nAPt[D], col=col_base, co_=cstO2[i % 2]):
                        eng.activation(co_[:, 0:TC], ct_[:, :], AF.Exp,
                                       scale=nap_[:, col:col + 1])
                        return eng.activation(co_[:, TC - 16:TC], ct_[:, TC - 16:TC], AF.Exp,
                                              scale=nap_[:, col:col + 1])
                    p.phase("scalar", corrA, lag=_Lc(lag))

                def c_B(si, i):
                    col_base = dgs[i] * 16 + si

                    def corrB(eng, q_=qq[i], hp_=Hprv[D], ho_=Hown[D], cb_=Cbc2[si % 2],
                              col=col_base, co_=cstO2[i % 2]):
                        eng.scalar_tensor_tensor(co_[:, 0:H], co_[:, 0:H], hp_[:, col:col + 1],
                                                 cb_[:, 0:H], ALU.mult, ALU.mult)
                        eng.scalar_tensor_tensor(co_[:, H:TC], co_[:, H:TC], ho_[:, col:col + 1],
                                                 cb_[:, H:TC], ALU.mult, ALU.mult)
                        return eng.tensor_tensor(q_[:, :], q_[:, :], co_[:, 0:TC], ALU.add)
                    p.phase("vector", corrB)

                c_dma(0, 0)
                c_mm(0, 0, 0); c_ev(0, 0, 0); c_mm(0, 1, 0); c_ev(0, 1, 0)
                c_dma(1, 0)
                for s in range(16):
                    c_A(s, 0, 5 if s > 0 else 0)
                    c_B(s, 0)
                    if s + 1 < 16:
                        c_mm(s + 1, 0, 2)
                    c_A(s, 1, 2)
                    c_B(s, 1)
                    if s + 1 < 16:
                        c_ev(s + 1, 0, 2)
                    c_A(s, 2, 2)
                    c_B(s, 2)
                    if s + 1 < 16:
                        c_mm(s + 1, 1, 2)
                    if s + 2 < 16:
                        c_dma(s + 2, 2)
                    c_A(s, 3, 2)
                    c_B(s, 3)
                    if s + 1 < 16:
                        c_ev(s + 1, 1, 2)
                for i, dg in enumerate(dgs):
                    p.dma("sync", [(bf1[:, :], ios["ypc_" + D][dg * 128:(dg + 1) * 128, :]),
                                   (bf2[:, :], ios["szc_" + D][dg * 128:(dg + 1) * 128, :])])

                    def yf(eng, q_=qq[i]):
                        eng.tensor_tensor(q_[:, :], q_[:, :], bf2[:, :], ALU.mult)
                        eng.tensor_copy(cstO[:, :], bf1[:, :])
                        eng.tensor_tensor(cstO[:, :], cstO[:, :], q_[:, :], ALU.add)
                        return eng.tensor_copy(bf1[:, :], cstO[:, :])
                    p.phase("vector", yf)
                    p.dma("gpsimd", [(ios["yfd_" + D][dg * 128:(dg + 1) * 128, :], bf1[:, :])])
                p.free(*ctq, *qq)
            p.free(Cbc, CbcB)

        r_t = [p.sbuf(f"r{D}{m}", [128, TC], F32, side="right") for m in range(8)]
        if kind == "first":
            p.dma("sync", [(r_t[m][:, :], ios["rT_" + D][m * 128:(m + 1) * 128, :]) for m in range(8)])
        else:
            ow_wb = p.sbuf("owwb" + D, [128, 64 * 128], BF16, side="right")
            ystr = p.sbuf("ystr" + D, [128, 16 * 512], BF16, side="right")
            p.dma("sync", [(r_t[m][:, :], ios["rT_" + D][m * 128:(m + 1) * 128, :])
                           for m in range(8)])
            for mg in range(2):
                p.dma("gpsimd", [(ow_wb[:, (m * 16 + k) * 128:(m * 16 + k + 1) * 128],
                                  ios["owTP_" + D][k * 128:(k + 1) * 128,
                                                   (mg * 4 + m) * 128:(mg * 4 + m + 1) * 128])
                                 for m in range(4) for k in range(16)])
                for ci, (o, w) in enumerate(TCC):
                    p.dma("sync", [(ystr[:, k * 512:k * 512 + w],
                                    ios["yfd_" + D][k * 128:(k + 1) * 128, o:o + w])
                                   for k in range(16)])
                    for m in range(4):
                        def omm(eng, m=m, o=o, w=w):
                            last = None
                            for k in range(16):
                                last = eng.matmul(cm.mm_ps[m % 2][:, 0:w],
                                                  ow_wb[:, (m * 16 + k) * 128:(m * 16 + k + 1) * 128],
                                                  ystr[:, k * 512:k * 512 + w],
                                                  start=(k == 0), stop=(k == 15))
                            return last
                        p.phase("tensor", omm)

                        def oev(eng, m=m, mg=mg, o=o, w=w):
                            return eng.copy(cstO[:, o:o + w], cm.mm_ps[m % 2][:, 0:w])
                        p.phase("scalar", oev)

                        def radd(eng, rt_=r_t[mg * 4 + m], o=o, w=w):
                            return eng.tensor_tensor(rt_[:, o:o + w], rt_[:, o:o + w],
                                                     cstO[:, o:o + w], ALU.add)
                        p.phase("vector", radd)
            p.free(ow_wb, ystr)


        if kind == "last":
            emit_ln(p, cm, "fin" + D, r_t, r_t, TC,
                    gamma=[prm[D][:, k:k + 1] for k in range(8)],
                    beta=[prm[D][:, 8 + k:9 + k] for k in range(8)])
            p.dma("gpsimd", [(ios["fh2_" + D][m * 128:(m + 1) * 128, :], r_t[m][:, H:TC])
                             for m in range(8)])
            p.free(*r_t)
            continue

        p.dma("gpsimd", [(ios["ro_" + D][m * 128:(m + 1) * 128, :], r_t[m][:, :]) for m in range(8)])

        hnb = [p.sbuf(f"hnb{D}{m}", [128, TC], BF16, side="right") for m in range(8)]
        emit_ln(p, cm, "hn" + D, r_t, hnb, TC,
                gamma=[prm[D][:, k:k + 1] for k in range(8)],
                beta=[prm[D][:, 8 + k:9 + k] for k in range(8)])

        def wload(mm_, lag=0):
            hb = (mm_ % 2) * 1024
            p.dma("gpsimd", [(web[:, hb + k * 128:hb + (k + 1) * 128],
                              ios["inWT_" + D][k * 128:(k + 1) * 128, mm_ * 128:(mm_ + 1) * 128])
                             for k in range(8)], lag=lag)
        wload(0)
        for m in range(32):
            for ci, (o, w) in enumerate(TCC):
                def mmx(eng, hnb_=hnb, m=m, o=o, w=w, ci=ci):
                    hb = (m % 2) * 1024
                    last = None
                    for k in range(8):
                        last = eng.matmul(cm.mm_ps[ci % 2][:, 0:w], web[:, hb + k * 128:hb + (k + 1) * 128],
                                          hnb_[k][:, o:o + w], start=(k == 0), stop=(k == 7))
                    return last
                p.phase("tensor", mmx, lag=_Li(1) if ci > 0 else 0)

                def evx(eng, o=o, w=w, ci=ci):
                    return eng.copy(upre[:, o:o + w], cm.mm_ps[ci % 2][:, 0:w])
                p.phase("scalar", evx)
            if m + 1 < 32:
                wload(m + 1)
            if m < 16:
                def hm0(eng):
                    return eng.tensor_tensor(upre[:, 0:H], upre[:, 0:H], hmk[:, :], ALU.mult)
                p.phase("vector", hm0)

                def conv2(eng, m=m, cw_=cwt[D]):
                    eng.tensor_scalar_mul(cstA[:, 0:C], upre[:, 0:C], cw_[:, 4 * m:4 * m + 1])
                    for j in range(1, 4):
                        eng.scalar_tensor_tensor(cstA[:, 0:C], upre[:, j:j + C],
                                                 cw_[:, 4 * m + j:4 * m + j + 1],
                                                 cstA[:, 0:C], ALU.mult, ALU.add)
                    return None
                p.phase("vector", conv2)

                def sil(eng, m=m, pr_=prm[D]):
                    return eng.activation(bf1[:, 0:C], cstA[:, 0:C], AF.Silu, bias=pr_[:, 16 + m:17 + m])
                p.phase("scalar", sil)
                p.dma("gpsimd", [(ios["usp_" + D][m * 128:(m + 1) * 128, :], bf1[:, 0:C])])

            else:
                def silz(eng, m=m):
                    return eng.activation(bf1[:, 0:C], upre[:, H:TC], AF.Silu)
                p.phase("scalar", silz)
                p.dma("gpsimd", [(ios["szo_" + D][(m - 16) * 128:(m - 15) * 128, :], bf1[:, 0:C])])
        p.free(*hnb)
        p.free(*r_t)

        ones_c = p.sbuf("ones_c" + D, [128, C], F32, side="right")
        BCst = p.sbuf("BCst" + D, [32, TC], F32, side="right")
        dtrB = p.sbuf("dtrB" + D, [64, TC], BF16, side="right")

        def initones(eng, ones_c=ones_c):
            eng.memset(ones_c[:, :], 0.0)
            return eng.tensor_scalar_add(ones_c[:, :], ones_c[:, :], 1.0)
        p.phase("vector", initones)

        ustr = p.sbuf("ustr" + D, [128, 16 * 512], BF16, side="right")
        p.dma("gpsimd", [(web[:, k * 96:(k + 1) * 96], ios["xpWT_" + D][k * 128:(k + 1) * 128, :])
                         for k in range(16)])
        for ci, (o, w) in enumerate(CC):
            p.dma("sync", [(ustr[:, k * 512:k * 512 + w],
                            ios["usp_" + D][k * 128:(k + 1) * 128, o:o + w]) for k in range(16)])

            def xmm2(eng, o=o, w=w, ustr=ustr):
                last = None
                for k in range(16):
                    last = eng.matmul(cm.mm_ps[0][0:96, 0:w], web[:, k * 96:(k + 1) * 96],
                                      ustr[:, k * 512:k * 512 + w], start=(k == 0), stop=(k == 15))
                return last
            p.phase("tensor", xmm2)

            def xev(eng, o=o, w=w, dtrB=dtrB, BCst=BCst):
                eng.copy(dtrB[0:64, H + o:H + o + w], cm.mm_ps[0][0:64, 0:w])
                return eng.copy(BCst[0:32, H + o:H + o + w], cm.mm_ps[0][64:96, 0:w])
            p.phase("scalar", xev)
        p.free(ustr)
        p.dma("gpsimd", [(ios["cro_" + D], BCst[16:32, H:TC])])

        for half in range(4):
            dgs = [half * 4 + i for i in range(4)]
            dtb16 = [p.sbuf(f"dt{D}{half}{i}", [128, C], BF16, side="right") for i in range(4)]
            dtu = [p.sbuf(f"du{D}{half}{i}", [128, C], BF16, side="right") for i in range(4)]
            yacc = [p.sbuf(f"ya{D}{half}{i}", [128, C], F32, side="right") for i in range(4)]
            Bbc2 = [p.sbuf("Bbc" + D + str(half) + str(j), [128, C], BF16, side="right") for j in range(2)]
            Cbc2 = [p.sbuf("Cbc" + D + str(half) + str(j), [128, C], BF16, side="right") for j in range(2)]
            cstA2 = [cstA, upre]
            cstO2s = [cstO, p.sbuf("cstO2s" + D + str(half), [128, C], F32, side="right")]
            browB2 = [cm.mu, cm.rs]
            browC2 = [cm.tp, browb]

            for i, dg in enumerate(dgs):
                p.dma("gpsimd", [(web[0:64, 0:128], ios["dtWT_" + D][0:64, dg * 128:(dg + 1) * 128])])
                for (o, w) in CC:
                    def dmm(eng, o=o, w=w, dtrB=dtrB):
                        return eng.matmul(cm.mm_ps[0][:, 0:w], web[0:64, 0:128],
                                          dtrB[0:64, H + o:H + o + w], start=True, stop=True)
                    p.phase("tensor", dmm)

                    def dev(eng, dg=dg, o=o, w=w):
                        return eng.copy(cstA[:, o:o + w], cm.mm_ps[0][:, 0:w])
                    p.phase("scalar", dev)

                def dbias(eng, dg=dg, pr_=prm[D]):
                    return eng.tensor_scalar_add(cstA[:, 0:C], cstA[:, 0:C], pr_[:, 32 + dg:33 + dg])
                p.phase("vector", dbias)

                def sp1(eng, _x=None):
                    eng.tensor_scalar_mul(cstB[:, 0:C], cstA[:, 0:C], -1.0)
                    eng.tensor_tensor(cstB[:, 0:C], cstB[:, 0:C], cstA[:, 0:C], ALU.min)
                    return eng.tensor_relu(cstO[:, 0:C], cstA[:, 0:C])
                p.phase("vector", sp1)

                def sp2(eng):
                    eng.activation(cstB[:, 0:C], cstB[:, 0:C], AF.Exp)
                    return eng.activation(cstB[:, 0:C], cstB[:, 0:C], AF.Ln, bias=1.0)
                p.phase("scalar", sp2)

                def sp3(eng, dt_=dtb16[i], ones_c=ones_c):
                    eng.tensor_tensor(cstA[:, 0:C], cstB[:, 0:C], cstO[:, 0:C], ALU.add)
                    eng.tensor_copy(dt_[:, :], cstA[:, 0:C])
                    return eng.tensor_tensor_scan(cstB[:, 0:C], ones_c[:, 0:C], cstA[:, 0:C],
                                                  0.0, ALU.mult, ALU.add)
                p.phase("vector", sp3)
                p.dma("gpsimd", [(ios["cto_" + D][dg * 128:(dg + 1) * 128, :], cstB[:, 0:C])])

                def cte1(eng, dg=dg, ct_=cte[D]):
                    return eng.tensor_copy(ct_[:, dg:dg + 1], cstB[:, C - 1:C])
                p.phase("vector", cte1)
                p.dma("sync", [(bf1[:, 0:C], ios["usp_" + D][dg * 128:(dg + 1) * 128, :])])

                def dtu1(eng, du_=dtu[i], dt_=dtb16[i], ya_=yacc[i], dg=dg, pr_=prm[D]):
                    eng.tensor_tensor(du_[:, :], dt_[:, :], bf1[:, 0:C], ALU.mult)
                    return eng.tensor_scalar(ya_[:, :], bf1[:, 0:C],
                                             pr_[:, 48 + dg:49 + dg],
                                             None, ALU.mult)
                p.phase("vector", dtu1)

            banks = [cm.bc_ps, cm.bc2_ps, cm.mm_ps[0], cm.mm_ps[1]]

            def mk_dma(si):
                pr2 = si % 2
                p.dma("sync", [(browB2[pr2][0:1, 0:C], BCst[si:si + 1, H:TC]),
                               (browC2[pr2][0:1, 0:C], BCst[16 + si:17 + si, H:TC])],
                      lag=_Ls(2) if si >= 2 else 0)

            def mk_bmm(si, lag):
                def bmm(eng, bB=browB2[si % 2]):
                    last = None
                    for bi, (o, w) in enumerate(chunks_of(C)):
                        last = eng.matmul(banks[bi][:, 0:w], cm.ones_row[0:1, :],
                                          bB[0:1, o:o + w], start=True, stop=True)
                    return last
                p.phase("tensor", bmm, lag=_Ls(lag))

            def mk_bev(si, lag):
                def bev(eng, B_=Bbc2[si % 2]):
                    for bi, (o, w) in enumerate(chunks_of(C)):
                        eng.copy(B_[:, o:o + w], banks[bi][:, 0:w])
                    return eng.copy(B_[:, C - 16:C], banks[3][:, 512 - 16:512])
                p.phase("scalar", bev, lag=_Ls(lag))

            def mk_cmm(si, lag):
                def cmm(eng, bC=browC2[si % 2]):
                    last = None
                    for bi, (o, w) in enumerate(chunks_of(C)):
                        last = eng.matmul(banks[bi][:, 0:w], cm.ones_row[0:1, :],
                                          bC[0:1, o:o + w], start=True, stop=True)
                    return last
                p.phase("tensor", cmm, lag=_Ls(lag))

            def mk_cev(si, lag):
                def cev(eng, C_=Cbc2[si % 2]):
                    for bi, (o, w) in enumerate(chunks_of(C)):
                        eng.copy(C_[:, o:o + w], banks[bi][:, 0:w])
                    return eng.copy(C_[:, C - 16:C], banks[3][:, 512 - 16:512])
                p.phase("scalar", cev, lag=_Ls(lag))

            def mk_sca(si, i, dg, lag):
                def sca(eng, dt_=dtb16[i], na_=nAt[D], col=dg * 16 + si, ca_=cstA2[i % 2]):
                    eng.activation(ca_[:, 0:C], dt_[:, :], AF.Exp,
                                   scale=na_[:, col:col + 1])
                    return eng.activation(ca_[:, C - 16:C], dt_[:, C - 16:C], AF.Exp,
                                          scale=na_[:, col:col + 1])
                p.phase("scalar", sca, lag=_Ls(lag))

            def mk_scb(si, i, dg):
                def scb(eng, du_=dtu[i], is_=Ist[D], B_=Bbc2[si % 2],
                        C_=Cbc2[si % 2], col=dg * 16 + si, ca_=cstA2[i % 2], co_=cstO2s[i % 2]):
                    eng.tensor_tensor(cstB[:, 0:C], du_[:, :], B_[:, 0:C], ALU.mult)
                    eng.tensor_tensor_scan(co_[:, 0:C], ca_[:, 0:C], cstB[:, 0:C],
                                           0.0, ALU.mult, ALU.add)
                    eng.tensor_copy(is_[:, col:col + 1], co_[:, C - 1:C])
                    return eng.tensor_tensor(co_[:, 0:C], co_[:, 0:C], C_[:, 0:C], ALU.mult)
                p.phase("vector", scb, lag=_Lsy(1))

            def mk_yadd(i, lag=1):
                def yadd(eng, ya_=yacc[i], co_=cstO2s[i % 2]):
                    return eng.tensor_tensor(ya_[:, :], ya_[:, :], co_[:, 0:C], ALU.add)
                p.phase("gpsimd", yadd, lag=_Lsy(lag))

            mk_dma(0)
            mk_bmm(0, 0); mk_bev(0, 0); mk_cmm(0, 0); mk_cev(0, 0)
            mk_dma(1)
            for s in range(16):
                mk_sca(s, 0, dgs[0], lag=_Ls(5) if s > 0 else 0)
                if s > 0:
                    mk_yadd(3)
                mk_scb(s, 0, dgs[0])
                if s + 1 < 16:
                    mk_bmm(s + 1, 3)
                mk_sca(s, 1, dgs[1], lag=2)
                mk_yadd(0)
                mk_scb(s, 1, dgs[1])
                if s + 1 < 16:
                    mk_bev(s + 1, 3)
                mk_sca(s, 2, dgs[2], lag=2)
                mk_yadd(1)
                mk_scb(s, 2, dgs[2])
                if s + 1 < 16:
                    mk_cmm(s + 1, 3)
                if s + 2 < 16:
                    mk_dma(s + 2)
                mk_sca(s, 3, dgs[3], lag=2)
                mk_yadd(2)
                mk_scb(s, 3, dgs[3])
                if s + 1 < 16:
                    mk_cev(s + 1, 3)
            mk_yadd(3, lag=0)
            for i, dg in enumerate(dgs):
                p.dma("sync", [(bf2[:, 0:C], ios["szo_" + D][dg * 128:(dg + 1) * 128, :])])

                def yp(eng, ya_=yacc[i]):
                    return eng.tensor_tensor(bf1[:, 0:C], ya_[:, :], bf2[:, 0:C], ALU.mult)
                p.phase("vector", yp)
                p.dma("gpsimd", [(ios["ypost_" + D][dg * 128:(dg + 1) * 128, :], bf1[:, 0:C])])
            p.free(*dtb16, *dtu, *yacc, *Bbc2, *Cbc2, cstO2s[1])

        def ecalc(eng, D=D):
            return eng.tensor_tensor(Est[D][:, :].rearrange("p (a b) -> p a b", b=16),
                                     nAt[D][:, :].rearrange("p (a b) -> p a b", b=16),
                                     cte[D][:, :].rearrange("p (a b) -> p a b", b=1).broadcast_to([128, 16, 16]),
                                     ALU.mult)
        p.phase("vector", ecalc)

        def ecalc2(eng, D=D):
            return eng.activation(Est[D][:, :], Est[D][:, :], AF.Exp)
        p.phase("scalar", ecalc2)
        p.dma("gpsimd", [(ios["Io_" + D], Ist[D][:, :]), (ios["Eo_" + D], Est[D][:, :])])
        p.free(ones_c, BCst, dtrB)
    return p


def _l1_wt(I, tag, l):
    pre = f"L1_attn_{tag}_"
    nw = _r8(I[pre + "norm_w"][l]); nb = _r8(I[pre + "norm_b"][l])
    out = {}
    out["nwb"] = np.ascontiguousarray(np.concatenate([nw, nb], 1))
    out["inWT"] = np.ascontiguousarray(I[pre + "in_W"][l].T)
    cw = I[pre + "conv_w"][l]
    out["cwb"] = np.ascontiguousarray(cw.reshape(16, 128, 4).transpose(1, 0, 2).reshape(128, 64))
    out["cbb"] = _r16(I[pre + "conv_b"][l])
    out["xpWT"] = np.ascontiguousarray(I[pre + "xproj_W"][l].T)
    out["dtWT"] = np.ascontiguousarray(I[pre + "dt_W"][l].T)
    out["dtbb"] = _r16(I[pre + "dt_b"][l])
    al = I[pre + "Alog"][l]
    out["nA"] = np.ascontiguousarray(al.reshape(16, 128, 16).transpose(1, 0, 2).reshape(128, 256))
    out["Dpb"] = _r16(I[pre + "D"][l])
    out["owT"] = np.ascontiguousarray(I[pre + "out_W"][l].T)
    return out


def _concat_tail(prev, own, tail_w=H):
    if prev is None:
        t = np.zeros((own.shape[0], tail_w), own.dtype)
    else:
        t = prev[:, -tail_w:]
    return np.ascontiguousarray(np.concatenate([t, own], axis=1))


def _iord(res_list, key, r):
    out = np.zeros((1024, 256), np.float32)
    for j in range(min(r, 8)):
        out[j * 128:(j + 1) * 128, :] = res_list[j][key]
    return out


def _eord(res_list, key, r):
    out = np.ones((1024, 256), np.float32)
    for j in range(min(r, 8)):
        out[j * 128:(j + 1) * 128, :] = res_list[j][key]
    return out


def run_l1_chain(I, h3_full):
    h3b = np.ascontiguousarray(h3_full[:, ::-1])
    sh_f = shard_tokens(h3_full)
    sh_b = shard_tokens(h3b)
    wts = {("f", l): _l1_wt(I, "fw", l) for l in range(4)}
    wts.update({("b", l): _l1_wt(I, "bw", l) for l in range(4)})

    def weight_map(l):
        m = {}
        for D in ("f", "b"):
            w = wts[(D, l)]
            for k in ("nwb", "inWT", "cwb", "cbb", "xpWT", "dtWT", "dtbb", "nA", "Dpb"):
                m[k + "_" + D] = w[k]
        return m

    def prev_map(l, res, r):
        m = {}
        for D in ("f", "b"):
            w = wts[(D, l - 1)]
            pr = res[r - 1] if r > 0 else None
            m["ypc_" + D] = _concat_tail(None if pr is None else pr["ypost_" + D], res[r]["ypost_" + D])
            m["szc_" + D] = _concat_tail(None if pr is None else pr["szo_" + D], res[r]["szo_" + D])
            m["ctc_" + D] = _concat_tail(None if pr is None else pr["cto_" + D], res[r]["cto_" + D])
            m["crc_" + D] = _concat_tail(None if pr is None else pr["cro_" + D], res[r]["cro_" + D])
            m["IoO_" + D] = _iord([res[q] for q in range(NCORES)], "Io_" + D, r)
            m["EoO_" + D] = _eord([res[q] for q in range(NCORES)], "Eo_" + D, r)
            m["IoP_" + D] = _iord([res[q] for q in range(NCORES)], "Io_" + D, r - 1 if r > 0 else 0)
            m["EoP_" + D] = _eord([res[q] for q in range(NCORES)], "Eo_" + D, r - 1 if r > 0 else 0)
            m["nAP_" + D] = w["nA"]
            m["owTP_" + D] = w["owT"]
        return m

    maps = []
    for r in range(NCORES):
        m = dict(rT_f=sh_f[r], rT_b=sh_b[r],
                 hmask=(np.zeros((128, H), np.float32) if r == 0 else np.ones((128, H), np.float32)))
        m.update(weight_map(0))
        maps.append(m)
    res = run_prog("l1s_first", lambda: build_l1s("first"), maps)

    for l in (1, 2, 3):
        maps = []
        for r in range(NCORES):
            m = dict(rT_f=res[r]["ro_f"], rT_b=res[r]["ro_b"],
                     hmask=(np.zeros((128, H), np.float32) if r == 0 else np.ones((128, H), np.float32)))
            m.update(weight_map(l))
            m.update(prev_map(l, res, r))
            maps.append(m)
        res = run_prog("l1s_mid", lambda: build_l1s("mid"), maps)

    nf = np.ascontiguousarray(np.concatenate([_r8(I["L1_attn_normf_w"]), _r8(I["L1_attn_normf_b"])], 1))
    maps = []
    for r in range(NCORES):
        m = dict(rT_f=res[r]["ro_f"], rT_b=res[r]["ro_b"], nfb_f=nf, nfb_b=nf,
                 hmask=(np.zeros((128, H), np.float32) if r == 0 else np.ones((128, H), np.float32)))
        m.update(prev_map(4, res, r))
        maps.append(m)
    resf = run_prog("l1s_last", lambda: build_l1s("last"), maps)
    fh2 = np.concatenate([resf[r]["fh2_f"] for r in range(NCORES)], axis=1)
    bh2f = np.concatenate([resf[r]["fh2_b"] for r in range(NCORES)], axis=1)
    bh2u = np.ascontiguousarray(bh2f[:, ::-1])
    return fh2, bh2u, res, resf


def emit_part_max(p, cm, tiles, width, out_ap, pmx, pmd):
    def pm1(eng, tiles=tiles):
        for i, t in enumerate(tiles):
            eng.tensor_reduce(cm.sq[:, i:i + 1].rearrange("p (a b) -> p a b", b=1),
                              t[:, 0:width].rearrange("p (a b) -> p a b", b=width),
                              AX.X, ALU.max, apply_absolute_value=True)
        return eng.tensor_reduce(pmx[:, 0:1].rearrange("p (a b) -> p a b", b=1),
                                 cm.sq[:, 0:len(tiles)].rearrange("p (a b) -> p a b", b=len(tiles)),
                                 AX.X, ALU.max)
    p.phase("vector", pm1)
    p.dma("sync", [(pmd[:, :], pmx[:, 0:1])])
    p.dma("sync", [(pmx[0:1, 0:128], pmd.rearrange("a b -> b a"))], allow_slow=True)

    def pm2(eng):
        return eng.tensor_reduce(pmx[0:1, 128:129].rearrange("p (a b) -> p a b", b=1),
                                 pmx[0:1, 0:128].rearrange("p (a b) -> p a b", b=128),
                                 AX.X, ALU.max)
    p.phase("vector", pm2)
    p.dma("gpsimd", [(out_ap, pmx[0:1, 128:129])])


def build_l1m():
    p = Prog("l1m")
    fh2T = p.inp("fh2T", [1024, C])
    bh2T = p.inp("bh2T", [1024, C])
    x2T = p.inp("x2T", [1024, C])
    cT8b = p.inp("cT8b", [128, 8])
    wmodb = p.inp("wmodb", [1024, 6144])
    WfT = p.inp("WfT", [1024, 1024])
    WbT = p.inp("WbT", [1024, 1024])
    mb8 = p.inp("mb8", [128, 8])
    x3_o = p.out("x3out", [1024, C])
    h4_o = p.out("h4out", [1024, C])
    pm_o = p.out("pmax", [1, 1])

    cm = Common(p, width=C)
    web = p.sbuf("web", [128, 16 * 128], BF16)
    mbs = p.sbuf("mbs", [128, 8], F32)
    pmx = p.sbuf("pmx", [128, 129], F32)
    pmd = p.dram("pmd", [128, 1])
    fhb = [p.sbuf(f"fhb{k}", [128, C], BF16) for k in range(8)]
    bhb = [p.sbuf(f"bhb{k}", [128, C], BF16) for k in range(8)]
    x3 = [p.sbuf(f"x3_{k}", [128, C], F32) for k in range(8)]
    xst = p.sbuf("xst", [128, C], F32)
    modx = emit_mod_vectors(p, cm, cT8b, wmodb, tag="m")
    p.dma("gpsimd", [(fhb[k][:, :], fh2T[k * 128:(k + 1) * 128, :]) for k in range(8)])
    p.dma("gpsimd", [(bhb[k][:, :], bh2T[k * 128:(k + 1) * 128, :]) for k in range(8)])
    p.dma("sync", [(mbs[:, :], mb8)])

    for m in range(8):
        p.dma("gpsimd", [(web[:, k * 128:(k + 1) * 128],
                          WfT[k * 128:(k + 1) * 128, m * 128:(m + 1) * 128]) for k in range(8)]
              + [(web[:, (8 + k) * 128:(9 + k) * 128],
                  WbT[k * 128:(k + 1) * 128, m * 128:(m + 1) * 128]) for k in range(8)])
        p.dma("sync", [(xst[:, :], x2T[m * 128:(m + 1) * 128, :])])
        for (o, w) in CC:
            def mm(eng, m=m, o=o, w=w):
                last = None
                for k in range(8):
                    eng.matmul(cm.mm_ps[0][:, 0:w], web[:, k * 128:(k + 1) * 128],
                               fhb[k][:, o:o + w], start=(k == 0), stop=False)
                    last = eng.matmul(cm.mm_ps[0][:, 0:w], web[:, (8 + k) * 128:(9 + k) * 128],
                                      bhb[k][:, o:o + w], start=False, stop=(k == 7))
                return last
            p.phase("tensor", mm)

            def ev(eng, o=o, w=w):
                return eng.copy(cm.bcs1[:, 0:w], cm.mm_ps[0][:, 0:w])
            p.phase("scalar", ev)

            def fin(eng, m=m, o=o, w=w):
                eng.tensor_scalar_add(cm.bcs1[:, 0:w], cm.bcs1[:, 0:w], mbs[:, m:m + 1])
                return eng.scalar_tensor_tensor(x3[m][:, o:o + w], cm.bcs1[:, 0:w],
                                                modx[:, 16 + m:17 + m], xst[:, o:o + w],
                                                ALU.mult, ALU.add)
            p.phase("vector", fin)
        p.dma("gpsimd", [(x3_o[m * 128:(m + 1) * 128, :], x3[m][:, :])])

    emit_ln(p, cm, "lnm", x3, x3, C,
            gamma=[modx[:, 56 + k:57 + k] for k in range(8)],
            beta=[modx[:, 24 + k:25 + k] for k in range(8)])
    p.dma("gpsimd", [(h4_o[m * 128:(m + 1) * 128, :], x3[m][:, :]) for m in range(8)])
    emit_part_max(p, cm, x3, C, pm_o, pmx, pmd)
    return p


_GMN = [0]


def emit_gmax_scale(p, cm, maxes_in, s128, rs128, qmax, smin):
    _GMN[0] += 1
    m8 = p.sbuf("gm8_" + str(_GMN[0]), [1, 8], F32)
    p.dma("sync", [(m8[:, :], maxes_in)])

    def g1(eng):
        eng.tensor_reduce(s128[0:1, 0:1].rearrange("p (a b) -> p a b", b=1),
                          m8[0:1, 0:8].rearrange("p (a b) -> p a b", b=8), AX.X, ALU.max)
        eng.tensor_scalar_mul(s128[0:1, 0:1], s128[0:1, 0:1], float(np.float32(1.0 / qmax)))
        eng.tensor_scalar_max(s128[0:1, 0:1], s128[0:1, 0:1], smin)
        eng.tensor_copy(rs128[0:1, 0:1], s128[0:1, 0:1])
        return eng.reciprocal(rs128[0:1, 0:1], rs128[0:1, 0:1])
    p.phase("vector", g1)
    emit_bcast_row(p, s128, s128, width=1)
    emit_bcast_row(p, rs128, rs128, width=1)


def emit_qdq(eng, dst, src_ap, w, rs_ap, s_ap, iqt, f32t, lo, hi):
    eng.tensor_scalar(f32t[:, 0:w], src_ap, rs_ap, None, ALU.mult)
    eng.tensor_copy(iqt[:, 0:w], f32t[:, 0:w])
    eng.tensor_copy(f32t[:, 0:w], iqt[:, 0:w])
    eng.tensor_scalar_min(f32t[:, 0:w], f32t[:, 0:w], float(hi))
    eng.tensor_scalar_max(f32t[:, 0:w], f32t[:, 0:w], float(lo))
    return eng.tensor_scalar(dst, f32t[:, 0:w], s_ap, None, ALU.mult)


def build_f1():
    p = Prog("f1")
    h4T = p.inp("h4T", [1024, C])
    maxes = p.inp("maxes", [1, 8])
    wup = p.inp("wup", [3072, 1024])
    wgate = p.inp("wgate", [3072, 1024])
    up_o = p.out("up", [3072, C])
    gate_o = p.out("gate", [3072, C])
    pmu_o = p.out("pmu", [1, 1])
    pmg_o = p.out("pmg", [1, 1])

    cm = Common(p, width=C)
    s128 = p.sbuf("s128", [128, 1], F32)
    rs128 = p.sbuf("rs128", [128, 1], F32)
    iqt = p.sbuf("iqt", [128, C], I32)
    f32t = p.sbuf("f32t", [128, C], F32)
    wst = p.sbuf("wst", [128, 1024], F32)
    sw = p.sbuf("sw", [128, 32], F32)
    rsw = p.sbuf("rsw", [128, 32], F32)
    wqb = p.sbuf("wqb", [128, 1024], BF16)
    wqT = p.sbuf("wqT", [128, 1024], BF16)
    xqb = [p.sbuf(f"xqb{k}", [128, C], BF16) for k in range(8)]
    ust = p.sbuf("ust", [128, C], F32)
    pmx = p.sbuf("pmx", [128, 129], F32)
    pmacc = [p.sbuf("pma_u", [128, 24], F32), p.sbuf("pma_g", [128, 24], F32)]
    pmd = p.dram("pmd", [128, 1])

    emit_gmax_scale(p, cm, maxes, s128, rs128, 32767.0, 1e-12)

    for k in range(8):
        p.dma("sync", [(f32t[:, 0:C] if False else ust[:, :], h4T[k * 128:(k + 1) * 128, :])])

        def xq(eng, k=k):
            return emit_qdq(eng, xqb[k][:, :], ust[:, :], C, rs128[:, 0:1], s128[:, 0:1],
                            iqt, f32t, -32768.0, 32767.0)
        p.phase("vector", xq)

    for wi, (w_in, out_d, pma) in enumerate(((wup, up_o, pmacc[0]), (wgate, gate_o, pmacc[1]))):
        for m in range(24):
            p.dma("sync", [(wst[:, :], w_in[m * 128:(m + 1) * 128, :])])

            def dq(eng, m=m):
                eng.tensor_reduce(sw[:, :].rearrange("p (a b) -> p a b", b=1),
                                  wst[:, :].rearrange("p (a b) -> p a b", b=32),
                                  AX.X, ALU.max, apply_absolute_value=True)
                eng.tensor_scalar_mul(sw[:, :], sw[:, :], float(np.float32(1.0 / 7.0)))
                eng.tensor_scalar_max(sw[:, :], sw[:, :], 1e-12)
                eng.tensor_copy(rsw[:, :], sw[:, :])
                eng.reciprocal(rsw[:, :], rsw[:, :])
                eng.tensor_tensor(f32t[:, 0:1024].rearrange("p (a b) -> p a b", b=32),
                                  wst[:, :].rearrange("p (a b) -> p a b", b=32),
                                  rsw[:, :].rearrange("p (a b) -> p a b", b=1).broadcast_to([128, 32, 32]),
                                  ALU.mult)
                eng.tensor_copy(iqt[:, 0:1024], f32t[:, 0:1024])
                eng.tensor_copy(f32t[:, 0:1024], iqt[:, 0:1024])
                eng.tensor_scalar_min(f32t[:, 0:1024], f32t[:, 0:1024], 7.0)
                eng.tensor_scalar_max(f32t[:, 0:1024], f32t[:, 0:1024], -8.0)
                return eng.tensor_tensor(wqb[:, :].rearrange("p (a b) -> p a b", b=32),
                                         f32t[:, 0:1024].rearrange("p (a b) -> p a b", b=32),
                                         sw[:, :].rearrange("p (a b) -> p a b", b=1).broadcast_to([128, 32, 32]),
                                         ALU.mult)
            p.phase("vector", dq)
            for k in range(8):
                def trf(eng, k=k):
                    return [eng.dma_start(out=wqT[:, k * 128:(k + 1) * 128],
                                          in_=wqb[:, k * 128:(k + 1) * 128], transpose=True)]
                p.phase("sync", trf, n_dma=1)
            for (o, w) in CC:
                def umm(eng, o=o, w=w):
                    last = None
                    for k in range(8):
                        last = eng.matmul(cm.mm_ps[0][:, 0:w], wqT[:, k * 128:(k + 1) * 128],
                                          xqb[k][:, o:o + w], start=(k == 0), stop=(k == 7))
                    return last
                p.phase("tensor", umm)

                def uev(eng, o=o, w=w):
                    return eng.copy(ust[:, o:o + w], cm.mm_ps[0][:, 0:w])
                p.phase("scalar", uev)

            def upm(eng, m=m, pma=pma):
                return eng.tensor_reduce(pma[:, m:m + 1].rearrange("p (a b) -> p a b", b=1),
                                         ust[:, :].rearrange("p (a b) -> p a b", b=C),
                                         AX.X, ALU.max, apply_absolute_value=True)
            p.phase("vector", upm)
            p.dma("gpsimd", [(out_d[m * 128:(m + 1) * 128, :], ust[:, :])])

    for pma, po in ((pmacc[0], pmu_o), (pmacc[1], pmg_o)):
        def red(eng, pma=pma):
            return eng.tensor_reduce(pmx[:, 0:1].rearrange("p (a b) -> p a b", b=1),
                                     pma[:, :].rearrange("p (a b) -> p a b", b=24),
                                     AX.X, ALU.max)
        p.phase("vector", red)
        p.dma("sync", [(pmd[:, :], pmx[:, 0:1])])
        p.dma("sync", [(pmx[0:1, 0:128], pmd.rearrange("a b -> b a"))], allow_slow=True)

        def red2(eng):
            return eng.tensor_reduce(pmx[0:1, 128:129].rearrange("p (a b) -> p a b", b=1),
                                     pmx[0:1, 0:128].rearrange("p (a b) -> p a b", b=128),
                                     AX.X, ALU.max)
        p.phase("vector", red2)
        p.dma("gpsimd", [(po, pmx[0:1, 128:129])])
    return p


def build_f2():
    p = Prog("f2")
    gate_i = p.inp("gate", [3072, C])
    maxg = p.inp("maxg", [1, 8])
    t_o = p.out("tout", [3072, C])
    pmt_o = p.out("pmt", [1, 1])
    cm = Common(p, width=C)
    s128 = p.sbuf("s128", [128, 1], F32)
    rs128 = p.sbuf("rs128", [128, 1], F32)
    iqt = p.sbuf("iqt", [128, C], I32)
    f32t = p.sbuf("f32t", [128, C], F32)
    gst = p.sbuf("gst", [128, C], F32)
    gq = p.sbuf("gq", [128, C], F32)
    sg = p.sbuf("sg", [128, C], F32)
    pma = p.sbuf("pma", [128, 24], F32)
    pmx = p.sbuf("pmx", [128, 129], F32)
    pmd = p.dram("pmd", [128, 1])
    emit_gmax_scale(p, cm, maxg, s128, rs128, 32767.0, 1e-12)
    for m in range(24):
        p.dma("sync", [(gst[:, :], gate_i[m * 128:(m + 1) * 128, :])])

        def q1(eng):
            return emit_qdq(eng, gq[:, 0:C], gst[:, :], C, rs128[:, 0:1], s128[:, 0:1],
                            iqt, f32t, -32768.0, 32767.0)
        p.phase("vector", q1)

        def sg1(eng):
            return eng.activation(sg[:, :], gq[:, :], AF.Exp, scale=-1.0)
        p.phase("scalar", sg1)

        def sg2(eng):
            eng.tensor_scalar_add(sg[:, :], sg[:, :], 1.0)
            eng.reciprocal(sg[:, :], sg[:, :])
            eng.tensor_scalar_mul(f32t[:, :], sg[:, :], 65536.0)
            eng.tensor_copy(iqt[:, :], f32t[:, :])
            eng.tensor_copy(f32t[:, :], iqt[:, :])
            eng.tensor_scalar_min(f32t[:, :], f32t[:, :], 65535.0)
            eng.tensor_scalar_max(f32t[:, :], f32t[:, :], 0.0)
            eng.tensor_scalar_mul(sg[:, :], f32t[:, :], float(np.float32(1.0 / 65536.0)))
            eng.tensor_tensor(gq[:, :], gq[:, :], sg[:, :], ALU.mult)
            return eng.tensor_reduce(pma[:, m:m + 1].rearrange("p (a b) -> p a b", b=1),
                                     gq[:, :].rearrange("p (a b) -> p a b", b=C),
                                     AX.X, ALU.max, apply_absolute_value=True)
        p.phase("vector", sg2)
        p.dma("gpsimd", [(t_o[m * 128:(m + 1) * 128, :], gq[:, :])])

    def red(eng):
        return eng.tensor_reduce(pmx[:, 0:1].rearrange("p (a b) -> p a b", b=1),
                                 pma[:, :].rearrange("p (a b) -> p a b", b=24),
                                 AX.X, ALU.max)
    p.phase("vector", red)
    p.dma("sync", [(pmd[:, :], pmx[:, 0:1])])
    p.dma("sync", [(pmx[0:1, 0:128], pmd.rearrange("a b -> b a"))], allow_slow=True)

    def red2(eng):
        return eng.tensor_reduce(pmx[0:1, 128:129].rearrange("p (a b) -> p a b", b=1),
                                 pmx[0:1, 0:128].rearrange("p (a b) -> p a b", b=128),
                                 AX.X, ALU.max)
    p.phase("vector", red2)
    p.dma("gpsimd", [(pmt_o, pmx[0:1, 128:129])])
    return p


def build_f3():
    p = Prog("f3")
    t_i = p.inp("tin", [3072, C])
    up_i = p.inp("upin", [3072, C])
    maxt = p.inp("maxt", [1, 8])
    maxu = p.inp("maxu", [1, 8])
    o_o = p.out("opre", [3072, C])
    pmo_o = p.out("pmo", [1, 1])
    cm = Common(p, width=C)
    sT = p.sbuf("sT", [128, 1], F32); rsT = p.sbuf("rsT", [128, 1], F32)
    sU = p.sbuf("sU", [128, 1], F32); rsU = p.sbuf("rsU", [128, 1], F32)
    iqt = p.sbuf("iqt", [128, C], I32)
    f32t = p.sbuf("f32t", [128, C], F32)
    ast = p.sbuf("ast", [128, C], F32)
    bst = p.sbuf("bst", [128, C], F32)
    g2 = p.sbuf("g2", [128, C], F32)
    pma = p.sbuf("pma", [128, 24], F32)
    pmx = p.sbuf("pmx", [128, 129], F32)
    pmd = p.dram("pmd", [128, 1])
    emit_gmax_scale(p, cm, maxt, sT, rsT, 32767.0, 1e-12)
    emit_gmax_scale(p, cm, maxu, sU, rsU, 32767.0, 1e-12)
    for m in range(24):
        p.dma("sync", [(ast[:, :], t_i[m * 128:(m + 1) * 128, :]),
                       (bst[:, :], up_i[m * 128:(m + 1) * 128, :])])

        def q1(eng, m=m):
            emit_qdq(eng, g2[:, 0:C], ast[:, :], C, rsT[:, 0:1], sT[:, 0:1],
                     iqt, f32t, -32768.0, 32767.0)
            emit_qdq(eng, bst[:, 0:C], bst[:, :], C, rsU[:, 0:1], sU[:, 0:1],
                     iqt, f32t, -32768.0, 32767.0)
            eng.tensor_tensor(g2[:, :], g2[:, :], bst[:, :], ALU.mult)
            return eng.tensor_reduce(pma[:, m:m + 1].rearrange("p (a b) -> p a b", b=1),
                                     g2[:, :].rearrange("p (a b) -> p a b", b=C),
                                     AX.X, ALU.max, apply_absolute_value=True)
        p.phase("vector", q1)
        p.dma("gpsimd", [(o_o[m * 128:(m + 1) * 128, :], g2[:, :])])

    def red(eng):
        return eng.tensor_reduce(pmx[:, 0:1].rearrange("p (a b) -> p a b", b=1),
                                 pma[:, :].rearrange("p (a b) -> p a b", b=24),
                                 AX.X, ALU.max)
    p.phase("vector", red)
    p.dma("sync", [(pmd[:, :], pmx[:, 0:1])])
    p.dma("sync", [(pmx[0:1, 0:128], pmd.rearrange("a b -> b a"))], allow_slow=True)

    def red2(eng):
        return eng.tensor_reduce(pmx[0:1, 128:129].rearrange("p (a b) -> p a b", b=1),
                                 pmx[0:1, 0:128].rearrange("p (a b) -> p a b", b=128),
                                 AX.X, ALU.max)
    p.phase("vector", red2)
    p.dma("gpsimd", [(pmo_o, pmx[0:1, 128:129])])
    return p


def build_f4():
    p = Prog("f4")
    o_i = p.inp("oin", [3072, C])
    maxo = p.inp("maxo", [1, 8])
    wd_i = p.inp("wd", [1024, 3072])
    x3T = p.inp("x3T", [1024, C])
    cT8b = p.inp("cT8b", [128, 8])
    wmodb = p.inp("wmodb", [1024, 6144])
    x4_o = p.out("x4out", [1024, C])
    cm = Common(p, width=C)
    s128 = p.sbuf("s128", [128, 1], F32)
    rs128 = p.sbuf("rs128", [128, 1], F32)
    iqt = p.sbuf("iqt", [128, 3072], I32)
    f32t = p.sbuf("f32t", [128, 3072], F32)
    wst = p.sbuf("wst", [128, 3072], F32)
    sw = p.sbuf("sw", [128, 96], F32)
    rsw = p.sbuf("rsw", [128, 96], F32)
    wqb = p.sbuf("wqb", [128, 3072], BF16)
    wqT = p.sbuf("wqT", [128, 3072], BF16)
    ost = p.sbuf("ost", [128, C], F32)
    oqb = p.sbuf("oqb", [128, C], BF16)
    ostr = p.sbuf("ostr", [128, 24 * 512], BF16)
    xst = p.sbuf("xst", [128, C], F32)
    oq_d = p.dram("oq_d", [3072, C], BF16)
    modx = emit_mod_vectors(p, cm, cT8b, wmodb, tag="m")
    emit_gmax_scale(p, cm, maxo, s128, rs128, 32767.0, 1e-12)
    for k in range(24):
        p.dma("sync", [(ost[:, :], o_i[k * 128:(k + 1) * 128, :])])

        def oq(eng):
            return emit_qdq(eng, oqb[:, 0:C], ost[:, :], C, rs128[:, 0:1], s128[:, 0:1],
                            iqt, f32t, -32768.0, 32767.0)
        p.phase("vector", oq)
        p.dma("gpsimd", [(oq_d[k * 128:(k + 1) * 128, :], oqb[:, :])])

    for m in range(8):
        p.dma("sync", [(wst[:, :], wd_i[m * 128:(m + 1) * 128, :])])

        def dq(eng, m=m):
            eng.tensor_reduce(sw[:, :].rearrange("p (a b) -> p a b", b=1),
                              wst[:, :].rearrange("p (a b) -> p a b", b=32),
                              AX.X, ALU.max, apply_absolute_value=True)
            eng.tensor_scalar_mul(sw[:, :], sw[:, :], float(np.float32(1.0 / 7.0)))
            eng.tensor_scalar_max(sw[:, :], sw[:, :], 1e-12)
            eng.tensor_copy(rsw[:, :], sw[:, :])
            eng.reciprocal(rsw[:, :], rsw[:, :])
            eng.tensor_tensor(f32t[:, :].rearrange("p (a b) -> p a b", b=32),
                              wst[:, :].rearrange("p (a b) -> p a b", b=32),
                              rsw[:, :].rearrange("p (a b) -> p a b", b=1).broadcast_to([128, 96, 32]),
                              ALU.mult)
            eng.tensor_copy(iqt[:, :], f32t[:, :])
            eng.tensor_copy(f32t[:, :], iqt[:, :])
            eng.tensor_scalar_min(f32t[:, :], f32t[:, :], 7.0)
            eng.tensor_scalar_max(f32t[:, :], f32t[:, :], -8.0)
            return eng.tensor_tensor(wqb[:, :].rearrange("p (a b) -> p a b", b=32),
                                     f32t[:, :].rearrange("p (a b) -> p a b", b=32),
                                     sw[:, :].rearrange("p (a b) -> p a b", b=1).broadcast_to([128, 96, 32]),
                                     ALU.mult)
        p.phase("vector", dq)
        for k in range(24):
            def trf(eng, k=k):
                return [eng.dma_start(out=wqT[:, k * 128:(k + 1) * 128],
                                      in_=wqb[:, k * 128:(k + 1) * 128], transpose=True)]
            p.phase("sync", trf, n_dma=1)
        p.dma("sync", [(xst[:, :], x3T[m * 128:(m + 1) * 128, :])])
        for (o, w) in CC:
            p.dma("sync", [(ostr[:, k * 512:k * 512 + w], oq_d[k * 128:(k + 1) * 128, o:o + w])
                           for k in range(24)])

            def dmm(eng, o=o, w=w):
                last = None
                for k in range(24):
                    last = eng.matmul(cm.mm_ps[0][:, 0:w], wqT[:, k * 128:(k + 1) * 128],
                                      ostr[:, k * 512:k * 512 + w], start=(k == 0), stop=(k == 23))
                return last
            p.phase("tensor", dmm)

            def dev(eng, o=o, w=w):
                return eng.copy(ost[:, o:o + w], cm.mm_ps[0][:, 0:w])
            p.phase("scalar", dev)

        def fin(eng, m=m):
            return eng.scalar_tensor_tensor(ost[:, :], ost[:, :], modx[:, 40 + m:41 + m],
                                            xst[:, :], ALU.mult, ALU.add)
        p.phase("vector", fin)
        p.dma("gpsimd", [(x4_o[m * 128:(m + 1) * 128, :], ost[:, :])])
    return p



def kernel(**I):
    I = {k: np.asarray(v) for k, v in I.items()}

    res_a = run_prog("l0a", build_l0a, l0a_inputs(I))
    res_b = run_prog("l0b", build_l0b, l0b_inputs(I, res_a))
    x1_shards = [res_b[r]["x1out"] for r in range(NCORES)]
    res_c = run_prog("l0c", build_l0c, l0c_inputs(I, x1_shards))
    h3_full = np.ascontiguousarray(np.concatenate([res_c[r]["h3out"] for r in range(NCORES)], axis=1))

    fh2, bh2u, _, _ = run_l1_chain(I, h3_full)

    mW = I["L1_attn_merge_W"]
    WfT = np.ascontiguousarray(mW[:, 0:1024].T)
    WbT = np.ascontiguousarray(mW[:, 1024:2048].T)
    mb8 = _r8(I["L1_attn_merge_b"])
    cT8b = _r8(I["L1_c"][0])
    maps = []
    for r in range(NCORES):
        lo = r * C
        maps.append(dict(fh2T=np.ascontiguousarray(fh2[:, lo:lo + C]),
                         bh2T=np.ascontiguousarray(bh2u[:, lo:lo + C]),
                         x2T=res_c[r]["x2out"], cT8b=cT8b, wmodb=I["L1_wmod"],
                         WfT=WfT, WbT=WbT, mb8=mb8))
    res_m = run_prog("l1m", build_l1m, maps)

    maxes = np.ascontiguousarray(np.array([[res_m[r]["pmax"][0, 0] for r in range(NCORES)]],
                                          np.float32))
    maps = [dict(h4T=res_m[r]["h4out"], maxes=maxes, wup=I["L1_ffn_w_up"],
                 wgate=I["L1_ffn_w_gate"]) for r in range(NCORES)]
    res_1 = run_prog("f1", build_f1, maps)

    maxg = np.ascontiguousarray(np.array([[res_1[r]["pmg"][0, 0] for r in range(NCORES)]], np.float32))
    maps = [dict(gate=res_1[r]["gate"], maxg=maxg) for r in range(NCORES)]
    res_2 = run_prog("f2", build_f2, maps)

    maxu = np.ascontiguousarray(np.array([[res_1[r]["pmu"][0, 0] for r in range(NCORES)]], np.float32))
    maxt = np.ascontiguousarray(np.array([[res_2[r]["pmt"][0, 0] for r in range(NCORES)]], np.float32))
    maps = [dict(tin=res_2[r]["tout"], upin=res_1[r]["up"], maxt=maxt, maxu=maxu)
            for r in range(NCORES)]
    res_3 = run_prog("f3", build_f3, maps)

    maxo = np.ascontiguousarray(np.array([[res_3[r]["pmo"][0, 0] for r in range(NCORES)]], np.float32))
    maps = [dict(oin=res_3[r]["opre"], maxo=maxo, wd=I["L1_ffn_w_down"],
                 x3T=res_m[r]["x3out"], cT8b=cT8b, wmodb=I["L1_wmod"]) for r in range(NCORES)]
    res_4 = run_prog("f4", build_f4, maps)

    full = np.concatenate([res_4[r]["x4out"] for r in range(NCORES)], axis=1)
    return np.ascontiguousarray(full.T)[None, :, :]
